# version 10
# baseline (speedup 1.0000x reference)
import math
import numpy as np
import ml_dtypes
import contextlib
import concourse.bass as bass
import concourse.mybir as mybir
from concourse.bass_utils import run_bass_kernel_spmd

F32 = mybir.dt.float32
BF16 = mybir.dt.bfloat16
I8 = mybir.dt.int8
U8 = mybir.dt.uint8
AF = mybir.ActivationFunctionType
OP = mybir.AluOpType

T = 1024
D = 2048
H = 16
NTB = 8
NKC = 16
NCG = 4
NIB = 64
EPS = 1e-5
SEG = 1024
NSEG = 4
KB = 1024


def _host_consts(core_id):
    h = np.arange(H, dtype=np.float64)
    gamma = 1.0 - np.exp2(-5.0 - h)
    seg = core_id % NSEG
    pos0 = seg * SEG

    inv = 10000.0 ** (-(np.arange(64, dtype=np.float64) / 64.0))
    t_loc = np.arange(T, dtype=np.float64).reshape(NTB, 128)
    ang = (pos0 + t_loc)[:, :, None] * inv[None, None, :]
    cos = np.cos(ang)
    sin = np.sin(ang)

    def tmtab(a, scale=1.0):
        return np.ascontiguousarray(
            (a * scale).transpose(1, 0, 2).reshape(128, NTB * 64)).astype(np.float32)

    qs = 1.0 / math.sqrt(128.0)
    idx = np.arange(128, dtype=np.float64)
    rel = idx[None, :] - idx[:, None]
    dmatT = np.where(rel >= 0, gamma[:, None, None] ** rel[None], 0.0)
    dmatT = np.ascontiguousarray(
        dmatT.transpose(1, 0, 2).reshape(128, H * 128)).astype(np.float32)
    kdec = (gamma[None, :] ** (127.0 - idx[:, None])).astype(np.float32)
    gC = np.broadcast_to(gamma ** 128.0, (128, H)).astype(np.float32).copy()
    cross_pp = (gamma[None, :] ** (idx[:, None] + 1.0)).astype(np.float32)
    cs = np.zeros((128, NTB * H), dtype=np.float64)
    for c in range(NTB):
        cs[:, c * H:(c + 1) * H] = gamma[None, :] ** (c * 128.0 + idx[:, None] + 1.0)
    crossseg = cs.astype(np.float32)
    wj = np.zeros((128, NSEG * H), dtype=np.float64)
    for j in range(seg):
        wj[:, j * H:(j + 1) * H] = gamma[None, :] ** (1024.0 * (seg - 1 - j))
    wj = wj.astype(np.float32)
    ident = np.eye(128, dtype=ml_dtypes.bfloat16)
    return dict(cosq=tmtab(cos, qs), sinq=tmtab(sin, qs),
                cosk=tmtab(cos), sink=tmtab(sin), dmatT=dmatT,
                kdec=kdec, gC=gC, cross_pp=cross_pp, crossseg=crossseg,
                wj=wj, ident=ident,
                epsc=np.broadcast_to(np.array([EPS, 1.0 / 128.0], np.float32),
                                     (128, 2)).copy())


def build(dump=(), max_phase=99):
    nc = bass.Bass("TRN2", target_bir_lowering=False, debug=False)

    x_h = nc.dram_tensor("x", [T, D], F32, kind="ExternalInput").ap()
    out_h = nc.dram_tensor("out", [T, D], F32, kind="ExternalOutput").ap()
    wq_h = nc.dram_tensor("wq", [NCG, 4, 128, D], F32, kind="ExternalInput").ap()
    wk_h = nc.dram_tensor("wk", [NCG, 4, 128, D], F32, kind="ExternalInput").ap()
    wv_h = nc.dram_tensor("wv", [NCG, 4, 128, D], F32, kind="ExternalInput").ap()
    wg_h = nc.dram_tensor("wg", [NCG, 4, 128, D], F32, kind="ExternalInput").ap()
    wo_h = nc.dram_tensor("wo", [NCG, 4, 128, D], F32, kind="ExternalInput").ap()
    wgate_h = nc.dram_tensor("wgate_l", [16, 128, 16, 512], I8, kind="ExternalInput").ap()
    wup_h = nc.dram_tensor("wup_l", [16, 128, 16, 512], I8, kind="ExternalInput").ap()
    wdown_h = nc.dram_tensor("wdown_l", [2, 16, 128, 4096], I8, kind="ExternalInput").ap()
    sgate_h = nc.dram_tensor("sgate_pp", [128, NIB], F32, kind="ExternalInput").ap()
    sup_h = nc.dram_tensor("sup_pp", [128, NIB], F32, kind="ExternalInput").ap()
    sdown_h = nc.dram_tensor("sdown_rep", [128, D], F32, kind="ExternalInput").ap()
    gnw_h = nc.dram_tensor("gnw_rep", [128, 128], F32, kind="ExternalInput").ap()
    gnwpp_h = nc.dram_tensor("gnw_pp", [128, 1], F32, kind="ExternalInput").ap()
    cosq_h = nc.dram_tensor("cosq", [128, NTB * 64], F32, kind="ExternalInput").ap()
    sinq_h = nc.dram_tensor("sinq", [128, NTB * 64], F32, kind="ExternalInput").ap()
    cosk_h = nc.dram_tensor("cosk", [128, NTB * 64], F32, kind="ExternalInput").ap()
    sink_h = nc.dram_tensor("sink", [128, NTB * 64], F32, kind="ExternalInput").ap()
    dmat_h = nc.dram_tensor("dmatT", [128, H * 128], F32, kind="ExternalInput").ap()
    kdec_h = nc.dram_tensor("kdec", [128, H], F32, kind="ExternalInput").ap()
    gC_h = nc.dram_tensor("gC", [128, H], F32, kind="ExternalInput").ap()
    crpp_h = nc.dram_tensor("cross_pp", [128, H], F32, kind="ExternalInput").ap()
    crsg_h = nc.dram_tensor("crossseg", [128, NTB * H], F32, kind="ExternalInput").ap()
    wj_h = nc.dram_tensor("wj", [128, NSEG * H], F32, kind="ExternalInput").ap()
    ident_h = nc.dram_tensor("ident", [128, 128], BF16, kind="ExternalInput").ap()
    epsc_h = nc.dram_tensor("epsc", [128, 2], F32, kind="ExternalInput").ap()

    cc_in = nc.dram_tensor("cc_in", [128, D], BF16)
    cc_out = nc.dram_tensor("cc_out", [NSEG * 128, D], BF16)

    dump_h = {}
    for name, shape, dt in (
        ("d_xnfm", [128, NKC * T], BF16), ("d_qtm", [128, NTB * D], BF16),
        ("d_ktm", [128, NTB * D], BF16), ("d_vtm", [128, NTB * D], BF16),
        ("d_gtm", [128, NTB * D], BF16), ("d_oraw", [128, NTB * D], BF16),
        ("d_S", [128, D], F32), ("d_sinit", [128, D], BF16),
        ("d_gated", [128, H * T], BF16), ("d_xn2fm", [128, NKC * T], BF16),
    ):
        if name in dump:
            dump_h[name] = nc.dram_tensor(name, shape, dt, kind="ExternalOutput").ap()

    bufA = nc.alloc_sbuf_tensor("bufA", [128, 32 * KB], U8)
    bufB = nc.alloc_sbuf_tensor("bufB", [128, 32 * KB], U8)
    bufC = nc.alloc_sbuf_tensor("bufC", [128, 32 * KB], U8)
    bufD = nc.alloc_sbuf_tensor("bufD", [128, 32 * KB], U8)
    bufE = nc.alloc_sbuf_tensor("bufE", [128, 32 * KB], U8)
    bufW = nc.alloc_sbuf_tensor("bufW", [128, 38 * KB], U8)
    bufS = nc.alloc_sbuf_tensor("bufS", [128, 4 * KB], U8)

    def view(buf, off_b, dt, n):
        esz = {F32: 4, BF16: 2, I8: 1, U8: 1}[dt]
        return buf.ap()[:, off_b:off_b + n * esz].bitcast(dt)

    q_tm = view(bufA, 0, BF16, NTB * D)
    xn2_fm = view(bufA, 0, BF16, NKC * T)
    k_tm = view(bufB, 0, BF16, NTB * D)
    Tall = view(bufW, 8 * KB, BF16, 3 * D)
    gated_fm = view(bufB, 0, BF16, H * T)
    fuseA = view(bufB, 0, BF16, 32 * 512)
    v_tm = view(bufC, 0, BF16, NTB * D)
    fuseB = view(bufC, 0, BF16, 32 * 512)
    g_tm = view(bufD, 0, BF16, NTB * D)
    xn_fm = view(bufE, 0, BF16, NKC * T)
    o_raw = view(bufE, 0, BF16, NTB * D)

    sdown_sb = view(bufD, 0, F32, D)
    wo_stage = [view(bufD, 8 * KB + 2 * KB * i, F32, 512) for i in range(2)]
    silu_sb11 = [view(bufD, 8 * KB + 1 * KB * i, BF16, 512) for i in range(4)]
    xst = [view(bufD, 12 * KB + 8 * KB * i, F32, D) for i in range(2)]
    xn2tm_st = view(bufD, 28 * KB, BF16, D)

    x_stage = [view(bufW, 8 * KB * i, F32, D) for i in range(2)]
    xntm_st = view(bufD, 0, BF16, D)
    wsl_f32 = [view(bufW, 8 * KB * i, F32, D) for i in range(2)]
    wsl_bf = [view(bufW, 16 * KB, BF16, D), view(bufW, 20 * KB, BF16, D),
              view(bufW, 0, BF16, D), view(bufW, 4 * KB, BF16, D)]
    rope_tabs = view(bufW, 24 * KB, F32, 4 * 512)
    rp_st = [view(bufW, 32 * KB + KB * i, BF16, 512) for i in range(4)]
    rope_t1 = view(bufW, 36 * KB, BF16, 256)
    rope_t2 = view(bufW, 36 * KB + 512, BF16, 256)
    dmatT_sb = view(bufW, 0, F32, H * 128)
    S_sb = view(bufW, 8 * KB, F32, D)
    Sb_sb = view(bufW, 16 * KB, BF16, D)
    qf_sb = view(bufW, 20 * KB, BF16, D)
    kf_sb = view(bufW, 24 * KB, BF16, D)
    amt_sb = view(bufW, 28 * KB, BF16, D)
    kp_sb = view(bufW, 32 * KB, BF16, D)
    sg_tmp = view(bufW, 36 * KB, F32, 512)
    sinit_sb = view(bufW, 8 * KB, BF16, D)
    gated_c = view(bufW, 0, BF16, D)
    o_f = view(bufW, 24 * KB, BF16, D)
    sq_sb = view(bufW, 28 * KB, BF16, D)
    sgs_sb = view(bufW, 32 * KB, BF16, D)
    gu_bfg2 = [view(bufE, 16 * KB * i, BF16, 8192) for i in range(2)]
    gu_bfu2 = [view(bufW, 16 * KB * i, BF16, 8192) for i in range(2)]
    wd_bf = [view(bufD, 12 * KB + 8 * KB * i, BF16, 4096) for i in range(2)]
    dstage = [view(bufW, 32 * KB + 2 * KB * i, F32, 512) for i in range(2)]

    ident_sb = view(bufS, 0, BF16, 128)
    kdec_sb = view(bufS, 256, F32, H)
    gC_sb = view(bufS, 320, F32, H)
    crpp_sb = view(bufS, 384, F32, H)
    ssq_sb = view(bufS, 448, F32, H)
    rinv_sb = view(bufS, 512, F32, H)
    stats_sb = view(bufS, 576, F32, 24)
    mv_sb = view(bufS, 672, F32, 2)
    sd_sb = view(bufS, 680, F32, 2)
    inv_sb = view(bufS, 688, F32, 2)
    b2_sb = view(bufS, 696, F32, 2)
    crsg_sb = view(bufS, 704, F32, NTB * H)
    wj_sb = view(bufS, 1216, F32, NSEG * H)
    gnw_sb = view(bufS, 1472, F32, 128)
    sgate_sb = view(bufS, 1984, F32, NIB)
    sup_sb = view(bufS, 2240, F32, NIB)
    sd2_sb = view(bufS, 2496, F32, H)
    epsc_sb = view(bufS, 2560, F32, 2)
    gnwpp_sb = view(bufS, 2568, F32, 1)

    PSt = nc.alloc_psum_tensor("PSall", [128, 8 * 512], F32)
    PS = PSt.ap()
    PSb = PS.bitcast(BF16)

    def psf(bank, nbanks=1):
        return PS[:, bank * 512:(bank + nbanks) * 512]

    stack = contextlib.ExitStack()
    sems = {}
    cnt = {}
    for name in ("sxc", "sxa", "sxb", "swa", "swb", "sio", "sioa", "siob",
                 "sgda", "sgdb", "sgdc", "sgdd", "scc", "sv", "ss", "sg", "stt"):
        sems[name] = stack.enter_context(nc.semaphore(name))
        cnt[name] = 0

    def inc(instr, sem, n=1):
        cnt[sem] += n
        instr.then_inc(sems[sem], n)

    def dma(eng, sem, out, in_, **kw):
        eng.wait_ge(sems[sem], cnt[sem])
        cnt[sem] += 16
        eng.dma_start(out=out, in_=in_, **kw).then_inc(sems[sem], 16)
        return cnt[sem]

    block = stack.enter_context(nc.Block())
    S_, Sc, V_, Te, G_ = block.sync, block.scalar, block.vector, block.tensor, block.gpsimd

    def f(g):
        dma(g, "sio", out_h, x_h)
    G_(f)

    def f(s):
        dma(s, "sxc", ident_sb, ident_h)
        dma(s, "sxc", epsc_sb, epsc_h)
        dma(s, "sxc", kdec_sb, kdec_h)
        dma(s, "sxc", gC_sb, gC_h)
        dma(s, "sxc", crpp_sb, crpp_h)
        dma(s, "sxc", crsg_sb, crsg_h)
        dma(s, "sxc", wj_sb, wj_h)
        dma(s, "sxc", gnw_sb, gnw_h)
        dma(s, "sxc", gnwpp_sb, gnwpp_h)
        dma(s, "sxc", sgate_sb, sgate_h)
        dma(s, "sxc", sup_sb, sup_h)
        dma(s, "sxc", rope_tabs[:, 0:512], cosq_h)
        dma(s, "sxc", rope_tabs[:, 512:1024], sinq_h)
        dma(s, "sxc", rope_tabs[:, 1024:1536], cosk_h)
        dma(s, "sxc", rope_tabs[:, 1536:2048], sink_h)
    S_(f)
    const_done = cnt["sxc"]

    def ln_transpose(src_dram, fm_dst, stage_pair, tmst, dma_gate, tensor_gate):
        xsem = {tb: ("sxa", "sxb")[tb % 2] for tb in range(NTB)}
        sx0 = {"sxa": cnt["sxa"], "sxb": cnt["sxb"]}
        sv0 = cnt["sv"]
        ss0 = cnt["ss"]
        st0 = cnt["stt"]
        ld_p = {tb: sx0[xsem[tb]] + 16 * (tb // 2 + 1) for tb in range(NTB)}
        stats_p = {tb: sv0 + 3 * tb + 1 for tb in range(NTB)}
        inv_p = {tb: sv0 + 3 * tb + 2 for tb in range(NTB)}
        evac_p = {tb: sv0 + 3 * tb + 3 for tb in range(NTB)}
        sd_p = {tb: ss0 + 2 * tb + 1 for tb in range(NTB)}
        app_p = {tb: ss0 + 2 * tb + 2 for tb in range(NTB)}
        tr_p = {tb: st0 + tb + 1 for tb in range(NTB)}

        def f(s):
            for tb in range(NTB):
                if dma_gate is not None:
                    sm, vv = dma_gate(tb)
                    s.wait_ge(sems[sm], vv)
                if tb >= 2:
                    s.wait_ge(sems["ss"], app_p[tb - 2])
                got = dma(s, xsem[tb], stage_pair[tb % 2],
                          src_dram[tb * 128:(tb + 1) * 128, :])
                assert got == ld_p[tb]
        S_(f)

        def f(v):
            for tb in range(NTB):
                v.wait_ge(sems[xsem[tb]], ld_p[tb])
                if tb >= 1:
                    v.wait_ge(sems["ss"], app_p[tb - 1])
                xs = stage_pair[tb % 2]
                for i in range(4):
                    v.bn_stats(out=stats_sb[:, 6 * i:6 * i + 6],
                               in_=xs[:, 512 * i:512 * (i + 1)])
                v.drain()
                v.bn_aggr(out=mv_sb[:, 0:2], in_=stats_sb[:, 0:24])
                v.drain()
                inc(v.nop(), "sv")
                assert cnt["sv"] == stats_p[tb]
                v.wait_ge(sems["ss"], sd_p[tb])
                v.reciprocal(out=inv_sb[:, 0:1], in_=sd_sb[:, 0:1])
                v.drain()
                v.scalar_tensor_tensor(out=b2_sb[:, 0:1], in0=mv_sb[:, 0:1], scalar=-1.0,
                                       in1=inv_sb[:, 0:1], op0=OP.mult, op1=OP.mult)
                v.drain()
                inc(v.nop(), "sv")
                assert cnt["sv"] == inv_p[tb]
                v.wait_ge(sems["stt"], tr_p[tb])
                for half in range(2):
                    src = PSb[:, (4 + half) * 1024:(5 + half) * 1024]
                    dst = fm_dst.rearrange("p (kc t) -> p kc t", kc=NKC)[
                        :, half * 8:(half + 1) * 8, tb * 128:tb * 128 + 128]
                    v.tensor_copy(out=dst, in_=src.rearrange("p (kc t) -> p kc t", kc=8))
                v.drain()
                inc(v.nop(), "sv")
                assert cnt["sv"] == evac_p[tb]
        V_(f)

        def f(sc):
            sc.wait_ge(sems["sxc"], const_done)
            for tb in range(NTB):
                sc.wait_ge(sems["sv"], stats_p[tb])
                sc.activation(out=sd_sb[:, 0:1], in_=mv_sb[:, 1:2], func=AF.Sqrt,
                              bias=epsc_sb[:, 0:1])
                sc.drain()
                inc(sc.nop(), "ss")
                assert cnt["ss"] == sd_p[tb]
                sc.wait_ge(sems["sv"], inv_p[tb])
                if tb >= 1:
                    sc.wait_ge(sems["stt"], tr_p[tb - 1])
                sc.activation(out=tmst[:, 0:D], in_=stage_pair[tb % 2], func=AF.Identity,
                              bias=b2_sb[:, 0:1], scale=inv_sb[:, 0:1])
                sc.drain()
                inc(sc.nop(), "ss")
                assert cnt["ss"] == app_p[tb]
        Sc(f)

        def f(te):
            te.wait_ge(sems["sxc"], const_done)
            if tensor_gate is not None:
                te.wait_ge(sems[tensor_gate[0]], tensor_gate[1])
            for tb in range(NTB):
                te.wait_ge(sems["ss"], app_p[tb])
                if tb >= 1:
                    te.wait_ge(sems["sv"], evac_p[tb - 1])
                for kc in range(NKC):
                    bank = 4 + kc // 8
                    te.transpose(out=PSb[:, bank * 1024 + (kc % 8) * 128:
                                         bank * 1024 + (kc % 8) * 128 + 128],
                                 in_=tmst[:, kc * 128:(kc + 1) * 128], identity=ident_sb)
                inc(te.drain(), "stt")
                assert cnt["stt"] == tr_p[tb]
        Te(f)
        return evac_p

    xn_evac = ln_transpose(x_h, xn_fm, x_stage, xntm_st, None, None)
    last_xn = xn_evac[NTB - 1]

    def projection(w_dram, kind, dst_tm=None, first_gate=None, slab_gate=None):
        wsem = {i: ("sgda", "sgdb")[i % 2] for i in range(16)}
        sx0 = {"sgda": cnt["sgda"], "sgdb": cnt["sgdb"]}
        st0, sv0, ss0, sxc0 = cnt["stt"], cnt["sv"], cnt["ss"], cnt["sxc"]
        io0a, io0b = cnt["sioa"], cnt["siob"]
        ld_p = {i: sx0[wsem[i]] + 16 * (i // 2 + 1) for i in range(16)}
        mm_p = {i: st0 + i + 1 for i in range(16)}
        ev_p = {(cg, tb): sv0 + cg * 8 + tb + 1 for cg in range(NCG) for tb in range(NTB)}
        cp_p = {(cg, tb): ss0 + cg * 8 + tb + 1 for cg in range(NCG) for tb in range(NTB)}
        if kind == "wo":
            asem = {n: ("sioa", "siob")[n % 2] for n in range(32)}
            acc_p = {n: (io0a if n % 2 == 0 else io0b) + 16 * (n // 2 + 1)
                     for n in range(32)}
        bank_sem = "sv" if kind == "wo" else "ss"
        bank_p = ev_p if kind == "wo" else cp_p if kind in ("q", "k") else cp_p

        def emit_slab(g, i):
            cg, kcg = divmod(i, 4)
            if i < 4:
                if slab_gate is not None and slab_gate[i] is not None:
                    sm, vv = slab_gate[i]
                    g.wait_ge(sems[sm], vv)
                g.wait_ge(sems["sxc"], sxc0)
            else:
                g.wait_ge(sems["stt"], mm_p[i - 4])
            got = dma(g, wsem[i], wsl_bf[i % 4][:, 0:D], w_dram[cg, kcg])
            assert got == ld_p[i]

        if kind != "wo":
            def f(g):
                for i in range(16):
                    emit_slab(g, i)
            G_(f)
        else:
            def emit_acc(g, cg):
                for tb in range(NTB):
                    n = cg * 8 + tb
                    g.wait_ge(sems["sv"], ev_p[(cg, tb)])
                    got = dma(g, asem[n],
                              out_h[tb * 128:(tb + 1) * 128, cg * 512:(cg + 1) * 512],
                              wo_stage[n % 2][:, 0:512], accum_op=OP.add)
                    assert got == acc_p[n]

            def f(g):
                g.wait_ge(sems["sio"], 16)
                for i in range(6):
                    emit_slab(g, i)
                emit_acc(g, 0)
                for i in (6, 7):
                    emit_slab(g, i)
                emit_acc(g, 1)
                for i in (8, 9, 10, 11):
                    emit_slab(g, i)
                emit_acc(g, 2)
                for i in range(12, 16):
                    emit_slab(g, i)
                emit_acc(g, 3)
            G_(f)

        lhs = gated_fm if kind == "wo" else xn_fm

        def f(te):
            te.wait_ge(sems["sv"], sv0)
            te.wait_ge(sems["ss"], ss0)
            for cg in range(NCG):
                for kcg in range(4):
                    i = cg * 4 + kcg
                    te.wait_ge(sems[wsem[i]], ld_p[i])
                    for tb in range(NTB):
                        if kcg == 0:
                            if cg > 0:
                                te.wait_ge(sems[bank_sem], bank_p[(cg - 1, tb)])
                            elif first_gate is not None:
                                sm, vv = first_gate[tb]
                                te.wait_ge(sems[sm], vv)
                        for k4 in range(4):
                            kc = kcg * 4 + k4
                            te.matmul(psf(tb),
                                      lhs[:, kc * T + tb * 128: kc * T + tb * 128 + 128],
                                      wsl_bf[i % 4][:, k4 * 512:(k4 + 1) * 512],
                                      start=(kc == 0), stop=(kc == NKC - 1))
                    inc(te.drain(), "stt")
                    assert cnt["stt"] == mm_p[i]
        Te(f)

        if kind in ("q", "k"):
            coff = 0 if kind == "q" else 1024

            def f(sc):
                for cg in range(NCG):
                    sc.wait_ge(sems["stt"], mm_p[cg * 4 + 3])
                    for tb in range(NTB):
                        idx = cg * 8 + tb
                        if idx >= 4:
                            pcg, ptb = divmod(idx - 4, 8)
                            sc.wait_ge(sems["sv"], ev_p[(pcg, ptb)])
                        sc.copy(out=rp_st[tb % 4][:, 0:512], in_=psf(tb))
                        sc.drain()
                        inc(sc.nop(), "ss")
                        assert cnt["ss"] == cp_p[(cg, tb)]
            Sc(f)

            def f(v):
                v.wait_ge(sems["sxc"], const_done)
                for cg in range(NCG):
                    for tb in range(NTB):
                        v.wait_ge(sems["ss"], cp_p[(cg, tb)])
                        ps = rp_st[tb % 4][:, 0:512]
                        cosT = rope_tabs[:, coff + tb * 64: coff + tb * 64 + 64]
                        sinT = rope_tabs[:, coff + 512 + tb * 64: coff + 512 + tb * 64 + 64]
                        x1 = ps.rearrange("p (h j) -> p h j", h=4)[:, :, 0:64]
                        x2 = ps.rearrange("p (h j) -> p h j", h=4)[:, :, 64:128]
                        cB = cosT.rearrange("p (a j) -> p a j", a=1).broadcast_to([128, 4, 64])
                        sB = sinT.rearrange("p (a j) -> p a j", a=1).broadcast_to([128, 4, 64])
                        t1 = rope_t1[:, 0:256].rearrange("p (h j) -> p h j", h=4)
                        t2 = rope_t2[:, 0:256].rearrange("p (h j) -> p h j", h=4)
                        d1 = dst_tm[:, tb * D + cg * 512: tb * D + (cg + 1) * 512]
                        o1 = d1.rearrange("p (h j) -> p h j", h=4)[:, :, 0:64]
                        o2 = d1.rearrange("p (h j) -> p h j", h=4)[:, :, 64:128]
                        v.tensor_tensor(out=t1, in0=x1, in1=cB, op=OP.mult)
                        v.tensor_tensor(out=t2, in0=x2, in1=sB, op=OP.mult)
                        v.drain()
                        v.tensor_tensor(out=o1, in0=t1, in1=t2, op=OP.subtract)
                        v.drain()
                        v.tensor_tensor(out=t1, in0=x2, in1=cB, op=OP.mult)
                        v.tensor_tensor(out=t2, in0=x1, in1=sB, op=OP.mult)
                        v.drain()
                        v.tensor_tensor(out=o2, in0=t1, in1=t2, op=OP.add)
                        v.drain()
                        inc(v.nop(), "sv")
                        assert cnt["sv"] == ev_p[(cg, tb)]
            V_(f)
            return dict(ev=cp_p, sem="ss", rope=ev_p, mm=mm_p)
        elif kind in ("v", "g"):
            def f(sc):
                for cg in range(NCG):
                    sc.wait_ge(sems["stt"], mm_p[cg * 4 + 3])
                    for tb in range(NTB):
                        sc.copy(out=dst_tm[:, tb * D + cg * 512: tb * D + (cg + 1) * 512],
                                in_=psf(tb))
                        sc.drain()
                        inc(sc.nop(), "ss")
                        assert cnt["ss"] == cp_p[(cg, tb)]
            Sc(f)
            return dict(ev=cp_p, sem="ss", mm=mm_p)
        else:
            def f(v):
                for cg in range(NCG):
                    v.wait_ge(sems["stt"], mm_p[cg * 4 + 3])
                    for tb in range(NTB):
                        n = cg * 8 + tb
                        if n >= 2:
                            v.wait_ge(sems[asem[n - 2]], acc_p[n - 2])
                        v.tensor_copy(out=wo_stage[n % 2][:, 0:512], in_=psf(tb))
                        v.drain()
                        inc(v.nop(), "sv")
                        assert cnt["sv"] == ev_p[(cg, tb)]
            V_(f)
            return dict(ev=ev_p, sem="sv", mm=mm_p, acc=acc_p, asem=asem)

    def tb_gate(res):
        return {tb: (res["sem"], res["ev"][(3, tb)]) for tb in range(NTB)}

    if max_phase >= 3:
        pq = projection(wq_h, "q", q_tm,
                        first_gate={tb: ("sv", xn_evac[tb]) for tb in range(NTB)},
                        slab_gate=(None, None, ("sv", xn_evac[NTB - 1]),
                                   ("sv", xn_evac[NTB - 1])))
        def sg4(p):
            return tuple(("stt", p["mm"][12 + i]) for i in range(4))
        pk = projection(wk_h, "k", k_tm, first_gate=tb_gate(pq), slab_gate=sg4(pq))
        pv = projection(wv_h, "v", v_tm, first_gate=tb_gate(pk), slab_gate=sg4(pk))
        pg = projection(wg_h, "g", g_tm, first_gate=tb_gate(pv), slab_gate=sg4(pv))

    if "d_xnfm" in dump_h:
        def f(s):
            s.wait_ge(sems["sv"], last_xn)
            dma(s, "sxc", dump_h["d_xnfm"], xn_fm)
        S_(f)
    if max_phase >= 3:
        for nm, buf in (("d_qtm", q_tm), ("d_ktm", k_tm), ("d_vtm", v_tm), ("d_gtm", g_tm)):
            if nm in dump_h:
                def f(s, buf=buf):
                    s.wait_ge(sems["sv"], cnt["sv"])
                    s.wait_ge(sems["ss"], cnt["ss"])
                    dma(s, "sxc", dump_h[nm], buf)
                S_(f)

    if max_phase >= 5:
        def f(s):
            s.wait_ge(sems["ss"], cnt["ss"])
            s.wait_ge(sems["sv"], cnt["sv"])
            dma(s, "sxc", dmatT_sb, dmat_h)
        S_(f)
        dmat_ld = cnt["sxc"]

        def f(v):
            v.wait_ge(sems["sxc"], dmat_ld)
            v.memset(S_sb[:, 0:D], 0.0)
            v.drain()
            inc(v.nop(), "sv")
        V_(f)
        s_zero = cnt["sv"]

        sv5 = cnt["sv"]
        st5 = cnt["stt"]
        ss5 = cnt["ss"]
        kp_p, state_p = {}, {}
        snap_p = {}
        vv = sv5
        ssim = ss5
        for c in range(NTB):
            ssim += 1; snap_p[c] = ssim
            vv += 1; kp_p[c] = vv
            for grp in range(4):
                vv += 1; state_p[(c, grp)] = vv
        tu_p = {}
        tt = st5
        for c in range(NTB):
            for grp in range(4):
                tt += 1; tu_p[(c, grp)] = tt

        def f(sc):
            sc.wait_ge(sems["stt"], pg["mm"][15])
            for c in range(NTB):
                if c > 0:
                    sc.wait_ge(sems["sv"], state_p[(c - 1, 3)])
                else:
                    sc.wait_ge(sems["sv"], s_zero)
                sc.copy(out=o_raw[:, c * D:(c + 1) * D], in_=S_sb[:, 0:D])
                sc.drain()
                inc(sc.nop(), "ss")
                assert cnt["ss"] == snap_p[c]
        Sc(f)

        def f(te):
            te.wait_ge(sems["sv"], s_zero)
            te.wait_ge(sems["ss"], pg["ev"][(3, 7)])
            for c in range(NTB):
                te.wait_ge(sems["sv"], kp_p[c])
                for grp in range(4):
                    bank = 6 + grp % 2
                    if c > 0 or grp >= 2:
                        pc, pg_ = (c, grp - 2) if grp >= 2 else (c - 1, grp + 2)
                        te.wait_ge(sems["sv"], state_p[(pc, pg_)])
                    for hh in range(4):
                        h = grp * 4 + hh
                        te.matmul(psf(bank)[:, hh * 128:(hh + 1) * 128],
                                  kp_sb[:, h * 128:(h + 1) * 128],
                                  v_tm[:, c * D + h * 128:c * D + (h + 1) * 128],
                                  start=True, stop=True)
                    inc(te.drain(), "stt")
                    assert cnt["stt"] == tu_p[(c, grp)]
        Te(f)

        def f(v):
            for c in range(NTB):
                v.tensor_tensor(
                    out=kp_sb[:, 0:D].rearrange("p (h j) -> p h j", h=H),
                    in0=k_tm[:, c * D:(c + 1) * D].rearrange("p (h j) -> p h j", h=H),
                    in1=kdec_sb.rearrange("p (h a) -> p h a", a=1).broadcast_to(
                        [128, H, 128]),
                    op=OP.mult)
                v.drain()
                inc(v.nop(), "sv")
                assert cnt["sv"] == kp_p[c]
                for grp in range(4):
                    bank = 6 + grp % 2
                    v.wait_ge(sems["stt"], tu_p[(c, grp)])
                    if grp == 0:
                        v.wait_ge(sems["ss"], snap_p[c])
                    sl3 = S_sb[:, grp * 512:(grp + 1) * 512].rearrange(
                        "p (h j) -> p h j", h=4)
                    v.tensor_tensor(
                        out=sg_tmp[:, 0:512].rearrange("p (h j) -> p h j", h=4),
                        in0=sl3,
                        in1=gC_sb[:, grp * 4:(grp + 1) * 4].rearrange(
                            "p (h a) -> p h a", a=1).broadcast_to([128, 4, 128]),
                        op=OP.mult)
                    v.drain()
                    v.tensor_add(out=S_sb[:, grp * 512:(grp + 1) * 512],
                                 in0=sg_tmp[:, 0:512], in1=psf(bank))
                    v.drain()
                    inc(v.nop(), "sv")
                    assert cnt["sv"] == state_p[(c, grp)]
        V_(f)
        scan_done = state_p[(NTB - 1, 3)]

        if "d_S" in dump_h:
            def f(s):
                s.wait_ge(sems["sv"], scan_done)
                dma(s, "sxc", dump_h["d_S"], S_sb[:, 0:D])
            S_(f)

        dump_gate = cnt["sxc"]

        def f(g):
            g.wait_ge(sems["sv"], scan_done)
            g.wait_ge(sems["sxc"], dump_gate)
            dma(g, "sio", cc_in.ap(), S_sb[:, 0:D])
            g.wait_ge(sems["sio"], cnt["sio"])
            inc(g.collective_compute(
                "AllGather", OP.bypass,
                replica_groups=[[0, 1, 2, 3], [4, 5, 6, 7]],
                ins=[cc_in.ap().opt()], outs=[cc_out.ap().opt()]), "scc")
            g.wait_ge(sems["scc"], cnt["scc"])
            dma(g, "sio", Tall[:, 0:3 * D].rearrange("p (j c) -> p j c", j=3),
                cc_out.ap()[0:3 * 128, :].rearrange("(j p) c -> p j c", p=128))
            g.wait_ge(sems["sio"], cnt["sio"])
            for j in range(3):
                wjb = wj_sb[:, j * H:(j + 1) * H].rearrange(
                    "p (h a) -> p h a", a=1).broadcast_to([128, H, 128])
                tj = Tall[:, j * D:(j + 1) * D].rearrange("p (h j) -> p h j", h=H)
                g.tensor_tensor(out=tj, in0=tj, in1=wjb, op=OP.mult)
                g.drain()
            g.tensor_add(out=sinit_sb[:, 0:D], in0=Tall[:, 0:D], in1=Tall[:, D:2 * D])
            g.drain()
            g.tensor_add(out=sinit_sb[:, 0:D], in0=sinit_sb[:, 0:D],
                         in1=Tall[:, 2 * D:3 * D])
            g.drain()
            inc(g.nop(), "sg")
        G_(f)
        sinit_done = ("sg", cnt["sg"])

        sv5b = cnt["sv"]
        st5b = cnt["stt"]
        ss5b = cnt["ss"]
        evq2_p, evk2_p, amt_p, oin_p, oad_p = {}, {}, {}, {}, {}
        vv = sv5b
        ssim = ss5b
        for c in range(NTB):
            ssim += 1; evq2_p[c] = ssim
            if c > 0:
                vv += 1; oin_p[c] = vv
            ssim += 1; evk2_p[c] = ssim
            for grp in range(4):
                vv += 1; amt_p[(c, grp)] = vv
            vv += 1; oad_p[c] = vv
        tq2_p, tn2_p, tk2_p, ta_p, ti_p = {}, {}, {}, {}, {}
        tt = st5b
        for c in range(NTB):
            tt += 1; tq2_p[c] = tt
            if c > 0:
                tt += 1; tn2_p[c] = tt
            tt += 1; tk2_p[c] = tt
            for grp in range(4):
                tt += 1; ta_p[(c, grp)] = tt
            tt += 1; ti_p[c] = tt

        def f(te):
            te.wait_ge(sems["stt"], pg["mm"][15])
            te.wait_ge(sems["ss"], pg["ev"][(3, 7)])
            te.wait_ge(sems["sv"], scan_done)
            for c in range(NTB):
                if c > 0:
                    te.wait_ge(sems["ss"], evk2_p[c - 1])
                for h in range(H):
                    bank = 4 + h // 8
                    te.transpose(out=PSb[:, bank * 1024 + (h % 8) * 128:
                                         bank * 1024 + (h % 8) * 128 + 128],
                                 in_=q_tm[:, c * D + h * 128: c * D + (h + 1) * 128],
                                 identity=ident_sb)
                inc(te.drain(), "stt")
                assert cnt["stt"] == tq2_p[c]
                te.wait_ge(sems["ss"], evq2_p[c])
                if c > 0:
                    te.wait_ge(sems["sv"], oad_p[c - 1])
                    for h in range(H):
                        te.matmul(psf(h // 4)[:, (h % 4) * 128:(h % 4) * 128 + 128],
                                  qf_sb[:, h * 128:(h + 1) * 128],
                                  o_raw[:, c * D + h * 128:c * D + (h + 1) * 128],
                                  start=True, stop=True)
                    inc(te.drain(), "stt")
                    assert cnt["stt"] == tn2_p[c]
                for h in range(H):
                    bank = 4 + h // 8
                    te.transpose(out=PSb[:, bank * 1024 + (h % 8) * 128:
                                         bank * 1024 + (h % 8) * 128 + 128],
                                 in_=k_tm[:, c * D + h * 128: c * D + (h + 1) * 128],
                                 identity=ident_sb)
                inc(te.drain(), "stt")
                assert cnt["stt"] == tk2_p[c]
                te.wait_ge(sems["ss"], evk2_p[c])
                for grp in range(4):
                    bank = 6 + grp % 2
                    if c > 0 or grp >= 2:
                        pc, pg_ = (c, grp - 2) if grp >= 2 else (c - 1, grp + 2)
                        te.wait_ge(sems["sv"], amt_p[(pc, pg_)])
                    for hh in range(4):
                        h = grp * 4 + hh
                        te.matmul(psf(bank)[:, hh * 128:(hh + 1) * 128],
                                  kf_sb[:, h * 128:(h + 1) * 128],
                                  qf_sb[:, h * 128:(h + 1) * 128],
                                  start=True, stop=True)
                    inc(te.drain(), "stt")
                    assert cnt["stt"] == ta_p[(c, grp)]
                te.wait_ge(sems["sv"], amt_p[(c, 3)])
                if c > 0:
                    te.wait_ge(sems["sv"], oin_p[c])
                elif c == 0:
                    pass
                if c > 0:
                    te.wait_ge(sems["sv"], oad_p[c - 1])
                for h in range(H):
                    te.matmul(psf(h // 4)[:, (h % 4) * 128:(h % 4) * 128 + 128],
                              amt_sb[:, h * 128:(h + 1) * 128],
                              v_tm[:, c * D + h * 128:c * D + (h + 1) * 128],
                              start=True, stop=True)
                inc(te.drain(), "stt")
                assert cnt["stt"] == ti_p[c]
        Te(f)

        def f(sc):
            for c in range(NTB):
                sc.wait_ge(sems["stt"], tq2_p[c])
                sc.copy(out=qf_sb[:, 0:D], in_=PSb[:, 4 * 1024:6 * 1024])
                sc.drain()
                inc(sc.nop(), "ss")
                assert cnt["ss"] == evq2_p[c]
                sc.wait_ge(sems["stt"], tk2_p[c])
                sc.copy(out=kf_sb[:, 0:D], in_=PSb[:, 4 * 1024:6 * 1024])
                sc.drain()
                inc(sc.nop(), "ss")
                assert cnt["ss"] == evk2_p[c]
        Sc(f)

        def f(v):
            for c in range(NTB):
                if c > 0:
                    v.wait_ge(sems["stt"], tn2_p[c])
                    v.tensor_tensor(
                        out=o_raw[:, c * D:(c + 1) * D].rearrange(
                            "p (h j) -> p h j", h=H),
                        in0=PS[:, 0:2048].rearrange("p (h j) -> p h j", h=H),
                        in1=crpp_sb[:, 0:H].rearrange(
                            "p (h a) -> p h a", a=1).broadcast_to([128, H, 128]),
                        op=OP.mult)
                    v.drain()
                    inc(v.nop(), "sv")
                    assert cnt["sv"] == oin_p[c]
                for grp in range(4):
                    bank = 6 + grp % 2
                    v.wait_ge(sems["stt"], ta_p[(c, grp)])
                    v.tensor_tensor(out=amt_sb[:, grp * 512:(grp + 1) * 512],
                                    in0=psf(bank),
                                    in1=dmatT_sb[:, grp * 512:(grp + 1) * 512],
                                    op=OP.mult)
                    v.drain()
                    inc(v.nop(), "sv")
                    assert cnt["sv"] == amt_p[(c, grp)]
                v.wait_ge(sems["stt"], ti_p[c])
                if c > 0:
                    v.tensor_add(out=o_raw[:, c * D:(c + 1) * D],
                                 in0=PS[:, 0:2048], in1=o_raw[:, c * D:(c + 1) * D])
                else:
                    v.tensor_copy(out=o_raw[:, 0:D], in_=PS[:, 0:2048])
                v.drain()
                inc(v.nop(), "sv")
                assert cnt["sv"] == oad_p[c]
        V_(f)
        scan2_done = oad_p[NTB - 1]

        if "d_oraw" in dump_h:
            def f(s):
                s.wait_ge(sems["sv"], scan2_done)
                dma(s, "sxc", dump_h["d_oraw"], o_raw)
            S_(f)

    if max_phase >= 6:
        if "d_sinit" in dump_h:
            def f(s):
                s.wait_ge(sems[sinit_done[0]], sinit_done[1])
                dma(s, "sxc", dump_h["d_sinit"], sinit_sb[:, 0:D])
            S_(f)

    if max_phase >= 8:
        o_f2 = [o_f, view(bufW, 32 * KB, BF16, D)]
        sq2 = [sq_sb, view(bufW, 12 * KB, BF16, D)]
        gated2 = [gated_c, view(bufW, 4 * KB, BF16, D)]
        rinv2 = [rinv_sb, view(bufS, 3100, F32, H)]

        sv8 = cnt["sv"]
        ss8 = cnt["ss"]
        st8 = cnt["stt"]
        qf8_p, of8_p, ssq8_p, rinv8_p, gc8_p, gf8_p = {}, {}, {}, {}, {}, {}
        sq8_p, sd8_p, sil8_p = {}, {}, {}
        ssim = ss8
        for c in range(NTB):
            ssim += 1
            sil8_p[c] = ssim
        vv = sv8
        for c in range(NTB):
            ssim += 1; qf8_p[c] = ssim
            vv += 1; of8_p[c] = vv
            ssim += 1; sq8_p[c] = ssim
            vv += 1; ssq8_p[c] = vv
            ssim += 1; sd8_p[c] = ssim
            vv += 1; rinv8_p[c] = vv
            vv += 1; gc8_p[c] = vv
            ssim += 1; gf8_p[c] = ssim
        t8q_p, t8c_p, t8g_p = {}, {}, {}
        tt = st8
        for c in range(NTB):
            tt += 1; t8q_p[c] = tt
            tt += 1; t8c_p[c] = tt
            tt += 1; t8g_p[c] = tt

        def f(te):
            te.wait_ge(sems["sv"], scan2_done)
            for c in range(NTB):
                if c > 0:
                    te.wait_ge(sems["ss"], qf8_p[c - 1])
                for h in range(H):
                    bank = 4 + h // 8
                    te.transpose(out=PSb[:, bank * 1024 + (h % 8) * 128:
                                         bank * 1024 + (h % 8) * 128 + 128],
                                 in_=q_tm[:, c * D + h * 128: c * D + (h + 1) * 128],
                                 identity=ident_sb)
                inc(te.drain(), "stt")
                assert cnt["stt"] == t8q_p[c]
                te.wait_ge(sems["ss"], qf8_p[c])
                if c == 0:
                    te.wait_ge(sems[sinit_done[0]], sinit_done[1])
                if c > 0:
                    te.wait_ge(sems["sv"], of8_p[c - 1])
                for h in range(H):
                    te.matmul(psf(h // 4)[:, (h % 4) * 128:(h % 4) * 128 + 128],
                              qf_sb[:, h * 128:(h + 1) * 128],
                              sinit_sb[:, h * 128:(h + 1) * 128],
                              start=True, stop=True)
                inc(te.drain(), "stt")
                assert cnt["stt"] == t8c_p[c]
                te.wait_ge(sems["sv"], gc8_p[c])
                if c > 0:
                    te.wait_ge(sems["ss"], gf8_p[c - 1])
                for h in range(H):
                    bank = 6 + h // 8
                    te.transpose(out=PSb[:, bank * 1024 + (h % 8) * 128:
                                         bank * 1024 + (h % 8) * 128 + 128],
                                 in_=gated2[c % 2][:, h * 128:(h + 1) * 128],
                                 identity=ident_sb)
                inc(te.drain(), "stt")
                assert cnt["stt"] == t8g_p[c]
        Te(f)

        def f(v):
            for c in range(NTB):
                v.wait_ge(sems["stt"], t8c_p[c])
                if c > 1:
                    v.wait_ge(sems["ss"], sq8_p[c - 2])
                v.tensor_tensor(
                    out=o_f2[c % 2][:, 0:D].rearrange("p (h j) -> p h j", h=H),
                    in0=PS[:, 0:2048].rearrange("p (h j) -> p h j", h=H),
                    in1=crsg_sb[:, c * H:(c + 1) * H].rearrange(
                        "p (h a) -> p h a", a=1).broadcast_to([128, H, 128]),
                    op=OP.mult)
                v.drain()
                v.tensor_add(out=o_f2[c % 2][:, 0:D], in0=o_f2[c % 2][:, 0:D],
                             in1=o_raw[:, c * D:(c + 1) * D])
                v.drain()
                inc(v.nop(), "sv")
                assert cnt["sv"] == of8_p[c]
                v.wait_ge(sems["ss"], sq8_p[c])
                v.tensor_reduce(out=ssq_sb[:, 0:H],
                                in_=sq2[c % 2][:, 0:D].rearrange("p (h j) -> p h j", h=H),
                                axis=mybir.AxisListType.X, op=OP.add)
                v.drain()
                inc(v.nop(), "sv")
                assert cnt["sv"] == ssq8_p[c]
                v.wait_ge(sems["ss"], sd8_p[c])
                v.reciprocal(out=rinv2[c % 2][:, 0:H], in_=sd2_sb[:, 0:H])
                v.drain()
                inc(v.nop(), "sv")
                assert cnt["sv"] == rinv8_p[c]
                v.tensor_tensor(
                    out=sq2[c % 2][:, 0:D].rearrange("p (h j) -> p h j", h=H),
                    in0=o_f2[c % 2][:, 0:D].rearrange("p (h j) -> p h j", h=H),
                    in1=rinv2[c % 2][:, 0:H].rearrange(
                        "p (h a) -> p h a", a=1).broadcast_to([128, H, 128]),
                    op=OP.mult)
                v.drain()
                if c > 1:
                    v.wait_ge(sems["stt"], t8g_p[c - 2])
                v.wait_ge(sems["ss"], sil8_p[c])
                v.tensor_mul(out=gated2[c % 2][:, 0:D], in0=sq2[c % 2][:, 0:D],
                             in1=g_tm[:, c * D:(c + 1) * D])
                v.drain()
                inc(v.nop(), "sv")
                assert cnt["sv"] == gc8_p[c]
        V_(f)

        def f(sc):
            sc.wait_ge(sems["sv"], scan_done)
            for c in range(NTB):
                sc.activation(out=g_tm[:, c * D:(c + 1) * D],
                              in_=g_tm[:, c * D:(c + 1) * D], func=AF.Silu)
                sc.drain()
                inc(sc.nop(), "ss")
                assert cnt["ss"] == sil8_p[c]
            for c in range(NTB):
                sc.wait_ge(sems["stt"], t8q_p[c])
                sc.copy(out=qf_sb[:, 0:D], in_=PSb[:, 4 * 1024:6 * 1024])
                sc.drain()
                inc(sc.nop(), "ss")
                assert cnt["ss"] == qf8_p[c]
                sc.wait_ge(sems["sv"], of8_p[c])
                sc.activation(out=sq2[c % 2][:, 0:D], in_=o_f2[c % 2][:, 0:D],
                              func=AF.Square)
                sc.drain()
                inc(sc.nop(), "ss")
                assert cnt["ss"] == sq8_p[c]
                sc.wait_ge(sems["sv"], ssq8_p[c])
                sc.activation(out=sd2_sb[:, 0:H], in_=ssq_sb[:, 0:H], func=AF.Sqrt,
                              bias=epsc_sb[:, 0:1], scale=epsc_sb[:, 1:2])
                sc.drain()
                inc(sc.nop(), "ss")
                assert cnt["ss"] == sd8_p[c]
                sc.wait_ge(sems["stt"], t8g_p[c])
                dst = gated_fm.rearrange("p (h t) -> p h t", h=H)[
                    :, :, c * 128:(c + 1) * 128]
                sc.mul(out=dst[:, 0:8, :],
                       in_=PSb[:, 6 * 1024:7 * 1024].rearrange("p (h t) -> p h t", h=8),
                       mul=gnwpp_sb[:, 0:1])
                sc.mul(out=dst[:, 8:16, :],
                       in_=PSb[:, 7 * 1024:8 * 1024].rearrange("p (h t) -> p h t", h=8),
                       mul=gnwpp_sb[:, 0:1])
                sc.drain()
                inc(sc.nop(), "ss")
                assert cnt["ss"] == gf8_p[c]
        Sc(f)
        gated_done = gf8_p[NTB - 1]
        t8c_last = t8c_p[NTB - 1]
        t8g_last = t8g_p[NTB - 1]

        if "d_gated" in dump_h:
            def f(s):
                s.wait_ge(sems["ss"], gated_done)
                dma(s, "sxc", dump_h["d_gated"], gated_fm)
            S_(f)

    if max_phase >= 9:
        r9 = projection(wo_h, "wo", None,
                        first_gate={tb: ("ss", gated_done) for tb in range(NTB)},
                        slab_gate=(("stt", t8c_last), ("stt", t8c_last),
                                   ("stt", t8g_last), ("stt", t8g_last)))
        acc9_p = r9["acc"]

    if max_phase >= 10:
        def gate10(tb):
            n = 3 * 8 + tb
            return (r9["asem"][n], acc9_p[n])
        xn2_evac = ln_transpose(out_h, xn2_fm, xst, xn2tm_st, gate10,
                                ("sv", r9["ev"][(3, 7)]))
        xn2_done = xn2_evac[NTB - 1]

        if "d_xn2fm" in dump_h:
            def f(s):
                s.wait_ge(sems["sv"], xn2_done)
                dma(s, "sxc", dump_h["d_xn2fm"], xn2_fm)
            S_(f)

    if max_phase >= 11:
        def f(s):
            s.wait_ge(sems["sv"], xn2_done)
            dma(s, "sxc", sdown_sb, sdown_h)
        S_(f)
        sdown_ld = cnt["sxc"]

        def fuse_view(ib):
            return (fuseA[:, ib * 512:(ib + 1) * 512] if ib < 32
                    else fuseB[:, (ib - 32) * 512:(ib - 31) * 512])

        prev_down_ev = None
        prev_dmm_last = None
        for hf in range(2):
            gsem = {g16: ("sgda", "sgdb")[g16 % 2] for g16 in range(16)}
            usem = {g16: ("sgdc", "sgdd")[g16 % 2] for g16 in range(16)}
            ioa11, iob11 = cnt["sioa"], cnt["siob"]
            gbase = {nm: cnt[nm] for nm in ("sgda", "sgdb", "sgdc", "sgdd")}
            st11, ss11, sv11 = cnt["stt"], cnt["ss"], cnt["sv"]
            gld_p = {g16: gbase[gsem[g16]] + 16 * (g16 // 2 + 1) for g16 in range(16)}
            uld_p = {g16: gbase[usem[g16]] + 16 * (g16 // 2 + 1) for g16 in range(16)}
            mm_p = {(g16, w): st11 + 2 * g16 + w + 1
                    for g16 in range(16) for w in range(2)}
            silu_p = {(g16, b): ss11 + 4 * g16 + b + 1
                      for g16 in range(16) for b in range(4)}
            fuse_p = {(g16, b): sv11 + 4 * g16 + b + 1
                      for g16 in range(16) for b in range(4)}

            def f(g, gld_p=gld_p, uld_p=uld_p, mm_p=mm_p, st11=st11,
                  ss11=ss11, sv11=sv11, ioa11=ioa11, iob11=iob11, hf=hf,
                  gsem=gsem, usem=usem):
                for g16 in range(16):
                    if g16 < 2:
                        g.wait_ge(sems["stt"], st11)
                        g.wait_ge(sems["ss"], ss11)
                        g.wait_ge(sems["sv"], sv11)
                        g.wait_ge(sems["sioa"], ioa11)
                        g.wait_ge(sems["siob"], iob11)
                    else:
                        g.wait_ge(sems["stt"], mm_p[(g16 - 2, 0)])
                    got = dma(g, gsem[g16], gu_bfg2[g16 % 2][:, 0:8192],
                              wgate_h[g16].rearrange("p kc j -> p (kc j)"))
                    assert got == gld_p[g16]
                    if g16 >= 2:
                        g.wait_ge(sems["stt"], mm_p[(g16 - 2, 1)])
                    got = dma(g, usem[g16], gu_bfu2[g16 % 2][:, 0:8192],
                              wup_h[g16].rearrange("p kc j -> p (kc j)"))
                    assert got == uld_p[g16]
            G_(f)

            def f(te, gld_p=gld_p, uld_p=uld_p, mm_p=mm_p, silu_p=silu_p,
                  fuse_p=fuse_p, hf=hf, prev_down_ev=prev_down_ev, gsem=gsem,
                  usem=usem):
                if hf == 0:
                    te.wait_ge(sems["sv"], xn2_done)
                else:
                    te.wait_ge(sems["sv"], prev_down_ev)
                for g16 in range(16):
                    for w in range(2):
                        te.wait_ge(sems[gsem[g16] if w == 0 else usem[g16]],
                                   (gld_p if w == 0 else uld_p)[g16])
                        if w == 0 and g16 > 0:
                            te.wait_ge(sems["ss"], silu_p[(g16 - 1, 3)])
                        if w == 1 and g16 > 0:
                            te.wait_ge(sems["sv"], fuse_p[(g16 - 1, 3)])
                        wbf = (gu_bfg2 if w == 0 else gu_bfu2)[g16 % 2]
                        for kc in range(NKC):
                            for b in range(4):
                                te.matmul(psf(4 * w + b),
                                          wbf[:, kc * 512 + b * 128:
                                              kc * 512 + (b + 1) * 128],
                                          xn2_fm[:, kc * T + hf * 512:
                                                 kc * T + (hf + 1) * 512],
                                          start=(kc == 0), stop=(kc == NKC - 1))
                        inc(te.drain(), "stt")
                        assert cnt["stt"] == mm_p[(g16, w)]
            Te(f)

            def f(sc, mm_p=mm_p, silu_p=silu_p, fuse_p=fuse_p, hf=hf):
                for g16 in range(16):
                    sc.wait_ge(sems["stt"], mm_p[(g16, 0)])
                    if g16 > 0:
                        sc.wait_ge(sems["sv"], fuse_p[(g16 - 1, 3)])
                    for b in range(4):
                        ib = g16 * 4 + b
                        sc.activation(out=silu_sb11[b][:, 0:512], in_=psf(b),
                                      func=AF.Silu, scale=sgate_sb[:, ib:ib + 1])
                        sc.drain()
                        inc(sc.nop(), "ss")
                        assert cnt["ss"] == silu_p[(g16, b)]
            Sc(f)

            def f(v, mm_p=mm_p, silu_p=silu_p, fuse_p=fuse_p, hf=hf):
                for g16 in range(16):
                    v.wait_ge(sems["stt"], mm_p[(g16, 1)])
                    v.wait_ge(sems["ss"], silu_p[(g16, 3)])
                    for b in range(4):
                        ib = g16 * 4 + b
                        v.scalar_tensor_tensor(out=fuse_view(ib), in0=psf(4 + b),
                                               scalar=sup_sb[:, ib:ib + 1],
                                               in1=silu_sb11[b][:, 0:512],
                                               op0=OP.mult, op1=OP.mult)
                        v.drain()
                        inc(v.nop(), "sv")
                        assert cnt["sv"] == fuse_p[(g16, b)]
            V_(f)

            NGRP = 16
            dsem = {n: ("sgda", "sgdb")[n % 2] for n in range(2 * NGRP)}
            dasem = {n: ("sioa", "siob")[n % 2] for n in range(16)}
            dbase = {nm: cnt[nm] for nm in ("sgda", "sgdb")}
            st11d, sv11d = cnt["stt"], cnt["sv"]
            io11d = {"sioa": cnt["sioa"], "siob": cnt["siob"]}
            dld_p = {n: dbase[dsem[n]] + 16 * (n // 2 + 1) for n in range(2 * NGRP)}
            dmm_p = {n: st11d + n + 1 for n in range(2 * NGRP)}
            dev_p = {}
            vv = sv11d
            for cgp in range(2):
                for q8 in range(8):
                    vv += 1
                    dev_p[(cgp, q8)] = vv
            dacc_p = {n: io11d[dasem[n]] + 16 * (n // 2 + 1) for n in range(16)}

            def mk_dld(cgp):
                def f(g, cgp=cgp, dld_p=dld_p, dmm_p=dmm_p, mm_p=mm_p,
                      dsem=dsem, hf=hf, prev_dmm_last=prev_dmm_last):
                    for grp in range(NGRP):
                        n = cgp * NGRP + grp
                        if n < 2:
                            if hf == 0:
                                g.wait_ge(sems["sv"], xn2_done)
                            else:
                                g.wait_ge(sems["stt"], prev_dmm_last)
                        else:
                            g.wait_ge(sems["stt"], dmm_p[n - 2])
                        got = dma(g, dsem[n], wd_bf[n % 2][:, 0:4096],
                                  wdown_h[cgp, grp])
                        assert got == dld_p[n]
                return f

            def mk_dacc(cgp):
                def f(g, cgp=cgp, dev_p=dev_p, dacc_p=dacc_p, hf=hf, dasem=dasem):
                    for q8 in range(8):
                        q4, cg2 = divmod(q8, 2)
                        n = cgp * 8 + q8
                        g.wait_ge(sems["sv"], dev_p[(cgp, q8)])
                        tb = hf * 4 + q4
                        cg = cgp * 2 + cg2
                        got = dma(g, dasem[n],
                                  out_h[tb * 128:(tb + 1) * 128, cg * 512:(cg + 1) * 512],
                                  dstage[n % 2][:, 0:512], accum_op=OP.add)
                        assert got == dacc_p[n]
                return f

            G_(mk_dld(0))

            def f(te, dld_p=dld_p, dmm_p=dmm_p, dev_p=dev_p,
                  fuse_p=fuse_p, silu_p=silu_p, dsem=dsem):
                te.wait_ge(sems["sv"], fuse_p[(15, 3)])
                te.wait_ge(sems["ss"], silu_p[(15, 3)])
                for cgp in range(2):
                    if cgp > 0:
                        for q8 in range(8):
                            te.wait_ge(sems["sv"], dev_p[(0, q8)])
                    for grp in range(NGRP):
                        n = cgp * NGRP + grp
                        te.wait_ge(sems[dsem[n]], dld_p[n])
                        for ib4 in range(4):
                            ib = grp * 4 + ib4
                            for q4 in range(4):
                                for cg2 in range(2):
                                    te.matmul(
                                        psf(q4 * 2 + cg2),
                                        fuse_view(ib)[:, q4 * 128:(q4 + 1) * 128],
                                        wd_bf[n % 2][:, ib4 * 1024 + cg2 * 512:
                                                     ib4 * 1024 + (cg2 + 1) * 512],
                                        start=(grp == 0 and ib4 == 0),
                                        stop=(grp == NGRP - 1 and ib4 == 3))
                        inc(te.drain(), "stt")
                        assert cnt["stt"] == dmm_p[n]
            Te(f)

            def f(v, dmm_p=dmm_p, dev_p=dev_p, dacc_p=dacc_p, dasem=dasem):
                v.wait_ge(sems["sxc"], sdown_ld)
                for cgp in range(2):
                    v.wait_ge(sems["stt"], dmm_p[cgp * NGRP + NGRP - 1])
                    for q8 in range(8):
                        q4, cg2 = divmod(q8, 2)
                        cg = cgp * 2 + cg2
                        n = cgp * 8 + q8
                        if n >= 2:
                            v.wait_ge(sems[dasem[n - 2]], dacc_p[n - 2])
                        v.tensor_mul(out=dstage[n % 2][:, 0:512], in0=psf(q8),
                                     in1=sdown_sb[:, cg * 512:(cg + 1) * 512])
                        v.drain()
                        inc(v.nop(), "sv")
                        assert cnt["sv"] == dev_p[(cgp, q8)]
            V_(f)

            G_(mk_dacc(0))
            G_(mk_dld(1))
            G_(mk_dacc(1))
            prev_down_ev = dev_p[(1, 7)]
            prev_dmm_last = dmm_p[2 * NGRP - 1]

    def f(g):
        for nm in ("sio", "sioa", "siob", "sxa", "sxb", "sxc", "swa", "swb"):
            g.wait_ge(sems[nm], cnt[nm])
    G_(f)

    stack.close()
    return nc



def _prep_weights(Wq, Wk, Wv, Wg, Wo, w_gate, w_up, w_down,
                  s_gate, s_up, s_down, g_norm_w):
    def proj_r(W):
        return np.ascontiguousarray(
            W.reshape(4, 4, 128, 4, 512).transpose(3, 0, 2, 1, 4).reshape(4, 4, 128, 2048))

    def gu_r(W8):
        return np.ascontiguousarray(W8.reshape(16, 128, 16, 512).transpose(2, 1, 0, 3))

    wd = np.ascontiguousarray(w_down.reshape(16, 4, 128, 2, 1024).transpose(3, 0, 2, 1, 4).reshape(2, 16, 128, 4096))
    return dict(
        wq=proj_r(Wq), wk=proj_r(Wk), wv=proj_r(Wv), wg=proj_r(Wg), wo=proj_r(Wo),
        wgate_l=gu_r(w_gate), wup_l=gu_r(w_up), wdown_l=wd,
        sgate_pp=np.ascontiguousarray(s_gate.reshape(NIB, 128).T),
        sup_pp=np.ascontiguousarray(s_up.reshape(NIB, 128).T),
        sdown_rep=np.ascontiguousarray(np.broadcast_to(s_down, (128, D))),
        gnw_rep=np.ascontiguousarray(np.broadcast_to(g_norm_w, (128, 128))),
        gnw_pp=np.ascontiguousarray(g_norm_w.reshape(128, 1)),
    )


def kernel(L0_attn_x, L0_attn_Wq, L0_attn_Wk, L0_attn_Wv, L0_attn_Wg,
           L0_attn_g_norm_w, L0_attn_Wo, L0_ffn_w_gate, L0_ffn_w_up,
           L0_ffn_w_down, L0_ffn_s_gate, L0_ffn_s_up, L0_ffn_s_down):
    x = np.ascontiguousarray(np.asarray(L0_attn_x, dtype=np.float32))
    B, S, _ = x.shape
    shared = _prep_weights(
        np.asarray(L0_attn_Wq, np.float32), np.asarray(L0_attn_Wk, np.float32),
        np.asarray(L0_attn_Wv, np.float32), np.asarray(L0_attn_Wg, np.float32),
        np.asarray(L0_attn_Wo, np.float32),
        np.asarray(L0_ffn_w_gate), np.asarray(L0_ffn_w_up), np.asarray(L0_ffn_w_down),
        np.asarray(L0_ffn_s_gate, np.float32), np.asarray(L0_ffn_s_up, np.float32),
        np.asarray(L0_ffn_s_down, np.float32),
        np.asarray(L0_attn_g_norm_w, np.float32))

    nc = build()
    in_maps = []
    for core in range(8):
        b = core // NSEG
        seg = core % NSEG
        m = dict(shared)
        m["x"] = np.ascontiguousarray(x[b, seg * SEG:(seg + 1) * SEG, :])
        m.update(_host_consts(core))
        in_maps.append({k: np.asarray(v) for k, v in m.items()})
    res = run_bass_kernel_spmd(nc, in_maps, core_ids=list(range(8)))
    out = np.empty((B, S, D), dtype=np.float32)
    for core in range(8):
        b = core // NSEG
        seg = core % NSEG
        out[b, seg * SEG:(seg + 1) * SEG, :] = res.results[core]["out"]
    return out


# version 12
# speedup vs baseline: 1.0544x; 1.0544x over previous
import math
import numpy as np
import ml_dtypes
import contextlib
import concourse.bass as bass
import concourse.mybir as mybir
from concourse.bass_utils import run_bass_kernel_spmd

F32 = mybir.dt.float32
BF16 = mybir.dt.bfloat16
I8 = mybir.dt.int8
U8 = mybir.dt.uint8
AF = mybir.ActivationFunctionType
OP = mybir.AluOpType

T = 1024
D = 2048
H = 16
NTB = 8
NKC = 16
NCG = 4
NIB = 64
EPS = 1e-5
SEG = 1024
NSEG = 4
KB = 1024


def _host_consts(core_id):
    h = np.arange(H, dtype=np.float64)
    gamma = 1.0 - np.exp2(-5.0 - h)
    seg = core_id % NSEG
    pos0 = seg * SEG

    inv = 10000.0 ** (-(np.arange(64, dtype=np.float64) / 64.0))
    t_loc = np.arange(T, dtype=np.float64).reshape(NTB, 128)
    ang = (pos0 + t_loc)[:, :, None] * inv[None, None, :]
    cos = np.cos(ang)
    sin = np.sin(ang)

    def tmtab(a, scale=1.0):
        return np.ascontiguousarray(
            (a * scale).transpose(1, 0, 2).reshape(128, NTB * 64)).astype(np.float32)

    qs = 1.0 / math.sqrt(128.0)
    idx = np.arange(128, dtype=np.float64)
    rel = idx[None, :] - idx[:, None]
    dmatT = np.where(rel >= 0, gamma[:, None, None] ** rel[None], 0.0)
    dmatT = np.ascontiguousarray(
        dmatT.transpose(1, 0, 2).reshape(128, H * 128)).astype(np.float32)
    kdec = (gamma[None, :] ** (127.0 - idx[:, None])).astype(np.float32)
    gC = np.broadcast_to(gamma ** 128.0, (128, H)).astype(np.float32).copy()
    cross_pp = (gamma[None, :] ** (idx[:, None] + 1.0)).astype(np.float32)
    cs = np.zeros((128, NTB * H), dtype=np.float64)
    for c in range(NTB):
        cs[:, c * H:(c + 1) * H] = gamma[None, :] ** (c * 128.0 + idx[:, None] + 1.0)
    crossseg = cs.astype(np.float32)
    wj = np.zeros((128, NSEG * H), dtype=np.float64)
    for j in range(seg):
        wj[:, j * H:(j + 1) * H] = gamma[None, :] ** (1024.0 * (seg - 1 - j))
    wj = wj.astype(np.float32)
    ident = np.eye(128, dtype=ml_dtypes.bfloat16)
    return dict(cosq=tmtab(cos, qs), sinq=tmtab(sin, qs),
                cosk=tmtab(cos), sink=tmtab(sin), dmatT=dmatT,
                kdec=kdec, gC=gC, cross_pp=cross_pp, crossseg=crossseg,
                wj=wj, ident=ident,
                epsc=np.broadcast_to(np.array([EPS, 1.0 / 128.0], np.float32),
                                     (128, 2)).copy())


def build(dump=(), max_phase=99):
    nc = bass.Bass("TRN2", target_bir_lowering=False, debug=False)

    x_h = nc.dram_tensor("x", [T, D], F32, kind="ExternalInput").ap()
    out_h = nc.dram_tensor("out", [T, D], F32, kind="ExternalOutput").ap()
    wq_h = nc.dram_tensor("wq", [NCG, 4, 128, D], F32, kind="ExternalInput").ap()
    wk_h = nc.dram_tensor("wk", [NCG, 4, 128, D], F32, kind="ExternalInput").ap()
    wv_h = nc.dram_tensor("wv", [NCG, 4, 128, D], F32, kind="ExternalInput").ap()
    wg_h = nc.dram_tensor("wg", [NCG, 4, 128, D], F32, kind="ExternalInput").ap()
    wo_h = nc.dram_tensor("wo", [NCG, 4, 128, D], F32, kind="ExternalInput").ap()
    wgate_h = nc.dram_tensor("wgate_l", [16, 128, 16, 512], I8, kind="ExternalInput").ap()
    wup_h = nc.dram_tensor("wup_l", [16, 128, 16, 512], I8, kind="ExternalInput").ap()
    wdown_h = nc.dram_tensor("wdown_l", [2, 16, 128, 4096], I8, kind="ExternalInput").ap()
    sgate_h = nc.dram_tensor("sgate_pp", [128, NIB], F32, kind="ExternalInput").ap()
    sup_h = nc.dram_tensor("sup_pp", [128, NIB], F32, kind="ExternalInput").ap()
    sdown_h = nc.dram_tensor("sdown_rep", [128, D], F32, kind="ExternalInput").ap()
    gnw_h = nc.dram_tensor("gnw_rep", [128, 128], F32, kind="ExternalInput").ap()
    gnwpp_h = nc.dram_tensor("gnw_pp", [128, 1], F32, kind="ExternalInput").ap()
    cosq_h = nc.dram_tensor("cosq", [128, NTB * 64], F32, kind="ExternalInput").ap()
    sinq_h = nc.dram_tensor("sinq", [128, NTB * 64], F32, kind="ExternalInput").ap()
    cosk_h = nc.dram_tensor("cosk", [128, NTB * 64], F32, kind="ExternalInput").ap()
    sink_h = nc.dram_tensor("sink", [128, NTB * 64], F32, kind="ExternalInput").ap()
    dmat_h = nc.dram_tensor("dmatT", [128, H * 128], F32, kind="ExternalInput").ap()
    kdec_h = nc.dram_tensor("kdec", [128, H], F32, kind="ExternalInput").ap()
    gC_h = nc.dram_tensor("gC", [128, H], F32, kind="ExternalInput").ap()
    crpp_h = nc.dram_tensor("cross_pp", [128, H], F32, kind="ExternalInput").ap()
    crsg_h = nc.dram_tensor("crossseg", [128, NTB * H], F32, kind="ExternalInput").ap()
    wj_h = nc.dram_tensor("wj", [128, NSEG * H], F32, kind="ExternalInput").ap()
    ident_h = nc.dram_tensor("ident", [128, 128], BF16, kind="ExternalInput").ap()
    epsc_h = nc.dram_tensor("epsc", [128, 2], F32, kind="ExternalInput").ap()

    cc_in = nc.dram_tensor("cc_in", [128, D], BF16)
    cc_out = nc.dram_tensor("cc_out", [NSEG * 128, D], BF16)

    dump_h = {}
    for name, shape, dt in (
        ("d_xnfm", [128, NKC * T], BF16), ("d_qtm", [128, NTB * D], BF16),
        ("d_ktm", [128, NTB * D], BF16), ("d_vtm", [128, NTB * D], BF16),
        ("d_gtm", [128, NTB * D], BF16), ("d_oraw", [128, NTB * D], BF16),
        ("d_S", [128, D], F32), ("d_sinit", [128, D], BF16),
        ("d_gated", [128, H * T], BF16), ("d_xn2fm", [128, NKC * T], BF16),
    ):
        if name in dump:
            dump_h[name] = nc.dram_tensor(name, shape, dt, kind="ExternalOutput").ap()

    bufA = nc.alloc_sbuf_tensor("bufA", [128, 32 * KB], U8)
    bufB = nc.alloc_sbuf_tensor("bufB", [128, 32 * KB], U8)
    bufC = nc.alloc_sbuf_tensor("bufC", [128, 32 * KB], U8)
    bufD = nc.alloc_sbuf_tensor("bufD", [128, 32 * KB], U8)
    bufE = nc.alloc_sbuf_tensor("bufE", [128, 32 * KB], U8)
    bufW = nc.alloc_sbuf_tensor("bufW", [128, 38 * KB], U8)
    bufS = nc.alloc_sbuf_tensor("bufS", [128, 4 * KB], U8)

    def view(buf, off_b, dt, n):
        esz = {F32: 4, BF16: 2, I8: 1, U8: 1}[dt]
        return buf.ap()[:, off_b:off_b + n * esz].bitcast(dt)

    q_tm = view(bufA, 0, BF16, NTB * D)
    xn2_fm = view(bufA, 0, BF16, NKC * T)
    k_tm = view(bufB, 0, BF16, NTB * D)
    Tall = view(bufW, 8 * KB, BF16, 3 * D)
    gated_fm = view(bufB, 0, BF16, H * T)
    fuseA = view(bufB, 0, BF16, 32 * 512)
    v_tm = view(bufC, 0, BF16, NTB * D)
    fuseB = view(bufC, 0, BF16, 32 * 512)
    g_tm = view(bufD, 0, BF16, NTB * D)
    xn_fm = view(bufE, 0, BF16, NKC * T)
    o_raw = view(bufE, 0, BF16, NTB * D)

    sdown_sb = view(bufD, 0, F32, D)
    xnew = [view(bufD, 8 * KB * i, F32, D) for i in range(4)] + \
           [view(bufE, 8 * KB * i, F32, D) for i in range(4)]
    xo_st = [view(bufW, 24 * KB + 2 * KB * i, F32, 512) for i in range(2)]
    silu_sb11 = [view(bufD, 8 * KB + 1 * KB * i, BF16, 512) for i in range(4)]
    xst = [view(bufD, 12 * KB + 8 * KB * i, F32, D) for i in range(2)]
    xn2tm_st = view(bufW, 8 * KB, BF16, D)

    x_stage = [view(bufW, 8 * KB * i, F32, D) for i in range(2)]
    xntm_st = view(bufD, 0, BF16, D)
    wsl_f32 = [view(bufW, 8 * KB * i, F32, D) for i in range(2)]
    wsl_bf = [view(bufW, 16 * KB, BF16, D), view(bufW, 20 * KB, BF16, D),
              view(bufW, 0, BF16, D), view(bufW, 4 * KB, BF16, D)]
    rope_tabs = view(bufW, 24 * KB, F32, 4 * 512)
    rp_st = ([view(bufW, 32 * KB + KB * i, BF16, 512) for i in range(4)] +
             [view(bufW, 8 * KB + KB * i, BF16, 512) for i in range(4)])
    rope_t1 = view(bufW, 36 * KB, BF16, 256)
    rope_t2 = view(bufW, 36 * KB + 512, BF16, 256)
    dmatT_sb = view(bufW, 0, F32, H * 128)
    S_sb = view(bufW, 8 * KB, F32, D)
    Sb_sb = view(bufW, 16 * KB, BF16, D)
    qf_sb = view(bufW, 20 * KB, BF16, D)
    kf_sb = view(bufW, 24 * KB, BF16, D)
    amt_sb = view(bufW, 28 * KB, BF16, D)
    kp_sb = view(bufW, 32 * KB, BF16, D)
    sg_tmp = view(bufW, 36 * KB, F32, 512)
    sinit_sb = view(bufW, 8 * KB, BF16, D)
    gated_c = view(bufW, 0, BF16, D)
    o_f = view(bufW, 24 * KB, BF16, D)
    sq_sb = view(bufW, 28 * KB, BF16, D)
    sgs_sb = view(bufW, 32 * KB, BF16, D)
    gu_bfg2 = [view(bufE, 16 * KB * i, BF16, 8192) for i in range(2)]
    gu_bfu2 = [view(bufW, 16 * KB * i, BF16, 8192) for i in range(2)]
    wd_bf = [view(bufD, 12 * KB + 8 * KB * i, BF16, 4096) for i in range(2)]
    dstage = [view(bufW, 24 * KB + 4 * KB * i, F32, 1024) for i in range(2)]

    ident_sb = view(bufS, 0, BF16, 128)
    kdec_sb = view(bufS, 256, F32, H)
    gC_sb = view(bufS, 320, F32, H)
    crpp_sb = view(bufS, 384, F32, H)
    ssq_sb = view(bufS, 448, F32, H)
    rinv_sb = view(bufS, 512, F32, H)
    stats_sb = view(bufS, 576, F32, 24)
    mv_sb = view(bufS, 672, F32, 2)
    sd_sb = view(bufS, 680, F32, 2)
    inv_sb = view(bufS, 688, F32, 2)
    b2_sb = view(bufS, 696, F32, 2)
    crsg_sb = view(bufS, 704, F32, NTB * H)
    wj_sb = view(bufS, 1216, F32, NSEG * H)
    gnw_sb = view(bufS, 1472, F32, 128)
    sgate_sb = view(bufS, 1984, F32, NIB)
    sup_sb = view(bufS, 2240, F32, NIB)
    sd2_sb = view(bufS, 2496, F32, H)
    epsc_sb = view(bufS, 2560, F32, 2)
    gnwpp_sb = view(bufS, 2568, F32, 1)

    PSt = nc.alloc_psum_tensor("PSall", [128, 8 * 512], F32)
    PS = PSt.ap()
    PSb = PS.bitcast(BF16)

    def psf(bank, nbanks=1):
        return PS[:, bank * 512:(bank + nbanks) * 512]

    stack = contextlib.ExitStack()
    sems = {}
    cnt = {}
    for name in ("sxc", "sxa", "sxb", "swa", "swb", "sio", "sioa", "siob",
                 "sgda", "sgdb", "sgdc", "sgdd", "sgde", "sgdf", "scc", "sv", "ss", "sg", "stt"):
        sems[name] = stack.enter_context(nc.semaphore(name))
        cnt[name] = 0

    def inc(instr, sem, n=1):
        cnt[sem] += n
        instr.then_inc(sems[sem], n)

    def dma(eng, sem, out, in_, **kw):
        eng.wait_ge(sems[sem], cnt[sem])
        cnt[sem] += 16
        eng.dma_start(out=out, in_=in_, **kw).then_inc(sems[sem], 16)
        return cnt[sem]

    block = stack.enter_context(nc.Block())
    S_, Sc, V_, Te, G_ = block.sync, block.scalar, block.vector, block.tensor, block.gpsimd

    def f(s):
        dma(s, "sxc", ident_sb, ident_h)
        dma(s, "sxc", epsc_sb, epsc_h)
        dma(s, "sxc", kdec_sb, kdec_h)
        dma(s, "sxc", gC_sb, gC_h)
        dma(s, "sxc", crpp_sb, crpp_h)
        dma(s, "sxc", crsg_sb, crsg_h)
        dma(s, "sxc", wj_sb, wj_h)
        dma(s, "sxc", gnw_sb, gnw_h)
        dma(s, "sxc", gnwpp_sb, gnwpp_h)
        dma(s, "sxc", sgate_sb, sgate_h)
        dma(s, "sxc", sup_sb, sup_h)
        dma(s, "sxc", rope_tabs[:, 0:512], cosq_h)
        dma(s, "sxc", rope_tabs[:, 512:1024], sinq_h)
        dma(s, "sxc", rope_tabs[:, 1024:1536], cosk_h)
        dma(s, "sxc", rope_tabs[:, 1536:2048], sink_h)
    S_(f)
    const_done = cnt["sxc"]

    def ln_transpose(src_dram, fm_dst, stage_pair, tmst, dma_gate, tensor_gate,
                     sbuf_tiles=None):
        xsem = {tb: ("sxa", "sxb")[tb % 2] for tb in range(NTB)}
        sx0 = {"sxa": cnt["sxa"], "sxb": cnt["sxb"]}
        sv0 = cnt["sv"]
        ss0 = cnt["ss"]
        st0 = cnt["stt"]
        ld_p = {tb: sx0[xsem[tb]] + 16 * (tb // 2 + 1) for tb in range(NTB)}
        stats_p = {tb: sv0 + 3 * tb + 1 for tb in range(NTB)}
        inv_p = {tb: sv0 + 3 * tb + 2 for tb in range(NTB)}
        evac_p = {tb: sv0 + 3 * tb + 3 for tb in range(NTB)}
        sd_p = {tb: ss0 + 2 * tb + 1 for tb in range(NTB)}
        app_p = {tb: ss0 + 2 * tb + 2 for tb in range(NTB)}
        tr_p = {tb: st0 + tb + 1 for tb in range(NTB)}

        if sbuf_tiles is None:
            def f(s):
                for tb in range(NTB):
                    if dma_gate is not None:
                        sm, vv = dma_gate(tb)
                        s.wait_ge(sems[sm], vv)
                    if tb >= 2:
                        s.wait_ge(sems["ss"], app_p[tb - 2])
                    got = dma(s, xsem[tb], stage_pair[tb % 2],
                              src_dram[tb * 128:(tb + 1) * 128, :])
                    assert got == ld_p[tb]
            S_(f)

        def f(v):
            for tb in range(NTB):
                if sbuf_tiles is None:
                    v.wait_ge(sems[xsem[tb]], ld_p[tb])
                elif dma_gate is not None:
                    sm, vv = dma_gate(tb)
                    v.wait_ge(sems[sm], vv)
                if tb >= 1:
                    v.wait_ge(sems["ss"], app_p[tb - 1])
                xs = stage_pair[tb % 2] if sbuf_tiles is None else sbuf_tiles[tb]
                for i in range(4):
                    v.bn_stats(out=stats_sb[:, 6 * i:6 * i + 6],
                               in_=xs[:, 512 * i:512 * (i + 1)])
                v.drain()
                v.bn_aggr(out=mv_sb[:, 0:2], in_=stats_sb[:, 0:24])
                v.drain()
                inc(v.nop(), "sv")
                assert cnt["sv"] == stats_p[tb]
                v.wait_ge(sems["ss"], sd_p[tb])
                v.reciprocal(out=inv_sb[:, 0:1], in_=sd_sb[:, 0:1])
                v.drain()
                v.scalar_tensor_tensor(out=b2_sb[:, 0:1], in0=mv_sb[:, 0:1], scalar=-1.0,
                                       in1=inv_sb[:, 0:1], op0=OP.mult, op1=OP.mult)
                v.drain()
                inc(v.nop(), "sv")
                assert cnt["sv"] == inv_p[tb]
                v.wait_ge(sems["stt"], tr_p[tb])
                for half in range(2):
                    src = PSb[:, (4 + half) * 1024:(5 + half) * 1024]
                    dst = fm_dst.rearrange("p (kc t) -> p kc t", kc=NKC)[
                        :, half * 8:(half + 1) * 8, tb * 128:tb * 128 + 128]
                    v.tensor_copy(out=dst, in_=src.rearrange("p (kc t) -> p kc t", kc=8))
                v.drain()
                inc(v.nop(), "sv")
                assert cnt["sv"] == evac_p[tb]
        V_(f)

        def f(sc):
            sc.wait_ge(sems["sxc"], const_done)
            for tb in range(NTB):
                sc.wait_ge(sems["sv"], stats_p[tb])
                sc.activation(out=sd_sb[:, 0:1], in_=mv_sb[:, 1:2], func=AF.Sqrt,
                              bias=epsc_sb[:, 0:1])
                sc.drain()
                inc(sc.nop(), "ss")
                assert cnt["ss"] == sd_p[tb]
                sc.wait_ge(sems["sv"], inv_p[tb])
                if tb >= 1:
                    sc.wait_ge(sems["stt"], tr_p[tb - 1])
                xs2 = stage_pair[tb % 2] if sbuf_tiles is None else sbuf_tiles[tb]
                sc.activation(out=tmst[:, 0:D], in_=xs2, func=AF.Identity,
                              bias=b2_sb[:, 0:1], scale=inv_sb[:, 0:1])
                sc.drain()
                inc(sc.nop(), "ss")
                assert cnt["ss"] == app_p[tb]
        Sc(f)

        def f(te):
            te.wait_ge(sems["sxc"], const_done)
            if tensor_gate is not None:
                te.wait_ge(sems[tensor_gate[0]], tensor_gate[1])
            for tb in range(NTB):
                te.wait_ge(sems["ss"], app_p[tb])
                if tb >= 1:
                    te.wait_ge(sems["sv"], evac_p[tb - 1])
                for kc in range(NKC):
                    bank = 4 + kc // 8
                    te.transpose(out=PSb[:, bank * 1024 + (kc % 8) * 128:
                                         bank * 1024 + (kc % 8) * 128 + 128],
                                 in_=tmst[:, kc * 128:(kc + 1) * 128], identity=ident_sb)
                inc(te.drain(), "stt")
                assert cnt["stt"] == tr_p[tb]
        Te(f)
        return evac_p

    xn_evac = ln_transpose(x_h, xn_fm, x_stage, xntm_st, None, None)
    last_xn = xn_evac[NTB - 1]

    def projection(w_dram, kind, dst_tm=None, first_gate=None, slab_gate=None):
        wsem = {i: ("sgda", "sgdb")[i % 2] for i in range(16)}
        sx0 = {"sgda": cnt["sgda"], "sgdb": cnt["sgdb"]}
        st0, sv0, ss0, sxc0 = cnt["stt"], cnt["sv"], cnt["ss"], cnt["sxc"]
        io0a, io0b = cnt["sioa"], cnt["siob"]
        ld_p = {i: sx0[wsem[i]] + 16 * (i // 2 + 1) for i in range(16)}
        mm_p = {i: st0 + i + 1 for i in range(16)}
        ev_p = {(cg, tb): sv0 + cg * 8 + tb + 1 for cg in range(NCG) for tb in range(NTB)}
        cp_p = {(cg, tb): ss0 + cg * 8 + tb + 1 for cg in range(NCG) for tb in range(NTB)}
        bank_sem = "sv" if kind == "wo" else "ss"
        bank_p = ev_p if kind == "wo" else cp_p if kind in ("q", "k") else cp_p

        def emit_slab(g, i):
            cg, kcg = divmod(i, 4)
            if i < 4:
                if slab_gate is not None and slab_gate[i] is not None:
                    sm, vv = slab_gate[i]
                    g.wait_ge(sems[sm], vv)
                g.wait_ge(sems["sxc"], sxc0)
            else:
                g.wait_ge(sems["stt"], mm_p[i - 4])
            got = dma(g, wsem[i], wsl_bf[i % 4][:, 0:D], w_dram[cg, kcg])
            assert got == ld_p[i]

        if kind != "wo":
            def f(g):
                for i in range(16):
                    emit_slab(g, i)
            G_(f)
        else:
            def f(g):
                for i in range(16):
                    emit_slab(g, i)
            G_(f)
            xsem9 = {n: ("sxa", "sxb")[n % 2] for n in range(32)}
            sx9 = {"sxa": cnt["sxa"], "sxb": cnt["sxb"]}
            xld_p = {n: sx9[xsem9[n]] + 16 * (n // 2 + 1) for n in range(32)}

            def f(s):
                s.wait_ge(sems["sv"], sv0)
                for n in range(32):
                    cg, tb = divmod(n, 8)
                    if n >= 2:
                        pcg, ptb = divmod(n - 2, 8)
                        s.wait_ge(sems["sv"], ev_p[(pcg, ptb)])
                    got = dma(s, xsem9[n], xo_st[n % 2][:, 0:512],
                              x_h[tb * 128:(tb + 1) * 128, cg * 512:(cg + 1) * 512])
                    assert got == xld_p[n]
            S_(f)

        lhs = gated_fm if kind == "wo" else xn_fm

        def f(te):
            te.wait_ge(sems["sv"], sv0)
            te.wait_ge(sems["ss"], ss0)
            for cg in range(NCG):
                for kcg in range(4):
                    i = cg * 4 + kcg
                    te.wait_ge(sems[wsem[i]], ld_p[i])
                    for tb in range(NTB):
                        if kcg == 0:
                            if cg > 0:
                                te.wait_ge(sems[bank_sem], bank_p[(cg - 1, tb)])
                            elif first_gate is not None:
                                sm, vv = first_gate[tb]
                                te.wait_ge(sems[sm], vv)
                        for k4 in range(4):
                            kc = kcg * 4 + k4
                            te.matmul(psf(tb),
                                      lhs[:, kc * T + tb * 128: kc * T + tb * 128 + 128],
                                      wsl_bf[i % 4][:, k4 * 512:(k4 + 1) * 512],
                                      start=(kc == 0), stop=(kc == NKC - 1))
                    inc(te.drain(), "stt")
                    assert cnt["stt"] == mm_p[i]
        Te(f)

        if kind in ("q", "k"):
            coff = 0 if kind == "q" else 1024

            def f(sc):
                for cg in range(NCG):
                    sc.wait_ge(sems["stt"], mm_p[cg * 4 + 3])
                    for tb in range(NTB):
                        idx = cg * 8 + tb
                        if idx >= 8:
                            sc.wait_ge(sems["sv"], ev_p[(cg - 1, tb)])
                        sc.copy(out=rp_st[tb % 8][:, 0:512], in_=psf(tb))
                        sc.drain()
                        inc(sc.nop(), "ss")
                        assert cnt["ss"] == cp_p[(cg, tb)]
            Sc(f)

            def f(v):
                v.wait_ge(sems["sxc"], const_done)
                for cg in range(NCG):
                    for tb in range(NTB):
                        v.wait_ge(sems["ss"], cp_p[(cg, tb)])
                        ps = rp_st[tb % 8][:, 0:512]
                        cosT = rope_tabs[:, coff + tb * 64: coff + tb * 64 + 64]
                        sinT = rope_tabs[:, coff + 512 + tb * 64: coff + 512 + tb * 64 + 64]
                        x1 = ps.rearrange("p (h j) -> p h j", h=4)[:, :, 0:64]
                        x2 = ps.rearrange("p (h j) -> p h j", h=4)[:, :, 64:128]
                        cB = cosT.rearrange("p (a j) -> p a j", a=1).broadcast_to([128, 4, 64])
                        sB = sinT.rearrange("p (a j) -> p a j", a=1).broadcast_to([128, 4, 64])
                        t1 = rope_t1[:, 0:256].rearrange("p (h j) -> p h j", h=4)
                        t2 = rope_t2[:, 0:256].rearrange("p (h j) -> p h j", h=4)
                        d1 = dst_tm[:, tb * D + cg * 512: tb * D + (cg + 1) * 512]
                        o1 = d1.rearrange("p (h j) -> p h j", h=4)[:, :, 0:64]
                        o2 = d1.rearrange("p (h j) -> p h j", h=4)[:, :, 64:128]
                        v.tensor_tensor(out=t1, in0=x1, in1=cB, op=OP.mult)
                        v.tensor_tensor(out=t2, in0=x2, in1=sB, op=OP.mult)
                        v.drain()
                        v.tensor_tensor(out=o1, in0=t1, in1=t2, op=OP.subtract)
                        v.drain()
                        v.tensor_tensor(out=t1, in0=x2, in1=cB, op=OP.mult)
                        v.tensor_tensor(out=t2, in0=x1, in1=sB, op=OP.mult)
                        v.drain()
                        v.tensor_tensor(out=o2, in0=t1, in1=t2, op=OP.add)
                        v.drain()
                        inc(v.nop(), "sv")
                        assert cnt["sv"] == ev_p[(cg, tb)]
            V_(f)
            return dict(ev=cp_p, sem="ss", rope=ev_p, mm=mm_p)
        elif kind in ("v", "g"):
            def f(sc):
                for cg in range(NCG):
                    sc.wait_ge(sems["stt"], mm_p[cg * 4 + 3])
                    for tb in range(NTB):
                        sc.copy(out=dst_tm[:, tb * D + cg * 512: tb * D + (cg + 1) * 512],
                                in_=psf(tb))
                        sc.drain()
                        inc(sc.nop(), "ss")
                        assert cnt["ss"] == cp_p[(cg, tb)]
            Sc(f)
            return dict(ev=cp_p, sem="ss", mm=mm_p)
        else:
            def f(v):
                for cg in range(NCG):
                    v.wait_ge(sems["stt"], mm_p[cg * 4 + 3])
                    for tb in range(NTB):
                        n = cg * 8 + tb
                        v.wait_ge(sems[xsem9[n]], xld_p[n])
                        v.tensor_add(out=xnew[tb][:, cg * 512:(cg + 1) * 512],
                                     in0=psf(tb), in1=xo_st[n % 2][:, 0:512])
                        v.drain()
                        inc(v.nop(), "sv")
                        assert cnt["sv"] == ev_p[(cg, tb)]
            V_(f)
            stsem = {tb: ("sxa", "sxb")[tb % 2] for tb in range(NTB)}
            sxs = {"sxa": cnt["sxa"], "sxb": cnt["sxb"]}
            store_p = {tb: (stsem[tb], sxs[stsem[tb]] + 16 * (tb // 2 + 1))
                       for tb in range(NTB)}

            def f(s):
                for tb in range(NTB):
                    s.wait_ge(sems["sv"], ev_p[(3, tb)])
                    got = dma(s, stsem[tb], out_h[tb * 128:(tb + 1) * 128, :],
                              xnew[tb][:, 0:D])
                    assert got == store_p[tb][1]
            S_(f)
            return dict(ev=ev_p, sem="sv", mm=mm_p, store=store_p)

    def tb_gate(res):
        return {tb: (res["sem"], res["ev"][(3, tb)]) for tb in range(NTB)}

    if max_phase >= 3:
        pq = projection(wq_h, "q", q_tm,
                        first_gate={tb: ("sv", xn_evac[tb]) for tb in range(NTB)},
                        slab_gate=(None, None, ("sv", xn_evac[NTB - 1]),
                                   ("sv", xn_evac[NTB - 1])))
        def sg4(p):
            return tuple(("stt", p["mm"][12 + i]) for i in range(4))
        pk = projection(wk_h, "k", k_tm, first_gate=tb_gate(pq), slab_gate=sg4(pq))
        pv = projection(wv_h, "v", v_tm, first_gate=tb_gate(pk), slab_gate=sg4(pk))
        pg = projection(wg_h, "g", g_tm, first_gate=tb_gate(pv), slab_gate=sg4(pv))

    if "d_xnfm" in dump_h:
        def f(s):
            s.wait_ge(sems["sv"], last_xn)
            dma(s, "sxc", dump_h["d_xnfm"], xn_fm)
        S_(f)
    if max_phase >= 3:
        for nm, buf in (("d_qtm", q_tm), ("d_ktm", k_tm), ("d_vtm", v_tm), ("d_gtm", g_tm)):
            if nm in dump_h:
                def f(s, buf=buf):
                    s.wait_ge(sems["sv"], cnt["sv"])
                    s.wait_ge(sems["ss"], cnt["ss"])
                    dma(s, "sxc", dump_h[nm], buf)
                S_(f)

    if max_phase >= 5:
        def f(s):
            s.wait_ge(sems["ss"], cnt["ss"])
            s.wait_ge(sems["sv"], cnt["sv"])
            dma(s, "sxc", dmatT_sb, dmat_h)
        S_(f)
        dmat_ld = cnt["sxc"]

        def f(v):
            v.wait_ge(sems["sxc"], dmat_ld)
            v.memset(S_sb[:, 0:D], 0.0)
            v.drain()
            inc(v.nop(), "sv")
        V_(f)
        s_zero = cnt["sv"]

        sv5 = cnt["sv"]
        st5 = cnt["stt"]
        ss5 = cnt["ss"]
        kp_p, state_p = {}, {}
        snap_p = {}
        vv = sv5
        ssim = ss5
        for c in range(NTB):
            ssim += 1; snap_p[c] = ssim
            vv += 1; kp_p[c] = vv
            for grp in range(4):
                vv += 1; state_p[(c, grp)] = vv
        tu_p = {}
        tt = st5
        for c in range(NTB):
            for grp in range(4):
                tt += 1; tu_p[(c, grp)] = tt

        def f(sc):
            sc.wait_ge(sems["stt"], pg["mm"][15])
            for c in range(NTB):
                if c > 0:
                    sc.wait_ge(sems["sv"], state_p[(c - 1, 3)])
                else:
                    sc.wait_ge(sems["sv"], s_zero)
                sc.copy(out=o_raw[:, c * D:(c + 1) * D], in_=S_sb[:, 0:D])
                sc.drain()
                inc(sc.nop(), "ss")
                assert cnt["ss"] == snap_p[c]
        Sc(f)

        def f(te):
            te.wait_ge(sems["sv"], s_zero)
            te.wait_ge(sems["ss"], pg["ev"][(3, 7)])
            for c in range(NTB):
                te.wait_ge(sems["sv"], kp_p[c])
                for grp in range(4):
                    bank = 6 + grp % 2
                    if c > 0 or grp >= 2:
                        pc, pg_ = (c, grp - 2) if grp >= 2 else (c - 1, grp + 2)
                        te.wait_ge(sems["sv"], state_p[(pc, pg_)])
                    for hh in range(4):
                        h = grp * 4 + hh
                        te.matmul(psf(bank)[:, hh * 128:(hh + 1) * 128],
                                  kp_sb[:, h * 128:(h + 1) * 128],
                                  v_tm[:, c * D + h * 128:c * D + (h + 1) * 128],
                                  start=True, stop=True)
                    inc(te.drain(), "stt")
                    assert cnt["stt"] == tu_p[(c, grp)]
        Te(f)

        def f(v):
            for c in range(NTB):
                v.tensor_tensor(
                    out=kp_sb[:, 0:D].rearrange("p (h j) -> p h j", h=H),
                    in0=k_tm[:, c * D:(c + 1) * D].rearrange("p (h j) -> p h j", h=H),
                    in1=kdec_sb.rearrange("p (h a) -> p h a", a=1).broadcast_to(
                        [128, H, 128]),
                    op=OP.mult)
                v.drain()
                inc(v.nop(), "sv")
                assert cnt["sv"] == kp_p[c]
                for grp in range(4):
                    bank = 6 + grp % 2
                    v.wait_ge(sems["stt"], tu_p[(c, grp)])
                    if grp == 0:
                        v.wait_ge(sems["ss"], snap_p[c])
                    sl3 = S_sb[:, grp * 512:(grp + 1) * 512].rearrange(
                        "p (h j) -> p h j", h=4)
                    v.tensor_tensor(
                        out=sg_tmp[:, 0:512].rearrange("p (h j) -> p h j", h=4),
                        in0=sl3,
                        in1=gC_sb[:, grp * 4:(grp + 1) * 4].rearrange(
                            "p (h a) -> p h a", a=1).broadcast_to([128, 4, 128]),
                        op=OP.mult)
                    v.drain()
                    v.tensor_add(out=S_sb[:, grp * 512:(grp + 1) * 512],
                                 in0=sg_tmp[:, 0:512], in1=psf(bank))
                    v.drain()
                    inc(v.nop(), "sv")
                    assert cnt["sv"] == state_p[(c, grp)]
        V_(f)
        scan_done = state_p[(NTB - 1, 3)]

        if "d_S" in dump_h:
            def f(s):
                s.wait_ge(sems["sv"], scan_done)
                dma(s, "sxc", dump_h["d_S"], S_sb[:, 0:D])
            S_(f)

        dump_gate = cnt["sxc"]

        def f(g):
            g.wait_ge(sems["sv"], scan_done)
            g.wait_ge(sems["sxc"], dump_gate)
            dma(g, "sio", cc_in.ap(), S_sb[:, 0:D])
            g.wait_ge(sems["sio"], cnt["sio"])
            inc(g.collective_compute(
                "AllGather", OP.bypass,
                replica_groups=[[0, 1, 2, 3], [4, 5, 6, 7]],
                ins=[cc_in.ap().opt()], outs=[cc_out.ap().opt()]), "scc")
            g.wait_ge(sems["scc"], cnt["scc"])
            dma(g, "sio", Tall[:, 0:3 * D].rearrange("p (j c) -> p j c", j=3),
                cc_out.ap()[0:3 * 128, :].rearrange("(j p) c -> p j c", p=128))
            g.wait_ge(sems["sio"], cnt["sio"])
            for j in range(3):
                wjb = wj_sb[:, j * H:(j + 1) * H].rearrange(
                    "p (h a) -> p h a", a=1).broadcast_to([128, H, 128])
                tj = Tall[:, j * D:(j + 1) * D].rearrange("p (h j) -> p h j", h=H)
                g.tensor_tensor(out=tj, in0=tj, in1=wjb, op=OP.mult)
                g.drain()
            g.tensor_add(out=sinit_sb[:, 0:D], in0=Tall[:, 0:D], in1=Tall[:, D:2 * D])
            g.drain()
            g.tensor_add(out=sinit_sb[:, 0:D], in0=sinit_sb[:, 0:D],
                         in1=Tall[:, 2 * D:3 * D])
            g.drain()
            inc(g.nop(), "sg")
        G_(f)
        sinit_done = ("sg", cnt["sg"])

        sv5b = cnt["sv"]
        st5b = cnt["stt"]
        ss5b = cnt["ss"]
        evq2_p, evk2_p, amt_p, oin_p, oad_p = {}, {}, {}, {}, {}
        vv = sv5b
        ssim = ss5b
        for c in range(NTB):
            ssim += 1; evq2_p[c] = ssim
            if c > 0:
                vv += 1; oin_p[c] = vv
            ssim += 1; evk2_p[c] = ssim
            for grp in range(4):
                vv += 1; amt_p[(c, grp)] = vv
            vv += 1; oad_p[c] = vv
        tq2_p, tn2_p, tk2_p, ta_p, ti_p = {}, {}, {}, {}, {}
        tt = st5b
        for c in range(NTB):
            tt += 1; tq2_p[c] = tt
            if c > 0:
                tt += 1; tn2_p[c] = tt
            tt += 1; tk2_p[c] = tt
            for grp in range(4):
                tt += 1; ta_p[(c, grp)] = tt
            tt += 1; ti_p[c] = tt

        def f(te):
            te.wait_ge(sems["stt"], pg["mm"][15])
            te.wait_ge(sems["ss"], pg["ev"][(3, 7)])
            te.wait_ge(sems["sv"], scan_done)
            for c in range(NTB):
                if c > 0:
                    te.wait_ge(sems["ss"], evk2_p[c - 1])
                for h in range(H):
                    bank = 4 + h // 8
                    te.transpose(out=PSb[:, bank * 1024 + (h % 8) * 128:
                                         bank * 1024 + (h % 8) * 128 + 128],
                                 in_=q_tm[:, c * D + h * 128: c * D + (h + 1) * 128],
                                 identity=ident_sb)
                inc(te.drain(), "stt")
                assert cnt["stt"] == tq2_p[c]
                te.wait_ge(sems["ss"], evq2_p[c])
                if c > 0:
                    te.wait_ge(sems["sv"], oad_p[c - 1])
                    for h in range(H):
                        te.matmul(psf(h // 4)[:, (h % 4) * 128:(h % 4) * 128 + 128],
                                  qf_sb[:, h * 128:(h + 1) * 128],
                                  o_raw[:, c * D + h * 128:c * D + (h + 1) * 128],
                                  start=True, stop=True)
                    inc(te.drain(), "stt")
                    assert cnt["stt"] == tn2_p[c]
                for h in range(H):
                    bank = 4 + h // 8
                    te.transpose(out=PSb[:, bank * 1024 + (h % 8) * 128:
                                         bank * 1024 + (h % 8) * 128 + 128],
                                 in_=k_tm[:, c * D + h * 128: c * D + (h + 1) * 128],
                                 identity=ident_sb)
                inc(te.drain(), "stt")
                assert cnt["stt"] == tk2_p[c]
                te.wait_ge(sems["ss"], evk2_p[c])
                for grp in range(4):
                    bank = 6 + grp % 2
                    if c > 0 or grp >= 2:
                        pc, pg_ = (c, grp - 2) if grp >= 2 else (c - 1, grp + 2)
                        te.wait_ge(sems["sv"], amt_p[(pc, pg_)])
                    for hh in range(4):
                        h = grp * 4 + hh
                        te.matmul(psf(bank)[:, hh * 128:(hh + 1) * 128],
                                  kf_sb[:, h * 128:(h + 1) * 128],
                                  qf_sb[:, h * 128:(h + 1) * 128],
                                  start=True, stop=True)
                    inc(te.drain(), "stt")
                    assert cnt["stt"] == ta_p[(c, grp)]
                te.wait_ge(sems["sv"], amt_p[(c, 3)])
                if c > 0:
                    te.wait_ge(sems["sv"], oin_p[c])
                elif c == 0:
                    pass
                if c > 0:
                    te.wait_ge(sems["sv"], oad_p[c - 1])
                for h in range(H):
                    te.matmul(psf(h // 4)[:, (h % 4) * 128:(h % 4) * 128 + 128],
                              amt_sb[:, h * 128:(h + 1) * 128],
                              v_tm[:, c * D + h * 128:c * D + (h + 1) * 128],
                              start=True, stop=True)
                inc(te.drain(), "stt")
                assert cnt["stt"] == ti_p[c]
        Te(f)

        def f(sc):
            for c in range(NTB):
                sc.wait_ge(sems["stt"], tq2_p[c])
                sc.copy(out=qf_sb[:, 0:D], in_=PSb[:, 4 * 1024:6 * 1024])
                sc.drain()
                inc(sc.nop(), "ss")
                assert cnt["ss"] == evq2_p[c]
                sc.wait_ge(sems["stt"], tk2_p[c])
                sc.copy(out=kf_sb[:, 0:D], in_=PSb[:, 4 * 1024:6 * 1024])
                sc.drain()
                inc(sc.nop(), "ss")
                assert cnt["ss"] == evk2_p[c]
        Sc(f)

        def f(v):
            for c in range(NTB):
                if c > 0:
                    v.wait_ge(sems["stt"], tn2_p[c])
                    v.tensor_tensor(
                        out=o_raw[:, c * D:(c + 1) * D].rearrange(
                            "p (h j) -> p h j", h=H),
                        in0=PS[:, 0:2048].rearrange("p (h j) -> p h j", h=H),
                        in1=crpp_sb[:, 0:H].rearrange(
                            "p (h a) -> p h a", a=1).broadcast_to([128, H, 128]),
                        op=OP.mult)
                    v.drain()
                    inc(v.nop(), "sv")
                    assert cnt["sv"] == oin_p[c]
                for grp in range(4):
                    bank = 6 + grp % 2
                    v.wait_ge(sems["stt"], ta_p[(c, grp)])
                    v.tensor_tensor(out=amt_sb[:, grp * 512:(grp + 1) * 512],
                                    in0=psf(bank),
                                    in1=dmatT_sb[:, grp * 512:(grp + 1) * 512],
                                    op=OP.mult)
                    v.drain()
                    inc(v.nop(), "sv")
                    assert cnt["sv"] == amt_p[(c, grp)]
                v.wait_ge(sems["stt"], ti_p[c])
                if c > 0:
                    v.tensor_add(out=o_raw[:, c * D:(c + 1) * D],
                                 in0=PS[:, 0:2048], in1=o_raw[:, c * D:(c + 1) * D])
                else:
                    v.tensor_copy(out=o_raw[:, 0:D], in_=PS[:, 0:2048])
                v.drain()
                inc(v.nop(), "sv")
                assert cnt["sv"] == oad_p[c]
        V_(f)
        scan2_done = oad_p[NTB - 1]

        if "d_oraw" in dump_h:
            def f(s):
                s.wait_ge(sems["sv"], scan2_done)
                dma(s, "sxc", dump_h["d_oraw"], o_raw)
            S_(f)

    if max_phase >= 6:
        if "d_sinit" in dump_h:
            def f(s):
                s.wait_ge(sems[sinit_done[0]], sinit_done[1])
                dma(s, "sxc", dump_h["d_sinit"], sinit_sb[:, 0:D])
            S_(f)

    if max_phase >= 8:
        o_f2 = [o_f, view(bufW, 32 * KB, BF16, D)]
        sq2 = [sq_sb, view(bufW, 12 * KB, BF16, D)]
        gated2 = [gated_c, view(bufW, 4 * KB, BF16, D)]
        rinv2 = [rinv_sb, view(bufS, 3100, F32, H)]

        sv8 = cnt["sv"]
        ss8 = cnt["ss"]
        st8 = cnt["stt"]
        qf8_p, of8_p, ssq8_p, rinv8_p, gc8_p, gf8_p = {}, {}, {}, {}, {}, {}
        sq8_p, sd8_p, sil8_p = {}, {}, {}
        ssim = ss8
        for c in range(NTB):
            ssim += 1
            sil8_p[c] = ssim
        vv = sv8
        for c in range(NTB):
            ssim += 1; qf8_p[c] = ssim
            if c > 0:
                ssim += 1; sq8_p[c - 1] = ssim
                ssim += 1; sd8_p[c - 1] = ssim
            if c > 1:
                ssim += 1; gf8_p[c - 2] = ssim
        ssim += 1; sq8_p[NTB - 1] = ssim
        ssim += 1; sd8_p[NTB - 1] = ssim
        ssim += 1; gf8_p[NTB - 2] = ssim
        ssim += 1; gf8_p[NTB - 1] = ssim
        for c in range(NTB):
            vv += 1; of8_p[c] = vv
            if c > 0:
                vv += 1; ssq8_p[c - 1] = vv
                vv += 1; rinv8_p[c - 1] = vv
                vv += 1; gc8_p[c - 1] = vv
        vv += 1; ssq8_p[NTB - 1] = vv
        vv += 1; rinv8_p[NTB - 1] = vv
        vv += 1; gc8_p[NTB - 1] = vv
        t8q_p, t8c_p, t8g_p = {}, {}, {}
        tt = st8
        for c in range(NTB):
            tt += 1; t8q_p[c] = tt
            tt += 1; t8c_p[c] = tt
            if c > 0:
                tt += 1; t8g_p[c - 1] = tt
        tt += 1; t8g_p[NTB - 1] = tt

        def emit_t8g(te, c):
            te.wait_ge(sems["sv"], gc8_p[c])
            if c > 0:
                te.wait_ge(sems["ss"], gf8_p[c - 1])
            for h in range(H):
                bank = 6 + h // 8
                te.transpose(out=PSb[:, bank * 1024 + (h % 8) * 128:
                                     bank * 1024 + (h % 8) * 128 + 128],
                             in_=gated2[c % 2][:, h * 128:(h + 1) * 128],
                             identity=ident_sb)
            inc(te.drain(), "stt")
            assert cnt["stt"] == t8g_p[c]

        def f(te):
            te.wait_ge(sems["sv"], scan2_done)
            for c in range(NTB):
                if c > 0:
                    te.wait_ge(sems["ss"], qf8_p[c - 1])
                for h in range(H):
                    bank = 4 + h // 8
                    te.transpose(out=PSb[:, bank * 1024 + (h % 8) * 128:
                                         bank * 1024 + (h % 8) * 128 + 128],
                                 in_=q_tm[:, c * D + h * 128: c * D + (h + 1) * 128],
                                 identity=ident_sb)
                inc(te.drain(), "stt")
                assert cnt["stt"] == t8q_p[c]
                te.wait_ge(sems["ss"], qf8_p[c])
                if c == 0:
                    te.wait_ge(sems[sinit_done[0]], sinit_done[1])
                if c > 0:
                    te.wait_ge(sems["sv"], of8_p[c - 1])
                for h in range(H):
                    te.matmul(psf(h // 4)[:, (h % 4) * 128:(h % 4) * 128 + 128],
                              qf_sb[:, h * 128:(h + 1) * 128],
                              sinit_sb[:, h * 128:(h + 1) * 128],
                              start=True, stop=True)
                inc(te.drain(), "stt")
                assert cnt["stt"] == t8c_p[c]
                if c > 0:
                    emit_t8g(te, c - 1)
            emit_t8g(te, NTB - 1)
        Te(f)

        def emit_tail(v, c):
            v.wait_ge(sems["ss"], sq8_p[c])
            v.tensor_reduce(out=ssq_sb[:, 0:H],
                            in_=sq2[c % 2][:, 0:D].rearrange("p (h j) -> p h j", h=H),
                            axis=mybir.AxisListType.X, op=OP.add)
            v.drain()
            inc(v.nop(), "sv")
            assert cnt["sv"] == ssq8_p[c]
            v.wait_ge(sems["ss"], sd8_p[c])
            v.reciprocal(out=rinv2[c % 2][:, 0:H], in_=sd2_sb[:, 0:H])
            v.drain()
            inc(v.nop(), "sv")
            assert cnt["sv"] == rinv8_p[c]
            v.tensor_tensor(
                out=sq2[c % 2][:, 0:D].rearrange("p (h j) -> p h j", h=H),
                in0=o_f2[c % 2][:, 0:D].rearrange("p (h j) -> p h j", h=H),
                in1=rinv2[c % 2][:, 0:H].rearrange(
                    "p (h a) -> p h a", a=1).broadcast_to([128, H, 128]),
                op=OP.mult)
            v.drain()
            if c > 1:
                v.wait_ge(sems["stt"], t8g_p[c - 2])
            v.wait_ge(sems["ss"], sil8_p[c])
            v.tensor_mul(out=gated2[c % 2][:, 0:D], in0=sq2[c % 2][:, 0:D],
                         in1=g_tm[:, c * D:(c + 1) * D])
            v.drain()
            inc(v.nop(), "sv")
            assert cnt["sv"] == gc8_p[c]

        def f(v):
            for c in range(NTB):
                v.wait_ge(sems["stt"], t8c_p[c])
                if c > 1:
                    v.wait_ge(sems["ss"], sq8_p[c - 2])
                v.tensor_tensor(
                    out=o_f2[c % 2][:, 0:D].rearrange("p (h j) -> p h j", h=H),
                    in0=PS[:, 0:2048].rearrange("p (h j) -> p h j", h=H),
                    in1=crsg_sb[:, c * H:(c + 1) * H].rearrange(
                        "p (h a) -> p h a", a=1).broadcast_to([128, H, 128]),
                    op=OP.mult)
                v.drain()
                v.tensor_add(out=o_f2[c % 2][:, 0:D], in0=o_f2[c % 2][:, 0:D],
                             in1=o_raw[:, c * D:(c + 1) * D])
                v.drain()
                inc(v.nop(), "sv")
                assert cnt["sv"] == of8_p[c]
                if c > 0:
                    emit_tail(v, c - 1)
            emit_tail(v, NTB - 1)
        V_(f)

        def emit_gfm(sc, c):
            sc.wait_ge(sems["stt"], t8g_p[c])
            dst = gated_fm.rearrange("p (h t) -> p h t", h=H)[
                :, :, c * 128:(c + 1) * 128]
            sc.mul(out=dst[:, 0:8, :],
                   in_=PSb[:, 6 * 1024:7 * 1024].rearrange("p (h t) -> p h t", h=8),
                   mul=gnwpp_sb[:, 0:1])
            sc.mul(out=dst[:, 8:16, :],
                   in_=PSb[:, 7 * 1024:8 * 1024].rearrange("p (h t) -> p h t", h=8),
                   mul=gnwpp_sb[:, 0:1])
            sc.drain()
            inc(sc.nop(), "ss")
            assert cnt["ss"] == gf8_p[c]

        def f(sc):
            sc.wait_ge(sems["sv"], scan_done)
            for c in range(NTB):
                sc.activation(out=g_tm[:, c * D:(c + 1) * D],
                              in_=g_tm[:, c * D:(c + 1) * D], func=AF.Silu)
                sc.drain()
                inc(sc.nop(), "ss")
                assert cnt["ss"] == sil8_p[c]
            def emit_sqsd(c):
                sc.wait_ge(sems["sv"], of8_p[c])
                sc.activation(out=sq2[c % 2][:, 0:D], in_=o_f2[c % 2][:, 0:D],
                              func=AF.Square)
                sc.drain()
                inc(sc.nop(), "ss")
                assert cnt["ss"] == sq8_p[c]
                sc.wait_ge(sems["sv"], ssq8_p[c])
                sc.activation(out=sd2_sb[:, 0:H], in_=ssq_sb[:, 0:H], func=AF.Sqrt,
                              bias=epsc_sb[:, 0:1], scale=epsc_sb[:, 1:2])
                sc.drain()
                inc(sc.nop(), "ss")
                assert cnt["ss"] == sd8_p[c]

            for c in range(NTB):
                sc.wait_ge(sems["stt"], t8q_p[c])
                sc.copy(out=qf_sb[:, 0:D], in_=PSb[:, 4 * 1024:6 * 1024])
                sc.drain()
                inc(sc.nop(), "ss")
                assert cnt["ss"] == qf8_p[c]
                if c > 0:
                    emit_sqsd(c - 1)
                if c > 1:
                    emit_gfm(sc, c - 2)
            emit_sqsd(NTB - 1)
            emit_gfm(sc, NTB - 2)
            emit_gfm(sc, NTB - 1)
        Sc(f)
        gated_done = gf8_p[NTB - 1]
        t8c_last = t8c_p[NTB - 1]
        t8g_last = t8g_p[NTB - 1]

        if "d_gated" in dump_h:
            def f(s):
                s.wait_ge(sems["ss"], gated_done)
                dma(s, "sxc", dump_h["d_gated"], gated_fm)
            S_(f)

    if max_phase >= 9:
        r9 = projection(wo_h, "wo", None,
                        first_gate={tb: ("ss", gated_done) for tb in range(NTB)},
                        slab_gate=(("stt", t8c_last), ("stt", t8c_last),
                                   ("stt", t8g_last), ("stt", t8g_last)))

    if max_phase >= 10:
        def gate10(tb):
            return ("sv", r9["ev"][(3, tb)])
        xn2_evac = ln_transpose(None, xn2_fm, xst, xn2tm_st, gate10,
                                ("sv", r9["ev"][(3, 7)]),
                                sbuf_tiles=[xnew[tb][:, 0:D] for tb in range(NTB)])
        xn2_done = xn2_evac[NTB - 1]

        if "d_xn2fm" in dump_h:
            def f(s):
                s.wait_ge(sems["sv"], xn2_done)
                dma(s, "sxc", dump_h["d_xn2fm"], xn2_fm)
            S_(f)

    if max_phase >= 11:
        def f(s):
            s.wait_ge(sems["sv"], xn2_done)
            s.wait_ge(sems[r9["store"][0][0]], r9["store"][0][1])
            dma(s, "sxc", sdown_sb, sdown_h)
        S_(f)
        sdown_ld = cnt["sxc"]

        def fuse_view(ib):
            return (fuseA[:, ib * 512:(ib + 1) * 512] if ib < 32
                    else fuseB[:, (ib - 32) * 512:(ib - 31) * 512])

        prev_down_ev = None
        prev_dmm_last = None
        for hf in range(2):
            gsem = {g16: ("sgda", "sgdb")[g16 % 2] for g16 in range(16)}
            usem = {g16: ("sgdc", "sgdd")[g16 % 2] for g16 in range(16)}
            ioa11, iob11 = cnt["sioa"], cnt["siob"]
            gbase = {nm: cnt[nm] for nm in ("sgda", "sgdb", "sgdc", "sgdd")}
            st11, ss11, sv11 = cnt["stt"], cnt["ss"], cnt["sv"]
            gld_p = {g16: gbase[gsem[g16]] + 16 * (g16 // 2 + 1) for g16 in range(16)}
            uld_p = {g16: gbase[usem[g16]] + 16 * (g16 // 2 + 1) for g16 in range(16)}
            mm_p = {(g16, w): st11 + 2 * g16 + w + 1
                    for g16 in range(16) for w in range(2)}
            silu_p = {(g16, b): ss11 + 4 * g16 + b + 1
                      for g16 in range(16) for b in range(4)}
            fuse_p = {(g16, b): sv11 + 4 * g16 + b + 1
                      for g16 in range(16) for b in range(4)}

            def f(g, gld_p=gld_p, uld_p=uld_p, mm_p=mm_p, st11=st11,
                  ss11=ss11, sv11=sv11, ioa11=ioa11, iob11=iob11, hf=hf,
                  gsem=gsem, usem=usem):
                for g16 in range(16):
                    if g16 < 2:
                        g.wait_ge(sems["stt"], st11)
                        g.wait_ge(sems["ss"], ss11)
                        g.wait_ge(sems["sv"], sv11)
                        g.wait_ge(sems["sioa"], ioa11)
                        g.wait_ge(sems["siob"], iob11)
                        for tb in range(NTB):
                            g.wait_ge(sems[r9["store"][tb][0]], r9["store"][tb][1])
                    else:
                        g.wait_ge(sems["stt"], mm_p[(g16 - 2, 0)])
                    got = dma(g, gsem[g16], gu_bfg2[g16 % 2][:, 0:8192],
                              wgate_h[g16].rearrange("p kc j -> p (kc j)"))
                    assert got == gld_p[g16]
                    if g16 >= 2:
                        g.wait_ge(sems["stt"], mm_p[(g16 - 2, 1)])
                    got = dma(g, usem[g16], gu_bfu2[g16 % 2][:, 0:8192],
                              wup_h[g16].rearrange("p kc j -> p (kc j)"))
                    assert got == uld_p[g16]
            G_(f)

            def f(te, gld_p=gld_p, uld_p=uld_p, mm_p=mm_p, silu_p=silu_p,
                  fuse_p=fuse_p, hf=hf, prev_down_ev=prev_down_ev, gsem=gsem,
                  usem=usem):
                if hf == 0:
                    te.wait_ge(sems["sv"], xn2_done)
                else:
                    te.wait_ge(sems["sv"], prev_down_ev)
                for g16 in range(16):
                    for w in range(2):
                        te.wait_ge(sems[gsem[g16] if w == 0 else usem[g16]],
                                   (gld_p if w == 0 else uld_p)[g16])
                        if w == 0 and g16 > 0:
                            te.wait_ge(sems["ss"], silu_p[(g16 - 1, 3)])
                        if w == 1 and g16 > 0:
                            te.wait_ge(sems["sv"], fuse_p[(g16 - 1, 3)])
                        wbf = (gu_bfg2 if w == 0 else gu_bfu2)[g16 % 2]
                        for kc in range(NKC):
                            for b in range(4):
                                te.matmul(psf(4 * w + b),
                                          wbf[:, kc * 512 + b * 128:
                                              kc * 512 + (b + 1) * 128],
                                          xn2_fm[:, kc * T + hf * 512:
                                                 kc * T + (hf + 1) * 512],
                                          start=(kc == 0), stop=(kc == NKC - 1))
                        inc(te.drain(), "stt")
                        assert cnt["stt"] == mm_p[(g16, w)]
            Te(f)

            def f(sc, mm_p=mm_p, silu_p=silu_p, fuse_p=fuse_p, hf=hf):
                for g16 in range(16):
                    sc.wait_ge(sems["stt"], mm_p[(g16, 0)])
                    if g16 > 0:
                        sc.wait_ge(sems["sv"], fuse_p[(g16 - 1, 3)])
                    for b in range(4):
                        ib = g16 * 4 + b
                        sc.activation(out=silu_sb11[b][:, 0:512], in_=psf(b),
                                      func=AF.Silu, scale=sgate_sb[:, ib:ib + 1])
                        sc.drain()
                        inc(sc.nop(), "ss")
                        assert cnt["ss"] == silu_p[(g16, b)]
            Sc(f)

            def f(v, mm_p=mm_p, silu_p=silu_p, fuse_p=fuse_p, hf=hf):
                for g16 in range(16):
                    v.wait_ge(sems["stt"], mm_p[(g16, 1)])
                    v.wait_ge(sems["ss"], silu_p[(g16, 3)])
                    for b in range(4):
                        ib = g16 * 4 + b
                        v.scalar_tensor_tensor(out=fuse_view(ib), in0=psf(4 + b),
                                               scalar=sup_sb[:, ib:ib + 1],
                                               in1=silu_sb11[b][:, 0:512],
                                               op0=OP.mult, op1=OP.mult)
                        v.drain()
                        inc(v.nop(), "sv")
                        assert cnt["sv"] == fuse_p[(g16, b)]
            V_(f)

            NGRP = 16
            dsem = {n: ("sgde", "sgdf")[n % 2] for n in range(2 * NGRP)}
            dasem = {n: ("sioa", "siob")[n % 2] for n in range(8)}
            dbase = {nm: cnt[nm] for nm in ("sgde", "sgdf")}
            st11d, sv11d = cnt["stt"], cnt["sv"]
            io11d = {"sioa": cnt["sioa"], "siob": cnt["siob"]}
            dld_p = {n: dbase[dsem[n]] + 16 * (n // 2 + 1) for n in range(2 * NGRP)}
            dmm_p = {n: st11d + n + 1 for n in range(2 * NGRP)}
            dev_p = {}
            vv = sv11d
            for cgp in range(2):
                for q4 in range(4):
                    vv += 1
                    dev_p[(cgp, q4)] = vv
            dacc_p = {n: io11d[dasem[n]] + 16 * (n // 2 + 1) for n in range(8)}

            def mk_dld(n0, n1):
                def f(g, dld_p=dld_p, dmm_p=dmm_p, mm_p=mm_p,
                      dsem=dsem, hf=hf, prev_dmm_last=prev_dmm_last):
                    for n in range(n0, n1):
                        cgp, grp = divmod(n, NGRP)
                        if n < 2:
                            if hf == 0:
                                g.wait_ge(sems["sv"], xn2_done)
                                for tb2 in range(1, 4):
                                    g.wait_ge(sems[r9["store"][tb2][0]],
                                              r9["store"][tb2][1])
                            else:
                                g.wait_ge(sems["stt"], prev_dmm_last)
                        else:
                            g.wait_ge(sems["stt"], dmm_p[n - 2])
                        got = dma(g, dsem[n], wd_bf[n % 2][:, 0:4096],
                                  wdown_h[cgp, grp])
                        assert got == dld_p[n]
                return f

            def mk_dacc(cgp):
                def f(g, cgp=cgp, dev_p=dev_p, dacc_p=dacc_p, hf=hf, dasem=dasem):
                    for q4 in range(4):
                        n = cgp * 4 + q4
                        g.wait_ge(sems["sv"], dev_p[(cgp, q4)])
                        tb = hf * 4 + q4
                        g.wait_ge(sems[r9["store"][tb][0]], r9["store"][tb][1])
                        got = dma(g, dasem[n],
                                  out_h[tb * 128:(tb + 1) * 128,
                                        cgp * 1024:(cgp + 1) * 1024],
                                  dstage[n % 2][:, 0:1024], accum_op=OP.add)
                        assert got == dacc_p[n]
                return f

            def f(te, dld_p=dld_p, dmm_p=dmm_p, dev_p=dev_p,
                  fuse_p=fuse_p, silu_p=silu_p, dsem=dsem):
                te.wait_ge(sems["sv"], fuse_p[(15, 3)])
                te.wait_ge(sems["ss"], silu_p[(15, 3)])
                for cgp in range(2):
                    if cgp > 0:
                        for q4 in range(4):
                            te.wait_ge(sems["sv"], dev_p[(0, q4)])
                    for grp in range(NGRP):
                        n = cgp * NGRP + grp
                        te.wait_ge(sems[dsem[n]], dld_p[n])
                        for ib4 in range(4):
                            ib = grp * 4 + ib4
                            for q4 in range(4):
                                for cg2 in range(2):
                                    te.matmul(
                                        psf(q4 * 2 + cg2),
                                        fuse_view(ib)[:, q4 * 128:(q4 + 1) * 128],
                                        wd_bf[n % 2][:, ib4 * 1024 + cg2 * 512:
                                                     ib4 * 1024 + (cg2 + 1) * 512],
                                        start=(grp == 0 and ib4 == 0),
                                        stop=(grp == NGRP - 1 and ib4 == 3))
                        inc(te.drain(), "stt")
                        assert cnt["stt"] == dmm_p[n]
            down_te = f

            def f(v, dmm_p=dmm_p, dev_p=dev_p, dacc_p=dacc_p, dasem=dasem):
                v.wait_ge(sems["sxc"], sdown_ld)
                for cgp in range(2):
                    v.wait_ge(sems["stt"], dmm_p[cgp * NGRP + NGRP - 1])
                    for q4 in range(4):
                        n = cgp * 4 + q4
                        if n >= 2:
                            v.wait_ge(sems[dasem[n - 2]], dacc_p[n - 2])
                        v.tensor_mul(out=dstage[n % 2][:, 0:1024],
                                     in0=PS[:, q4 * 1024:(q4 + 1) * 1024],
                                     in1=sdown_sb[:, cgp * 1024:(cgp + 1) * 1024])
                        v.drain()
                        inc(v.nop(), "sv")
                        assert cnt["sv"] == dev_p[(cgp, q4)]
            down_v = f

            G_(mk_dld(0, 2))
            Te(down_te)
            V_(down_v)
            G_(mk_dld(2, NGRP))
            G_(mk_dacc(0))
            G_(mk_dld(NGRP, 2 * NGRP))
            G_(mk_dacc(1))
            prev_down_ev = dev_p[(1, 3)]
            prev_dmm_last = dmm_p[2 * NGRP - 1]

    def f(g):
        for nm in ("sio", "sioa", "siob", "sxa", "sxb", "sxc", "swa", "swb"):
            g.wait_ge(sems[nm], cnt[nm])
    G_(f)

    stack.close()
    return nc



def _prep_weights(Wq, Wk, Wv, Wg, Wo, w_gate, w_up, w_down,
                  s_gate, s_up, s_down, g_norm_w):
    def proj_r(W):
        return np.ascontiguousarray(
            W.reshape(4, 4, 128, 4, 512).transpose(3, 0, 2, 1, 4).reshape(4, 4, 128, 2048))

    def gu_r(W8):
        return np.ascontiguousarray(W8.reshape(16, 128, 16, 512).transpose(2, 1, 0, 3))

    wd = np.ascontiguousarray(w_down.reshape(16, 4, 128, 2, 1024).transpose(3, 0, 2, 1, 4).reshape(2, 16, 128, 4096))
    return dict(
        wq=proj_r(Wq), wk=proj_r(Wk), wv=proj_r(Wv), wg=proj_r(Wg), wo=proj_r(Wo),
        wgate_l=gu_r(w_gate), wup_l=gu_r(w_up), wdown_l=wd,
        sgate_pp=np.ascontiguousarray(s_gate.reshape(NIB, 128).T),
        sup_pp=np.ascontiguousarray(s_up.reshape(NIB, 128).T),
        sdown_rep=np.ascontiguousarray(np.broadcast_to(s_down, (128, D))),
        gnw_rep=np.ascontiguousarray(np.broadcast_to(g_norm_w, (128, 128))),
        gnw_pp=np.ascontiguousarray(g_norm_w.reshape(128, 1)),
    )


def kernel(L0_attn_x, L0_attn_Wq, L0_attn_Wk, L0_attn_Wv, L0_attn_Wg,
           L0_attn_g_norm_w, L0_attn_Wo, L0_ffn_w_gate, L0_ffn_w_up,
           L0_ffn_w_down, L0_ffn_s_gate, L0_ffn_s_up, L0_ffn_s_down):
    x = np.ascontiguousarray(np.asarray(L0_attn_x, dtype=np.float32))
    B, S, _ = x.shape
    shared = _prep_weights(
        np.asarray(L0_attn_Wq, np.float32), np.asarray(L0_attn_Wk, np.float32),
        np.asarray(L0_attn_Wv, np.float32), np.asarray(L0_attn_Wg, np.float32),
        np.asarray(L0_attn_Wo, np.float32),
        np.asarray(L0_ffn_w_gate), np.asarray(L0_ffn_w_up), np.asarray(L0_ffn_w_down),
        np.asarray(L0_ffn_s_gate, np.float32), np.asarray(L0_ffn_s_up, np.float32),
        np.asarray(L0_ffn_s_down, np.float32),
        np.asarray(L0_attn_g_norm_w, np.float32))

    nc = build()
    in_maps = []
    for core in range(8):
        b = core // NSEG
        seg = core % NSEG
        m = dict(shared)
        m["x"] = np.ascontiguousarray(x[b, seg * SEG:(seg + 1) * SEG, :])
        m.update(_host_consts(core))
        in_maps.append({k: np.asarray(v) for k, v in m.items()})
    res = run_bass_kernel_spmd(nc, in_maps, core_ids=list(range(8)))
    out = np.empty((B, S, D), dtype=np.float32)
    for core in range(8):
        b = core // NSEG
        seg = core % NSEG
        out[b, seg * SEG:(seg + 1) * SEG, :] = res.results[core]["out"]
    return out


# version 13
# speedup vs baseline: 1.0680x; 1.0130x over previous
import math
import numpy as np
import ml_dtypes
import contextlib
import concourse.bass as bass
import concourse.mybir as mybir
from concourse.bass_utils import run_bass_kernel_spmd

F32 = mybir.dt.float32
BF16 = mybir.dt.bfloat16
I8 = mybir.dt.int8
U8 = mybir.dt.uint8
AF = mybir.ActivationFunctionType
OP = mybir.AluOpType

T = 1024
D = 2048
H = 16
NTB = 8
NKC = 16
NCG = 4
NIB = 64
EPS = 1e-5
SEG = 1024
NSEG = 4
KB = 1024


def _host_consts(core_id):
    h = np.arange(H, dtype=np.float64)
    gamma = 1.0 - np.exp2(-5.0 - h)
    seg = core_id % NSEG
    pos0 = seg * SEG

    inv = 10000.0 ** (-(np.arange(64, dtype=np.float64) / 64.0))
    t_loc = np.arange(T, dtype=np.float64).reshape(NTB, 128)
    ang = (pos0 + t_loc)[:, :, None] * inv[None, None, :]
    cos = np.cos(ang)
    sin = np.sin(ang)

    def tmtab(a, scale=1.0):
        return np.ascontiguousarray(
            (a * scale).transpose(1, 0, 2).reshape(128, NTB * 64)).astype(np.float32)

    qs = 1.0 / math.sqrt(128.0)
    idx = np.arange(128, dtype=np.float64)
    rel = idx[None, :] - idx[:, None]
    dmatT = np.where(rel >= 0, gamma[:, None, None] ** rel[None], 0.0)
    dmatT = np.ascontiguousarray(
        dmatT.transpose(1, 0, 2).reshape(128, H * 128)).astype(np.float32)
    kdec = (gamma[None, :] ** (127.0 - idx[:, None])).astype(np.float32)
    gC = np.broadcast_to(gamma ** 128.0, (128, H)).astype(np.float32).copy()
    cross_pp = (gamma[None, :] ** (idx[:, None] + 1.0)).astype(np.float32)
    cs = np.zeros((128, NTB * H), dtype=np.float64)
    for c in range(NTB):
        cs[:, c * H:(c + 1) * H] = gamma[None, :] ** (c * 128.0 + idx[:, None] + 1.0)
    crossseg = cs.astype(np.float32)
    wj = np.zeros((128, NSEG * H), dtype=np.float64)
    for j in range(seg):
        wj[:, j * H:(j + 1) * H] = gamma[None, :] ** (1024.0 * (seg - 1 - j))
    wj = wj.astype(np.float32)
    ident = np.eye(128, dtype=ml_dtypes.bfloat16)
    return dict(cosq=tmtab(cos, qs), sinq=tmtab(sin, qs),
                cosk=tmtab(cos), sink=tmtab(sin), dmatT=dmatT,
                kdec=kdec, gC=gC, cross_pp=cross_pp, crossseg=crossseg,
                wj=wj, ident=ident,
                epsc=np.broadcast_to(np.array([EPS, 1.0 / 128.0], np.float32),
                                     (128, 2)).copy())


def build(dump=(), max_phase=99):
    nc = bass.Bass("TRN2", target_bir_lowering=False, debug=False)

    x_h = nc.dram_tensor("x", [T, D], F32, kind="ExternalInput").ap()
    out_h = nc.dram_tensor("out", [T, D], F32, kind="ExternalOutput").ap()
    wq_h = nc.dram_tensor("wq", [NCG, 4, 128, D], F32, kind="ExternalInput").ap()
    wk_h = nc.dram_tensor("wk", [NCG, 4, 128, D], F32, kind="ExternalInput").ap()
    wv_h = nc.dram_tensor("wv", [NCG, 4, 128, D], F32, kind="ExternalInput").ap()
    wg_h = nc.dram_tensor("wg", [NCG, 4, 128, D], F32, kind="ExternalInput").ap()
    wo_h = nc.dram_tensor("wo", [NCG, 4, 128, D], F32, kind="ExternalInput").ap()
    wgate_h = nc.dram_tensor("wgate_l", [16, 128, 16, 512], I8, kind="ExternalInput").ap()
    wup_h = nc.dram_tensor("wup_l", [16, 128, 16, 512], I8, kind="ExternalInput").ap()
    wdown_h = nc.dram_tensor("wdown_l", [2, 16, 128, 4096], I8, kind="ExternalInput").ap()
    sgate_h = nc.dram_tensor("sgate_pp", [128, NIB], F32, kind="ExternalInput").ap()
    sup_h = nc.dram_tensor("sup_pp", [128, NIB], F32, kind="ExternalInput").ap()
    sdown_h = nc.dram_tensor("sdown_rep", [128, D], F32, kind="ExternalInput").ap()
    gnw_h = nc.dram_tensor("gnw_rep", [128, 128], F32, kind="ExternalInput").ap()
    gnwpp_h = nc.dram_tensor("gnw_pp", [128, 1], F32, kind="ExternalInput").ap()
    cosq_h = nc.dram_tensor("cosq", [128, NTB * 64], F32, kind="ExternalInput").ap()
    sinq_h = nc.dram_tensor("sinq", [128, NTB * 64], F32, kind="ExternalInput").ap()
    cosk_h = nc.dram_tensor("cosk", [128, NTB * 64], F32, kind="ExternalInput").ap()
    sink_h = nc.dram_tensor("sink", [128, NTB * 64], F32, kind="ExternalInput").ap()
    dmat_h = nc.dram_tensor("dmatT", [128, H * 128], F32, kind="ExternalInput").ap()
    kdec_h = nc.dram_tensor("kdec", [128, H], F32, kind="ExternalInput").ap()
    gC_h = nc.dram_tensor("gC", [128, H], F32, kind="ExternalInput").ap()
    crpp_h = nc.dram_tensor("cross_pp", [128, H], F32, kind="ExternalInput").ap()
    crsg_h = nc.dram_tensor("crossseg", [128, NTB * H], F32, kind="ExternalInput").ap()
    wj_h = nc.dram_tensor("wj", [128, NSEG * H], F32, kind="ExternalInput").ap()
    ident_h = nc.dram_tensor("ident", [128, 128], BF16, kind="ExternalInput").ap()
    epsc_h = nc.dram_tensor("epsc", [128, 2], F32, kind="ExternalInput").ap()

    cc_in = nc.dram_tensor("cc_in", [128, D], BF16)
    cc_out = nc.dram_tensor("cc_out", [NSEG * 128, D], BF16)

    dump_h = {}
    for name, shape, dt in (
        ("d_xnfm", [128, NKC * T], BF16), ("d_qtm", [128, NTB * D], BF16),
        ("d_ktm", [128, NTB * D], BF16), ("d_vtm", [128, NTB * D], BF16),
        ("d_gtm", [128, NTB * D], BF16), ("d_oraw", [128, NTB * D], BF16),
        ("d_S", [128, D], F32), ("d_sinit", [128, D], BF16),
        ("d_gated", [128, H * T], BF16), ("d_xn2fm", [128, NKC * T], BF16),
    ):
        if name in dump:
            dump_h[name] = nc.dram_tensor(name, shape, dt, kind="ExternalOutput").ap()

    bufA = nc.alloc_sbuf_tensor("bufA", [128, 32 * KB], U8)
    bufB = nc.alloc_sbuf_tensor("bufB", [128, 32 * KB], U8)
    bufC = nc.alloc_sbuf_tensor("bufC", [128, 32 * KB], U8)
    bufD = nc.alloc_sbuf_tensor("bufD", [128, 32 * KB], U8)
    bufE = nc.alloc_sbuf_tensor("bufE", [128, 32 * KB], U8)
    bufW = nc.alloc_sbuf_tensor("bufW", [128, 38 * KB], U8)
    bufS = nc.alloc_sbuf_tensor("bufS", [128, 4 * KB], U8)

    def view(buf, off_b, dt, n):
        esz = {F32: 4, BF16: 2, I8: 1, U8: 1}[dt]
        return buf.ap()[:, off_b:off_b + n * esz].bitcast(dt)

    q_tm = view(bufA, 0, BF16, NTB * D)
    xn2_fm = view(bufA, 0, BF16, NKC * T)
    k_tm = view(bufB, 0, BF16, NTB * D)
    Tall = view(bufW, 8 * KB, BF16, 3 * D)
    gated_fm = view(bufB, 0, BF16, H * T)
    fuseA = view(bufB, 0, BF16, 32 * 512)
    v_tm = view(bufC, 0, BF16, NTB * D)
    fuseB = view(bufC, 0, BF16, 32 * 512)
    g_tm = view(bufD, 0, BF16, NTB * D)
    xn_fm = view(bufE, 0, BF16, NKC * T)
    o_raw = view(bufE, 0, BF16, NTB * D)

    sdown_sb = view(bufD, 0, F32, D)
    xnew = [view(bufD, 8 * KB * i, F32, D) for i in range(4)] + \
           [view(bufE, 8 * KB * i, F32, D) for i in range(4)]
    xo_st = [view(bufW, 24 * KB + 2 * KB * i, F32, 512) for i in range(2)]
    silu_sb11 = [view(bufD, 8 * KB + 1 * KB * i, BF16, 512) for i in range(4)]
    xst = [view(bufD, 12 * KB + 8 * KB * i, F32, D) for i in range(2)]
    xn2tm_st = [view(bufW, 8 * KB, BF16, D), view(bufW, 12 * KB, BF16, D)]

    x_stage = [view(bufW, 8 * KB * i, F32, D) for i in range(2)]
    xntm_st = [view(bufD, 0, BF16, D), view(bufD, 4 * KB, BF16, D)]
    wsl_f32 = [view(bufW, 8 * KB * i, F32, D) for i in range(2)]
    wsl_bf = [view(bufW, 16 * KB, BF16, D), view(bufW, 20 * KB, BF16, D),
              view(bufW, 0, BF16, D), view(bufW, 4 * KB, BF16, D)]
    rope_tabs = view(bufW, 24 * KB, F32, 4 * 512)
    rp_st = ([view(bufW, 32 * KB + KB * i, BF16, 512) for i in range(4)] +
             [view(bufW, 8 * KB + KB * i, BF16, 512) for i in range(4)])
    rope_t1 = view(bufW, 36 * KB, BF16, 256)
    rope_t2 = view(bufW, 36 * KB + 512, BF16, 256)
    dmatT_sb = view(bufW, 0, F32, H * 128)
    S_sb = view(bufW, 8 * KB, F32, D)
    Sb_sb = view(bufW, 16 * KB, BF16, D)
    qf_sb = view(bufW, 20 * KB, BF16, D)
    kf_sb = view(bufW, 24 * KB, BF16, D)
    amt_sb = view(bufW, 28 * KB, BF16, D)
    kp_sb = view(bufW, 32 * KB, BF16, D)
    sg_tmp = view(bufW, 36 * KB, F32, 512)
    sinit_sb = view(bufW, 8 * KB, BF16, D)
    gated_c = view(bufW, 0, BF16, D)
    o_f = view(bufW, 24 * KB, BF16, D)
    sq_sb = view(bufW, 28 * KB, BF16, D)
    sgs_sb = view(bufW, 32 * KB, BF16, D)
    gu_bfg2 = [view(bufE, 16 * KB * i, BF16, 8192) for i in range(2)]
    gu_bfu2 = [view(bufW, 16 * KB * i, BF16, 8192) for i in range(2)]
    wd_bf = [view(bufD, 12 * KB + 8 * KB * i, BF16, 4096) for i in range(2)]
    dstage = [view(bufW, 24 * KB + 4 * KB * i, F32, 1024) for i in range(2)]

    ident_sb = view(bufS, 0, BF16, 128)
    kdec_sb = view(bufS, 256, F32, H)
    gC_sb = view(bufS, 320, F32, H)
    crpp_sb = view(bufS, 384, F32, H)
    ssq_sb = view(bufS, 448, F32, H)
    rinv_sb = view(bufS, 512, F32, H)
    stats_sb = view(bufS, 576, F32, 24)
    mv_sb = view(bufS, 672, F32, 2)
    sd_sb = view(bufS, 680, F32, 2)
    inv_sb = view(bufS, 688, F32, 2)
    b2_sb = view(bufS, 696, F32, 2)
    crsg_sb = view(bufS, 704, F32, NTB * H)
    wj_sb = view(bufS, 1216, F32, NSEG * H)
    gnw_sb = view(bufS, 1472, F32, 128)
    sgate_sb = view(bufS, 1984, F32, NIB)
    sup_sb = view(bufS, 2240, F32, NIB)
    sd2_sb = view(bufS, 2496, F32, H)
    statsP = [view(bufS, 576, F32, 24), view(bufS, 3200, F32, 24)]
    mvP = [view(bufS, 672, F32, 2), view(bufS, 3296, F32, 2)]
    sdP = [view(bufS, 680, F32, 2), view(bufS, 3304, F32, 2)]
    invP = [view(bufS, 688, F32, 2), view(bufS, 3312, F32, 2)]
    b2P = [view(bufS, 696, F32, 2), view(bufS, 3320, F32, 2)]
    epsc_sb = view(bufS, 2560, F32, 2)
    gnwpp_sb = view(bufS, 2568, F32, 1)

    PSt = nc.alloc_psum_tensor("PSall", [128, 8 * 512], F32)
    PS = PSt.ap()
    PSb = PS.bitcast(BF16)

    def psf(bank, nbanks=1):
        return PS[:, bank * 512:(bank + nbanks) * 512]

    stack = contextlib.ExitStack()
    sems = {}
    cnt = {}
    for name in ("sxc", "sxa", "sxb", "swa", "swb", "sio", "sioa", "siob",
                 "sgda", "sgdb", "sgdc", "sgdd", "sgde", "sgdf", "scc", "sv", "ss", "sg", "stt"):
        sems[name] = stack.enter_context(nc.semaphore(name))
        cnt[name] = 0

    def inc(instr, sem, n=1):
        cnt[sem] += n
        instr.then_inc(sems[sem], n)

    def dma(eng, sem, out, in_, **kw):
        eng.wait_ge(sems[sem], cnt[sem])
        cnt[sem] += 16
        eng.dma_start(out=out, in_=in_, **kw).then_inc(sems[sem], 16)
        return cnt[sem]

    block = stack.enter_context(nc.Block())
    S_, Sc, V_, Te, G_ = block.sync, block.scalar, block.vector, block.tensor, block.gpsimd

    def f(s):
        dma(s, "sxc", ident_sb, ident_h)
        dma(s, "sxc", epsc_sb, epsc_h)
        dma(s, "sxc", kdec_sb, kdec_h)
        dma(s, "sxc", gC_sb, gC_h)
        dma(s, "sxc", crpp_sb, crpp_h)
        dma(s, "sxc", crsg_sb, crsg_h)
        dma(s, "sxc", wj_sb, wj_h)
        dma(s, "sxc", gnw_sb, gnw_h)
        dma(s, "sxc", gnwpp_sb, gnwpp_h)
        dma(s, "sxc", sgate_sb, sgate_h)
        dma(s, "sxc", sup_sb, sup_h)
        dma(s, "sxc", rope_tabs[:, 0:512], cosq_h)
        dma(s, "sxc", rope_tabs[:, 512:1024], sinq_h)
        dma(s, "sxc", rope_tabs[:, 1024:1536], cosk_h)
        dma(s, "sxc", rope_tabs[:, 1536:2048], sink_h)
    S_(f)
    const_done = cnt["sxc"]

    def ln_transpose(src_dram, fm_dst, stage_pair, tmst, dma_gate, tensor_gate,
                     sbuf_tiles=None):
        xsem = {tb: ("sxa", "sxb")[tb % 2] for tb in range(NTB)}
        sx0 = {"sxa": cnt["sxa"], "sxb": cnt["sxb"]}
        sv0 = cnt["sv"]
        ss0 = cnt["ss"]
        st0 = cnt["stt"]
        ld_p = {tb: sx0[xsem[tb]] + 16 * (tb // 2 + 1) for tb in range(NTB)}
        stats_p = {tb: sv0 + 3 * tb + 1 for tb in range(NTB)}
        inv_p = {tb: sv0 + 3 * tb + 2 for tb in range(NTB)}
        evac_p = {tb: sv0 + 3 * tb + 3 for tb in range(NTB)}
        sd_p = {tb: ss0 + 2 * tb + 1 for tb in range(NTB)}
        app_p = {tb: ss0 + 2 * tb + 2 for tb in range(NTB)}
        tr_p = {tb: st0 + tb + 1 for tb in range(NTB)}

        if sbuf_tiles is None:
            def f(s):
                for tb in range(NTB):
                    if dma_gate is not None:
                        sm, vv = dma_gate(tb)
                        s.wait_ge(sems[sm], vv)
                    if tb >= 2:
                        s.wait_ge(sems["ss"], app_p[tb - 2])
                    got = dma(s, xsem[tb], stage_pair[tb % 2],
                              src_dram[tb * 128:(tb + 1) * 128, :])
                    assert got == ld_p[tb]
            S_(f)

        def f(v):
            for tb in range(NTB):
                if sbuf_tiles is None:
                    v.wait_ge(sems[xsem[tb]], ld_p[tb])
                elif dma_gate is not None:
                    sm, vv = dma_gate(tb)
                    v.wait_ge(sems[sm], vv)
                par = tb % 2
                if tb >= 2:
                    v.wait_ge(sems["ss"], app_p[tb - 2])
                xs = stage_pair[tb % 2] if sbuf_tiles is None else sbuf_tiles[tb]
                for i in range(4):
                    v.bn_stats(out=statsP[par][:, 6 * i:6 * i + 6],
                               in_=xs[:, 512 * i:512 * (i + 1)])
                v.drain()
                v.bn_aggr(out=mvP[par][:, 0:2], in_=statsP[par][:, 0:24])
                v.drain()
                inc(v.nop(), "sv")
                assert cnt["sv"] == stats_p[tb]
                v.wait_ge(sems["ss"], sd_p[tb])
                v.reciprocal(out=invP[par][:, 0:1], in_=sdP[par][:, 0:1])
                v.drain()
                v.scalar_tensor_tensor(out=b2P[par][:, 0:1], in0=mvP[par][:, 0:1],
                                       scalar=-1.0, in1=invP[par][:, 0:1],
                                       op0=OP.mult, op1=OP.mult)
                v.drain()
                inc(v.nop(), "sv")
                assert cnt["sv"] == inv_p[tb]
                v.wait_ge(sems["stt"], tr_p[tb])
                for half in range(2):
                    bank = 4 + 2 * par + half
                    src = PSb[:, bank * 1024:(bank + 1) * 1024]
                    dst = fm_dst.rearrange("p (kc t) -> p kc t", kc=NKC)[
                        :, half * 8:(half + 1) * 8, tb * 128:tb * 128 + 128]
                    v.tensor_copy(out=dst, in_=src.rearrange("p (kc t) -> p kc t", kc=8))
                v.drain()
                inc(v.nop(), "sv")
                assert cnt["sv"] == evac_p[tb]
        V_(f)

        def f(sc):
            sc.wait_ge(sems["sxc"], const_done)
            for tb in range(NTB):
                par = tb % 2
                sc.wait_ge(sems["sv"], stats_p[tb])
                sc.activation(out=sdP[par][:, 0:1], in_=mvP[par][:, 1:2], func=AF.Sqrt,
                              bias=epsc_sb[:, 0:1])
                sc.drain()
                inc(sc.nop(), "ss")
                assert cnt["ss"] == sd_p[tb]
                sc.wait_ge(sems["sv"], inv_p[tb])
                if tb >= 2:
                    sc.wait_ge(sems["stt"], tr_p[tb - 2])
                xs2 = stage_pair[tb % 2] if sbuf_tiles is None else sbuf_tiles[tb]
                sc.activation(out=tmst[par][:, 0:D], in_=xs2, func=AF.Identity,
                              bias=b2P[par][:, 0:1], scale=invP[par][:, 0:1])
                sc.drain()
                inc(sc.nop(), "ss")
                assert cnt["ss"] == app_p[tb]
        Sc(f)

        def f(te):
            te.wait_ge(sems["sxc"], const_done)
            if tensor_gate is not None:
                te.wait_ge(sems[tensor_gate[0]], tensor_gate[1])
            for tb in range(NTB):
                par = tb % 2
                te.wait_ge(sems["ss"], app_p[tb])
                if tb >= 2:
                    te.wait_ge(sems["sv"], evac_p[tb - 2])
                for kc in range(NKC):
                    bank = 4 + 2 * par + kc // 8
                    te.transpose(out=PSb[:, bank * 1024 + (kc % 8) * 128:
                                         bank * 1024 + (kc % 8) * 128 + 128],
                                 in_=tmst[par][:, kc * 128:(kc + 1) * 128],
                                 identity=ident_sb)
                inc(te.drain(), "stt")
                assert cnt["stt"] == tr_p[tb]
        Te(f)
        return evac_p

    xn_evac = ln_transpose(x_h, xn_fm, x_stage, xntm_st, None, None)
    last_xn = xn_evac[NTB - 1]

    def projection(w_dram, kind, dst_tm=None, first_gate=None, slab_gate=None):
        wsem = {i: ("sgda", "sgdb")[i % 2] for i in range(16)}
        sx0 = {"sgda": cnt["sgda"], "sgdb": cnt["sgdb"]}
        st0, sv0, ss0, sxc0 = cnt["stt"], cnt["sv"], cnt["ss"], cnt["sxc"]
        io0a, io0b = cnt["sioa"], cnt["siob"]
        ld_p = {i: sx0[wsem[i]] + 16 * (i // 2 + 1) for i in range(16)}
        mm_p = {i: st0 + i + 1 for i in range(16)}
        ev_p = {(cg, tb): sv0 + cg * 8 + tb + 1 for cg in range(NCG) for tb in range(NTB)}
        cp_p = {(cg, tb): ss0 + cg * 8 + tb + 1 for cg in range(NCG) for tb in range(NTB)}
        bank_sem = "sv" if kind == "wo" else "ss"
        bank_p = ev_p if kind == "wo" else cp_p if kind in ("q", "k") else cp_p

        def emit_slab(g, i):
            cg, kcg = divmod(i, 4)
            if i < 4:
                if slab_gate is not None and slab_gate[i] is not None:
                    sm, vv = slab_gate[i]
                    g.wait_ge(sems[sm], vv)
                g.wait_ge(sems["sxc"], sxc0)
            else:
                g.wait_ge(sems["stt"], mm_p[i - 4])
            got = dma(g, wsem[i], wsl_bf[i % 4][:, 0:D], w_dram[cg, kcg])
            assert got == ld_p[i]

        if kind != "wo":
            def f(g):
                for i in range(16):
                    emit_slab(g, i)
            G_(f)
        else:
            def f(g):
                for i in range(16):
                    emit_slab(g, i)
            G_(f)
            xsem9 = {n: ("sxa", "sxb")[n % 2] for n in range(32)}
            sx9 = {"sxa": cnt["sxa"], "sxb": cnt["sxb"]}
            xld_p = {n: sx9[xsem9[n]] + 16 * (n // 2 + 1) for n in range(32)}

            def f(s):
                s.wait_ge(sems["sv"], sv0)
                for n in range(32):
                    cg, tb = divmod(n, 8)
                    if n >= 2:
                        pcg, ptb = divmod(n - 2, 8)
                        s.wait_ge(sems["sv"], ev_p[(pcg, ptb)])
                    got = dma(s, xsem9[n], xo_st[n % 2][:, 0:512],
                              x_h[tb * 128:(tb + 1) * 128, cg * 512:(cg + 1) * 512])
                    assert got == xld_p[n]
            S_(f)

        lhs = gated_fm if kind == "wo" else xn_fm

        def f(te):
            te.wait_ge(sems["sv"], sv0)
            te.wait_ge(sems["ss"], ss0)
            for cg in range(NCG):
                for kcg in range(4):
                    i = cg * 4 + kcg
                    te.wait_ge(sems[wsem[i]], ld_p[i])
                    for tb in range(NTB):
                        if kcg == 0:
                            if cg > 0:
                                te.wait_ge(sems[bank_sem], bank_p[(cg - 1, tb)])
                            elif first_gate is not None:
                                sm, vv = first_gate[tb]
                                te.wait_ge(sems[sm], vv)
                        for k4 in range(4):
                            kc = kcg * 4 + k4
                            te.matmul(psf(tb),
                                      lhs[:, kc * T + tb * 128: kc * T + tb * 128 + 128],
                                      wsl_bf[i % 4][:, k4 * 512:(k4 + 1) * 512],
                                      start=(kc == 0), stop=(kc == NKC - 1))
                    inc(te.drain(), "stt")
                    assert cnt["stt"] == mm_p[i]
        Te(f)

        if kind in ("q", "k"):
            coff = 0 if kind == "q" else 1024

            def f(sc):
                for cg in range(NCG):
                    sc.wait_ge(sems["stt"], mm_p[cg * 4 + 3])
                    for tb in range(NTB):
                        idx = cg * 8 + tb
                        if idx >= 8:
                            sc.wait_ge(sems["sv"], ev_p[(cg - 1, tb)])
                        sc.copy(out=rp_st[tb % 8][:, 0:512], in_=psf(tb))
                        sc.drain()
                        inc(sc.nop(), "ss")
                        assert cnt["ss"] == cp_p[(cg, tb)]
            Sc(f)

            def f(v):
                v.wait_ge(sems["sxc"], const_done)
                for cg in range(NCG):
                    for tb in range(NTB):
                        v.wait_ge(sems["ss"], cp_p[(cg, tb)])
                        ps = rp_st[tb % 8][:, 0:512]
                        cosT = rope_tabs[:, coff + tb * 64: coff + tb * 64 + 64]
                        sinT = rope_tabs[:, coff + 512 + tb * 64: coff + 512 + tb * 64 + 64]
                        x1 = ps.rearrange("p (h j) -> p h j", h=4)[:, :, 0:64]
                        x2 = ps.rearrange("p (h j) -> p h j", h=4)[:, :, 64:128]
                        cB = cosT.rearrange("p (a j) -> p a j", a=1).broadcast_to([128, 4, 64])
                        sB = sinT.rearrange("p (a j) -> p a j", a=1).broadcast_to([128, 4, 64])
                        t1 = rope_t1[:, 0:256].rearrange("p (h j) -> p h j", h=4)
                        t2 = rope_t2[:, 0:256].rearrange("p (h j) -> p h j", h=4)
                        d1 = dst_tm[:, tb * D + cg * 512: tb * D + (cg + 1) * 512]
                        o1 = d1.rearrange("p (h j) -> p h j", h=4)[:, :, 0:64]
                        o2 = d1.rearrange("p (h j) -> p h j", h=4)[:, :, 64:128]
                        v.tensor_tensor(out=t1, in0=x1, in1=cB, op=OP.mult)
                        v.tensor_tensor(out=t2, in0=x2, in1=sB, op=OP.mult)
                        v.drain()
                        v.tensor_tensor(out=o1, in0=t1, in1=t2, op=OP.subtract)
                        v.drain()
                        v.tensor_tensor(out=t1, in0=x2, in1=cB, op=OP.mult)
                        v.tensor_tensor(out=t2, in0=x1, in1=sB, op=OP.mult)
                        v.drain()
                        v.tensor_tensor(out=o2, in0=t1, in1=t2, op=OP.add)
                        v.drain()
                        inc(v.nop(), "sv")
                        assert cnt["sv"] == ev_p[(cg, tb)]
            V_(f)
            return dict(ev=cp_p, sem="ss", rope=ev_p, mm=mm_p)
        elif kind in ("v", "g"):
            def f(sc):
                for cg in range(NCG):
                    sc.wait_ge(sems["stt"], mm_p[cg * 4 + 3])
                    for tb in range(NTB):
                        sc.copy(out=dst_tm[:, tb * D + cg * 512: tb * D + (cg + 1) * 512],
                                in_=psf(tb))
                        sc.drain()
                        inc(sc.nop(), "ss")
                        assert cnt["ss"] == cp_p[(cg, tb)]
            Sc(f)
            return dict(ev=cp_p, sem="ss", mm=mm_p)
        else:
            def f(v):
                for cg in range(NCG):
                    v.wait_ge(sems["stt"], mm_p[cg * 4 + 3])
                    for tb in range(NTB):
                        n = cg * 8 + tb
                        v.wait_ge(sems[xsem9[n]], xld_p[n])
                        v.tensor_add(out=xnew[tb][:, cg * 512:(cg + 1) * 512],
                                     in0=psf(tb), in1=xo_st[n % 2][:, 0:512])
                        v.drain()
                        inc(v.nop(), "sv")
                        assert cnt["sv"] == ev_p[(cg, tb)]
            V_(f)
            stsem = {tb: ("sxa", "sxb")[tb % 2] for tb in range(NTB)}
            sxs = {"sxa": cnt["sxa"], "sxb": cnt["sxb"]}
            store_p = {tb: (stsem[tb], sxs[stsem[tb]] + 16 * (tb // 2 + 1))
                       for tb in range(NTB)}

            def f(s):
                for tb in range(NTB):
                    s.wait_ge(sems["sv"], ev_p[(3, tb)])
                    got = dma(s, stsem[tb], out_h[tb * 128:(tb + 1) * 128, :],
                              xnew[tb][:, 0:D])
                    assert got == store_p[tb][1]
            S_(f)
            return dict(ev=ev_p, sem="sv", mm=mm_p, store=store_p)

    def tb_gate(res):
        return {tb: (res["sem"], res["ev"][(3, tb)]) for tb in range(NTB)}

    if max_phase >= 3:
        pq = projection(wq_h, "q", q_tm,
                        first_gate={tb: ("sv", xn_evac[tb]) for tb in range(NTB)},
                        slab_gate=(None, None, ("sv", xn_evac[NTB - 1]),
                                   ("sv", xn_evac[NTB - 1])))
        def sg4(p):
            return tuple(("stt", p["mm"][12 + i]) for i in range(4))
        pk = projection(wk_h, "k", k_tm, first_gate=tb_gate(pq), slab_gate=sg4(pq))
        pv = projection(wv_h, "v", v_tm, first_gate=tb_gate(pk), slab_gate=sg4(pk))
        pg = projection(wg_h, "g", g_tm, first_gate=tb_gate(pv), slab_gate=sg4(pv))

    if "d_xnfm" in dump_h:
        def f(s):
            s.wait_ge(sems["sv"], last_xn)
            dma(s, "sxc", dump_h["d_xnfm"], xn_fm)
        S_(f)
    if max_phase >= 3:
        for nm, buf in (("d_qtm", q_tm), ("d_ktm", k_tm), ("d_vtm", v_tm), ("d_gtm", g_tm)):
            if nm in dump_h:
                def f(s, buf=buf):
                    s.wait_ge(sems["sv"], cnt["sv"])
                    s.wait_ge(sems["ss"], cnt["ss"])
                    dma(s, "sxc", dump_h[nm], buf)
                S_(f)

    if max_phase >= 5:
        def f(s):
            s.wait_ge(sems["ss"], cnt["ss"])
            s.wait_ge(sems["sv"], cnt["sv"])
            dma(s, "sxc", dmatT_sb, dmat_h)
        S_(f)
        dmat_ld = cnt["sxc"]

        def f(v):
            v.wait_ge(sems["sxc"], dmat_ld)
            v.memset(S_sb[:, 0:D], 0.0)
            v.drain()
            inc(v.nop(), "sv")
        V_(f)
        s_zero = cnt["sv"]

        sv5 = cnt["sv"]
        st5 = cnt["stt"]
        ss5 = cnt["ss"]
        kp_p, state_p = {}, {}
        snap_p = {}
        vv = sv5
        ssim = ss5
        for c in range(NTB):
            ssim += 1; snap_p[c] = ssim
            vv += 1; kp_p[c] = vv
            for grp in range(4):
                vv += 1; state_p[(c, grp)] = vv
        tu_p = {}
        tt = st5
        for c in range(NTB):
            for grp in range(4):
                tt += 1; tu_p[(c, grp)] = tt

        def f(sc):
            sc.wait_ge(sems["stt"], pg["mm"][15])
            for c in range(NTB):
                if c > 0:
                    sc.wait_ge(sems["sv"], state_p[(c - 1, 3)])
                else:
                    sc.wait_ge(sems["sv"], s_zero)
                sc.copy(out=o_raw[:, c * D:(c + 1) * D], in_=S_sb[:, 0:D])
                sc.drain()
                inc(sc.nop(), "ss")
                assert cnt["ss"] == snap_p[c]
        Sc(f)

        def f(te):
            te.wait_ge(sems["sv"], s_zero)
            te.wait_ge(sems["ss"], pg["ev"][(3, 7)])
            for c in range(NTB):
                te.wait_ge(sems["sv"], kp_p[c])
                for grp in range(4):
                    bank = 6 + grp % 2
                    if c > 0 or grp >= 2:
                        pc, pg_ = (c, grp - 2) if grp >= 2 else (c - 1, grp + 2)
                        te.wait_ge(sems["sv"], state_p[(pc, pg_)])
                    for hh in range(4):
                        h = grp * 4 + hh
                        te.matmul(psf(bank)[:, hh * 128:(hh + 1) * 128],
                                  kp_sb[:, h * 128:(h + 1) * 128],
                                  v_tm[:, c * D + h * 128:c * D + (h + 1) * 128],
                                  start=True, stop=True)
                    inc(te.drain(), "stt")
                    assert cnt["stt"] == tu_p[(c, grp)]
        Te(f)

        def f(v):
            for c in range(NTB):
                v.tensor_tensor(
                    out=kp_sb[:, 0:D].rearrange("p (h j) -> p h j", h=H),
                    in0=k_tm[:, c * D:(c + 1) * D].rearrange("p (h j) -> p h j", h=H),
                    in1=kdec_sb.rearrange("p (h a) -> p h a", a=1).broadcast_to(
                        [128, H, 128]),
                    op=OP.mult)
                v.drain()
                inc(v.nop(), "sv")
                assert cnt["sv"] == kp_p[c]
                for grp in range(4):
                    bank = 6 + grp % 2
                    v.wait_ge(sems["stt"], tu_p[(c, grp)])
                    if grp == 0:
                        v.wait_ge(sems["ss"], snap_p[c])
                    sl3 = S_sb[:, grp * 512:(grp + 1) * 512].rearrange(
                        "p (h j) -> p h j", h=4)
                    v.tensor_tensor(
                        out=sg_tmp[:, 0:512].rearrange("p (h j) -> p h j", h=4),
                        in0=sl3,
                        in1=gC_sb[:, grp * 4:(grp + 1) * 4].rearrange(
                            "p (h a) -> p h a", a=1).broadcast_to([128, 4, 128]),
                        op=OP.mult)
                    v.drain()
                    v.tensor_add(out=S_sb[:, grp * 512:(grp + 1) * 512],
                                 in0=sg_tmp[:, 0:512], in1=psf(bank))
                    v.drain()
                    inc(v.nop(), "sv")
                    assert cnt["sv"] == state_p[(c, grp)]
        V_(f)
        scan_done = state_p[(NTB - 1, 3)]

        if "d_S" in dump_h:
            def f(s):
                s.wait_ge(sems["sv"], scan_done)
                dma(s, "sxc", dump_h["d_S"], S_sb[:, 0:D])
            S_(f)

        dump_gate = cnt["sxc"]

        def f(g):
            g.wait_ge(sems["sv"], scan_done)
            g.wait_ge(sems["sxc"], dump_gate)
            dma(g, "sio", cc_in.ap(), S_sb[:, 0:D])
            g.wait_ge(sems["sio"], cnt["sio"])
            inc(g.collective_compute(
                "AllGather", OP.bypass,
                replica_groups=[[0, 1, 2, 3], [4, 5, 6, 7]],
                ins=[cc_in.ap().opt()], outs=[cc_out.ap().opt()]), "scc")
            g.wait_ge(sems["scc"], cnt["scc"])
            dma(g, "sio", Tall[:, 0:3 * D].rearrange("p (j c) -> p j c", j=3),
                cc_out.ap()[0:3 * 128, :].rearrange("(j p) c -> p j c", p=128))
            g.wait_ge(sems["sio"], cnt["sio"])
            for j in range(3):
                wjb = wj_sb[:, j * H:(j + 1) * H].rearrange(
                    "p (h a) -> p h a", a=1).broadcast_to([128, H, 128])
                tj = Tall[:, j * D:(j + 1) * D].rearrange("p (h j) -> p h j", h=H)
                g.tensor_tensor(out=tj, in0=tj, in1=wjb, op=OP.mult)
                g.drain()
            g.tensor_add(out=sinit_sb[:, 0:D], in0=Tall[:, 0:D], in1=Tall[:, D:2 * D])
            g.drain()
            g.tensor_add(out=sinit_sb[:, 0:D], in0=sinit_sb[:, 0:D],
                         in1=Tall[:, 2 * D:3 * D])
            g.drain()
            inc(g.nop(), "sg")
        G_(f)
        sinit_done = ("sg", cnt["sg"])

        sv5b = cnt["sv"]
        st5b = cnt["stt"]
        ss5b = cnt["ss"]
        evq2_p, evk2_p, amt_p, oin_p, oad_p = {}, {}, {}, {}, {}
        vv = sv5b
        ssim = ss5b
        for c in range(NTB):
            ssim += 1; evq2_p[c] = ssim
            if c > 0:
                vv += 1; oin_p[c] = vv
            ssim += 1; evk2_p[c] = ssim
            for grp in range(4):
                vv += 1; amt_p[(c, grp)] = vv
            vv += 1; oad_p[c] = vv
        tq2_p, tn2_p, tk2_p, ta_p, ti_p = {}, {}, {}, {}, {}
        tt = st5b
        for c in range(NTB):
            tt += 1; tq2_p[c] = tt
            if c > 0:
                tt += 1; tn2_p[c] = tt
            tt += 1; tk2_p[c] = tt
            for grp in range(4):
                tt += 1; ta_p[(c, grp)] = tt
            tt += 1; ti_p[c] = tt

        def f(te):
            te.wait_ge(sems["stt"], pg["mm"][15])
            te.wait_ge(sems["ss"], pg["ev"][(3, 7)])
            te.wait_ge(sems["sv"], scan_done)
            for c in range(NTB):
                if c > 0:
                    te.wait_ge(sems["ss"], evk2_p[c - 1])
                for h in range(H):
                    bank = 4 + h // 8
                    te.transpose(out=PSb[:, bank * 1024 + (h % 8) * 128:
                                         bank * 1024 + (h % 8) * 128 + 128],
                                 in_=q_tm[:, c * D + h * 128: c * D + (h + 1) * 128],
                                 identity=ident_sb)
                inc(te.drain(), "stt")
                assert cnt["stt"] == tq2_p[c]
                te.wait_ge(sems["ss"], evq2_p[c])
                if c > 0:
                    te.wait_ge(sems["sv"], oad_p[c - 1])
                    for h in range(H):
                        te.matmul(psf(h // 4)[:, (h % 4) * 128:(h % 4) * 128 + 128],
                                  qf_sb[:, h * 128:(h + 1) * 128],
                                  o_raw[:, c * D + h * 128:c * D + (h + 1) * 128],
                                  start=True, stop=True)
                    inc(te.drain(), "stt")
                    assert cnt["stt"] == tn2_p[c]
                for h in range(H):
                    bank = 4 + h // 8
                    te.transpose(out=PSb[:, bank * 1024 + (h % 8) * 128:
                                         bank * 1024 + (h % 8) * 128 + 128],
                                 in_=k_tm[:, c * D + h * 128: c * D + (h + 1) * 128],
                                 identity=ident_sb)
                inc(te.drain(), "stt")
                assert cnt["stt"] == tk2_p[c]
                te.wait_ge(sems["ss"], evk2_p[c])
                for grp in range(4):
                    bank = 6 + grp % 2
                    if c > 0 or grp >= 2:
                        pc, pg_ = (c, grp - 2) if grp >= 2 else (c - 1, grp + 2)
                        te.wait_ge(sems["sv"], amt_p[(pc, pg_)])
                    for hh in range(4):
                        h = grp * 4 + hh
                        te.matmul(psf(bank)[:, hh * 128:(hh + 1) * 128],
                                  kf_sb[:, h * 128:(h + 1) * 128],
                                  qf_sb[:, h * 128:(h + 1) * 128],
                                  start=True, stop=True)
                    inc(te.drain(), "stt")
                    assert cnt["stt"] == ta_p[(c, grp)]
                te.wait_ge(sems["sv"], amt_p[(c, 3)])
                if c > 0:
                    te.wait_ge(sems["sv"], oin_p[c])
                elif c == 0:
                    pass
                if c > 0:
                    te.wait_ge(sems["sv"], oad_p[c - 1])
                for h in range(H):
                    te.matmul(psf(h // 4)[:, (h % 4) * 128:(h % 4) * 128 + 128],
                              amt_sb[:, h * 128:(h + 1) * 128],
                              v_tm[:, c * D + h * 128:c * D + (h + 1) * 128],
                              start=True, stop=True)
                inc(te.drain(), "stt")
                assert cnt["stt"] == ti_p[c]
        Te(f)

        def f(sc):
            for c in range(NTB):
                sc.wait_ge(sems["stt"], tq2_p[c])
                sc.copy(out=qf_sb[:, 0:D], in_=PSb[:, 4 * 1024:6 * 1024])
                sc.drain()
                inc(sc.nop(), "ss")
                assert cnt["ss"] == evq2_p[c]
                sc.wait_ge(sems["stt"], tk2_p[c])
                sc.copy(out=kf_sb[:, 0:D], in_=PSb[:, 4 * 1024:6 * 1024])
                sc.drain()
                inc(sc.nop(), "ss")
                assert cnt["ss"] == evk2_p[c]
        Sc(f)

        def f(v):
            for c in range(NTB):
                if c > 0:
                    v.wait_ge(sems["stt"], tn2_p[c])
                    v.tensor_tensor(
                        out=o_raw[:, c * D:(c + 1) * D].rearrange(
                            "p (h j) -> p h j", h=H),
                        in0=PS[:, 0:2048].rearrange("p (h j) -> p h j", h=H),
                        in1=crpp_sb[:, 0:H].rearrange(
                            "p (h a) -> p h a", a=1).broadcast_to([128, H, 128]),
                        op=OP.mult)
                    v.drain()
                    inc(v.nop(), "sv")
                    assert cnt["sv"] == oin_p[c]
                for grp in range(4):
                    bank = 6 + grp % 2
                    v.wait_ge(sems["stt"], ta_p[(c, grp)])
                    v.tensor_tensor(out=amt_sb[:, grp * 512:(grp + 1) * 512],
                                    in0=psf(bank),
                                    in1=dmatT_sb[:, grp * 512:(grp + 1) * 512],
                                    op=OP.mult)
                    v.drain()
                    inc(v.nop(), "sv")
                    assert cnt["sv"] == amt_p[(c, grp)]
                v.wait_ge(sems["stt"], ti_p[c])
                if c > 0:
                    v.tensor_add(out=o_raw[:, c * D:(c + 1) * D],
                                 in0=PS[:, 0:2048], in1=o_raw[:, c * D:(c + 1) * D])
                else:
                    v.tensor_copy(out=o_raw[:, 0:D], in_=PS[:, 0:2048])
                v.drain()
                inc(v.nop(), "sv")
                assert cnt["sv"] == oad_p[c]
        V_(f)
        scan2_done = oad_p[NTB - 1]

        if "d_oraw" in dump_h:
            def f(s):
                s.wait_ge(sems["sv"], scan2_done)
                dma(s, "sxc", dump_h["d_oraw"], o_raw)
            S_(f)

    if max_phase >= 6:
        if "d_sinit" in dump_h:
            def f(s):
                s.wait_ge(sems[sinit_done[0]], sinit_done[1])
                dma(s, "sxc", dump_h["d_sinit"], sinit_sb[:, 0:D])
            S_(f)

    if max_phase >= 8:
        o_f2 = [o_f, view(bufW, 32 * KB, BF16, D)]
        sq2 = [sq_sb, view(bufW, 12 * KB, BF16, D)]
        gated2 = [gated_c, view(bufW, 4 * KB, BF16, D)]
        rinv2 = [rinv_sb, view(bufS, 3100, F32, H)]

        sv8 = cnt["sv"]
        ss8 = cnt["ss"]
        st8 = cnt["stt"]
        qf8_p, of8_p, ssq8_p, rinv8_p, gc8_p, gf8_p = {}, {}, {}, {}, {}, {}
        sq8_p, sd8_p, sil8_p = {}, {}, {}
        ssim = ss8
        for c in range(NTB):
            ssim += 1
            sil8_p[c] = ssim
        vv = sv8
        for c in range(NTB):
            ssim += 1; qf8_p[c] = ssim
            if c > 0:
                ssim += 1; sq8_p[c - 1] = ssim
                ssim += 1; sd8_p[c - 1] = ssim
            if c > 1:
                ssim += 1; gf8_p[c - 2] = ssim
        ssim += 1; sq8_p[NTB - 1] = ssim
        ssim += 1; sd8_p[NTB - 1] = ssim
        ssim += 1; gf8_p[NTB - 2] = ssim
        ssim += 1; gf8_p[NTB - 1] = ssim
        for c in range(NTB):
            vv += 1; of8_p[c] = vv
            if c > 0:
                vv += 1; ssq8_p[c - 1] = vv
                vv += 1; rinv8_p[c - 1] = vv
                vv += 1; gc8_p[c - 1] = vv
        vv += 1; ssq8_p[NTB - 1] = vv
        vv += 1; rinv8_p[NTB - 1] = vv
        vv += 1; gc8_p[NTB - 1] = vv
        t8q_p, t8c_p, t8g_p = {}, {}, {}
        tt = st8
        for c in range(NTB):
            tt += 1; t8q_p[c] = tt
            tt += 1; t8c_p[c] = tt
            if c > 0:
                tt += 1; t8g_p[c - 1] = tt
        tt += 1; t8g_p[NTB - 1] = tt

        def emit_t8g(te, c):
            te.wait_ge(sems["sv"], gc8_p[c])
            if c > 0:
                te.wait_ge(sems["ss"], gf8_p[c - 1])
            for h in range(H):
                bank = 6 + h // 8
                te.transpose(out=PSb[:, bank * 1024 + (h % 8) * 128:
                                     bank * 1024 + (h % 8) * 128 + 128],
                             in_=gated2[c % 2][:, h * 128:(h + 1) * 128],
                             identity=ident_sb)
            inc(te.drain(), "stt")
            assert cnt["stt"] == t8g_p[c]

        def f(te):
            te.wait_ge(sems["sv"], scan2_done)
            for c in range(NTB):
                if c > 0:
                    te.wait_ge(sems["ss"], qf8_p[c - 1])
                for h in range(H):
                    bank = 4 + h // 8
                    te.transpose(out=PSb[:, bank * 1024 + (h % 8) * 128:
                                         bank * 1024 + (h % 8) * 128 + 128],
                                 in_=q_tm[:, c * D + h * 128: c * D + (h + 1) * 128],
                                 identity=ident_sb)
                inc(te.drain(), "stt")
                assert cnt["stt"] == t8q_p[c]
                te.wait_ge(sems["ss"], qf8_p[c])
                if c == 0:
                    te.wait_ge(sems[sinit_done[0]], sinit_done[1])
                if c > 0:
                    te.wait_ge(sems["sv"], of8_p[c - 1])
                for h in range(H):
                    te.matmul(psf(h // 4)[:, (h % 4) * 128:(h % 4) * 128 + 128],
                              qf_sb[:, h * 128:(h + 1) * 128],
                              sinit_sb[:, h * 128:(h + 1) * 128],
                              start=True, stop=True)
                inc(te.drain(), "stt")
                assert cnt["stt"] == t8c_p[c]
                if c > 0:
                    emit_t8g(te, c - 1)
            emit_t8g(te, NTB - 1)
        Te(f)

        def emit_tail(v, c):
            v.wait_ge(sems["ss"], sq8_p[c])
            v.tensor_reduce(out=ssq_sb[:, 0:H],
                            in_=sq2[c % 2][:, 0:D].rearrange("p (h j) -> p h j", h=H),
                            axis=mybir.AxisListType.X, op=OP.add)
            v.drain()
            inc(v.nop(), "sv")
            assert cnt["sv"] == ssq8_p[c]
            v.wait_ge(sems["ss"], sd8_p[c])
            v.reciprocal(out=rinv2[c % 2][:, 0:H], in_=sd2_sb[:, 0:H])
            v.drain()
            inc(v.nop(), "sv")
            assert cnt["sv"] == rinv8_p[c]
            v.tensor_tensor(
                out=sq2[c % 2][:, 0:D].rearrange("p (h j) -> p h j", h=H),
                in0=o_f2[c % 2][:, 0:D].rearrange("p (h j) -> p h j", h=H),
                in1=rinv2[c % 2][:, 0:H].rearrange(
                    "p (h a) -> p h a", a=1).broadcast_to([128, H, 128]),
                op=OP.mult)
            v.drain()
            if c > 1:
                v.wait_ge(sems["stt"], t8g_p[c - 2])
            v.wait_ge(sems["ss"], sil8_p[c])
            v.tensor_mul(out=gated2[c % 2][:, 0:D], in0=sq2[c % 2][:, 0:D],
                         in1=g_tm[:, c * D:(c + 1) * D])
            v.drain()
            inc(v.nop(), "sv")
            assert cnt["sv"] == gc8_p[c]

        def f(v):
            for c in range(NTB):
                v.wait_ge(sems["stt"], t8c_p[c])
                if c > 1:
                    v.wait_ge(sems["ss"], sq8_p[c - 2])
                v.tensor_tensor(
                    out=o_f2[c % 2][:, 0:D].rearrange("p (h j) -> p h j", h=H),
                    in0=PS[:, 0:2048].rearrange("p (h j) -> p h j", h=H),
                    in1=crsg_sb[:, c * H:(c + 1) * H].rearrange(
                        "p (h a) -> p h a", a=1).broadcast_to([128, H, 128]),
                    op=OP.mult)
                v.drain()
                v.tensor_add(out=o_f2[c % 2][:, 0:D], in0=o_f2[c % 2][:, 0:D],
                             in1=o_raw[:, c * D:(c + 1) * D])
                v.drain()
                inc(v.nop(), "sv")
                assert cnt["sv"] == of8_p[c]
                if c > 0:
                    emit_tail(v, c - 1)
            emit_tail(v, NTB - 1)
        V_(f)

        def emit_gfm(sc, c):
            sc.wait_ge(sems["stt"], t8g_p[c])
            dst = gated_fm.rearrange("p (h t) -> p h t", h=H)[
                :, :, c * 128:(c + 1) * 128]
            sc.mul(out=dst[:, 0:8, :],
                   in_=PSb[:, 6 * 1024:7 * 1024].rearrange("p (h t) -> p h t", h=8),
                   mul=gnwpp_sb[:, 0:1])
            sc.mul(out=dst[:, 8:16, :],
                   in_=PSb[:, 7 * 1024:8 * 1024].rearrange("p (h t) -> p h t", h=8),
                   mul=gnwpp_sb[:, 0:1])
            sc.drain()
            inc(sc.nop(), "ss")
            assert cnt["ss"] == gf8_p[c]

        def f(sc):
            sc.wait_ge(sems["sv"], scan_done)
            for c in range(NTB):
                sc.activation(out=g_tm[:, c * D:(c + 1) * D],
                              in_=g_tm[:, c * D:(c + 1) * D], func=AF.Silu)
                sc.drain()
                inc(sc.nop(), "ss")
                assert cnt["ss"] == sil8_p[c]
            def emit_sqsd(c):
                sc.wait_ge(sems["sv"], of8_p[c])
                sc.activation(out=sq2[c % 2][:, 0:D], in_=o_f2[c % 2][:, 0:D],
                              func=AF.Square)
                sc.drain()
                inc(sc.nop(), "ss")
                assert cnt["ss"] == sq8_p[c]
                sc.wait_ge(sems["sv"], ssq8_p[c])
                sc.activation(out=sd2_sb[:, 0:H], in_=ssq_sb[:, 0:H], func=AF.Sqrt,
                              bias=epsc_sb[:, 0:1], scale=epsc_sb[:, 1:2])
                sc.drain()
                inc(sc.nop(), "ss")
                assert cnt["ss"] == sd8_p[c]

            for c in range(NTB):
                sc.wait_ge(sems["stt"], t8q_p[c])
                sc.copy(out=qf_sb[:, 0:D], in_=PSb[:, 4 * 1024:6 * 1024])
                sc.drain()
                inc(sc.nop(), "ss")
                assert cnt["ss"] == qf8_p[c]
                if c > 0:
                    emit_sqsd(c - 1)
                if c > 1:
                    emit_gfm(sc, c - 2)
            emit_sqsd(NTB - 1)
            emit_gfm(sc, NTB - 2)
            emit_gfm(sc, NTB - 1)
        Sc(f)
        gated_done = gf8_p[NTB - 1]
        t8c_last = t8c_p[NTB - 1]
        t8g_last = t8g_p[NTB - 1]

        if "d_gated" in dump_h:
            def f(s):
                s.wait_ge(sems["ss"], gated_done)
                dma(s, "sxc", dump_h["d_gated"], gated_fm)
            S_(f)

    if max_phase >= 9:
        r9 = projection(wo_h, "wo", None,
                        first_gate={tb: ("ss", gated_done) for tb in range(NTB)},
                        slab_gate=(("stt", t8c_last), ("stt", t8c_last),
                                   ("stt", t8g_last), ("stt", t8g_last)))

    if max_phase >= 10:
        def gate10(tb):
            return ("sv", r9["ev"][(3, tb)])
        xn2_evac = ln_transpose(None, xn2_fm, xst, xn2tm_st, gate10,
                                ("sv", r9["ev"][(3, 7)]),
                                sbuf_tiles=[xnew[tb][:, 0:D] for tb in range(NTB)])
        xn2_done = xn2_evac[NTB - 1]

        if "d_xn2fm" in dump_h:
            def f(s):
                s.wait_ge(sems["sv"], xn2_done)
                dma(s, "sxc", dump_h["d_xn2fm"], xn2_fm)
            S_(f)

    if max_phase >= 11:
        def f(s):
            s.wait_ge(sems["sv"], xn2_done)
            s.wait_ge(sems[r9["store"][0][0]], r9["store"][0][1])
            dma(s, "sxc", sdown_sb, sdown_h)
        S_(f)
        sdown_ld = cnt["sxc"]

        def fuse_view(ib):
            return (fuseA[:, ib * 512:(ib + 1) * 512] if ib < 32
                    else fuseB[:, (ib - 32) * 512:(ib - 31) * 512])

        prev_down_ev = None
        prev_dmm_last = None
        for hf in range(2):
            gsem = {g16: ("sgda", "sgdb")[g16 % 2] for g16 in range(16)}
            usem = {g16: ("sgdc", "sgdd")[g16 % 2] for g16 in range(16)}
            ioa11, iob11 = cnt["sioa"], cnt["siob"]
            gbase = {nm: cnt[nm] for nm in ("sgda", "sgdb", "sgdc", "sgdd")}
            st11, ss11, sv11 = cnt["stt"], cnt["ss"], cnt["sv"]
            gld_p = {g16: gbase[gsem[g16]] + 16 * (g16 // 2 + 1) for g16 in range(16)}
            uld_p = {g16: gbase[usem[g16]] + 16 * (g16 // 2 + 1) for g16 in range(16)}
            mm_p = {(g16, w): st11 + 2 * g16 + w + 1
                    for g16 in range(16) for w in range(2)}
            silu_p = {(g16, b): ss11 + 4 * g16 + b + 1
                      for g16 in range(16) for b in range(4)}
            fuse_p = {(g16, b): sv11 + 4 * g16 + b + 1
                      for g16 in range(16) for b in range(4)}

            def f(g, gld_p=gld_p, uld_p=uld_p, mm_p=mm_p, st11=st11,
                  ss11=ss11, sv11=sv11, ioa11=ioa11, iob11=iob11, hf=hf,
                  gsem=gsem, usem=usem):
                for g16 in range(16):
                    if g16 < 2:
                        g.wait_ge(sems["stt"], st11)
                        g.wait_ge(sems["ss"], ss11)
                        g.wait_ge(sems["sv"], sv11)
                        g.wait_ge(sems["sioa"], ioa11)
                        g.wait_ge(sems["siob"], iob11)
                        for tb in range(NTB):
                            g.wait_ge(sems[r9["store"][tb][0]], r9["store"][tb][1])
                    else:
                        g.wait_ge(sems["stt"], mm_p[(g16 - 2, 0)])
                    got = dma(g, gsem[g16], gu_bfg2[g16 % 2][:, 0:8192],
                              wgate_h[g16].rearrange("p kc j -> p (kc j)"))
                    assert got == gld_p[g16]
                    if g16 >= 2:
                        g.wait_ge(sems["stt"], mm_p[(g16 - 2, 1)])
                    got = dma(g, usem[g16], gu_bfu2[g16 % 2][:, 0:8192],
                              wup_h[g16].rearrange("p kc j -> p (kc j)"))
                    assert got == uld_p[g16]
            G_(f)

            def f(te, gld_p=gld_p, uld_p=uld_p, mm_p=mm_p, silu_p=silu_p,
                  fuse_p=fuse_p, hf=hf, prev_down_ev=prev_down_ev, gsem=gsem,
                  usem=usem):
                if hf == 0:
                    te.wait_ge(sems["sv"], xn2_done)
                else:
                    te.wait_ge(sems["sv"], prev_down_ev)
                for g16 in range(16):
                    for w in range(2):
                        te.wait_ge(sems[gsem[g16] if w == 0 else usem[g16]],
                                   (gld_p if w == 0 else uld_p)[g16])
                        if w == 0 and g16 > 0:
                            te.wait_ge(sems["ss"], silu_p[(g16 - 1, 3)])
                        if w == 1 and g16 > 0:
                            te.wait_ge(sems["sv"], fuse_p[(g16 - 1, 3)])
                        wbf = (gu_bfg2 if w == 0 else gu_bfu2)[g16 % 2]
                        for kc in range(NKC):
                            for b in range(4):
                                te.matmul(psf(4 * w + b),
                                          wbf[:, kc * 512 + b * 128:
                                              kc * 512 + (b + 1) * 128],
                                          xn2_fm[:, kc * T + hf * 512:
                                                 kc * T + (hf + 1) * 512],
                                          start=(kc == 0), stop=(kc == NKC - 1))
                        inc(te.drain(), "stt")
                        assert cnt["stt"] == mm_p[(g16, w)]
            Te(f)

            def f(sc, mm_p=mm_p, silu_p=silu_p, fuse_p=fuse_p, hf=hf):
                for g16 in range(16):
                    sc.wait_ge(sems["stt"], mm_p[(g16, 0)])
                    if g16 > 0:
                        sc.wait_ge(sems["sv"], fuse_p[(g16 - 1, 3)])
                    for b in range(4):
                        ib = g16 * 4 + b
                        sc.activation(out=silu_sb11[b][:, 0:512], in_=psf(b),
                                      func=AF.Silu, scale=sgate_sb[:, ib:ib + 1])
                        sc.drain()
                        inc(sc.nop(), "ss")
                        assert cnt["ss"] == silu_p[(g16, b)]
            Sc(f)

            def f(v, mm_p=mm_p, silu_p=silu_p, fuse_p=fuse_p, hf=hf):
                for g16 in range(16):
                    v.wait_ge(sems["stt"], mm_p[(g16, 1)])
                    v.wait_ge(sems["ss"], silu_p[(g16, 3)])
                    for b in range(4):
                        ib = g16 * 4 + b
                        v.scalar_tensor_tensor(out=fuse_view(ib), in0=psf(4 + b),
                                               scalar=sup_sb[:, ib:ib + 1],
                                               in1=silu_sb11[b][:, 0:512],
                                               op0=OP.mult, op1=OP.mult)
                        v.drain()
                        inc(v.nop(), "sv")
                        assert cnt["sv"] == fuse_p[(g16, b)]
            V_(f)

            NGRP = 16
            dsem = {n: ("sgde", "sgdf")[n % 2] for n in range(2 * NGRP)}
            dasem = {n: ("sioa", "siob")[n % 2] for n in range(8)}
            dbase = {nm: cnt[nm] for nm in ("sgde", "sgdf")}
            st11d, sv11d = cnt["stt"], cnt["sv"]
            io11d = {"sioa": cnt["sioa"], "siob": cnt["siob"]}
            dld_p = {n: dbase[dsem[n]] + 16 * (n // 2 + 1) for n in range(2 * NGRP)}
            dmm_p = {n: st11d + n + 1 for n in range(2 * NGRP)}
            dev_p = {}
            vv = sv11d
            for cgp in range(2):
                for q4 in range(4):
                    vv += 1
                    dev_p[(cgp, q4)] = vv
            dacc_p = {n: io11d[dasem[n]] + 16 * (n // 2 + 1) for n in range(8)}

            def mk_dld(n0, n1):
                def f(g, dld_p=dld_p, dmm_p=dmm_p, mm_p=mm_p,
                      dsem=dsem, hf=hf, prev_dmm_last=prev_dmm_last):
                    for n in range(n0, n1):
                        cgp, grp = divmod(n, NGRP)
                        if n < 2:
                            if hf == 0:
                                g.wait_ge(sems["sv"], xn2_done)
                                for tb2 in range(1, 4):
                                    g.wait_ge(sems[r9["store"][tb2][0]],
                                              r9["store"][tb2][1])
                            else:
                                g.wait_ge(sems["stt"], prev_dmm_last)
                        else:
                            g.wait_ge(sems["stt"], dmm_p[n - 2])
                        got = dma(g, dsem[n], wd_bf[n % 2][:, 0:4096],
                                  wdown_h[cgp, grp])
                        assert got == dld_p[n]
                return f

            def mk_dacc(cgp):
                def f(g, cgp=cgp, dev_p=dev_p, dacc_p=dacc_p, hf=hf, dasem=dasem):
                    for q4 in range(4):
                        n = cgp * 4 + q4
                        g.wait_ge(sems["sv"], dev_p[(cgp, q4)])
                        tb = hf * 4 + q4
                        g.wait_ge(sems[r9["store"][tb][0]], r9["store"][tb][1])
                        got = dma(g, dasem[n],
                                  out_h[tb * 128:(tb + 1) * 128,
                                        cgp * 1024:(cgp + 1) * 1024],
                                  dstage[n % 2][:, 0:1024], accum_op=OP.add)
                        assert got == dacc_p[n]
                return f

            def f(te, dld_p=dld_p, dmm_p=dmm_p, dev_p=dev_p,
                  fuse_p=fuse_p, silu_p=silu_p, dsem=dsem):
                te.wait_ge(sems["sv"], fuse_p[(15, 3)])
                te.wait_ge(sems["ss"], silu_p[(15, 3)])
                for cgp in range(2):
                    if cgp > 0:
                        for q4 in range(4):
                            te.wait_ge(sems["sv"], dev_p[(0, q4)])
                    for grp in range(NGRP):
                        n = cgp * NGRP + grp
                        te.wait_ge(sems[dsem[n]], dld_p[n])
                        for ib4 in range(4):
                            ib = grp * 4 + ib4
                            for q4 in range(4):
                                for cg2 in range(2):
                                    te.matmul(
                                        psf(q4 * 2 + cg2),
                                        fuse_view(ib)[:, q4 * 128:(q4 + 1) * 128],
                                        wd_bf[n % 2][:, ib4 * 1024 + cg2 * 512:
                                                     ib4 * 1024 + (cg2 + 1) * 512],
                                        start=(grp == 0 and ib4 == 0),
                                        stop=(grp == NGRP - 1 and ib4 == 3))
                        inc(te.drain(), "stt")
                        assert cnt["stt"] == dmm_p[n]
            down_te = f

            def f(v, dmm_p=dmm_p, dev_p=dev_p, dacc_p=dacc_p, dasem=dasem):
                v.wait_ge(sems["sxc"], sdown_ld)
                for cgp in range(2):
                    v.wait_ge(sems["stt"], dmm_p[cgp * NGRP + NGRP - 1])
                    for q4 in range(4):
                        n = cgp * 4 + q4
                        if n >= 2:
                            v.wait_ge(sems[dasem[n - 2]], dacc_p[n - 2])
                        v.tensor_mul(out=dstage[n % 2][:, 0:1024],
                                     in0=PS[:, q4 * 1024:(q4 + 1) * 1024],
                                     in1=sdown_sb[:, cgp * 1024:(cgp + 1) * 1024])
                        v.drain()
                        inc(v.nop(), "sv")
                        assert cnt["sv"] == dev_p[(cgp, q4)]
            down_v = f

            G_(mk_dld(0, 2))
            Te(down_te)
            V_(down_v)
            G_(mk_dld(2, NGRP))
            G_(mk_dacc(0))
            G_(mk_dld(NGRP, 2 * NGRP))
            G_(mk_dacc(1))
            prev_down_ev = dev_p[(1, 3)]
            prev_dmm_last = dmm_p[2 * NGRP - 1]

    def f(g):
        for nm in ("sio", "sioa", "siob", "sxa", "sxb", "sxc", "swa", "swb"):
            g.wait_ge(sems[nm], cnt[nm])
    G_(f)

    stack.close()
    return nc



def _prep_weights(Wq, Wk, Wv, Wg, Wo, w_gate, w_up, w_down,
                  s_gate, s_up, s_down, g_norm_w):
    def proj_r(W):
        return np.ascontiguousarray(
            W.reshape(4, 4, 128, 4, 512).transpose(3, 0, 2, 1, 4).reshape(4, 4, 128, 2048))

    def gu_r(W8):
        return np.ascontiguousarray(W8.reshape(16, 128, 16, 512).transpose(2, 1, 0, 3))

    wd = np.ascontiguousarray(w_down.reshape(16, 4, 128, 2, 1024).transpose(3, 0, 2, 1, 4).reshape(2, 16, 128, 4096))
    return dict(
        wq=proj_r(Wq), wk=proj_r(Wk), wv=proj_r(Wv), wg=proj_r(Wg), wo=proj_r(Wo),
        wgate_l=gu_r(w_gate), wup_l=gu_r(w_up), wdown_l=wd,
        sgate_pp=np.ascontiguousarray(s_gate.reshape(NIB, 128).T),
        sup_pp=np.ascontiguousarray(s_up.reshape(NIB, 128).T),
        sdown_rep=np.ascontiguousarray(np.broadcast_to(s_down, (128, D))),
        gnw_rep=np.ascontiguousarray(np.broadcast_to(g_norm_w, (128, 128))),
        gnw_pp=np.ascontiguousarray(g_norm_w.reshape(128, 1)),
    )


def kernel(L0_attn_x, L0_attn_Wq, L0_attn_Wk, L0_attn_Wv, L0_attn_Wg,
           L0_attn_g_norm_w, L0_attn_Wo, L0_ffn_w_gate, L0_ffn_w_up,
           L0_ffn_w_down, L0_ffn_s_gate, L0_ffn_s_up, L0_ffn_s_down):
    x = np.ascontiguousarray(np.asarray(L0_attn_x, dtype=np.float32))
    B, S, _ = x.shape
    shared = _prep_weights(
        np.asarray(L0_attn_Wq, np.float32), np.asarray(L0_attn_Wk, np.float32),
        np.asarray(L0_attn_Wv, np.float32), np.asarray(L0_attn_Wg, np.float32),
        np.asarray(L0_attn_Wo, np.float32),
        np.asarray(L0_ffn_w_gate), np.asarray(L0_ffn_w_up), np.asarray(L0_ffn_w_down),
        np.asarray(L0_ffn_s_gate, np.float32), np.asarray(L0_ffn_s_up, np.float32),
        np.asarray(L0_ffn_s_down, np.float32),
        np.asarray(L0_attn_g_norm_w, np.float32))

    nc = build()
    in_maps = []
    for core in range(8):
        b = core // NSEG
        seg = core % NSEG
        m = dict(shared)
        m["x"] = np.ascontiguousarray(x[b, seg * SEG:(seg + 1) * SEG, :])
        m.update(_host_consts(core))
        in_maps.append({k: np.asarray(v) for k, v in m.items()})
    res = run_bass_kernel_spmd(nc, in_maps, core_ids=list(range(8)))
    out = np.empty((B, S, D), dtype=np.float32)
    for core in range(8):
        b = core // NSEG
        seg = core % NSEG
        out[b, seg * SEG:(seg + 1) * SEG, :] = res.results[core]["out"]
    return out


# version 14
# speedup vs baseline: 1.0938x; 1.0241x over previous
import math
import numpy as np
import ml_dtypes
import contextlib
import concourse.bass as bass
import concourse.mybir as mybir
from concourse.bass_utils import run_bass_kernel_spmd

F32 = mybir.dt.float32
BF16 = mybir.dt.bfloat16
I8 = mybir.dt.int8
U8 = mybir.dt.uint8
AF = mybir.ActivationFunctionType
OP = mybir.AluOpType

T = 1024
D = 2048
H = 16
NTB = 8
NKC = 16
NCG = 4
NIB = 64
EPS = 1e-5
SEG = 1024
NSEG = 4
KB = 1024


def _host_consts(core_id):
    h = np.arange(H, dtype=np.float64)
    gamma = 1.0 - np.exp2(-5.0 - h)
    seg = core_id % NSEG
    pos0 = seg * SEG

    inv = 10000.0 ** (-(np.arange(64, dtype=np.float64) / 64.0))
    t_loc = np.arange(T, dtype=np.float64).reshape(NTB, 128)
    ang = (pos0 + t_loc)[:, :, None] * inv[None, None, :]
    cos = np.cos(ang)
    sin = np.sin(ang)

    def tmtab(a, scale=1.0):
        return np.ascontiguousarray(
            (a * scale).transpose(1, 0, 2).reshape(128, NTB * 64)).astype(np.float32)

    qs = 1.0 / math.sqrt(128.0)
    idx = np.arange(128, dtype=np.float64)
    rel = idx[None, :] - idx[:, None]
    dmatT = np.where(rel >= 0, gamma[:, None, None] ** rel[None], 0.0)
    dmatT = np.ascontiguousarray(
        dmatT.transpose(1, 0, 2).reshape(128, H * 128)).astype(np.float32)
    kdec = (gamma[None, :] ** (127.0 - idx[:, None])).astype(np.float32)
    gC = np.broadcast_to(gamma ** 128.0, (128, H)).astype(np.float32).copy()
    cross_pp = (gamma[None, :] ** (idx[:, None] + 1.0)).astype(np.float32)
    cs = np.zeros((128, NTB * H), dtype=np.float64)
    for c in range(NTB):
        cs[:, c * H:(c + 1) * H] = gamma[None, :] ** (c * 128.0 + idx[:, None] + 1.0)
    crossseg = cs.astype(np.float32)
    wj = np.zeros((128, NSEG * H), dtype=np.float64)
    for j in range(seg):
        wj[:, j * H:(j + 1) * H] = gamma[None, :] ** (1024.0 * (seg - 1 - j))
    wj = wj.astype(np.float32)
    ident = np.eye(128, dtype=ml_dtypes.bfloat16)
    return dict(cosq=tmtab(cos, qs), sinq=tmtab(sin, qs),
                cosk=tmtab(cos), sink=tmtab(sin), dmatT=dmatT,
                kdec=kdec, gC=gC, cross_pp=cross_pp, crossseg=crossseg,
                wj=wj, ident=ident,
                epsc=np.broadcast_to(np.array([EPS, 1.0 / 128.0], np.float32),
                                     (128, 2)).copy())


def build(dump=(), max_phase=99):
    nc = bass.Bass("TRN2", target_bir_lowering=False, debug=False)

    x_h = nc.dram_tensor("x", [T, D], F32, kind="ExternalInput").ap()
    out_h = nc.dram_tensor("out", [T, D], F32, kind="ExternalOutput").ap()
    wq_h = nc.dram_tensor("wq", [NCG, 4, 128, D], F32, kind="ExternalInput").ap()
    wk_h = nc.dram_tensor("wk", [NCG, 4, 128, D], F32, kind="ExternalInput").ap()
    wv_h = nc.dram_tensor("wv", [NCG, 4, 128, D], F32, kind="ExternalInput").ap()
    wg_h = nc.dram_tensor("wg", [NCG, 4, 128, D], F32, kind="ExternalInput").ap()
    wo_h = nc.dram_tensor("wo", [NCG, 4, 128, D], F32, kind="ExternalInput").ap()
    wgate_h = nc.dram_tensor("wgate_l", [16, 128, 16, 512], I8, kind="ExternalInput").ap()
    wup_h = nc.dram_tensor("wup_l", [16, 128, 16, 512], I8, kind="ExternalInput").ap()
    wdown_h = nc.dram_tensor("wdown_l", [2, 16, 128, 4096], I8, kind="ExternalInput").ap()
    sgate_h = nc.dram_tensor("sgate_pp", [128, NIB], F32, kind="ExternalInput").ap()
    sup_h = nc.dram_tensor("sup_pp", [128, NIB], F32, kind="ExternalInput").ap()
    sdown_h = nc.dram_tensor("sdown_rep", [128, D], F32, kind="ExternalInput").ap()
    gnw_h = nc.dram_tensor("gnw_rep", [128, 128], F32, kind="ExternalInput").ap()
    gnwpp_h = nc.dram_tensor("gnw_pp", [128, 1], F32, kind="ExternalInput").ap()
    cosq_h = nc.dram_tensor("cosq", [128, NTB * 64], F32, kind="ExternalInput").ap()
    sinq_h = nc.dram_tensor("sinq", [128, NTB * 64], F32, kind="ExternalInput").ap()
    cosk_h = nc.dram_tensor("cosk", [128, NTB * 64], F32, kind="ExternalInput").ap()
    sink_h = nc.dram_tensor("sink", [128, NTB * 64], F32, kind="ExternalInput").ap()
    dmat_h = nc.dram_tensor("dmatT", [128, H * 128], F32, kind="ExternalInput").ap()
    kdec_h = nc.dram_tensor("kdec", [128, H], F32, kind="ExternalInput").ap()
    gC_h = nc.dram_tensor("gC", [128, H], F32, kind="ExternalInput").ap()
    crpp_h = nc.dram_tensor("cross_pp", [128, H], F32, kind="ExternalInput").ap()
    crsg_h = nc.dram_tensor("crossseg", [128, NTB * H], F32, kind="ExternalInput").ap()
    wj_h = nc.dram_tensor("wj", [128, NSEG * H], F32, kind="ExternalInput").ap()
    ident_h = nc.dram_tensor("ident", [128, 128], BF16, kind="ExternalInput").ap()
    epsc_h = nc.dram_tensor("epsc", [128, 2], F32, kind="ExternalInput").ap()

    cc_in = nc.dram_tensor("cc_in", [128, D], BF16)
    cc_out = nc.dram_tensor("cc_out", [NSEG * 128, D], BF16)

    dump_h = {}
    for name, shape, dt in (
        ("d_xnfm", [128, NKC * T], BF16), ("d_qtm", [128, NTB * D], BF16),
        ("d_ktm", [128, NTB * D], BF16), ("d_vtm", [128, NTB * D], BF16),
        ("d_gtm", [128, NTB * D], BF16), ("d_oraw", [128, NTB * D], BF16),
        ("d_S", [128, D], F32), ("d_sinit", [128, D], BF16),
        ("d_gated", [128, H * T], BF16), ("d_xn2fm", [128, NKC * T], BF16),
    ):
        if name in dump:
            dump_h[name] = nc.dram_tensor(name, shape, dt, kind="ExternalOutput").ap()

    bufA = nc.alloc_sbuf_tensor("bufA", [128, 32 * KB], U8)
    bufB = nc.alloc_sbuf_tensor("bufB", [128, 32 * KB], U8)
    bufC = nc.alloc_sbuf_tensor("bufC", [128, 32 * KB], U8)
    bufD = nc.alloc_sbuf_tensor("bufD", [128, 32 * KB], U8)
    bufE = nc.alloc_sbuf_tensor("bufE", [128, 32 * KB], U8)
    bufW = nc.alloc_sbuf_tensor("bufW", [128, 38 * KB], U8)
    bufS = nc.alloc_sbuf_tensor("bufS", [128, 4 * KB], U8)

    def view(buf, off_b, dt, n):
        esz = {F32: 4, BF16: 2, I8: 1, U8: 1}[dt]
        return buf.ap()[:, off_b:off_b + n * esz].bitcast(dt)

    q_tm = view(bufA, 0, BF16, NTB * D)
    xn2_fm = view(bufA, 0, BF16, NKC * T)
    k_tm = view(bufB, 0, BF16, NTB * D)
    Tall = view(bufW, 8 * KB, BF16, 3 * D)
    gated_fm = view(bufB, 0, BF16, H * T)
    fuseA = view(bufB, 0, BF16, 32 * 512)
    v_tm = view(bufC, 0, BF16, NTB * D)
    fuseB = view(bufC, 0, BF16, 32 * 512)
    g_tm = view(bufD, 0, BF16, NTB * D)
    xn_fm = view(bufE, 0, BF16, NKC * T)
    o_raw = view(bufE, 0, BF16, NTB * D)

    sdown_sb = view(bufD, 0, F32, D)
    xnew = [view(bufD, 8 * KB * i, F32, D) for i in range(4)] + \
           [view(bufE, 8 * KB * i, F32, D) for i in range(4)]
    xo_st = [view(bufW, 24 * KB + 2 * KB * i, F32, 512) for i in range(2)]
    silu_sb11 = [view(bufD, 8 * KB + 1 * KB * i, BF16, 512) for i in range(4)]
    xst = [view(bufD, 12 * KB + 8 * KB * i, F32, D) for i in range(2)]
    xn2tm_st = [view(bufW, 8 * KB, BF16, D), view(bufW, 12 * KB, BF16, D)]

    x_stage = [view(bufW, 8 * KB * i, F32, D) for i in range(2)]
    xntm_st = [view(bufD, 0, BF16, D), view(bufD, 4 * KB, BF16, D)]
    wsl_f32 = [view(bufW, 8 * KB * i, F32, D) for i in range(2)]
    wsl_bf = [view(bufW, 16 * KB, BF16, D), view(bufW, 20 * KB, BF16, D),
              view(bufW, 0, BF16, D), view(bufW, 4 * KB, BF16, D)]
    rope_tabs = view(bufW, 24 * KB, F32, 4 * 512)
    rp_st = ([view(bufW, 32 * KB + KB * i, BF16, 512) for i in range(4)] +
             [view(bufW, 8 * KB + KB * i, BF16, 512) for i in range(4)])
    rope_t1 = view(bufW, 36 * KB, BF16, 256)
    rope_t2 = view(bufW, 36 * KB + 512, BF16, 256)
    dmatT_sb = view(bufW, 0, F32, H * 128)
    S_sb = view(bufW, 8 * KB, F32, D)
    Sb_sb = view(bufW, 16 * KB, BF16, D)
    qf_sb = view(bufW, 20 * KB, BF16, D)
    kf_sb = view(bufW, 24 * KB, BF16, D)
    amt_sb = view(bufW, 28 * KB, BF16, D)
    kp_sb = view(bufW, 32 * KB, BF16, D)
    sg_tmp = view(bufW, 36 * KB, F32, 512)
    sinit_sb = view(bufW, 8 * KB, BF16, D)
    gated_c = view(bufW, 0, BF16, D)
    o_f = view(bufW, 24 * KB, BF16, D)
    sq_sb = view(bufW, 28 * KB, BF16, D)
    sgs_sb = view(bufW, 32 * KB, BF16, D)
    gu_bfg2 = [view(bufE, 16 * KB * i, BF16, 8192) for i in range(2)]
    gu_bfu2 = [view(bufW, 16 * KB * i, BF16, 8192) for i in range(2)]
    wd_bf = [view(bufD, 12 * KB + 8 * KB * i, BF16, 4096) for i in range(2)]
    dstage = [view(bufW, 32 * KB, F32, 1024), view(bufD, 28 * KB, F32, 1024)]

    ident_sb = view(bufS, 0, BF16, 128)
    kdec_sb = view(bufS, 256, F32, H)
    gC_sb = view(bufS, 320, F32, H)
    crpp_sb = view(bufS, 384, F32, H)
    ssq_sb = view(bufS, 448, F32, H)
    rinv_sb = view(bufS, 512, F32, H)
    stats_sb = view(bufS, 576, F32, 24)
    mv_sb = view(bufS, 672, F32, 2)
    sd_sb = view(bufS, 680, F32, 2)
    inv_sb = view(bufS, 688, F32, 2)
    b2_sb = view(bufS, 696, F32, 2)
    crsg_sb = view(bufS, 704, F32, NTB * H)
    wj_sb = view(bufS, 1216, F32, NSEG * H)
    gnw_sb = view(bufS, 1472, F32, 128)
    sgate_sb = view(bufS, 1984, F32, NIB)
    sup_sb = view(bufS, 2240, F32, NIB)
    sd2_sb = view(bufS, 2496, F32, H)
    statsP = [view(bufS, 576, F32, 24), view(bufS, 3200, F32, 24)]
    mvP = [view(bufS, 672, F32, 2), view(bufS, 3296, F32, 2)]
    sdP = [view(bufS, 680, F32, 2), view(bufS, 3304, F32, 2)]
    invP = [view(bufS, 688, F32, 2), view(bufS, 3312, F32, 2)]
    b2P = [view(bufS, 696, F32, 2), view(bufS, 3320, F32, 2)]
    epsc_sb = view(bufS, 2560, F32, 2)
    gnwpp_sb = view(bufS, 2568, F32, 1)

    PSt = nc.alloc_psum_tensor("PSall", [128, 8 * 512], F32)
    PS = PSt.ap()
    PSb = PS.bitcast(BF16)

    def psf(bank, nbanks=1):
        return PS[:, bank * 512:(bank + nbanks) * 512]

    stack = contextlib.ExitStack()
    sems = {}
    cnt = {}
    for name in ("sxc", "sxa", "sxb", "swa", "swb", "sio", "sioa", "siob",
                 "sgda", "sgdb", "sgdc", "sgdd", "sgde", "sgdf", "scc", "sv", "ss", "sg", "stt"):
        sems[name] = stack.enter_context(nc.semaphore(name))
        cnt[name] = 0

    def inc(instr, sem, n=1):
        cnt[sem] += n
        instr.then_inc(sems[sem], n)

    def dma(eng, sem, out, in_, **kw):
        eng.wait_ge(sems[sem], cnt[sem])
        cnt[sem] += 16
        eng.dma_start(out=out, in_=in_, **kw).then_inc(sems[sem], 16)
        return cnt[sem]

    block = stack.enter_context(nc.Block())
    S_, Sc, V_, Te, G_ = block.sync, block.scalar, block.vector, block.tensor, block.gpsimd

    def f(s):
        dma(s, "sxc", ident_sb, ident_h)
        dma(s, "sxc", epsc_sb, epsc_h)
        dma(s, "sxc", kdec_sb, kdec_h)
        dma(s, "sxc", gC_sb, gC_h)
        dma(s, "sxc", crpp_sb, crpp_h)
        dma(s, "sxc", crsg_sb, crsg_h)
        dma(s, "sxc", wj_sb, wj_h)
        dma(s, "sxc", gnw_sb, gnw_h)
        dma(s, "sxc", gnwpp_sb, gnwpp_h)
        dma(s, "sxc", sgate_sb, sgate_h)
        dma(s, "sxc", sup_sb, sup_h)
        dma(s, "sxc", rope_tabs[:, 0:512], cosq_h)
        dma(s, "sxc", rope_tabs[:, 512:1024], sinq_h)
        dma(s, "sxc", rope_tabs[:, 1024:1536], cosk_h)
        dma(s, "sxc", rope_tabs[:, 1536:2048], sink_h)
    S_(f)
    const_done = cnt["sxc"]

    def ln_transpose(src_dram, fm_dst, stage_pair, tmst, dma_gate, tensor_gate,
                     sbuf_tiles=None):
        xsem = {tb: ("sxa", "sxb")[tb % 2] for tb in range(NTB)}
        sx0 = {"sxa": cnt["sxa"], "sxb": cnt["sxb"]}
        sv0 = cnt["sv"]
        ss0 = cnt["ss"]
        st0 = cnt["stt"]
        ld_p = {tb: sx0[xsem[tb]] + 16 * (tb // 2 + 1) for tb in range(NTB)}
        stats_p = {tb: sv0 + (1 if tb == 0 else 3 * tb) for tb in range(NTB)}
        inv_p = {tb: sv0 + (2 if tb == 0 else 3 * tb + 1) for tb in range(NTB)}
        evac_p = {tb: sv0 + (3 * tb + 5 if tb < NTB - 1 else 3 * NTB) for tb in range(NTB)}
        sd_p = {tb: ss0 + 2 * tb + 1 for tb in range(NTB)}
        app_p = {tb: ss0 + 2 * tb + 2 for tb in range(NTB)}
        tr_p = {tb: st0 + tb + 1 for tb in range(NTB)}

        if sbuf_tiles is None:
            def f(s):
                for tb in range(NTB):
                    if dma_gate is not None:
                        sm, vv = dma_gate(tb)
                        s.wait_ge(sems[sm], vv)
                    if tb >= 2:
                        s.wait_ge(sems["ss"], app_p[tb - 2])
                    got = dma(s, xsem[tb], stage_pair[tb % 2],
                              src_dram[tb * 128:(tb + 1) * 128, :])
                    assert got == ld_p[tb]
            S_(f)

        def emit_stats_recip(v, tb):
            if sbuf_tiles is None:
                v.wait_ge(sems[xsem[tb]], ld_p[tb])
            elif dma_gate is not None:
                sm, vv = dma_gate(tb)
                v.wait_ge(sems[sm], vv)
            par = tb % 2
            if tb >= 2:
                v.wait_ge(sems["ss"], app_p[tb - 2])
            xs = stage_pair[tb % 2] if sbuf_tiles is None else sbuf_tiles[tb]
            for i in range(4):
                v.bn_stats(out=statsP[par][:, 6 * i:6 * i + 6],
                           in_=xs[:, 512 * i:512 * (i + 1)])
            v.drain()
            v.bn_aggr(out=mvP[par][:, 0:2], in_=statsP[par][:, 0:24])
            v.drain()
            inc(v.nop(), "sv")
            assert cnt["sv"] == stats_p[tb]
            v.wait_ge(sems["ss"], sd_p[tb])
            v.reciprocal(out=invP[par][:, 0:1], in_=sdP[par][:, 0:1])
            v.drain()
            v.scalar_tensor_tensor(out=b2P[par][:, 0:1], in0=mvP[par][:, 0:1],
                                   scalar=-1.0, in1=invP[par][:, 0:1],
                                   op0=OP.mult, op1=OP.mult)
            v.drain()
            inc(v.nop(), "sv")
            assert cnt["sv"] == inv_p[tb]

        def emit_evac(v, tb):
            par = tb % 2
            v.wait_ge(sems["stt"], tr_p[tb])
            for half in range(2):
                bank = 4 + 2 * par + half
                src = PSb[:, bank * 1024:(bank + 1) * 1024]
                dst = fm_dst.rearrange("p (kc t) -> p kc t", kc=NKC)[
                    :, half * 8:(half + 1) * 8, tb * 128:tb * 128 + 128]
                v.tensor_copy(out=dst, in_=src.rearrange("p (kc t) -> p kc t", kc=8))
            v.drain()
            inc(v.nop(), "sv")
            assert cnt["sv"] == evac_p[tb]

        def f(v):
            emit_stats_recip(v, 0)
            for tb in range(1, NTB):
                emit_stats_recip(v, tb)
                emit_evac(v, tb - 1)
            emit_evac(v, NTB - 1)
        V_(f)

        def f(sc):
            sc.wait_ge(sems["sxc"], const_done)
            for tb in range(NTB):
                par = tb % 2
                sc.wait_ge(sems["sv"], stats_p[tb])
                sc.activation(out=sdP[par][:, 0:1], in_=mvP[par][:, 1:2], func=AF.Sqrt,
                              bias=epsc_sb[:, 0:1])
                sc.drain()
                inc(sc.nop(), "ss")
                assert cnt["ss"] == sd_p[tb]
                sc.wait_ge(sems["sv"], inv_p[tb])
                if tb >= 2:
                    sc.wait_ge(sems["stt"], tr_p[tb - 2])
                xs2 = stage_pair[tb % 2] if sbuf_tiles is None else sbuf_tiles[tb]
                sc.activation(out=tmst[par][:, 0:D], in_=xs2, func=AF.Identity,
                              bias=b2P[par][:, 0:1], scale=invP[par][:, 0:1])
                sc.drain()
                inc(sc.nop(), "ss")
                assert cnt["ss"] == app_p[tb]
        Sc(f)

        def f(te):
            te.wait_ge(sems["sxc"], const_done)
            if tensor_gate is not None:
                te.wait_ge(sems[tensor_gate[0]], tensor_gate[1])
            for tb in range(NTB):
                par = tb % 2
                te.wait_ge(sems["ss"], app_p[tb])
                if tb >= 2:
                    te.wait_ge(sems["sv"], evac_p[tb - 2])
                for kc in range(NKC):
                    bank = 4 + 2 * par + kc // 8
                    te.transpose(out=PSb[:, bank * 1024 + (kc % 8) * 128:
                                         bank * 1024 + (kc % 8) * 128 + 128],
                                 in_=tmst[par][:, kc * 128:(kc + 1) * 128],
                                 identity=ident_sb)
                inc(te.drain(), "stt")
                assert cnt["stt"] == tr_p[tb]
        Te(f)
        return evac_p

    xn_evac = ln_transpose(x_h, xn_fm, x_stage, xntm_st, None, None)
    last_xn = xn_evac[NTB - 1]

    def projection(w_dram, kind, dst_tm=None, first_gate=None, slab_gate=None):
        wsem = {i: ("sgda", "sgdb", "sgdc", "sgdd")[i % 4] for i in range(16)}
        sx0 = {nm: cnt[nm] for nm in ("sgda", "sgdb", "sgdc", "sgdd")}
        st0, sv0, ss0, sxc0 = cnt["stt"], cnt["sv"], cnt["ss"], cnt["sxc"]
        io0a, io0b = cnt["sioa"], cnt["siob"]
        ld_p = {i: sx0[wsem[i]] + 16 * (i // 4 + 1) for i in range(16)}
        mm_p = {i: st0 + i + 1 for i in range(16)}
        ev_p = {(cg, tb): sv0 + cg * 8 + tb + 1 for cg in range(NCG) for tb in range(NTB)}
        cp_p = {(cg, tb): ss0 + cg * 8 + tb + 1 for cg in range(NCG) for tb in range(NTB)}
        bank_sem = "sv" if kind == "wo" else "ss"
        bank_p = ev_p if kind == "wo" else cp_p if kind in ("q", "k") else cp_p

        def emit_slab(g, i):
            cg, kcg = divmod(i, 4)
            if i < 4:
                if slab_gate is not None and slab_gate[i] is not None:
                    sm, vv = slab_gate[i]
                    g.wait_ge(sems[sm], vv)
                g.wait_ge(sems["sxc"], sxc0)
            else:
                g.wait_ge(sems["stt"], mm_p[i - 4])
            got = dma(g, wsem[i], wsl_bf[i % 4][:, 0:D], w_dram[cg, kcg])
            assert got == ld_p[i]

        if kind != "wo":
            def f(g):
                for i in range(16):
                    emit_slab(g, i)
            G_(f)
        else:
            def f(g):
                for i in range(16):
                    emit_slab(g, i)
            G_(f)
            xsem9 = {n: ("sxa", "sxb")[n % 2] for n in range(32)}
            sx9 = {"sxa": cnt["sxa"], "sxb": cnt["sxb"]}
            xld_p = {n: sx9[xsem9[n]] + 16 * (n // 2 + 1) for n in range(32)}

            def f(s):
                s.wait_ge(sems["sv"], sv0)
                for n in range(32):
                    cg, tb = divmod(n, 8)
                    if n >= 2:
                        pcg, ptb = divmod(n - 2, 8)
                        s.wait_ge(sems["sv"], ev_p[(pcg, ptb)])
                    got = dma(s, xsem9[n], xo_st[n % 2][:, 0:512],
                              x_h[tb * 128:(tb + 1) * 128, cg * 512:(cg + 1) * 512])
                    assert got == xld_p[n]
            S_(f)

        lhs = gated_fm if kind == "wo" else xn_fm

        def f(te):
            te.wait_ge(sems["sv"], sv0)
            te.wait_ge(sems["ss"], ss0)
            for cg in range(NCG):
                for kcg in range(4):
                    i = cg * 4 + kcg
                    te.wait_ge(sems[wsem[i]], ld_p[i])
                    for tb in range(NTB):
                        if kcg == 0:
                            if cg > 0:
                                te.wait_ge(sems[bank_sem], bank_p[(cg - 1, tb)])
                            elif first_gate is not None:
                                sm, vv = first_gate[tb]
                                te.wait_ge(sems[sm], vv)
                        for k4 in range(4):
                            kc = kcg * 4 + k4
                            te.matmul(psf(tb),
                                      lhs[:, kc * T + tb * 128: kc * T + tb * 128 + 128],
                                      wsl_bf[i % 4][:, k4 * 512:(k4 + 1) * 512],
                                      start=(kc == 0), stop=(kc == NKC - 1))
                    inc(te.drain(), "stt")
                    assert cnt["stt"] == mm_p[i]
        Te(f)

        if kind in ("q", "k"):
            coff = 0 if kind == "q" else 1024

            def f(sc):
                for cg in range(NCG):
                    sc.wait_ge(sems["stt"], mm_p[cg * 4 + 3])
                    for tb in range(NTB):
                        idx = cg * 8 + tb
                        if idx >= 8:
                            sc.wait_ge(sems["sv"], ev_p[(cg - 1, tb)])
                        sc.copy(out=rp_st[tb % 8][:, 0:512], in_=psf(tb))
                        sc.drain()
                        inc(sc.nop(), "ss")
                        assert cnt["ss"] == cp_p[(cg, tb)]
            Sc(f)

            def f(v):
                v.wait_ge(sems["sxc"], const_done)
                for cg in range(NCG):
                    for tb in range(NTB):
                        v.wait_ge(sems["ss"], cp_p[(cg, tb)])
                        ps = rp_st[tb % 8][:, 0:512]
                        cosT = rope_tabs[:, coff + tb * 64: coff + tb * 64 + 64]
                        sinT = rope_tabs[:, coff + 512 + tb * 64: coff + 512 + tb * 64 + 64]
                        x1 = ps.rearrange("p (h j) -> p h j", h=4)[:, :, 0:64]
                        x2 = ps.rearrange("p (h j) -> p h j", h=4)[:, :, 64:128]
                        cB = cosT.rearrange("p (a j) -> p a j", a=1).broadcast_to([128, 4, 64])
                        sB = sinT.rearrange("p (a j) -> p a j", a=1).broadcast_to([128, 4, 64])
                        t1 = rope_t1[:, 0:256].rearrange("p (h j) -> p h j", h=4)
                        t2 = rope_t2[:, 0:256].rearrange("p (h j) -> p h j", h=4)
                        d1 = dst_tm[:, tb * D + cg * 512: tb * D + (cg + 1) * 512]
                        o1 = d1.rearrange("p (h j) -> p h j", h=4)[:, :, 0:64]
                        o2 = d1.rearrange("p (h j) -> p h j", h=4)[:, :, 64:128]
                        v.tensor_tensor(out=t1, in0=x1, in1=cB, op=OP.mult)
                        v.tensor_tensor(out=t2, in0=x2, in1=sB, op=OP.mult)
                        v.drain()
                        v.tensor_tensor(out=o1, in0=t1, in1=t2, op=OP.subtract)
                        v.drain()
                        v.tensor_tensor(out=t1, in0=x2, in1=cB, op=OP.mult)
                        v.tensor_tensor(out=t2, in0=x1, in1=sB, op=OP.mult)
                        v.drain()
                        v.tensor_tensor(out=o2, in0=t1, in1=t2, op=OP.add)
                        v.drain()
                        inc(v.nop(), "sv")
                        assert cnt["sv"] == ev_p[(cg, tb)]
            V_(f)
            return dict(ev=cp_p, sem="ss", rope=ev_p, mm=mm_p)
        elif kind in ("v", "g"):
            def f(sc):
                for cg in range(NCG):
                    sc.wait_ge(sems["stt"], mm_p[cg * 4 + 3])
                    for tb in range(NTB):
                        sc.copy(out=dst_tm[:, tb * D + cg * 512: tb * D + (cg + 1) * 512],
                                in_=psf(tb))
                        sc.drain()
                        inc(sc.nop(), "ss")
                        assert cnt["ss"] == cp_p[(cg, tb)]
            Sc(f)
            return dict(ev=cp_p, sem="ss", mm=mm_p)
        else:
            def f(v):
                for cg in range(NCG):
                    v.wait_ge(sems["stt"], mm_p[cg * 4 + 3])
                    for tb in range(NTB):
                        n = cg * 8 + tb
                        v.wait_ge(sems[xsem9[n]], xld_p[n])
                        v.tensor_add(out=xnew[tb][:, cg * 512:(cg + 1) * 512],
                                     in0=psf(tb), in1=xo_st[n % 2][:, 0:512])
                        v.drain()
                        inc(v.nop(), "sv")
                        assert cnt["sv"] == ev_p[(cg, tb)]
            V_(f)
            stsem = {tb: ("sxa", "sxb")[tb % 2] for tb in range(NTB)}
            sxs = {"sxa": cnt["sxa"], "sxb": cnt["sxb"]}
            store_p = {tb: (stsem[tb], sxs[stsem[tb]] + 16 * (tb // 2 + 1))
                       for tb in range(NTB)}

            def f(s):
                for tb in range(NTB):
                    s.wait_ge(sems["sv"], ev_p[(3, tb)])
                    got = dma(s, stsem[tb], out_h[tb * 128:(tb + 1) * 128, :],
                              xnew[tb][:, 0:D])
                    assert got == store_p[tb][1]
            S_(f)
            return dict(ev=ev_p, sem="sv", mm=mm_p, store=store_p)

    def tb_gate(res):
        return {tb: (res["sem"], res["ev"][(3, tb)]) for tb in range(NTB)}

    if max_phase >= 3:
        pq = projection(wq_h, "q", q_tm,
                        first_gate={tb: ("sv", xn_evac[tb]) for tb in range(NTB)},
                        slab_gate=(None, None, ("sv", xn_evac[NTB - 1]),
                                   ("sv", xn_evac[NTB - 1])))
        def sg4(p):
            return tuple(("stt", p["mm"][12 + i]) for i in range(4))
        pk = projection(wk_h, "k", k_tm, first_gate=tb_gate(pq), slab_gate=sg4(pq))
        pv = projection(wv_h, "v", v_tm, first_gate=tb_gate(pk), slab_gate=sg4(pk))
        pg = projection(wg_h, "g", g_tm, first_gate=tb_gate(pv), slab_gate=sg4(pv))

    if "d_xnfm" in dump_h:
        def f(s):
            s.wait_ge(sems["sv"], last_xn)
            dma(s, "sxc", dump_h["d_xnfm"], xn_fm)
        S_(f)
    if max_phase >= 3:
        for nm, buf in (("d_qtm", q_tm), ("d_ktm", k_tm), ("d_vtm", v_tm), ("d_gtm", g_tm)):
            if nm in dump_h:
                def f(s, buf=buf):
                    s.wait_ge(sems["sv"], cnt["sv"])
                    s.wait_ge(sems["ss"], cnt["ss"])
                    dma(s, "sxc", dump_h[nm], buf)
                S_(f)

    if max_phase >= 5:
        def f(s):
            s.wait_ge(sems["ss"], cnt["ss"])
            s.wait_ge(sems["sv"], cnt["sv"])
            dma(s, "sxc", dmatT_sb, dmat_h)
        S_(f)
        dmat_ld = cnt["sxc"]

        def f(v):
            v.wait_ge(sems["sxc"], dmat_ld)
            v.memset(S_sb[:, 0:D], 0.0)
            v.drain()
            inc(v.nop(), "sv")
        V_(f)
        s_zero = cnt["sv"]

        sv5 = cnt["sv"]
        st5 = cnt["stt"]
        ss5 = cnt["ss"]
        kp_p, state_p = {}, {}
        snap_p = {}
        vv = sv5
        ssim = ss5
        for c in range(NTB):
            ssim += 1; snap_p[c] = ssim
            vv += 1; kp_p[c] = vv
            for grp in range(4):
                vv += 1; state_p[(c, grp)] = vv
        tu_p = {}
        tt = st5
        for c in range(NTB):
            for grp in range(4):
                tt += 1; tu_p[(c, grp)] = tt

        def f(sc):
            sc.wait_ge(sems["stt"], pg["mm"][15])
            for c in range(NTB):
                if c > 0:
                    sc.wait_ge(sems["sv"], state_p[(c - 1, 3)])
                else:
                    sc.wait_ge(sems["sv"], s_zero)
                sc.copy(out=o_raw[:, c * D:(c + 1) * D], in_=S_sb[:, 0:D])
                sc.drain()
                inc(sc.nop(), "ss")
                assert cnt["ss"] == snap_p[c]
        Sc(f)

        def f(te):
            te.wait_ge(sems["sv"], s_zero)
            te.wait_ge(sems["ss"], pg["ev"][(3, 7)])
            for c in range(NTB):
                te.wait_ge(sems["sv"], kp_p[c])
                for grp in range(4):
                    bank = 6 + grp % 2
                    if c > 0 or grp >= 2:
                        pc, pg_ = (c, grp - 2) if grp >= 2 else (c - 1, grp + 2)
                        te.wait_ge(sems["sv"], state_p[(pc, pg_)])
                    for hh in range(4):
                        h = grp * 4 + hh
                        te.matmul(psf(bank)[:, hh * 128:(hh + 1) * 128],
                                  kp_sb[:, h * 128:(h + 1) * 128],
                                  v_tm[:, c * D + h * 128:c * D + (h + 1) * 128],
                                  start=True, stop=True)
                    inc(te.drain(), "stt")
                    assert cnt["stt"] == tu_p[(c, grp)]
        Te(f)

        def f(v):
            for c in range(NTB):
                v.tensor_tensor(
                    out=kp_sb[:, 0:D].rearrange("p (h j) -> p h j", h=H),
                    in0=k_tm[:, c * D:(c + 1) * D].rearrange("p (h j) -> p h j", h=H),
                    in1=kdec_sb.rearrange("p (h a) -> p h a", a=1).broadcast_to(
                        [128, H, 128]),
                    op=OP.mult)
                v.drain()
                inc(v.nop(), "sv")
                assert cnt["sv"] == kp_p[c]
                for grp in range(4):
                    bank = 6 + grp % 2
                    v.wait_ge(sems["stt"], tu_p[(c, grp)])
                    if grp == 0:
                        v.wait_ge(sems["ss"], snap_p[c])
                    sl3 = S_sb[:, grp * 512:(grp + 1) * 512].rearrange(
                        "p (h j) -> p h j", h=4)
                    v.tensor_tensor(
                        out=sg_tmp[:, 0:512].rearrange("p (h j) -> p h j", h=4),
                        in0=sl3,
                        in1=gC_sb[:, grp * 4:(grp + 1) * 4].rearrange(
                            "p (h a) -> p h a", a=1).broadcast_to([128, 4, 128]),
                        op=OP.mult)
                    v.drain()
                    v.tensor_add(out=S_sb[:, grp * 512:(grp + 1) * 512],
                                 in0=sg_tmp[:, 0:512], in1=psf(bank))
                    v.drain()
                    inc(v.nop(), "sv")
                    assert cnt["sv"] == state_p[(c, grp)]
        V_(f)
        scan_done = state_p[(NTB - 1, 3)]

        if "d_S" in dump_h:
            def f(s):
                s.wait_ge(sems["sv"], scan_done)
                dma(s, "sxc", dump_h["d_S"], S_sb[:, 0:D])
            S_(f)

        dump_gate = cnt["sxc"]

        def f(g):
            g.wait_ge(sems["sv"], scan_done)
            g.wait_ge(sems["sxc"], dump_gate)
            dma(g, "sio", cc_in.ap(), S_sb[:, 0:D])
            g.wait_ge(sems["sio"], cnt["sio"])
            inc(g.collective_compute(
                "AllGather", OP.bypass,
                replica_groups=[[0, 1, 2, 3], [4, 5, 6, 7]],
                ins=[cc_in.ap().opt()], outs=[cc_out.ap().opt()]), "scc")
            g.wait_ge(sems["scc"], cnt["scc"])
            dma(g, "sio", Tall[:, 0:3 * D].rearrange("p (j c) -> p j c", j=3),
                cc_out.ap()[0:3 * 128, :].rearrange("(j p) c -> p j c", p=128))
            g.wait_ge(sems["sio"], cnt["sio"])
            for j in range(3):
                wjb = wj_sb[:, j * H:(j + 1) * H].rearrange(
                    "p (h a) -> p h a", a=1).broadcast_to([128, H, 128])
                tj = Tall[:, j * D:(j + 1) * D].rearrange("p (h j) -> p h j", h=H)
                g.tensor_tensor(out=tj, in0=tj, in1=wjb, op=OP.mult)
                g.drain()
            g.tensor_add(out=sinit_sb[:, 0:D], in0=Tall[:, 0:D], in1=Tall[:, D:2 * D])
            g.drain()
            g.tensor_add(out=sinit_sb[:, 0:D], in0=sinit_sb[:, 0:D],
                         in1=Tall[:, 2 * D:3 * D])
            g.drain()
            inc(g.nop(), "sg")
        G_(f)
        sinit_done = ("sg", cnt["sg"])

        sv5b = cnt["sv"]
        st5b = cnt["stt"]
        ss5b = cnt["ss"]
        evq2_p, evk2_p, amt_p, oin_p, oad_p = {}, {}, {}, {}, {}
        vv = sv5b
        ssim = ss5b
        for c in range(NTB):
            ssim += 1; evq2_p[c] = ssim
            if c > 0:
                vv += 1; oin_p[c] = vv
            ssim += 1; evk2_p[c] = ssim
            for grp in range(4):
                vv += 1; amt_p[(c, grp)] = vv
            vv += 1; oad_p[c] = vv
        tq2_p, tn2_p, tk2_p, ta_p, ti_p = {}, {}, {}, {}, {}
        tt = st5b
        for c in range(NTB):
            tt += 1; tq2_p[c] = tt
            if c > 0:
                tt += 1; tn2_p[c] = tt
            tt += 1; tk2_p[c] = tt
            for grp in range(4):
                tt += 1; ta_p[(c, grp)] = tt
            tt += 1; ti_p[c] = tt

        def f(te):
            te.wait_ge(sems["stt"], pg["mm"][15])
            te.wait_ge(sems["ss"], pg["ev"][(3, 7)])
            te.wait_ge(sems["sv"], scan_done)
            for c in range(NTB):
                if c > 0:
                    te.wait_ge(sems["ss"], evk2_p[c - 1])
                for h in range(H):
                    bank = 4 + h // 8
                    te.transpose(out=PSb[:, bank * 1024 + (h % 8) * 128:
                                         bank * 1024 + (h % 8) * 128 + 128],
                                 in_=q_tm[:, c * D + h * 128: c * D + (h + 1) * 128],
                                 identity=ident_sb)
                inc(te.drain(), "stt")
                assert cnt["stt"] == tq2_p[c]
                te.wait_ge(sems["ss"], evq2_p[c])
                if c > 0:
                    te.wait_ge(sems["sv"], oad_p[c - 1])
                    for h in range(H):
                        te.matmul(psf(h // 4)[:, (h % 4) * 128:(h % 4) * 128 + 128],
                                  qf_sb[:, h * 128:(h + 1) * 128],
                                  o_raw[:, c * D + h * 128:c * D + (h + 1) * 128],
                                  start=True, stop=True)
                    inc(te.drain(), "stt")
                    assert cnt["stt"] == tn2_p[c]
                for h in range(H):
                    bank = 4 + h // 8
                    te.transpose(out=PSb[:, bank * 1024 + (h % 8) * 128:
                                         bank * 1024 + (h % 8) * 128 + 128],
                                 in_=k_tm[:, c * D + h * 128: c * D + (h + 1) * 128],
                                 identity=ident_sb)
                inc(te.drain(), "stt")
                assert cnt["stt"] == tk2_p[c]
                te.wait_ge(sems["ss"], evk2_p[c])
                for grp in range(4):
                    bank = 6 + grp % 2
                    if c > 0 or grp >= 2:
                        pc, pg_ = (c, grp - 2) if grp >= 2 else (c - 1, grp + 2)
                        te.wait_ge(sems["sv"], amt_p[(pc, pg_)])
                    for hh in range(4):
                        h = grp * 4 + hh
                        te.matmul(psf(bank)[:, hh * 128:(hh + 1) * 128],
                                  kf_sb[:, h * 128:(h + 1) * 128],
                                  qf_sb[:, h * 128:(h + 1) * 128],
                                  start=True, stop=True)
                    inc(te.drain(), "stt")
                    assert cnt["stt"] == ta_p[(c, grp)]
                te.wait_ge(sems["sv"], amt_p[(c, 3)])
                if c > 0:
                    te.wait_ge(sems["sv"], oin_p[c])
                elif c == 0:
                    pass
                if c > 0:
                    te.wait_ge(sems["sv"], oad_p[c - 1])
                for h in range(H):
                    te.matmul(psf(h // 4)[:, (h % 4) * 128:(h % 4) * 128 + 128],
                              amt_sb[:, h * 128:(h + 1) * 128],
                              v_tm[:, c * D + h * 128:c * D + (h + 1) * 128],
                              start=True, stop=True)
                inc(te.drain(), "stt")
                assert cnt["stt"] == ti_p[c]
        Te(f)

        def f(sc):
            for c in range(NTB):
                sc.wait_ge(sems["stt"], tq2_p[c])
                sc.copy(out=qf_sb[:, 0:D], in_=PSb[:, 4 * 1024:6 * 1024])
                sc.drain()
                inc(sc.nop(), "ss")
                assert cnt["ss"] == evq2_p[c]
                sc.wait_ge(sems["stt"], tk2_p[c])
                sc.copy(out=kf_sb[:, 0:D], in_=PSb[:, 4 * 1024:6 * 1024])
                sc.drain()
                inc(sc.nop(), "ss")
                assert cnt["ss"] == evk2_p[c]
        Sc(f)

        def f(v):
            for c in range(NTB):
                if c > 0:
                    v.wait_ge(sems["stt"], tn2_p[c])
                    v.tensor_tensor(
                        out=o_raw[:, c * D:(c + 1) * D].rearrange(
                            "p (h j) -> p h j", h=H),
                        in0=PS[:, 0:2048].rearrange("p (h j) -> p h j", h=H),
                        in1=crpp_sb[:, 0:H].rearrange(
                            "p (h a) -> p h a", a=1).broadcast_to([128, H, 128]),
                        op=OP.mult)
                    v.drain()
                    inc(v.nop(), "sv")
                    assert cnt["sv"] == oin_p[c]
                for grp in range(4):
                    bank = 6 + grp % 2
                    v.wait_ge(sems["stt"], ta_p[(c, grp)])
                    v.tensor_tensor(out=amt_sb[:, grp * 512:(grp + 1) * 512],
                                    in0=psf(bank),
                                    in1=dmatT_sb[:, grp * 512:(grp + 1) * 512],
                                    op=OP.mult)
                    v.drain()
                    inc(v.nop(), "sv")
                    assert cnt["sv"] == amt_p[(c, grp)]
                v.wait_ge(sems["stt"], ti_p[c])
                if c > 0:
                    v.tensor_add(out=o_raw[:, c * D:(c + 1) * D],
                                 in0=PS[:, 0:2048], in1=o_raw[:, c * D:(c + 1) * D])
                else:
                    v.tensor_copy(out=o_raw[:, 0:D], in_=PS[:, 0:2048])
                v.drain()
                inc(v.nop(), "sv")
                assert cnt["sv"] == oad_p[c]
        V_(f)
        scan2_done = oad_p[NTB - 1]

        if "d_oraw" in dump_h:
            def f(s):
                s.wait_ge(sems["sv"], scan2_done)
                dma(s, "sxc", dump_h["d_oraw"], o_raw)
            S_(f)

    if max_phase >= 6:
        if "d_sinit" in dump_h:
            def f(s):
                s.wait_ge(sems[sinit_done[0]], sinit_done[1])
                dma(s, "sxc", dump_h["d_sinit"], sinit_sb[:, 0:D])
            S_(f)

    if max_phase >= 8:
        o_f2 = [o_f, view(bufW, 32 * KB, BF16, D)]
        sq2 = [sq_sb, view(bufW, 12 * KB, BF16, D)]
        gated2 = [gated_c, view(bufW, 4 * KB, BF16, D)]
        rinv2 = [rinv_sb, view(bufS, 3100, F32, H)]

        sv8 = cnt["sv"]
        ss8 = cnt["ss"]
        st8 = cnt["stt"]
        qf8_p, of8_p, ssq8_p, rinv8_p, gc8_p, gf8_p = {}, {}, {}, {}, {}, {}
        sq8_p, sd8_p, sil8_p = {}, {}, {}
        ssim = ss8
        for c in range(NTB):
            ssim += 1
            sil8_p[c] = ssim
        vv = sv8
        for c in range(NTB):
            ssim += 1; qf8_p[c] = ssim
            if c > 0:
                ssim += 1; sq8_p[c - 1] = ssim
                ssim += 1; sd8_p[c - 1] = ssim
            if c > 1:
                ssim += 1; gf8_p[c - 2] = ssim
        ssim += 1; sq8_p[NTB - 1] = ssim
        ssim += 1; sd8_p[NTB - 1] = ssim
        ssim += 1; gf8_p[NTB - 2] = ssim
        ssim += 1; gf8_p[NTB - 1] = ssim
        for c in range(NTB):
            vv += 1; of8_p[c] = vv
            if c > 0:
                vv += 1; ssq8_p[c - 1] = vv
                vv += 1; rinv8_p[c - 1] = vv
                vv += 1; gc8_p[c - 1] = vv
        vv += 1; ssq8_p[NTB - 1] = vv
        vv += 1; rinv8_p[NTB - 1] = vv
        vv += 1; gc8_p[NTB - 1] = vv
        t8q_p, t8c_p, t8g_p = {}, {}, {}
        tt = st8
        for c in range(NTB):
            tt += 1; t8q_p[c] = tt
            tt += 1; t8c_p[c] = tt
            if c > 0:
                tt += 1; t8g_p[c - 1] = tt
        tt += 1; t8g_p[NTB - 1] = tt

        def emit_t8g(te, c):
            te.wait_ge(sems["sv"], gc8_p[c])
            if c > 0:
                te.wait_ge(sems["ss"], gf8_p[c - 1])
            for h in range(H):
                bank = 6 + h // 8
                te.transpose(out=PSb[:, bank * 1024 + (h % 8) * 128:
                                     bank * 1024 + (h % 8) * 128 + 128],
                             in_=gated2[c % 2][:, h * 128:(h + 1) * 128],
                             identity=ident_sb)
            inc(te.drain(), "stt")
            assert cnt["stt"] == t8g_p[c]

        def f(te):
            te.wait_ge(sems["sv"], scan2_done)
            for c in range(NTB):
                if c > 0:
                    te.wait_ge(sems["ss"], qf8_p[c - 1])
                for h in range(H):
                    bank = 4 + h // 8
                    te.transpose(out=PSb[:, bank * 1024 + (h % 8) * 128:
                                         bank * 1024 + (h % 8) * 128 + 128],
                                 in_=q_tm[:, c * D + h * 128: c * D + (h + 1) * 128],
                                 identity=ident_sb)
                inc(te.drain(), "stt")
                assert cnt["stt"] == t8q_p[c]
                te.wait_ge(sems["ss"], qf8_p[c])
                if c == 0:
                    te.wait_ge(sems[sinit_done[0]], sinit_done[1])
                if c > 0:
                    te.wait_ge(sems["sv"], of8_p[c - 1])
                for h in range(H):
                    te.matmul(psf(h // 4)[:, (h % 4) * 128:(h % 4) * 128 + 128],
                              qf_sb[:, h * 128:(h + 1) * 128],
                              sinit_sb[:, h * 128:(h + 1) * 128],
                              start=True, stop=True)
                inc(te.drain(), "stt")
                assert cnt["stt"] == t8c_p[c]
                if c > 0:
                    emit_t8g(te, c - 1)
            emit_t8g(te, NTB - 1)
        Te(f)

        def emit_tail(v, c):
            v.wait_ge(sems["ss"], sq8_p[c])
            v.tensor_reduce(out=ssq_sb[:, 0:H],
                            in_=sq2[c % 2][:, 0:D].rearrange("p (h j) -> p h j", h=H),
                            axis=mybir.AxisListType.X, op=OP.add)
            v.drain()
            inc(v.nop(), "sv")
            assert cnt["sv"] == ssq8_p[c]
            v.wait_ge(sems["ss"], sd8_p[c])
            v.reciprocal(out=rinv2[c % 2][:, 0:H], in_=sd2_sb[:, 0:H])
            v.drain()
            inc(v.nop(), "sv")
            assert cnt["sv"] == rinv8_p[c]
            v.tensor_tensor(
                out=sq2[c % 2][:, 0:D].rearrange("p (h j) -> p h j", h=H),
                in0=o_f2[c % 2][:, 0:D].rearrange("p (h j) -> p h j", h=H),
                in1=rinv2[c % 2][:, 0:H].rearrange(
                    "p (h a) -> p h a", a=1).broadcast_to([128, H, 128]),
                op=OP.mult)
            v.drain()
            if c > 1:
                v.wait_ge(sems["stt"], t8g_p[c - 2])
            v.wait_ge(sems["ss"], sil8_p[c])
            v.tensor_mul(out=gated2[c % 2][:, 0:D], in0=sq2[c % 2][:, 0:D],
                         in1=g_tm[:, c * D:(c + 1) * D])
            v.drain()
            inc(v.nop(), "sv")
            assert cnt["sv"] == gc8_p[c]

        def f(v):
            for c in range(NTB):
                v.wait_ge(sems["stt"], t8c_p[c])
                if c > 1:
                    v.wait_ge(sems["ss"], sq8_p[c - 2])
                v.tensor_tensor(
                    out=o_f2[c % 2][:, 0:D].rearrange("p (h j) -> p h j", h=H),
                    in0=PS[:, 0:2048].rearrange("p (h j) -> p h j", h=H),
                    in1=crsg_sb[:, c * H:(c + 1) * H].rearrange(
                        "p (h a) -> p h a", a=1).broadcast_to([128, H, 128]),
                    op=OP.mult)
                v.drain()
                v.tensor_add(out=o_f2[c % 2][:, 0:D], in0=o_f2[c % 2][:, 0:D],
                             in1=o_raw[:, c * D:(c + 1) * D])
                v.drain()
                inc(v.nop(), "sv")
                assert cnt["sv"] == of8_p[c]
                if c > 0:
                    emit_tail(v, c - 1)
            emit_tail(v, NTB - 1)
        V_(f)

        def emit_gfm(sc, c):
            sc.wait_ge(sems["stt"], t8g_p[c])
            dst = gated_fm.rearrange("p (h t) -> p h t", h=H)[
                :, :, c * 128:(c + 1) * 128]
            sc.mul(out=dst[:, 0:8, :],
                   in_=PSb[:, 6 * 1024:7 * 1024].rearrange("p (h t) -> p h t", h=8),
                   mul=gnwpp_sb[:, 0:1])
            sc.mul(out=dst[:, 8:16, :],
                   in_=PSb[:, 7 * 1024:8 * 1024].rearrange("p (h t) -> p h t", h=8),
                   mul=gnwpp_sb[:, 0:1])
            sc.drain()
            inc(sc.nop(), "ss")
            assert cnt["ss"] == gf8_p[c]

        def f(sc):
            sc.wait_ge(sems["sv"], scan_done)
            for c in range(NTB):
                sc.activation(out=g_tm[:, c * D:(c + 1) * D],
                              in_=g_tm[:, c * D:(c + 1) * D], func=AF.Silu)
                sc.drain()
                inc(sc.nop(), "ss")
                assert cnt["ss"] == sil8_p[c]
            def emit_sqsd(c):
                sc.wait_ge(sems["sv"], of8_p[c])
                sc.activation(out=sq2[c % 2][:, 0:D], in_=o_f2[c % 2][:, 0:D],
                              func=AF.Square)
                sc.drain()
                inc(sc.nop(), "ss")
                assert cnt["ss"] == sq8_p[c]
                sc.wait_ge(sems["sv"], ssq8_p[c])
                sc.activation(out=sd2_sb[:, 0:H], in_=ssq_sb[:, 0:H], func=AF.Sqrt,
                              bias=epsc_sb[:, 0:1], scale=epsc_sb[:, 1:2])
                sc.drain()
                inc(sc.nop(), "ss")
                assert cnt["ss"] == sd8_p[c]

            for c in range(NTB):
                sc.wait_ge(sems["stt"], t8q_p[c])
                sc.copy(out=qf_sb[:, 0:D], in_=PSb[:, 4 * 1024:6 * 1024])
                sc.drain()
                inc(sc.nop(), "ss")
                assert cnt["ss"] == qf8_p[c]
                if c > 0:
                    emit_sqsd(c - 1)
                if c > 1:
                    emit_gfm(sc, c - 2)
            emit_sqsd(NTB - 1)
            emit_gfm(sc, NTB - 2)
            emit_gfm(sc, NTB - 1)
        Sc(f)
        gated_done = gf8_p[NTB - 1]
        t8c_last = t8c_p[NTB - 1]
        t8g_last = t8g_p[NTB - 1]

        if "d_gated" in dump_h:
            def f(s):
                s.wait_ge(sems["ss"], gated_done)
                dma(s, "sxc", dump_h["d_gated"], gated_fm)
            S_(f)

    if max_phase >= 9:
        r9 = projection(wo_h, "wo", None,
                        first_gate={tb: ("ss", gated_done) for tb in range(NTB)},
                        slab_gate=(("stt", t8c_last), ("stt", t8c_last),
                                   ("stt", t8g_last), ("stt", t8g_last)))

    if max_phase >= 10:
        def gate10(tb):
            return ("sv", r9["ev"][(3, tb)])
        xn2_evac = ln_transpose(None, xn2_fm, xst, xn2tm_st, gate10,
                                ("sv", r9["ev"][(3, 7)]),
                                sbuf_tiles=[xnew[tb][:, 0:D] for tb in range(NTB)])
        xn2_done = xn2_evac[NTB - 1]

        if "d_xn2fm" in dump_h:
            def f(s):
                s.wait_ge(sems["sv"], xn2_done)
                dma(s, "sxc", dump_h["d_xn2fm"], xn2_fm)
            S_(f)

    if max_phase >= 11:
        def f(s):
            s.wait_ge(sems["sv"], xn2_done)
            s.wait_ge(sems[r9["store"][0][0]], r9["store"][0][1])
            dma(s, "sxc", sdown_sb, sdown_h)
        S_(f)
        sdown_ld = cnt["sxc"]

        def fuse_view(ib):
            return (fuseA[:, ib * 512:(ib + 1) * 512] if ib < 32
                    else fuseB[:, (ib - 32) * 512:(ib - 31) * 512])

        prev_down_ev = None
        prev_dev11 = None
        prev_dmm_last = None
        prev_gu_mm = None
        prev_dacc_p = None
        NGRP = 16
        for hf in range(2):
            gsem = {g16: ("sgda", "sgdb")[g16 % 2] for g16 in range(16)}
            usem = {g16: ("sgdc", "sgdd")[g16 % 2] for g16 in range(16)}
            ioa11, iob11 = cnt["sioa"], cnt["siob"]
            gbase = {nm: cnt[nm] for nm in ("sgda", "sgdb", "sgdc", "sgdd")}
            st11, ss11, sv11 = cnt["stt"], cnt["ss"], cnt["sv"]
            if hf == 0:
                gld_p = {g16: gbase[gsem[g16]] + 16 * (g16 // 2 + 1)
                         for g16 in range(16)}
                uld_p = {g16: gbase[usem[g16]] + 16 * (g16 // 2 + 1)
                         for g16 in range(16)}
            else:
                gld_p = dict(pre_gu["gld"])
                uld_p = dict(pre_gu["uld"])
                for g16 in range(2, 16):
                    gld_p[g16] = gbase[gsem[g16]] + 16 * ((g16 - 2) // 2 + 1)
                    uld_p[g16] = gbase[usem[g16]] + 16 * ((g16 - 2) // 2 + 1)
            mm_p = {(g16, w): st11 + 2 * g16 + w + 1
                    for g16 in range(16) for w in range(2)}
            silu_p = {(g16, b): ss11 + 4 * g16 + b + 1
                      for g16 in range(16) for b in range(4)}
            fuse_p = {(g16, b): sv11 + 4 * g16 + b + 1
                      for g16 in range(16) for b in range(4)}

            def mk_gu(lo, hi):
                def f(g, gld_p=gld_p, uld_p=uld_p, mm_p=mm_p, st11=st11,
                      ss11=ss11, sv11=sv11, ioa11=ioa11, iob11=iob11, hf=hf,
                      gsem=gsem, usem=usem, prev_gu_mm=prev_gu_mm):
                    for g16 in range(lo, hi):
                        if g16 < 2:
                            if hf == 0:
                                g.wait_ge(sems["stt"], st11)
                                g.wait_ge(sems["ss"], ss11)
                                g.wait_ge(sems["sv"], sv11)
                                g.wait_ge(sems["sioa"], ioa11)
                                g.wait_ge(sems["siob"], iob11)
                                for tb in range(NTB):
                                    g.wait_ge(sems[r9["store"][tb][0]],
                                              r9["store"][tb][1])
                            else:
                                g.wait_ge(sems["stt"], prev_gu_mm[(14 + g16, 0)])
                        else:
                            g.wait_ge(sems["stt"], mm_p[(g16 - 2, 0)])
                        got = dma(g, gsem[g16], gu_bfg2[g16 % 2][:, 0:8192],
                                  wgate_h[g16].rearrange("p kc j -> p (kc j)"))
                        assert got == gld_p[g16]
                        if g16 >= 2:
                            g.wait_ge(sems["stt"], mm_p[(g16 - 2, 1)])
                        elif hf == 1:
                            g.wait_ge(sems["stt"], prev_gu_mm[(14 + g16, 1)])
                        got = dma(g, usem[g16], gu_bfu2[g16 % 2][:, 0:8192],
                                  wup_h[g16].rearrange("p kc j -> p (kc j)"))
                        assert got == uld_p[g16]
                return f

            dsem = {n: ("sgde", "sgdf")[n % 2] for n in range(2 * NGRP)}
            dbase = {nm: cnt[nm] for nm in ("sgde", "sgdf")}
            dld_p = {n: dbase[dsem[n]] + 16 * (n // 2 + 1) for n in range(2 * NGRP)}

            def mk_dld_early(g, hf=hf, prev_dmm_last=prev_dmm_last, dld_p=dld_p,
                             dsem=dsem):
                for n in range(2):
                    if hf == 0:
                        g.wait_ge(sems["sv"], xn2_done)
                        for tb2 in range(1, 4):
                            g.wait_ge(sems[r9["store"][tb2][0]],
                                      r9["store"][tb2][1])
                    else:
                        g.wait_ge(sems["stt"], prev_dmm_last)
                    got = dma(g, dsem[n], wd_bf[n % 2][:, 0:4096],
                              wdown_h[divmod(n, NGRP)])
                    assert got == dld_p[n]

            if hf == 0:
                G_(mk_gu(0, 2))
            G_(mk_dld_early)
            G_(mk_gu(2, 16))

            def f(te, gld_p=gld_p, uld_p=uld_p, mm_p=mm_p, silu_p=silu_p,
                  fuse_p=fuse_p, hf=hf, prev_down_ev=prev_down_ev, gsem=gsem,
                  usem=usem, prev_dev11=prev_dev11):
                for g16 in range(16):
                    for w in range(2):
                        if g16 == 0:
                            if w == 0:
                                te.wait_ge(sems["sv"], xn2_evac[3] if hf == 0
                                           else prev_dev11)
                            else:
                                te.wait_ge(sems["sv"], xn2_done if hf == 0
                                           else prev_down_ev)
                        te.wait_ge(sems[gsem[g16] if w == 0 else usem[g16]],
                                   (gld_p if w == 0 else uld_p)[g16])
                        if w == 0 and g16 > 0:
                            te.wait_ge(sems["ss"], silu_p[(g16 - 1, 3)])
                        if w == 1 and g16 > 0:
                            te.wait_ge(sems["sv"], fuse_p[(g16 - 1, 3)])
                        wbf = (gu_bfg2 if w == 0 else gu_bfu2)[g16 % 2]
                        for kc in range(NKC):
                            for b in range(4):
                                te.matmul(psf(4 * w + b),
                                          wbf[:, kc * 512 + b * 128:
                                              kc * 512 + (b + 1) * 128],
                                          xn2_fm[:, kc * T + hf * 512:
                                                 kc * T + (hf + 1) * 512],
                                          start=(kc == 0), stop=(kc == NKC - 1))
                        inc(te.drain(), "stt")
                        assert cnt["stt"] == mm_p[(g16, w)]
            Te(f)

            def f(sc, mm_p=mm_p, silu_p=silu_p, fuse_p=fuse_p, hf=hf):
                for g16 in range(16):
                    sc.wait_ge(sems["stt"], mm_p[(g16, 0)])
                    if g16 > 0:
                        sc.wait_ge(sems["sv"], fuse_p[(g16 - 1, 3)])
                    for b in range(4):
                        ib = g16 * 4 + b
                        sc.activation(out=silu_sb11[b][:, 0:512], in_=psf(b),
                                      func=AF.Silu, scale=sgate_sb[:, ib:ib + 1])
                        sc.drain()
                        inc(sc.nop(), "ss")
                        assert cnt["ss"] == silu_p[(g16, b)]
            Sc(f)

            def f(v, mm_p=mm_p, silu_p=silu_p, fuse_p=fuse_p, hf=hf):
                for g16 in range(16):
                    v.wait_ge(sems["stt"], mm_p[(g16, 1)])
                    v.wait_ge(sems["ss"], silu_p[(g16, 3)])
                    for b in range(4):
                        ib = g16 * 4 + b
                        v.scalar_tensor_tensor(out=fuse_view(ib), in0=psf(4 + b),
                                               scalar=sup_sb[:, ib:ib + 1],
                                               in1=silu_sb11[b][:, 0:512],
                                               op0=OP.mult, op1=OP.mult)
                        v.drain()
                        inc(v.nop(), "sv")
                        assert cnt["sv"] == fuse_p[(g16, b)]
            V_(f)

            dasem = {n: ("sioa", "siob")[n % 2] for n in range(8)}
            st11d, sv11d = cnt["stt"], cnt["sv"]
            io11d = {"sioa": cnt["sioa"], "siob": cnt["siob"]}
            dmm_p = {n: st11d + n + 1 for n in range(2 * NGRP)}
            dev_p = {}
            vv = sv11d
            for cgp in range(2):
                for q4 in range(4):
                    vv += 1
                    dev_p[(cgp, q4)] = vv
            dacc_p = {n: io11d[dasem[n]] + 16 * (n // 2 + 1) for n in range(8)}

            def mk_dld(n0, n1):
                def f(g, dld_p=dld_p, dmm_p=dmm_p, mm_p=mm_p,
                      dsem=dsem, hf=hf):
                    for n in range(n0, n1):
                        cgp, grp = divmod(n, NGRP)
                        g.wait_ge(sems["stt"], dmm_p[n - 2])
                        got = dma(g, dsem[n], wd_bf[n % 2][:, 0:4096],
                                  wdown_h[cgp, grp])
                        assert got == dld_p[n]
                return f

            def mk_dacc(cgp):
                def f(g, cgp=cgp, dev_p=dev_p, dacc_p=dacc_p, hf=hf, dasem=dasem):
                    for q4 in range(4):
                        n = cgp * 4 + q4
                        g.wait_ge(sems["sv"], dev_p[(cgp, q4)])
                        tb = hf * 4 + q4
                        g.wait_ge(sems[r9["store"][tb][0]], r9["store"][tb][1])
                        got = dma(g, dasem[n],
                                  out_h[tb * 128:(tb + 1) * 128,
                                        cgp * 1024:(cgp + 1) * 1024],
                                  dstage[n % 2][:, 0:1024], accum_op=OP.add)
                        assert got == dacc_p[n]
                return f

            def f(te, dld_p=dld_p, dmm_p=dmm_p, dev_p=dev_p,
                  fuse_p=fuse_p, silu_p=silu_p, dsem=dsem):
                te.wait_ge(sems["sv"], fuse_p[(15, 3)])
                te.wait_ge(sems["ss"], silu_p[(15, 3)])
                for cgp in range(2):
                    if cgp > 0:
                        for q4 in range(4):
                            te.wait_ge(sems["sv"], dev_p[(0, q4)])
                    for grp in range(NGRP):
                        n = cgp * NGRP + grp
                        te.wait_ge(sems[dsem[n]], dld_p[n])
                        for ib4 in range(4):
                            ib = grp * 4 + ib4
                            for q4 in range(4):
                                for cg2 in range(2):
                                    te.matmul(
                                        psf(q4 * 2 + cg2),
                                        fuse_view(ib)[:, q4 * 128:(q4 + 1) * 128],
                                        wd_bf[n % 2][:, ib4 * 1024 + cg2 * 512:
                                                     ib4 * 1024 + (cg2 + 1) * 512],
                                        start=(grp == 0 and ib4 == 0),
                                        stop=(grp == NGRP - 1 and ib4 == 3))
                        inc(te.drain(), "stt")
                        assert cnt["stt"] == dmm_p[n]
            down_te = f

            def f(v, dmm_p=dmm_p, dev_p=dev_p, dacc_p=dacc_p, dasem=dasem,
                  prev_dacc_p=prev_dacc_p, hf=hf):
                v.wait_ge(sems["sxc"], sdown_ld)
                for cgp in range(2):
                    v.wait_ge(sems["stt"], dmm_p[cgp * NGRP + NGRP - 1])
                    for q4 in range(4):
                        n = cgp * 4 + q4
                        if n >= 2:
                            v.wait_ge(sems[dasem[n - 2]], dacc_p[n - 2])
                        elif hf == 1:
                            v.wait_ge(sems[dasem[n + 6]], prev_dacc_p[n + 6])
                        v.tensor_mul(out=dstage[n % 2][:, 0:1024],
                                     in0=PS[:, q4 * 1024:(q4 + 1) * 1024],
                                     in1=sdown_sb[:, cgp * 1024:(cgp + 1) * 1024])
                        v.drain()
                        inc(v.nop(), "sv")
                        assert cnt["sv"] == dev_p[(cgp, q4)]
            down_v = f

            Te(down_te)
            V_(down_v)
            G_(mk_dld(2, NGRP))
            G_(mk_dacc(0))
            G_(mk_dld(NGRP, NGRP + 2))
            if hf == 0:
                pre_gu = {"gld": {}, "uld": {}}

                def f(g, mm_p=mm_p, pre_gu=pre_gu):
                    for g16 in range(2):
                        g.wait_ge(sems["stt"], mm_p[(14 + g16, 0)])
                        pre_gu["gld"][g16] = dma(
                            g, ("sgda", "sgdb")[g16], gu_bfg2[g16][:, 0:8192],
                            wgate_h[g16].rearrange("p kc j -> p (kc j)"))
                        g.wait_ge(sems["stt"], mm_p[(14 + g16, 1)])
                        pre_gu["uld"][g16] = dma(
                            g, ("sgdc", "sgdd")[g16], gu_bfu2[g16][:, 0:8192],
                            wup_h[g16].rearrange("p kc j -> p (kc j)"))
                G_(f)
            G_(mk_dld(NGRP + 2, 2 * NGRP))
            G_(mk_dacc(1))
            prev_down_ev = dev_p[(1, 3)]
            prev_dev11 = dev_p[(1, 1)]
            prev_dmm_last = dmm_p[2 * NGRP - 1]
            prev_gu_mm = mm_p
            prev_dacc_p = dacc_p

    def f(g):
        for nm in ("sio", "sioa", "siob", "sxa", "sxb", "sxc", "swa", "swb"):
            g.wait_ge(sems[nm], cnt[nm])
    G_(f)

    stack.close()
    return nc



def _prep_weights(Wq, Wk, Wv, Wg, Wo, w_gate, w_up, w_down,
                  s_gate, s_up, s_down, g_norm_w):
    def proj_r(W):
        return np.ascontiguousarray(
            W.reshape(4, 4, 128, 4, 512).transpose(3, 0, 2, 1, 4).reshape(4, 4, 128, 2048))

    def gu_r(W8):
        return np.ascontiguousarray(W8.reshape(16, 128, 16, 512).transpose(2, 1, 0, 3))

    wd = np.ascontiguousarray(w_down.reshape(16, 4, 128, 2, 1024).transpose(3, 0, 2, 1, 4).reshape(2, 16, 128, 4096))
    return dict(
        wq=proj_r(Wq), wk=proj_r(Wk), wv=proj_r(Wv), wg=proj_r(Wg), wo=proj_r(Wo),
        wgate_l=gu_r(w_gate), wup_l=gu_r(w_up), wdown_l=wd,
        sgate_pp=np.ascontiguousarray(s_gate.reshape(NIB, 128).T),
        sup_pp=np.ascontiguousarray(s_up.reshape(NIB, 128).T),
        sdown_rep=np.ascontiguousarray(np.broadcast_to(s_down, (128, D))),
        gnw_rep=np.ascontiguousarray(np.broadcast_to(g_norm_w, (128, 128))),
        gnw_pp=np.ascontiguousarray(g_norm_w.reshape(128, 1)),
    )


def kernel(L0_attn_x, L0_attn_Wq, L0_attn_Wk, L0_attn_Wv, L0_attn_Wg,
           L0_attn_g_norm_w, L0_attn_Wo, L0_ffn_w_gate, L0_ffn_w_up,
           L0_ffn_w_down, L0_ffn_s_gate, L0_ffn_s_up, L0_ffn_s_down):
    x = np.ascontiguousarray(np.asarray(L0_attn_x, dtype=np.float32))
    B, S, _ = x.shape
    shared = _prep_weights(
        np.asarray(L0_attn_Wq, np.float32), np.asarray(L0_attn_Wk, np.float32),
        np.asarray(L0_attn_Wv, np.float32), np.asarray(L0_attn_Wg, np.float32),
        np.asarray(L0_attn_Wo, np.float32),
        np.asarray(L0_ffn_w_gate), np.asarray(L0_ffn_w_up), np.asarray(L0_ffn_w_down),
        np.asarray(L0_ffn_s_gate, np.float32), np.asarray(L0_ffn_s_up, np.float32),
        np.asarray(L0_ffn_s_down, np.float32),
        np.asarray(L0_attn_g_norm_w, np.float32))

    nc = build()
    in_maps = []
    for core in range(8):
        b = core // NSEG
        seg = core % NSEG
        m = dict(shared)
        m["x"] = np.ascontiguousarray(x[b, seg * SEG:(seg + 1) * SEG, :])
        m.update(_host_consts(core))
        in_maps.append({k: np.asarray(v) for k, v in m.items()})
    res = run_bass_kernel_spmd(nc, in_maps, core_ids=list(range(8)))
    out = np.empty((B, S, D), dtype=np.float32)
    for core in range(8):
        b = core // NSEG
        seg = core % NSEG
        out[b, seg * SEG:(seg + 1) * SEG, :] = res.results[core]["out"]
    return out


# version 15
# speedup vs baseline: 1.0978x; 1.0037x over previous
import math
import numpy as np
import ml_dtypes
import contextlib
import concourse.bass as bass
import concourse.mybir as mybir
from concourse.bass_utils import run_bass_kernel_spmd

F32 = mybir.dt.float32
BF16 = mybir.dt.bfloat16
I8 = mybir.dt.int8
U8 = mybir.dt.uint8
AF = mybir.ActivationFunctionType
OP = mybir.AluOpType

T = 1024
D = 2048
H = 16
NTB = 8
NKC = 16
NCG = 4
NIB = 64
EPS = 1e-5
SEG = 1024
NSEG = 4
KB = 1024


def _host_consts(core_id):
    h = np.arange(H, dtype=np.float64)
    gamma = 1.0 - np.exp2(-5.0 - h)
    seg = core_id % NSEG
    pos0 = seg * SEG

    inv = 10000.0 ** (-(np.arange(64, dtype=np.float64) / 64.0))
    t_loc = np.arange(T, dtype=np.float64).reshape(NTB, 128)
    ang = (pos0 + t_loc)[:, :, None] * inv[None, None, :]
    cos = np.cos(ang)
    sin = np.sin(ang)

    def tmtab(a, scale=1.0):
        return np.ascontiguousarray(
            (a * scale).transpose(1, 0, 2).reshape(128, NTB * 64)).astype(np.float32)

    qs = 1.0 / math.sqrt(128.0)
    idx = np.arange(128, dtype=np.float64)
    rel = idx[None, :] - idx[:, None]
    dmatT = np.where(rel >= 0, gamma[:, None, None] ** rel[None], 0.0)
    dmatT = np.ascontiguousarray(
        dmatT.transpose(1, 0, 2).reshape(128, H * 128)).astype(np.float32)
    kdec = (gamma[None, :] ** (127.0 - idx[:, None])).astype(np.float32)
    gC = np.broadcast_to(gamma ** 128.0, (128, H)).astype(np.float32).copy()
    cross_pp = (gamma[None, :] ** (idx[:, None] + 1.0)).astype(np.float32)
    cs = np.zeros((128, NTB * H), dtype=np.float64)
    for c in range(NTB):
        cs[:, c * H:(c + 1) * H] = gamma[None, :] ** (c * 128.0 + idx[:, None] + 1.0)
    crossseg = cs.astype(np.float32)
    wj = np.zeros((128, NSEG * H), dtype=np.float64)
    for j in range(seg):
        wj[:, j * H:(j + 1) * H] = gamma[None, :] ** (1024.0 * (seg - 1 - j))
    wj = wj.astype(np.float32)
    ident = np.eye(128, dtype=ml_dtypes.bfloat16)
    return dict(cosq=tmtab(cos, qs), sinq=tmtab(sin, qs),
                cosk=tmtab(cos), sink=tmtab(sin), dmatT=dmatT,
                kdec=kdec, gC=gC, cross_pp=cross_pp, crossseg=crossseg,
                wj=wj, ident=ident,
                epsc=np.broadcast_to(np.array([EPS, 1.0 / 128.0], np.float32),
                                     (128, 2)).copy())


def build(dump=(), max_phase=99):
    nc = bass.Bass("TRN2", target_bir_lowering=False, debug=False)

    x_h = nc.dram_tensor("x", [T, D], F32, kind="ExternalInput").ap()
    out_h = nc.dram_tensor("out", [T, D], F32, kind="ExternalOutput").ap()
    wq_h = nc.dram_tensor("wq", [NCG, 4, 128, D], F32, kind="ExternalInput").ap()
    wk_h = nc.dram_tensor("wk", [NCG, 4, 128, D], F32, kind="ExternalInput").ap()
    wv_h = nc.dram_tensor("wv", [NCG, 4, 128, D], F32, kind="ExternalInput").ap()
    wg_h = nc.dram_tensor("wg", [NCG, 4, 128, D], F32, kind="ExternalInput").ap()
    wo_h = nc.dram_tensor("wo", [NCG, 4, 128, D], F32, kind="ExternalInput").ap()
    wgate_h = nc.dram_tensor("wgate_l", [16, 128, 16, 512], I8, kind="ExternalInput").ap()
    wup_h = nc.dram_tensor("wup_l", [16, 128, 16, 512], I8, kind="ExternalInput").ap()
    wdown_h = nc.dram_tensor("wdown_l", [2, 16, 128, 4096], I8, kind="ExternalInput").ap()
    sgate_h = nc.dram_tensor("sgate_pp", [128, NIB], F32, kind="ExternalInput").ap()
    sup_h = nc.dram_tensor("sup_pp", [128, NIB], F32, kind="ExternalInput").ap()
    sdown_h = nc.dram_tensor("sdown_rep", [128, D], F32, kind="ExternalInput").ap()
    gnw_h = nc.dram_tensor("gnw_rep", [128, 128], F32, kind="ExternalInput").ap()
    gnwpp_h = nc.dram_tensor("gnw_pp", [128, 1], F32, kind="ExternalInput").ap()
    cosq_h = nc.dram_tensor("cosq", [128, NTB * 64], F32, kind="ExternalInput").ap()
    sinq_h = nc.dram_tensor("sinq", [128, NTB * 64], F32, kind="ExternalInput").ap()
    cosk_h = nc.dram_tensor("cosk", [128, NTB * 64], F32, kind="ExternalInput").ap()
    sink_h = nc.dram_tensor("sink", [128, NTB * 64], F32, kind="ExternalInput").ap()
    dmat_h = nc.dram_tensor("dmatT", [128, H * 128], F32, kind="ExternalInput").ap()
    kdec_h = nc.dram_tensor("kdec", [128, H], F32, kind="ExternalInput").ap()
    gC_h = nc.dram_tensor("gC", [128, H], F32, kind="ExternalInput").ap()
    crpp_h = nc.dram_tensor("cross_pp", [128, H], F32, kind="ExternalInput").ap()
    crsg_h = nc.dram_tensor("crossseg", [128, NTB * H], F32, kind="ExternalInput").ap()
    wj_h = nc.dram_tensor("wj", [128, NSEG * H], F32, kind="ExternalInput").ap()
    ident_h = nc.dram_tensor("ident", [128, 128], BF16, kind="ExternalInput").ap()
    epsc_h = nc.dram_tensor("epsc", [128, 2], F32, kind="ExternalInput").ap()

    cc_in = nc.dram_tensor("cc_in", [128, D], BF16)
    cc_out = nc.dram_tensor("cc_out", [NSEG * 128, D], BF16)

    dump_h = {}
    for name, shape, dt in (
        ("d_xnfm", [128, NKC * T], BF16), ("d_qtm", [128, NTB * D], BF16),
        ("d_ktm", [128, NTB * D], BF16), ("d_vtm", [128, NTB * D], BF16),
        ("d_gtm", [128, NTB * D], BF16), ("d_oraw", [128, NTB * D], BF16),
        ("d_S", [128, D], F32), ("d_sinit", [128, D], BF16),
        ("d_gated", [128, H * T], BF16), ("d_xn2fm", [128, NKC * T], BF16),
    ):
        if name in dump:
            dump_h[name] = nc.dram_tensor(name, shape, dt, kind="ExternalOutput").ap()

    bufA = nc.alloc_sbuf_tensor("bufA", [128, 32 * KB], U8)
    bufB = nc.alloc_sbuf_tensor("bufB", [128, 32 * KB], U8)
    bufC = nc.alloc_sbuf_tensor("bufC", [128, 32 * KB], U8)
    bufD = nc.alloc_sbuf_tensor("bufD", [128, 32 * KB], U8)
    bufE = nc.alloc_sbuf_tensor("bufE", [128, 32 * KB], U8)
    bufW = nc.alloc_sbuf_tensor("bufW", [128, 38 * KB], U8)
    bufS = nc.alloc_sbuf_tensor("bufS", [128, 4 * KB], U8)

    def view(buf, off_b, dt, n):
        esz = {F32: 4, BF16: 2, I8: 1, U8: 1}[dt]
        return buf.ap()[:, off_b:off_b + n * esz].bitcast(dt)

    q_tm = view(bufA, 0, BF16, NTB * D)
    xn2_fm = view(bufA, 0, BF16, NKC * T)
    k_tm = view(bufB, 0, BF16, NTB * D)
    Tall = view(bufW, 8 * KB, BF16, 3 * D)
    gated_fm = view(bufB, 0, BF16, H * T)
    fuseA = view(bufB, 0, BF16, 32 * 512)
    v_tm = view(bufC, 0, BF16, NTB * D)
    fuseB = view(bufC, 0, BF16, 32 * 512)
    g_tm = view(bufD, 0, BF16, NTB * D)
    xn_fm = view(bufE, 0, BF16, NKC * T)
    o_raw = view(bufE, 0, BF16, NTB * D)

    sdown_sb = view(bufD, 0, F32, D)
    xnew = [view(bufD, 8 * KB * i, F32, D) for i in range(4)] + \
           [view(bufE, 8 * KB * i, F32, D) for i in range(4)]
    xo_st = [view(bufW, 24 * KB + 2 * KB * i, F32, 512) for i in range(2)]
    silu_sb11 = [view(bufD, 8 * KB + 1 * KB * i, BF16, 512) for i in range(4)]
    xst = [view(bufD, 12 * KB + 8 * KB * i, F32, D) for i in range(2)]
    xn2tm_st = [view(bufW, 8 * KB, BF16, D), view(bufW, 12 * KB, BF16, D)]

    x_stage = [view(bufW, 8 * KB * i, F32, D) for i in range(2)]
    xntm_st = [view(bufD, 0, BF16, D), view(bufD, 4 * KB, BF16, D)]
    wsl_f32 = [view(bufW, 8 * KB * i, F32, D) for i in range(2)]
    wsl_bf = [view(bufW, 16 * KB, BF16, D), view(bufW, 20 * KB, BF16, D),
              view(bufW, 0, BF16, D), view(bufW, 4 * KB, BF16, D)]
    rope_tabs = view(bufW, 24 * KB, F32, 4 * 512)
    rp_st = ([view(bufW, 32 * KB + KB * i, BF16, 512) for i in range(4)] +
             [view(bufW, 8 * KB + KB * i, BF16, 512) for i in range(4)])
    rope_t1 = view(bufW, 36 * KB, BF16, 256)
    rope_t2 = view(bufW, 36 * KB + 512, BF16, 256)
    dmatT_sb = view(bufW, 0, F32, H * 128)
    S_sb = view(bufW, 8 * KB, F32, D)
    Sb_sb = view(bufW, 16 * KB, BF16, D)
    qf_sb = view(bufW, 20 * KB, BF16, D)
    kf_sb = view(bufW, 24 * KB, BF16, D)
    amt_sb = view(bufW, 28 * KB, BF16, D)
    kp_sb = view(bufW, 32 * KB, BF16, D)
    sg_tmp = view(bufW, 36 * KB, F32, 512)
    sinit_sb = view(bufW, 8 * KB, BF16, D)
    gated_c = view(bufW, 0, BF16, D)
    o_f = view(bufW, 24 * KB, BF16, D)
    sq_sb = view(bufW, 28 * KB, BF16, D)
    sgs_sb = view(bufW, 32 * KB, BF16, D)
    gu_bfg2 = [view(bufE, 16 * KB * i, BF16, 8192) for i in range(2)]
    gu_bfu2 = [view(bufW, 16 * KB * i, BF16, 8192) for i in range(2)]
    wd_bf = [view(bufD, 12 * KB + 8 * KB * i, BF16, 4096) for i in range(2)]
    dstage = [view(bufW, 32 * KB, F32, 1024), view(bufD, 28 * KB, F32, 1024)]

    ident_sb = view(bufS, 0, BF16, 128)
    kdec_sb = view(bufS, 256, F32, H)
    gC_sb = view(bufS, 320, F32, H)
    crpp_sb = view(bufS, 384, F32, H)
    ssq_sb = view(bufS, 448, F32, H)
    rinv_sb = view(bufS, 512, F32, H)
    stats_sb = view(bufS, 576, F32, 24)
    mv_sb = view(bufS, 672, F32, 2)
    sd_sb = view(bufS, 680, F32, 2)
    inv_sb = view(bufS, 688, F32, 2)
    b2_sb = view(bufS, 696, F32, 2)
    crsg_sb = view(bufS, 704, F32, NTB * H)
    wj_sb = view(bufS, 1216, F32, NSEG * H)
    gnw_sb = view(bufS, 1472, F32, 128)
    sgate_sb = view(bufS, 1984, F32, NIB)
    sup_sb = view(bufS, 2240, F32, NIB)
    sd2_sb = view(bufS, 2496, F32, H)
    statsP = [view(bufS, 576, F32, 24), view(bufS, 3200, F32, 24)]
    mvP = [view(bufS, 672, F32, 2), view(bufS, 3296, F32, 2)]
    sdP = [view(bufS, 680, F32, 2), view(bufS, 3304, F32, 2)]
    invP = [view(bufS, 688, F32, 2), view(bufS, 3312, F32, 2)]
    b2P = [view(bufS, 696, F32, 2), view(bufS, 3320, F32, 2)]
    epsc_sb = view(bufS, 2560, F32, 2)
    gnwpp_sb = view(bufS, 2568, F32, 1)

    PSt = nc.alloc_psum_tensor("PSall", [128, 8 * 512], F32)
    PS = PSt.ap()
    PSb = PS.bitcast(BF16)

    def psf(bank, nbanks=1):
        return PS[:, bank * 512:(bank + nbanks) * 512]

    stack = contextlib.ExitStack()
    sems = {}
    cnt = {}
    for name in ("sxc", "sxa", "sxb", "swa", "swb", "sio", "sioa", "siob",
                 "sgda", "sgdb", "sgdc", "sgdd", "sgde", "sgdf", "scc", "sv", "ss", "sg", "stt"):
        sems[name] = stack.enter_context(nc.semaphore(name))
        cnt[name] = 0

    def inc(instr, sem, n=1):
        cnt[sem] += n
        instr.then_inc(sems[sem], n)

    def dma(eng, sem, out, in_, **kw):
        eng.wait_ge(sems[sem], cnt[sem])
        cnt[sem] += 16
        eng.dma_start(out=out, in_=in_, **kw).then_inc(sems[sem], 16)
        return cnt[sem]

    block = stack.enter_context(nc.Block())
    S_, Sc, V_, Te, G_ = block.sync, block.scalar, block.vector, block.tensor, block.gpsimd

    def f(s):
        dma(s, "sxc", ident_sb, ident_h)
        dma(s, "sxc", epsc_sb, epsc_h)
        dma(s, "sxc", kdec_sb, kdec_h)
        dma(s, "sxc", gC_sb, gC_h)
        dma(s, "sxc", crpp_sb, crpp_h)
        dma(s, "sxc", crsg_sb, crsg_h)
        dma(s, "sxc", wj_sb, wj_h)
        dma(s, "sxc", gnw_sb, gnw_h)
        dma(s, "sxc", gnwpp_sb, gnwpp_h)
        dma(s, "sxc", sgate_sb, sgate_h)
        dma(s, "sxc", sup_sb, sup_h)
        dma(s, "sxc", rope_tabs[:, 0:512], cosq_h)
        dma(s, "sxc", rope_tabs[:, 512:1024], sinq_h)
        dma(s, "sxc", rope_tabs[:, 1024:1536], cosk_h)
        dma(s, "sxc", rope_tabs[:, 1536:2048], sink_h)
    S_(f)
    const_done = cnt["sxc"]

    def ln_transpose(src_dram, fm_dst, stage_pair, tmst, dma_gate, tensor_gate,
                     sbuf_tiles=None):
        xsem = {tb: ("sxa", "sxb")[tb % 2] for tb in range(NTB)}
        sx0 = {"sxa": cnt["sxa"], "sxb": cnt["sxb"]}
        sv0 = cnt["sv"]
        ss0 = cnt["ss"]
        st0 = cnt["stt"]
        ld_p = {tb: sx0[xsem[tb]] + 16 * (tb // 2 + 1) for tb in range(NTB)}
        stats_p = {tb: sv0 + (1 if tb == 0 else 3 * tb) for tb in range(NTB)}
        inv_p = {tb: sv0 + (2 if tb == 0 else 3 * tb + 1) for tb in range(NTB)}
        evac_p = {tb: sv0 + (3 * tb + 5 if tb < NTB - 1 else 3 * NTB) for tb in range(NTB)}
        sd_p = {tb: ss0 + 2 * tb + 1 for tb in range(NTB)}
        app_p = {tb: ss0 + 2 * tb + 2 for tb in range(NTB)}
        tr_p = {tb: st0 + tb + 1 for tb in range(NTB)}

        if sbuf_tiles is None:
            def f(s):
                for tb in range(NTB):
                    if dma_gate is not None:
                        sm, vv = dma_gate(tb)
                        s.wait_ge(sems[sm], vv)
                    if tb >= 2:
                        s.wait_ge(sems["ss"], app_p[tb - 2])
                    got = dma(s, xsem[tb], stage_pair[tb % 2],
                              src_dram[tb * 128:(tb + 1) * 128, :])
                    assert got == ld_p[tb]
            S_(f)

        def emit_stats_recip(v, tb):
            if sbuf_tiles is None:
                v.wait_ge(sems[xsem[tb]], ld_p[tb])
            elif dma_gate is not None:
                sm, vv = dma_gate(tb)
                v.wait_ge(sems[sm], vv)
            par = tb % 2
            if tb >= 2:
                v.wait_ge(sems["ss"], app_p[tb - 2])
            xs = stage_pair[tb % 2] if sbuf_tiles is None else sbuf_tiles[tb]
            for i in range(4):
                v.bn_stats(out=statsP[par][:, 6 * i:6 * i + 6],
                           in_=xs[:, 512 * i:512 * (i + 1)])
            v.drain()
            v.bn_aggr(out=mvP[par][:, 0:2], in_=statsP[par][:, 0:24])
            v.drain()
            inc(v.nop(), "sv")
            assert cnt["sv"] == stats_p[tb]
            v.wait_ge(sems["ss"], sd_p[tb])
            v.reciprocal(out=invP[par][:, 0:1], in_=sdP[par][:, 0:1])
            v.drain()
            v.scalar_tensor_tensor(out=b2P[par][:, 0:1], in0=mvP[par][:, 0:1],
                                   scalar=-1.0, in1=invP[par][:, 0:1],
                                   op0=OP.mult, op1=OP.mult)
            v.drain()
            inc(v.nop(), "sv")
            assert cnt["sv"] == inv_p[tb]

        def emit_evac(v, tb):
            par = tb % 2
            v.wait_ge(sems["stt"], tr_p[tb])
            for half in range(2):
                bank = 4 + 2 * par + half
                src = PSb[:, bank * 1024:(bank + 1) * 1024]
                dst = fm_dst.rearrange("p (kc t) -> p kc t", kc=NKC)[
                    :, half * 8:(half + 1) * 8, tb * 128:tb * 128 + 128]
                v.tensor_copy(out=dst, in_=src.rearrange("p (kc t) -> p kc t", kc=8))
            v.drain()
            inc(v.nop(), "sv")
            assert cnt["sv"] == evac_p[tb]

        def f(v):
            emit_stats_recip(v, 0)
            for tb in range(1, NTB):
                emit_stats_recip(v, tb)
                emit_evac(v, tb - 1)
            emit_evac(v, NTB - 1)
        V_(f)

        def f(sc):
            sc.wait_ge(sems["sxc"], const_done)
            for tb in range(NTB):
                par = tb % 2
                sc.wait_ge(sems["sv"], stats_p[tb])
                sc.activation(out=sdP[par][:, 0:1], in_=mvP[par][:, 1:2], func=AF.Sqrt,
                              bias=epsc_sb[:, 0:1])
                sc.drain()
                inc(sc.nop(), "ss")
                assert cnt["ss"] == sd_p[tb]
                sc.wait_ge(sems["sv"], inv_p[tb])
                if tb >= 2:
                    sc.wait_ge(sems["stt"], tr_p[tb - 2])
                xs2 = stage_pair[tb % 2] if sbuf_tiles is None else sbuf_tiles[tb]
                sc.activation(out=tmst[par][:, 0:D], in_=xs2, func=AF.Identity,
                              bias=b2P[par][:, 0:1], scale=invP[par][:, 0:1])
                sc.drain()
                inc(sc.nop(), "ss")
                assert cnt["ss"] == app_p[tb]
        Sc(f)

        def f(te):
            te.wait_ge(sems["sxc"], const_done)
            if tensor_gate is not None:
                te.wait_ge(sems[tensor_gate[0]], tensor_gate[1])
            for tb in range(NTB):
                par = tb % 2
                te.wait_ge(sems["ss"], app_p[tb])
                if tb >= 2:
                    te.wait_ge(sems["sv"], evac_p[tb - 2])
                for kc in range(NKC):
                    bank = 4 + 2 * par + kc // 8
                    te.transpose(out=PSb[:, bank * 1024 + (kc % 8) * 128:
                                         bank * 1024 + (kc % 8) * 128 + 128],
                                 in_=tmst[par][:, kc * 128:(kc + 1) * 128],
                                 identity=ident_sb)
                inc(te.drain(), "stt")
                assert cnt["stt"] == tr_p[tb]
        Te(f)
        return evac_p

    xn_evac = ln_transpose(x_h, xn_fm, x_stage, xntm_st, None, None)
    last_xn = xn_evac[NTB - 1]

    def projection(w_dram, kind, dst_tm=None, first_gate=None, slab_gate=None):
        wsem = {i: ("sgda", "sgdb", "sgdc", "sgdd")[i % 4] for i in range(16)}
        sx0 = {nm: cnt[nm] for nm in ("sgda", "sgdb", "sgdc", "sgdd")}
        st0, sv0, ss0, sxc0 = cnt["stt"], cnt["sv"], cnt["ss"], cnt["sxc"]
        io0a, io0b = cnt["sioa"], cnt["siob"]
        ld_p = {i: sx0[wsem[i]] + 16 * (i // 4 + 1) for i in range(16)}
        mm_p = {i: st0 + i + 1 for i in range(16)}
        ev_p = {(cg, tb): sv0 + cg * 8 + tb + 1 for cg in range(NCG) for tb in range(NTB)}
        cp_p = {(cg, tb): ss0 + cg * 8 + tb + 1 for cg in range(NCG) for tb in range(NTB)}
        bank_sem = "sv" if kind == "wo" else "ss"
        bank_p = ev_p if kind == "wo" else cp_p if kind in ("q", "k") else cp_p

        def emit_slab(g, i):
            cg, kcg = divmod(i, 4)
            if i < 4:
                if slab_gate is not None and slab_gate[i] is not None:
                    sm, vv = slab_gate[i]
                    g.wait_ge(sems[sm], vv)
                g.wait_ge(sems["sxc"], sxc0)
            else:
                g.wait_ge(sems["stt"], mm_p[i - 4])
            got = dma(g, wsem[i], wsl_bf[i % 4][:, 0:D], w_dram[cg, kcg])
            assert got == ld_p[i]

        if kind != "wo":
            def f(g):
                for i in range(16):
                    emit_slab(g, i)
            G_(f)
        else:
            def f(g):
                for i in range(16):
                    emit_slab(g, i)
            G_(f)
            xsem9 = {n: ("sxa", "sxb")[n % 2] for n in range(32)}
            sx9 = {"sxa": cnt["sxa"], "sxb": cnt["sxb"]}
            xld_p = {n: sx9[xsem9[n]] + 16 * (n // 2 + 1) for n in range(32)}

            def f(s):
                s.wait_ge(sems["sv"], sv0)
                for n in range(32):
                    cg, tb = divmod(n, 8)
                    if n >= 2:
                        pcg, ptb = divmod(n - 2, 8)
                        s.wait_ge(sems["sv"], ev_p[(pcg, ptb)])
                    got = dma(s, xsem9[n], xo_st[n % 2][:, 0:512],
                              x_h[tb * 128:(tb + 1) * 128, cg * 512:(cg + 1) * 512])
                    assert got == xld_p[n]
            S_(f)

        lhs = gated_fm if kind == "wo" else xn_fm

        def f(te):
            te.wait_ge(sems["sv"], sv0)
            te.wait_ge(sems["ss"], ss0)
            for cg in range(NCG):
                for kcg in range(4):
                    i = cg * 4 + kcg
                    te.wait_ge(sems[wsem[i]], ld_p[i])
                    for tb in range(NTB):
                        if kcg == 0:
                            if cg > 0:
                                te.wait_ge(sems[bank_sem], bank_p[(cg - 1, tb)])
                            elif first_gate is not None:
                                sm, vv = first_gate[tb]
                                te.wait_ge(sems[sm], vv)
                        for k4 in range(4):
                            kc = kcg * 4 + k4
                            te.matmul(psf(tb),
                                      lhs[:, kc * T + tb * 128: kc * T + tb * 128 + 128],
                                      wsl_bf[i % 4][:, k4 * 512:(k4 + 1) * 512],
                                      start=(kc == 0), stop=(kc == NKC - 1))
                    inc(te.drain(), "stt")
                    assert cnt["stt"] == mm_p[i]
        Te(f)

        if kind in ("q", "k"):
            coff = 0 if kind == "q" else 1024

            def f(sc):
                for cg in range(NCG):
                    sc.wait_ge(sems["stt"], mm_p[cg * 4 + 3])
                    for tb in range(NTB):
                        idx = cg * 8 + tb
                        if idx >= 8:
                            sc.wait_ge(sems["sv"], ev_p[(cg - 1, tb)])
                        sc.copy(out=rp_st[tb % 8][:, 0:512], in_=psf(tb))
                        sc.drain()
                        inc(sc.nop(), "ss")
                        assert cnt["ss"] == cp_p[(cg, tb)]
            Sc(f)

            def f(v):
                v.wait_ge(sems["sxc"], const_done)
                for cg in range(NCG):
                    for tb in range(NTB):
                        v.wait_ge(sems["ss"], cp_p[(cg, tb)])
                        ps = rp_st[tb % 8][:, 0:512]
                        cosT = rope_tabs[:, coff + tb * 64: coff + tb * 64 + 64]
                        sinT = rope_tabs[:, coff + 512 + tb * 64: coff + 512 + tb * 64 + 64]
                        x1 = ps.rearrange("p (h j) -> p h j", h=4)[:, :, 0:64]
                        x2 = ps.rearrange("p (h j) -> p h j", h=4)[:, :, 64:128]
                        cB = cosT.rearrange("p (a j) -> p a j", a=1).broadcast_to([128, 4, 64])
                        sB = sinT.rearrange("p (a j) -> p a j", a=1).broadcast_to([128, 4, 64])
                        t1 = rope_t1[:, 0:256].rearrange("p (h j) -> p h j", h=4)
                        t2 = rope_t2[:, 0:256].rearrange("p (h j) -> p h j", h=4)
                        d1 = dst_tm[:, tb * D + cg * 512: tb * D + (cg + 1) * 512]
                        o1 = d1.rearrange("p (h j) -> p h j", h=4)[:, :, 0:64]
                        o2 = d1.rearrange("p (h j) -> p h j", h=4)[:, :, 64:128]
                        v.tensor_tensor(out=t1, in0=x1, in1=cB, op=OP.mult)
                        v.tensor_tensor(out=t2, in0=x2, in1=sB, op=OP.mult)
                        v.drain()
                        v.tensor_tensor(out=o1, in0=t1, in1=t2, op=OP.subtract)
                        v.drain()
                        v.tensor_tensor(out=t1, in0=x2, in1=cB, op=OP.mult)
                        v.tensor_tensor(out=t2, in0=x1, in1=sB, op=OP.mult)
                        v.drain()
                        v.tensor_tensor(out=o2, in0=t1, in1=t2, op=OP.add)
                        v.drain()
                        inc(v.nop(), "sv")
                        assert cnt["sv"] == ev_p[(cg, tb)]
            V_(f)
            return dict(ev=cp_p, sem="ss", rope=ev_p, mm=mm_p)
        elif kind in ("v", "g"):
            def f(sc):
                for cg in range(NCG):
                    sc.wait_ge(sems["stt"], mm_p[cg * 4 + 3])
                    for tb in range(NTB):
                        sc.copy(out=dst_tm[:, tb * D + cg * 512: tb * D + (cg + 1) * 512],
                                in_=psf(tb))
                        sc.drain()
                        inc(sc.nop(), "ss")
                        assert cnt["ss"] == cp_p[(cg, tb)]
            Sc(f)
            return dict(ev=cp_p, sem="ss", mm=mm_p)
        else:
            def f(v):
                for cg in range(NCG):
                    v.wait_ge(sems["stt"], mm_p[cg * 4 + 3])
                    for tb in range(NTB):
                        n = cg * 8 + tb
                        v.wait_ge(sems[xsem9[n]], xld_p[n])
                        v.tensor_add(out=xnew[tb][:, cg * 512:(cg + 1) * 512],
                                     in0=psf(tb), in1=xo_st[n % 2][:, 0:512])
                        v.drain()
                        inc(v.nop(), "sv")
                        assert cnt["sv"] == ev_p[(cg, tb)]
            V_(f)
            stsem = {tb: ("sxa", "sxb")[tb % 2] for tb in range(NTB)}
            sxs = {"sxa": cnt["sxa"], "sxb": cnt["sxb"]}
            store_p = {tb: (stsem[tb], sxs[stsem[tb]] + 16 * (tb // 2 + 1))
                       for tb in range(NTB)}

            def f(s):
                for tb in range(NTB):
                    s.wait_ge(sems["sv"], ev_p[(3, tb)])
                    got = dma(s, stsem[tb], out_h[tb * 128:(tb + 1) * 128, :],
                              xnew[tb][:, 0:D])
                    assert got == store_p[tb][1]
            S_(f)
            return dict(ev=ev_p, sem="sv", mm=mm_p, store=store_p)

    def tb_gate(res):
        return {tb: (res["sem"], res["ev"][(3, tb)]) for tb in range(NTB)}

    if max_phase >= 3:
        pq = projection(wq_h, "q", q_tm,
                        first_gate={tb: ("sv", xn_evac[tb]) for tb in range(NTB)},
                        slab_gate=(None, None, ("sv", xn_evac[NTB - 1]),
                                   ("sv", xn_evac[NTB - 1])))
        def sg4(p):
            return tuple(("stt", p["mm"][12 + i]) for i in range(4))
        pk = projection(wk_h, "k", k_tm, first_gate=tb_gate(pq), slab_gate=sg4(pq))
        pv = projection(wv_h, "v", v_tm, first_gate=tb_gate(pk), slab_gate=sg4(pk))
        pg = projection(wg_h, "g", g_tm, first_gate=tb_gate(pv), slab_gate=sg4(pv))

    if "d_xnfm" in dump_h:
        def f(s):
            s.wait_ge(sems["sv"], last_xn)
            dma(s, "sxc", dump_h["d_xnfm"], xn_fm)
        S_(f)
    if max_phase >= 3:
        for nm, buf in (("d_qtm", q_tm), ("d_ktm", k_tm), ("d_vtm", v_tm), ("d_gtm", g_tm)):
            if nm in dump_h:
                def f(s, buf=buf):
                    s.wait_ge(sems["sv"], cnt["sv"])
                    s.wait_ge(sems["ss"], cnt["ss"])
                    dma(s, "sxc", dump_h[nm], buf)
                S_(f)

    if max_phase >= 5:
        def f(s):
            s.wait_ge(sems["ss"], cnt["ss"])
            s.wait_ge(sems["sv"], cnt["sv"])
            dma(s, "sxc", dmatT_sb, dmat_h)
        S_(f)
        dmat_ld = cnt["sxc"]

        def f(v):
            v.wait_ge(sems["sxc"], dmat_ld)
            v.memset(S_sb[:, 0:D], 0.0)
            v.drain()
            inc(v.nop(), "sv")
        V_(f)
        s_zero = cnt["sv"]

        sv5 = cnt["sv"]
        st5 = cnt["stt"]
        ss5 = cnt["ss"]
        kp_p, state_p = {}, {}
        snap_p = {}
        vv = sv5
        ssim = ss5
        for c in range(NTB):
            ssim += 1; snap_p[c] = ssim
            vv += 1; kp_p[c] = vv
            for grp in range(4):
                vv += 1; state_p[(c, grp)] = vv
        tu_p = {}
        tt = st5
        for c in range(NTB):
            for grp in range(4):
                tt += 1; tu_p[(c, grp)] = tt

        def f(sc):
            sc.wait_ge(sems["stt"], pg["mm"][15])
            for c in range(NTB):
                if c > 0:
                    sc.wait_ge(sems["sv"], state_p[(c - 1, 3)])
                else:
                    sc.wait_ge(sems["sv"], s_zero)
                sc.copy(out=o_raw[:, c * D:(c + 1) * D], in_=S_sb[:, 0:D])
                sc.drain()
                inc(sc.nop(), "ss")
                assert cnt["ss"] == snap_p[c]
        Sc(f)

        def f(te):
            te.wait_ge(sems["sv"], s_zero)
            te.wait_ge(sems["ss"], pg["ev"][(3, 7)])
            for c in range(NTB):
                te.wait_ge(sems["sv"], kp_p[c])
                for grp in range(4):
                    bank = 6 + grp % 2
                    if c > 0 or grp >= 2:
                        pc, pg_ = (c, grp - 2) if grp >= 2 else (c - 1, grp + 2)
                        te.wait_ge(sems["sv"], state_p[(pc, pg_)])
                    for hh in range(4):
                        h = grp * 4 + hh
                        te.matmul(psf(bank)[:, hh * 128:(hh + 1) * 128],
                                  kp_sb[:, h * 128:(h + 1) * 128],
                                  v_tm[:, c * D + h * 128:c * D + (h + 1) * 128],
                                  start=True, stop=True)
                    inc(te.drain(), "stt")
                    assert cnt["stt"] == tu_p[(c, grp)]
        Te(f)

        def f(v):
            for c in range(NTB):
                v.tensor_tensor(
                    out=kp_sb[:, 0:D].rearrange("p (h j) -> p h j", h=H),
                    in0=k_tm[:, c * D:(c + 1) * D].rearrange("p (h j) -> p h j", h=H),
                    in1=kdec_sb.rearrange("p (h a) -> p h a", a=1).broadcast_to(
                        [128, H, 128]),
                    op=OP.mult)
                v.drain()
                inc(v.nop(), "sv")
                assert cnt["sv"] == kp_p[c]
                for grp in range(4):
                    bank = 6 + grp % 2
                    v.wait_ge(sems["stt"], tu_p[(c, grp)])
                    if grp == 0:
                        v.wait_ge(sems["ss"], snap_p[c])
                    sl3 = S_sb[:, grp * 512:(grp + 1) * 512].rearrange(
                        "p (h j) -> p h j", h=4)
                    v.tensor_tensor(
                        out=sg_tmp[:, 0:512].rearrange("p (h j) -> p h j", h=4),
                        in0=sl3,
                        in1=gC_sb[:, grp * 4:(grp + 1) * 4].rearrange(
                            "p (h a) -> p h a", a=1).broadcast_to([128, 4, 128]),
                        op=OP.mult)
                    v.drain()
                    v.tensor_add(out=S_sb[:, grp * 512:(grp + 1) * 512],
                                 in0=sg_tmp[:, 0:512], in1=psf(bank))
                    v.drain()
                    inc(v.nop(), "sv")
                    assert cnt["sv"] == state_p[(c, grp)]
        V_(f)
        scan_done = state_p[(NTB - 1, 3)]

        if "d_S" in dump_h:
            def f(s):
                s.wait_ge(sems["sv"], scan_done)
                dma(s, "sxc", dump_h["d_S"], S_sb[:, 0:D])
            S_(f)

        dump_gate = cnt["sxc"]

        def f(g):
            g.wait_ge(sems["sv"], scan_done)
            g.wait_ge(sems["sxc"], dump_gate)
            dma(g, "sio", cc_in.ap(), S_sb[:, 0:D])
            g.wait_ge(sems["sio"], cnt["sio"])
            inc(g.collective_compute(
                "AllGather", OP.bypass,
                replica_groups=[[0, 1, 2, 3], [4, 5, 6, 7]],
                ins=[cc_in.ap().opt()], outs=[cc_out.ap().opt()]), "scc")
            g.wait_ge(sems["scc"], cnt["scc"])
            dma(g, "sio", Tall[:, 0:3 * D].rearrange("p (j c) -> p j c", j=3),
                cc_out.ap()[0:3 * 128, :].rearrange("(j p) c -> p j c", p=128))
            g.wait_ge(sems["sio"], cnt["sio"])
            for j in range(3):
                wjb = wj_sb[:, j * H:(j + 1) * H].rearrange(
                    "p (h a) -> p h a", a=1).broadcast_to([128, H, 128])
                tj = Tall[:, j * D:(j + 1) * D].rearrange("p (h j) -> p h j", h=H)
                g.tensor_tensor(out=tj, in0=tj, in1=wjb, op=OP.mult)
                g.drain()
            g.tensor_add(out=sinit_sb[:, 0:D], in0=Tall[:, 0:D], in1=Tall[:, D:2 * D])
            g.drain()
            g.tensor_add(out=sinit_sb[:, 0:D], in0=sinit_sb[:, 0:D],
                         in1=Tall[:, 2 * D:3 * D])
            g.drain()
            inc(g.nop(), "sg")
        G_(f)
        sinit_done = ("sg", cnt["sg"])

        sv5b = cnt["sv"]
        st5b = cnt["stt"]
        ss5b = cnt["ss"]
        evq2_p, evk2_p, amt_p, oin_p, oad_p = {}, {}, {}, {}, {}
        vv = sv5b
        ssim = ss5b
        for c in range(NTB):
            ssim += 1; evq2_p[c] = ssim
            if c > 0:
                vv += 1; oin_p[c] = vv
            ssim += 1; evk2_p[c] = ssim
            for grp in range(4):
                vv += 1; amt_p[(c, grp)] = vv
            vv += 1; oad_p[c] = vv
        tq2_p, tn2_p, tk2_p, ta_p, ti_p = {}, {}, {}, {}, {}
        tt = st5b
        for c in range(NTB):
            tt += 1; tq2_p[c] = tt
            if c > 0:
                tt += 1; tn2_p[c] = tt
            tt += 1; tk2_p[c] = tt
            for grp in range(4):
                tt += 1; ta_p[(c, grp)] = tt
            tt += 1; ti_p[c] = tt

        def f(te):
            te.wait_ge(sems["stt"], pg["mm"][15])
            te.wait_ge(sems["ss"], pg["ev"][(3, 7)])
            te.wait_ge(sems["sv"], scan_done)
            for c in range(NTB):
                if c > 0:
                    te.wait_ge(sems["ss"], evk2_p[c - 1])
                for h in range(H):
                    bank = 4 + h // 8
                    te.transpose(out=PSb[:, bank * 1024 + (h % 8) * 128:
                                         bank * 1024 + (h % 8) * 128 + 128],
                                 in_=q_tm[:, c * D + h * 128: c * D + (h + 1) * 128],
                                 identity=ident_sb)
                inc(te.drain(), "stt")
                assert cnt["stt"] == tq2_p[c]
                te.wait_ge(sems["ss"], evq2_p[c])
                if c > 0:
                    te.wait_ge(sems["sv"], oad_p[c - 1])
                    for h in range(H):
                        te.matmul(psf(h // 4)[:, (h % 4) * 128:(h % 4) * 128 + 128],
                                  qf_sb[:, h * 128:(h + 1) * 128],
                                  o_raw[:, c * D + h * 128:c * D + (h + 1) * 128],
                                  start=True, stop=True)
                    inc(te.drain(), "stt")
                    assert cnt["stt"] == tn2_p[c]
                for h in range(H):
                    bank = 4 + h // 8
                    te.transpose(out=PSb[:, bank * 1024 + (h % 8) * 128:
                                         bank * 1024 + (h % 8) * 128 + 128],
                                 in_=k_tm[:, c * D + h * 128: c * D + (h + 1) * 128],
                                 identity=ident_sb)
                inc(te.drain(), "stt")
                assert cnt["stt"] == tk2_p[c]
                te.wait_ge(sems["ss"], evk2_p[c])
                for grp in range(4):
                    bank = 6 + grp % 2
                    if c > 0 or grp >= 2:
                        pc, pg_ = (c, grp - 2) if grp >= 2 else (c - 1, grp + 2)
                        te.wait_ge(sems["sv"], amt_p[(pc, pg_)])
                    for hh in range(4):
                        h = grp * 4 + hh
                        te.matmul(psf(bank)[:, hh * 128:(hh + 1) * 128],
                                  kf_sb[:, h * 128:(h + 1) * 128],
                                  qf_sb[:, h * 128:(h + 1) * 128],
                                  start=True, stop=True)
                    inc(te.drain(), "stt")
                    assert cnt["stt"] == ta_p[(c, grp)]
                te.wait_ge(sems["sv"], amt_p[(c, 3)])
                if c > 0:
                    te.wait_ge(sems["sv"], oin_p[c])
                elif c == 0:
                    pass
                if c > 0:
                    te.wait_ge(sems["sv"], oad_p[c - 1])
                for h in range(H):
                    te.matmul(psf(h // 4)[:, (h % 4) * 128:(h % 4) * 128 + 128],
                              amt_sb[:, h * 128:(h + 1) * 128],
                              v_tm[:, c * D + h * 128:c * D + (h + 1) * 128],
                              start=True, stop=True)
                inc(te.drain(), "stt")
                assert cnt["stt"] == ti_p[c]
        Te(f)

        def f(sc):
            for c in range(NTB):
                sc.wait_ge(sems["stt"], tq2_p[c])
                sc.copy(out=qf_sb[:, 0:D], in_=PSb[:, 4 * 1024:6 * 1024])
                sc.drain()
                inc(sc.nop(), "ss")
                assert cnt["ss"] == evq2_p[c]
                sc.wait_ge(sems["stt"], tk2_p[c])
                sc.copy(out=kf_sb[:, 0:D], in_=PSb[:, 4 * 1024:6 * 1024])
                sc.drain()
                inc(sc.nop(), "ss")
                assert cnt["ss"] == evk2_p[c]
        Sc(f)

        def f(v):
            for c in range(NTB):
                if c > 0:
                    v.wait_ge(sems["stt"], tn2_p[c])
                    v.tensor_tensor(
                        out=o_raw[:, c * D:(c + 1) * D].rearrange(
                            "p (h j) -> p h j", h=H),
                        in0=PS[:, 0:2048].rearrange("p (h j) -> p h j", h=H),
                        in1=crpp_sb[:, 0:H].rearrange(
                            "p (h a) -> p h a", a=1).broadcast_to([128, H, 128]),
                        op=OP.mult)
                    v.drain()
                    inc(v.nop(), "sv")
                    assert cnt["sv"] == oin_p[c]
                for grp in range(4):
                    bank = 6 + grp % 2
                    v.wait_ge(sems["stt"], ta_p[(c, grp)])
                    v.tensor_tensor(out=amt_sb[:, grp * 512:(grp + 1) * 512],
                                    in0=psf(bank),
                                    in1=dmatT_sb[:, grp * 512:(grp + 1) * 512],
                                    op=OP.mult)
                    v.drain()
                    inc(v.nop(), "sv")
                    assert cnt["sv"] == amt_p[(c, grp)]
                v.wait_ge(sems["stt"], ti_p[c])
                if c > 0:
                    v.tensor_add(out=o_raw[:, c * D:(c + 1) * D],
                                 in0=PS[:, 0:2048], in1=o_raw[:, c * D:(c + 1) * D])
                else:
                    v.tensor_copy(out=o_raw[:, 0:D], in_=PS[:, 0:2048])
                v.drain()
                inc(v.nop(), "sv")
                assert cnt["sv"] == oad_p[c]
        V_(f)
        scan2_done = oad_p[NTB - 1]

        if "d_oraw" in dump_h:
            def f(s):
                s.wait_ge(sems["sv"], scan2_done)
                dma(s, "sxc", dump_h["d_oraw"], o_raw)
            S_(f)

    if max_phase >= 6:
        if "d_sinit" in dump_h:
            def f(s):
                s.wait_ge(sems[sinit_done[0]], sinit_done[1])
                dma(s, "sxc", dump_h["d_sinit"], sinit_sb[:, 0:D])
            S_(f)

    if max_phase >= 8:
        o_f2 = [o_f, view(bufW, 32 * KB, BF16, D)]
        sq2 = [sq_sb, view(bufW, 12 * KB, BF16, D)]
        gated2 = [gated_c, view(bufW, 4 * KB, BF16, D)]
        rinv2 = [rinv_sb, view(bufS, 3100, F32, H)]

        sv8 = cnt["sv"]
        ss8 = cnt["ss"]
        st8 = cnt["stt"]
        qf8_p, of8_p, ssq8_p, rinv8_p, gc8_p, gf8_p = {}, {}, {}, {}, {}, {}
        sq8_p, sd8_p, sil8_p = {}, {}, {}
        ssim = ss8
        for c in range(NTB):
            ssim += 1
            sil8_p[c] = ssim
        vv = sv8
        for c in range(NTB):
            if c > 0:
                ssim += 1; sq8_p[c - 1] = ssim
            ssim += 1; qf8_p[c] = ssim
            if c > 0:
                ssim += 1; sd8_p[c - 1] = ssim
            if c > 1:
                ssim += 1; gf8_p[c - 2] = ssim
        ssim += 1; sq8_p[NTB - 1] = ssim
        ssim += 1; sd8_p[NTB - 1] = ssim
        ssim += 1; gf8_p[NTB - 2] = ssim
        ssim += 1; gf8_p[NTB - 1] = ssim
        for c in range(NTB):
            if c > 0:
                vv += 1; ssq8_p[c - 1] = vv
                vv += 1; rinv8_p[c - 1] = vv
                vv += 1; gc8_p[c - 1] = vv
            vv += 1; of8_p[c] = vv
        vv += 1; ssq8_p[NTB - 1] = vv
        vv += 1; rinv8_p[NTB - 1] = vv
        vv += 1; gc8_p[NTB - 1] = vv
        t8q_p, t8c_p, t8g_p = {}, {}, {}
        tt = st8
        for c in range(NTB):
            tt += 1; t8q_p[c] = tt
            tt += 1; t8c_p[c] = tt
            if c > 0:
                tt += 1; t8g_p[c - 1] = tt
        tt += 1; t8g_p[NTB - 1] = tt

        def emit_t8g(te, c):
            te.wait_ge(sems["sv"], gc8_p[c])
            if c > 0:
                te.wait_ge(sems["ss"], gf8_p[c - 1])
            for h in range(H):
                bank = 6 + h // 8
                te.transpose(out=PSb[:, bank * 1024 + (h % 8) * 128:
                                     bank * 1024 + (h % 8) * 128 + 128],
                             in_=gated2[c % 2][:, h * 128:(h + 1) * 128],
                             identity=ident_sb)
            inc(te.drain(), "stt")
            assert cnt["stt"] == t8g_p[c]

        def f(te):
            te.wait_ge(sems["sv"], scan2_done)
            for c in range(NTB):
                if c > 0:
                    te.wait_ge(sems["ss"], qf8_p[c - 1])
                for h in range(H):
                    bank = 4 + h // 8
                    te.transpose(out=PSb[:, bank * 1024 + (h % 8) * 128:
                                         bank * 1024 + (h % 8) * 128 + 128],
                                 in_=q_tm[:, c * D + h * 128: c * D + (h + 1) * 128],
                                 identity=ident_sb)
                inc(te.drain(), "stt")
                assert cnt["stt"] == t8q_p[c]
                te.wait_ge(sems["ss"], qf8_p[c])
                if c == 0:
                    te.wait_ge(sems[sinit_done[0]], sinit_done[1])
                if c > 0:
                    te.wait_ge(sems["sv"], of8_p[c - 1])
                for h in range(H):
                    te.matmul(psf(h // 4)[:, (h % 4) * 128:(h % 4) * 128 + 128],
                              qf_sb[:, h * 128:(h + 1) * 128],
                              sinit_sb[:, h * 128:(h + 1) * 128],
                              start=True, stop=True)
                inc(te.drain(), "stt")
                assert cnt["stt"] == t8c_p[c]
                if c > 0:
                    emit_t8g(te, c - 1)
            emit_t8g(te, NTB - 1)
        Te(f)

        def emit_tail(v, c):
            v.wait_ge(sems["ss"], sq8_p[c])
            v.tensor_reduce(out=ssq_sb[:, 0:H],
                            in_=sq2[c % 2][:, 0:D].rearrange("p (h j) -> p h j", h=H),
                            axis=mybir.AxisListType.X, op=OP.add)
            v.drain()
            inc(v.nop(), "sv")
            assert cnt["sv"] == ssq8_p[c]
            v.wait_ge(sems["ss"], sd8_p[c])
            v.reciprocal(out=rinv2[c % 2][:, 0:H], in_=sd2_sb[:, 0:H])
            v.drain()
            inc(v.nop(), "sv")
            assert cnt["sv"] == rinv8_p[c]
            v.tensor_tensor(
                out=sq2[c % 2][:, 0:D].rearrange("p (h j) -> p h j", h=H),
                in0=o_f2[c % 2][:, 0:D].rearrange("p (h j) -> p h j", h=H),
                in1=rinv2[c % 2][:, 0:H].rearrange(
                    "p (h a) -> p h a", a=1).broadcast_to([128, H, 128]),
                op=OP.mult)
            v.drain()
            if c > 1:
                v.wait_ge(sems["stt"], t8g_p[c - 2])
            v.wait_ge(sems["ss"], sil8_p[c])
            v.tensor_mul(out=gated2[c % 2][:, 0:D], in0=sq2[c % 2][:, 0:D],
                         in1=g_tm[:, c * D:(c + 1) * D])
            v.drain()
            inc(v.nop(), "sv")
            assert cnt["sv"] == gc8_p[c]

        def f(v):
            for c in range(NTB):
                if c > 0:
                    emit_tail(v, c - 1)
                v.wait_ge(sems["stt"], t8c_p[c])
                if c > 1:
                    v.wait_ge(sems["ss"], sq8_p[c - 2])
                v.tensor_tensor(
                    out=o_f2[c % 2][:, 0:D].rearrange("p (h j) -> p h j", h=H),
                    in0=PS[:, 0:2048].rearrange("p (h j) -> p h j", h=H),
                    in1=crsg_sb[:, c * H:(c + 1) * H].rearrange(
                        "p (h a) -> p h a", a=1).broadcast_to([128, H, 128]),
                    op=OP.mult)
                v.drain()
                v.tensor_add(out=o_f2[c % 2][:, 0:D], in0=o_f2[c % 2][:, 0:D],
                             in1=o_raw[:, c * D:(c + 1) * D])
                v.drain()
                inc(v.nop(), "sv")
                assert cnt["sv"] == of8_p[c]
            emit_tail(v, NTB - 1)
        V_(f)

        def emit_gfm(sc, c):
            sc.wait_ge(sems["stt"], t8g_p[c])
            dst = gated_fm.rearrange("p (h t) -> p h t", h=H)[
                :, :, c * 128:(c + 1) * 128]
            sc.mul(out=dst[:, 0:8, :],
                   in_=PSb[:, 6 * 1024:7 * 1024].rearrange("p (h t) -> p h t", h=8),
                   mul=gnwpp_sb[:, 0:1])
            sc.mul(out=dst[:, 8:16, :],
                   in_=PSb[:, 7 * 1024:8 * 1024].rearrange("p (h t) -> p h t", h=8),
                   mul=gnwpp_sb[:, 0:1])
            sc.drain()
            inc(sc.nop(), "ss")
            assert cnt["ss"] == gf8_p[c]

        def f(sc):
            sc.wait_ge(sems["sv"], scan_done)
            for c in range(NTB):
                sc.activation(out=g_tm[:, c * D:(c + 1) * D],
                              in_=g_tm[:, c * D:(c + 1) * D], func=AF.Silu)
                sc.drain()
                inc(sc.nop(), "ss")
                assert cnt["ss"] == sil8_p[c]
            def emit_sq(c):
                sc.wait_ge(sems["sv"], of8_p[c])
                sc.activation(out=sq2[c % 2][:, 0:D], in_=o_f2[c % 2][:, 0:D],
                              func=AF.Square)
                sc.drain()
                inc(sc.nop(), "ss")
                assert cnt["ss"] == sq8_p[c]

            def emit_sd(c):
                sc.wait_ge(sems["sv"], ssq8_p[c])
                sc.activation(out=sd2_sb[:, 0:H], in_=ssq_sb[:, 0:H], func=AF.Sqrt,
                              bias=epsc_sb[:, 0:1], scale=epsc_sb[:, 1:2])
                sc.drain()
                inc(sc.nop(), "ss")
                assert cnt["ss"] == sd8_p[c]

            for c in range(NTB):
                if c > 0:
                    emit_sq(c - 1)
                sc.wait_ge(sems["stt"], t8q_p[c])
                sc.copy(out=qf_sb[:, 0:D], in_=PSb[:, 4 * 1024:6 * 1024])
                sc.drain()
                inc(sc.nop(), "ss")
                assert cnt["ss"] == qf8_p[c]
                if c > 0:
                    emit_sd(c - 1)
                if c > 1:
                    emit_gfm(sc, c - 2)
            emit_sq(NTB - 1)
            emit_sd(NTB - 1)
            emit_gfm(sc, NTB - 2)
            emit_gfm(sc, NTB - 1)
        Sc(f)
        gated_done = gf8_p[NTB - 1]
        t8c_last = t8c_p[NTB - 1]
        t8g_last = t8g_p[NTB - 1]

        if "d_gated" in dump_h:
            def f(s):
                s.wait_ge(sems["ss"], gated_done)
                dma(s, "sxc", dump_h["d_gated"], gated_fm)
            S_(f)

    if max_phase >= 9:
        r9 = projection(wo_h, "wo", None,
                        first_gate={tb: ("ss", gated_done) for tb in range(NTB)},
                        slab_gate=(("stt", t8c_last), ("stt", t8c_last),
                                   ("stt", t8g_last), ("stt", t8g_last)))

    if max_phase >= 10:
        def gate10(tb):
            return ("sv", r9["ev"][(3, tb)])
        xn2_evac = ln_transpose(None, xn2_fm, xst, xn2tm_st, gate10,
                                ("sv", r9["ev"][(3, 7)]),
                                sbuf_tiles=[xnew[tb][:, 0:D] for tb in range(NTB)])
        xn2_done = xn2_evac[NTB - 1]

        if "d_xn2fm" in dump_h:
            def f(s):
                s.wait_ge(sems["sv"], xn2_done)
                dma(s, "sxc", dump_h["d_xn2fm"], xn2_fm)
            S_(f)

    if max_phase >= 11:
        def f(s):
            s.wait_ge(sems["sv"], xn2_done)
            s.wait_ge(sems[r9["store"][0][0]], r9["store"][0][1])
            dma(s, "sxc", sdown_sb, sdown_h)
        S_(f)
        sdown_ld = cnt["sxc"]

        def fuse_view(ib):
            return (fuseA[:, ib * 512:(ib + 1) * 512] if ib < 32
                    else fuseB[:, (ib - 32) * 512:(ib - 31) * 512])

        prev_down_ev = None
        prev_dev11 = None
        prev_dmm_last = None
        prev_gu_mm = None
        prev_dacc_p = None
        NGRP = 16
        for hf in range(2):
            gsem = {g16: ("sgda", "sgdb")[g16 % 2] for g16 in range(16)}
            usem = {g16: ("sgdc", "sgdd")[g16 % 2] for g16 in range(16)}
            ioa11, iob11 = cnt["sioa"], cnt["siob"]
            gbase = {nm: cnt[nm] for nm in ("sgda", "sgdb", "sgdc", "sgdd")}
            st11, ss11, sv11 = cnt["stt"], cnt["ss"], cnt["sv"]
            if hf == 0:
                gld_p = {g16: gbase[gsem[g16]] + 16 * (g16 // 2 + 1)
                         for g16 in range(16)}
                uld_p = {g16: gbase[usem[g16]] + 16 * (g16 // 2 + 1)
                         for g16 in range(16)}
            else:
                gld_p = dict(pre_gu["gld"])
                uld_p = dict(pre_gu["uld"])
                for g16 in range(2, 16):
                    gld_p[g16] = gbase[gsem[g16]] + 16 * ((g16 - 2) // 2 + 1)
                    uld_p[g16] = gbase[usem[g16]] + 16 * ((g16 - 2) // 2 + 1)
            mm_p = {(g16, w): st11 + 2 * g16 + w + 1
                    for g16 in range(16) for w in range(2)}
            silu_p = {(g16, b): ss11 + 4 * g16 + b + 1
                      for g16 in range(16) for b in range(4)}
            fuse_p = {(g16, b): sv11 + 4 * g16 + b + 1
                      for g16 in range(16) for b in range(4)}

            def mk_gu(lo, hi):
                def f(g, gld_p=gld_p, uld_p=uld_p, mm_p=mm_p, st11=st11,
                      ss11=ss11, sv11=sv11, ioa11=ioa11, iob11=iob11, hf=hf,
                      gsem=gsem, usem=usem, prev_gu_mm=prev_gu_mm):
                    for g16 in range(lo, hi):
                        if g16 < 2:
                            if hf == 0:
                                g.wait_ge(sems["stt"], st11)
                                g.wait_ge(sems["ss"], ss11)
                                g.wait_ge(sems["sv"], sv11)
                                g.wait_ge(sems["sioa"], ioa11)
                                g.wait_ge(sems["siob"], iob11)
                                for tb in range(NTB):
                                    g.wait_ge(sems[r9["store"][tb][0]],
                                              r9["store"][tb][1])
                            else:
                                g.wait_ge(sems["stt"], prev_gu_mm[(14 + g16, 0)])
                        else:
                            g.wait_ge(sems["stt"], mm_p[(g16 - 2, 0)])
                        got = dma(g, gsem[g16], gu_bfg2[g16 % 2][:, 0:8192],
                                  wgate_h[g16].rearrange("p kc j -> p (kc j)"))
                        assert got == gld_p[g16]
                        if g16 >= 2:
                            g.wait_ge(sems["stt"], mm_p[(g16 - 2, 1)])
                        elif hf == 1:
                            g.wait_ge(sems["stt"], prev_gu_mm[(14 + g16, 1)])
                        got = dma(g, usem[g16], gu_bfu2[g16 % 2][:, 0:8192],
                                  wup_h[g16].rearrange("p kc j -> p (kc j)"))
                        assert got == uld_p[g16]
                return f

            dsem = {n: ("sgde", "sgdf")[n % 2] for n in range(2 * NGRP)}
            dbase = {nm: cnt[nm] for nm in ("sgde", "sgdf")}
            dld_p = {n: dbase[dsem[n]] + 16 * (n // 2 + 1) for n in range(2 * NGRP)}

            def mk_dld_early(g, hf=hf, prev_dmm_last=prev_dmm_last, dld_p=dld_p,
                             dsem=dsem):
                for n in range(2):
                    if hf == 0:
                        g.wait_ge(sems["sv"], xn2_done)
                        for tb2 in range(1, 4):
                            g.wait_ge(sems[r9["store"][tb2][0]],
                                      r9["store"][tb2][1])
                    else:
                        g.wait_ge(sems["stt"], prev_dmm_last)
                    got = dma(g, dsem[n], wd_bf[n % 2][:, 0:4096],
                              wdown_h[divmod(n, NGRP)])
                    assert got == dld_p[n]

            if hf == 0:
                G_(mk_gu(0, 2))
            G_(mk_dld_early)
            G_(mk_gu(2, 16))

            def f(te, gld_p=gld_p, uld_p=uld_p, mm_p=mm_p, silu_p=silu_p,
                  fuse_p=fuse_p, hf=hf, prev_down_ev=prev_down_ev, gsem=gsem,
                  usem=usem, prev_dev11=prev_dev11):
                for g16 in range(16):
                    for w in range(2):
                        if g16 == 0:
                            if w == 0:
                                te.wait_ge(sems["sv"], xn2_evac[3] if hf == 0
                                           else prev_dev11)
                            else:
                                te.wait_ge(sems["sv"], xn2_done if hf == 0
                                           else prev_down_ev)
                        te.wait_ge(sems[gsem[g16] if w == 0 else usem[g16]],
                                   (gld_p if w == 0 else uld_p)[g16])
                        if w == 0 and g16 > 0:
                            te.wait_ge(sems["ss"], silu_p[(g16 - 1, 3)])
                        if w == 1 and g16 > 0:
                            te.wait_ge(sems["sv"], fuse_p[(g16 - 1, 3)])
                        wbf = (gu_bfg2 if w == 0 else gu_bfu2)[g16 % 2]
                        for kc in range(NKC):
                            for b in range(4):
                                te.matmul(psf(4 * w + b),
                                          wbf[:, kc * 512 + b * 128:
                                              kc * 512 + (b + 1) * 128],
                                          xn2_fm[:, kc * T + hf * 512:
                                                 kc * T + (hf + 1) * 512],
                                          start=(kc == 0), stop=(kc == NKC - 1))
                        inc(te.drain(), "stt")
                        assert cnt["stt"] == mm_p[(g16, w)]
            Te(f)

            def f(sc, mm_p=mm_p, silu_p=silu_p, fuse_p=fuse_p, hf=hf):
                for g16 in range(16):
                    sc.wait_ge(sems["stt"], mm_p[(g16, 0)])
                    if g16 > 0:
                        sc.wait_ge(sems["sv"], fuse_p[(g16 - 1, 3)])
                    for b in range(4):
                        ib = g16 * 4 + b
                        sc.activation(out=silu_sb11[b][:, 0:512], in_=psf(b),
                                      func=AF.Silu, scale=sgate_sb[:, ib:ib + 1])
                        sc.drain()
                        inc(sc.nop(), "ss")
                        assert cnt["ss"] == silu_p[(g16, b)]
            Sc(f)

            def f(v, mm_p=mm_p, silu_p=silu_p, fuse_p=fuse_p, hf=hf):
                for g16 in range(16):
                    v.wait_ge(sems["stt"], mm_p[(g16, 1)])
                    v.wait_ge(sems["ss"], silu_p[(g16, 3)])
                    for b in range(4):
                        ib = g16 * 4 + b
                        v.scalar_tensor_tensor(out=fuse_view(ib), in0=psf(4 + b),
                                               scalar=sup_sb[:, ib:ib + 1],
                                               in1=silu_sb11[b][:, 0:512],
                                               op0=OP.mult, op1=OP.mult)
                        v.drain()
                        inc(v.nop(), "sv")
                        assert cnt["sv"] == fuse_p[(g16, b)]
            V_(f)

            dasem = {n: ("sioa", "siob")[n % 2] for n in range(8)}
            st11d, sv11d = cnt["stt"], cnt["sv"]
            io11d = {"sioa": cnt["sioa"], "siob": cnt["siob"]}
            dmm_p = {n: st11d + n + 1 for n in range(2 * NGRP)}
            dev_p = {}
            vv = sv11d
            for cgp in range(2):
                for q4 in range(4):
                    vv += 1
                    dev_p[(cgp, q4)] = vv
            dacc_p = {n: io11d[dasem[n]] + 16 * (n // 2 + 1) for n in range(8)}

            def mk_dld(n0, n1):
                def f(g, dld_p=dld_p, dmm_p=dmm_p, mm_p=mm_p,
                      dsem=dsem, hf=hf):
                    for n in range(n0, n1):
                        cgp, grp = divmod(n, NGRP)
                        g.wait_ge(sems["stt"], dmm_p[n - 2])
                        got = dma(g, dsem[n], wd_bf[n % 2][:, 0:4096],
                                  wdown_h[cgp, grp])
                        assert got == dld_p[n]
                return f

            def mk_dacc(cgp):
                def f(g, cgp=cgp, dev_p=dev_p, dacc_p=dacc_p, hf=hf, dasem=dasem):
                    for q4 in range(4):
                        n = cgp * 4 + q4
                        g.wait_ge(sems["sv"], dev_p[(cgp, q4)])
                        tb = hf * 4 + q4
                        g.wait_ge(sems[r9["store"][tb][0]], r9["store"][tb][1])
                        got = dma(g, dasem[n],
                                  out_h[tb * 128:(tb + 1) * 128,
                                        cgp * 1024:(cgp + 1) * 1024],
                                  dstage[n % 2][:, 0:1024], accum_op=OP.add)
                        assert got == dacc_p[n]
                return f

            def f(te, dld_p=dld_p, dmm_p=dmm_p, dev_p=dev_p,
                  fuse_p=fuse_p, silu_p=silu_p, dsem=dsem):
                te.wait_ge(sems["sv"], fuse_p[(15, 3)])
                te.wait_ge(sems["ss"], silu_p[(15, 3)])
                for cgp in range(2):
                    if cgp > 0:
                        for q4 in range(4):
                            te.wait_ge(sems["sv"], dev_p[(0, q4)])
                    for grp in range(NGRP):
                        n = cgp * NGRP + grp
                        te.wait_ge(sems[dsem[n]], dld_p[n])
                        for ib4 in range(4):
                            ib = grp * 4 + ib4
                            for q4 in range(4):
                                for cg2 in range(2):
                                    te.matmul(
                                        psf(q4 * 2 + cg2),
                                        fuse_view(ib)[:, q4 * 128:(q4 + 1) * 128],
                                        wd_bf[n % 2][:, ib4 * 1024 + cg2 * 512:
                                                     ib4 * 1024 + (cg2 + 1) * 512],
                                        start=(grp == 0 and ib4 == 0),
                                        stop=(grp == NGRP - 1 and ib4 == 3))
                        inc(te.drain(), "stt")
                        assert cnt["stt"] == dmm_p[n]
            down_te = f

            def f(v, dmm_p=dmm_p, dev_p=dev_p, dacc_p=dacc_p, dasem=dasem,
                  prev_dacc_p=prev_dacc_p, hf=hf):
                v.wait_ge(sems["sxc"], sdown_ld)
                for cgp in range(2):
                    v.wait_ge(sems["stt"], dmm_p[cgp * NGRP + NGRP - 1])
                    for q4 in range(4):
                        n = cgp * 4 + q4
                        if n >= 2:
                            v.wait_ge(sems[dasem[n - 2]], dacc_p[n - 2])
                        elif hf == 1:
                            v.wait_ge(sems[dasem[n + 6]], prev_dacc_p[n + 6])
                        v.tensor_mul(out=dstage[n % 2][:, 0:1024],
                                     in0=PS[:, q4 * 1024:(q4 + 1) * 1024],
                                     in1=sdown_sb[:, cgp * 1024:(cgp + 1) * 1024])
                        v.drain()
                        inc(v.nop(), "sv")
                        assert cnt["sv"] == dev_p[(cgp, q4)]
            down_v = f

            Te(down_te)
            V_(down_v)
            G_(mk_dld(2, NGRP))
            G_(mk_dacc(0))
            G_(mk_dld(NGRP, NGRP + 2))
            if hf == 0:
                pre_gu = {"gld": {}, "uld": {}}

                def f(g, mm_p=mm_p, pre_gu=pre_gu):
                    for g16 in range(2):
                        g.wait_ge(sems["stt"], mm_p[(14 + g16, 0)])
                        pre_gu["gld"][g16] = dma(
                            g, ("sgda", "sgdb")[g16], gu_bfg2[g16][:, 0:8192],
                            wgate_h[g16].rearrange("p kc j -> p (kc j)"))
                        g.wait_ge(sems["stt"], mm_p[(14 + g16, 1)])
                        pre_gu["uld"][g16] = dma(
                            g, ("sgdc", "sgdd")[g16], gu_bfu2[g16][:, 0:8192],
                            wup_h[g16].rearrange("p kc j -> p (kc j)"))
                G_(f)
            G_(mk_dld(NGRP + 2, 2 * NGRP))
            G_(mk_dacc(1))
            prev_down_ev = dev_p[(1, 3)]
            prev_dev11 = dev_p[(1, 1)]
            prev_dmm_last = dmm_p[2 * NGRP - 1]
            prev_gu_mm = mm_p
            prev_dacc_p = dacc_p

    def f(g):
        for nm in ("sio", "sioa", "siob", "sxa", "sxb", "sxc", "swa", "swb"):
            g.wait_ge(sems[nm], cnt[nm])
    G_(f)

    stack.close()
    return nc



def _prep_weights(Wq, Wk, Wv, Wg, Wo, w_gate, w_up, w_down,
                  s_gate, s_up, s_down, g_norm_w):
    def proj_r(W):
        return np.ascontiguousarray(
            W.reshape(4, 4, 128, 4, 512).transpose(3, 0, 2, 1, 4).reshape(4, 4, 128, 2048))

    def gu_r(W8):
        return np.ascontiguousarray(W8.reshape(16, 128, 16, 512).transpose(2, 1, 0, 3))

    wd = np.ascontiguousarray(w_down.reshape(16, 4, 128, 2, 1024).transpose(3, 0, 2, 1, 4).reshape(2, 16, 128, 4096))
    return dict(
        wq=proj_r(Wq), wk=proj_r(Wk), wv=proj_r(Wv), wg=proj_r(Wg), wo=proj_r(Wo),
        wgate_l=gu_r(w_gate), wup_l=gu_r(w_up), wdown_l=wd,
        sgate_pp=np.ascontiguousarray(s_gate.reshape(NIB, 128).T),
        sup_pp=np.ascontiguousarray(s_up.reshape(NIB, 128).T),
        sdown_rep=np.ascontiguousarray(np.broadcast_to(s_down, (128, D))),
        gnw_rep=np.ascontiguousarray(np.broadcast_to(g_norm_w, (128, 128))),
        gnw_pp=np.ascontiguousarray(g_norm_w.reshape(128, 1)),
    )


def kernel(L0_attn_x, L0_attn_Wq, L0_attn_Wk, L0_attn_Wv, L0_attn_Wg,
           L0_attn_g_norm_w, L0_attn_Wo, L0_ffn_w_gate, L0_ffn_w_up,
           L0_ffn_w_down, L0_ffn_s_gate, L0_ffn_s_up, L0_ffn_s_down):
    x = np.ascontiguousarray(np.asarray(L0_attn_x, dtype=np.float32))
    B, S, _ = x.shape
    shared = _prep_weights(
        np.asarray(L0_attn_Wq, np.float32), np.asarray(L0_attn_Wk, np.float32),
        np.asarray(L0_attn_Wv, np.float32), np.asarray(L0_attn_Wg, np.float32),
        np.asarray(L0_attn_Wo, np.float32),
        np.asarray(L0_ffn_w_gate), np.asarray(L0_ffn_w_up), np.asarray(L0_ffn_w_down),
        np.asarray(L0_ffn_s_gate, np.float32), np.asarray(L0_ffn_s_up, np.float32),
        np.asarray(L0_ffn_s_down, np.float32),
        np.asarray(L0_attn_g_norm_w, np.float32))

    nc = build()
    in_maps = []
    for core in range(8):
        b = core // NSEG
        seg = core % NSEG
        m = dict(shared)
        m["x"] = np.ascontiguousarray(x[b, seg * SEG:(seg + 1) * SEG, :])
        m.update(_host_consts(core))
        in_maps.append({k: np.asarray(v) for k, v in m.items()})
    res = run_bass_kernel_spmd(nc, in_maps, core_ids=list(range(8)))
    out = np.empty((B, S, D), dtype=np.float32)
    for core in range(8):
        b = core // NSEG
        seg = core % NSEG
        out[b, seg * SEG:(seg + 1) * SEG, :] = res.results[core]["out"]
    return out


# version 16
# speedup vs baseline: 1.1133x; 1.0141x over previous
import math
import numpy as np
import ml_dtypes
import contextlib
import concourse.bass as bass
import concourse.mybir as mybir
from concourse.bass_utils import run_bass_kernel_spmd

F32 = mybir.dt.float32
BF16 = mybir.dt.bfloat16
I8 = mybir.dt.int8
U8 = mybir.dt.uint8
AF = mybir.ActivationFunctionType
OP = mybir.AluOpType

T = 1024
D = 2048
H = 16
NTB = 8
NKC = 16
NCG = 4
NIB = 64
EPS = 1e-5
SEG = 1024
NSEG = 4
KB = 1024


def _host_consts(core_id):
    h = np.arange(H, dtype=np.float64)
    gamma = 1.0 - np.exp2(-5.0 - h)
    seg = core_id % NSEG
    pos0 = seg * SEG

    inv = 10000.0 ** (-(np.arange(64, dtype=np.float64) / 64.0))
    t_loc = np.arange(T, dtype=np.float64).reshape(NTB, 128)
    ang = (pos0 + t_loc)[:, :, None] * inv[None, None, :]
    cos = np.cos(ang)
    sin = np.sin(ang)

    def tmtab(a, scale=1.0):
        return np.ascontiguousarray(
            (a * scale).transpose(1, 0, 2).reshape(128, NTB * 64)).astype(np.float32)

    qs = 1.0 / math.sqrt(128.0)
    idx = np.arange(128, dtype=np.float64)
    rel = idx[None, :] - idx[:, None]
    dmatT = np.where(rel >= 0, gamma[:, None, None] ** rel[None], 0.0)
    dmatT = np.ascontiguousarray(
        dmatT.transpose(1, 0, 2).reshape(128, H * 128)).astype(np.float32)
    kdec = (gamma[None, :] ** (127.0 - idx[:, None])).astype(np.float32)
    gC = np.broadcast_to(gamma ** 128.0, (128, H)).astype(np.float32).copy()
    cross_pp = (gamma[None, :] ** (idx[:, None] + 1.0)).astype(np.float32)
    cs = np.zeros((128, NTB * H), dtype=np.float64)
    for c in range(NTB):
        cs[:, c * H:(c + 1) * H] = gamma[None, :] ** (c * 128.0 + idx[:, None] + 1.0)
    crossseg = cs.astype(np.float32)
    wj = np.zeros((128, NSEG * H), dtype=np.float64)
    for j in range(seg):
        wj[:, j * H:(j + 1) * H] = gamma[None, :] ** (1024.0 * (seg - 1 - j))
    wj = wj.astype(np.float32)
    ident = np.eye(128, dtype=ml_dtypes.bfloat16)
    return dict(cosq=tmtab(cos, qs), sinq=tmtab(sin, qs),
                cosk=tmtab(cos), sink=tmtab(sin), dmatT=dmatT,
                kdec=kdec, gC=gC, cross_pp=cross_pp, crossseg=crossseg,
                wj=wj, ident=ident,
                epsc=np.broadcast_to(np.array([EPS, 1.0 / 128.0], np.float32),
                                     (128, 2)).copy())


def build(dump=(), max_phase=99):
    nc = bass.Bass("TRN2", target_bir_lowering=False, debug=False)

    x_h = nc.dram_tensor("x", [T, D], F32, kind="ExternalInput").ap()
    out_h = nc.dram_tensor("out", [T, D], F32, kind="ExternalOutput").ap()
    wq_h = nc.dram_tensor("wq", [NCG, 4, 128, D], F32, kind="ExternalInput").ap()
    wk_h = nc.dram_tensor("wk", [NCG, 4, 128, D], F32, kind="ExternalInput").ap()
    wv_h = nc.dram_tensor("wv", [NCG, 4, 128, D], F32, kind="ExternalInput").ap()
    wg_h = nc.dram_tensor("wg", [NCG, 4, 128, D], F32, kind="ExternalInput").ap()
    wo_h = nc.dram_tensor("wo", [NCG, 4, 128, D], F32, kind="ExternalInput").ap()
    wgate_h = nc.dram_tensor("wgate_l", [16, 128, 16, 512], I8, kind="ExternalInput").ap()
    wup_h = nc.dram_tensor("wup_l", [16, 128, 16, 512], I8, kind="ExternalInput").ap()
    wdown_h = nc.dram_tensor("wdown_l", [2, 16, 128, 4096], I8, kind="ExternalInput").ap()
    sgate_h = nc.dram_tensor("sgate_pp", [128, NIB], F32, kind="ExternalInput").ap()
    sup_h = nc.dram_tensor("sup_pp", [128, NIB], F32, kind="ExternalInput").ap()
    sdown_h = nc.dram_tensor("sdown_rep", [128, D], F32, kind="ExternalInput").ap()
    gnw_h = nc.dram_tensor("gnw_rep", [128, 128], F32, kind="ExternalInput").ap()
    gnwpp_h = nc.dram_tensor("gnw_pp", [128, 1], F32, kind="ExternalInput").ap()
    cosq_h = nc.dram_tensor("cosq", [128, NTB * 64], F32, kind="ExternalInput").ap()
    sinq_h = nc.dram_tensor("sinq", [128, NTB * 64], F32, kind="ExternalInput").ap()
    cosk_h = nc.dram_tensor("cosk", [128, NTB * 64], F32, kind="ExternalInput").ap()
    sink_h = nc.dram_tensor("sink", [128, NTB * 64], F32, kind="ExternalInput").ap()
    dmat_h = nc.dram_tensor("dmatT", [128, H * 128], F32, kind="ExternalInput").ap()
    kdec_h = nc.dram_tensor("kdec", [128, H], F32, kind="ExternalInput").ap()
    gC_h = nc.dram_tensor("gC", [128, H], F32, kind="ExternalInput").ap()
    crpp_h = nc.dram_tensor("cross_pp", [128, H], F32, kind="ExternalInput").ap()
    crsg_h = nc.dram_tensor("crossseg", [128, NTB * H], F32, kind="ExternalInput").ap()
    wj_h = nc.dram_tensor("wj", [128, NSEG * H], F32, kind="ExternalInput").ap()
    ident_h = nc.dram_tensor("ident", [128, 128], BF16, kind="ExternalInput").ap()
    epsc_h = nc.dram_tensor("epsc", [128, 2], F32, kind="ExternalInput").ap()

    cc_in = nc.dram_tensor("cc_in", [128, D], BF16)
    cc_out = nc.dram_tensor("cc_out", [NSEG * 128, D], BF16)

    dump_h = {}
    for name, shape, dt in (
        ("d_xnfm", [128, NKC * T], BF16), ("d_qtm", [128, NTB * D], BF16),
        ("d_ktm", [128, NTB * D], BF16), ("d_vtm", [128, NTB * D], BF16),
        ("d_gtm", [128, NTB * D], BF16), ("d_oraw", [128, NTB * D], BF16),
        ("d_S", [128, D], F32), ("d_sinit", [128, D], BF16),
        ("d_gated", [128, H * T], BF16), ("d_xn2fm", [128, NKC * T], BF16),
    ):
        if name in dump:
            dump_h[name] = nc.dram_tensor(name, shape, dt, kind="ExternalOutput").ap()

    bufA = nc.alloc_sbuf_tensor("bufA", [128, 32 * KB], U8)
    bufB = nc.alloc_sbuf_tensor("bufB", [128, 32 * KB], U8)
    bufC = nc.alloc_sbuf_tensor("bufC", [128, 32 * KB], U8)
    bufD = nc.alloc_sbuf_tensor("bufD", [128, 32 * KB], U8)
    bufE = nc.alloc_sbuf_tensor("bufE", [128, 32 * KB], U8)
    bufW = nc.alloc_sbuf_tensor("bufW", [128, 38 * KB], U8)
    bufS = nc.alloc_sbuf_tensor("bufS", [128, 4 * KB], U8)

    def view(buf, off_b, dt, n):
        esz = {F32: 4, BF16: 2, I8: 1, U8: 1}[dt]
        return buf.ap()[:, off_b:off_b + n * esz].bitcast(dt)

    q_tm = view(bufA, 0, BF16, NTB * D)
    xn2_fm = view(bufA, 0, BF16, NKC * T)
    k_tm = view(bufB, 0, BF16, NTB * D)
    Tall = view(bufW, 8 * KB, BF16, 3 * D)
    gated_fm = view(bufB, 0, BF16, H * T)
    fuseA = view(bufB, 0, BF16, 32 * 512)
    v_tm = view(bufC, 0, BF16, NTB * D)
    fuseB = view(bufC, 0, BF16, 32 * 512)
    g_tm = view(bufD, 0, BF16, NTB * D)
    xn_fm = view(bufE, 0, BF16, NKC * T)
    o_raw = view(bufE, 0, BF16, NTB * D)

    sdown_sb = view(bufD, 0, F32, D)
    xnew = [view(bufD, 8 * KB * i, F32, D) for i in range(4)] + \
           [view(bufE, 8 * KB * i, F32, D) for i in range(4)]
    xo_st = [view(bufW, 24 * KB + 2 * KB * i, F32, 512) for i in range(2)]
    silu_sb11 = [view(bufD, 8 * KB + 1 * KB * i, BF16, 512) for i in range(4)]
    xst = [view(bufD, 12 * KB + 8 * KB * i, F32, D) for i in range(2)]
    xn2tm_st = [view(bufW, 8 * KB, BF16, D), view(bufW, 12 * KB, BF16, D)]

    x_stage = [view(bufW, 8 * KB * i, F32, D) for i in range(2)]
    xntm_st = [view(bufD, 0, BF16, D), view(bufD, 4 * KB, BF16, D)]
    wsl_f32 = [view(bufW, 8 * KB * i, F32, D) for i in range(2)]
    wsl_bf = [view(bufW, 16 * KB, BF16, D), view(bufW, 20 * KB, BF16, D),
              view(bufW, 0, BF16, D), view(bufW, 4 * KB, BF16, D)]
    rope_tabs = view(bufW, 24 * KB, F32, 4 * 512)
    rp_st = ([view(bufW, 32 * KB + KB * i, BF16, 512) for i in range(4)] +
             [view(bufW, 8 * KB + KB * i, BF16, 512) for i in range(4)])
    rope_t1 = view(bufW, 36 * KB, BF16, 256)
    rope_t2 = view(bufW, 36 * KB + 512, BF16, 256)
    dmatT_sb = view(bufW, 0, F32, H * 128)
    S_sb = view(bufW, 8 * KB, F32, D)
    Sb_sb = view(bufW, 16 * KB, BF16, D)
    qf_sb = view(bufW, 20 * KB, BF16, D)
    kf_sb = view(bufW, 24 * KB, BF16, D)
    amt_sb = view(bufW, 28 * KB, BF16, D)
    kp_sb = view(bufW, 32 * KB, BF16, D)
    sg_tmp = view(bufW, 36 * KB, F32, 512)
    sinit_sb = view(bufW, 8 * KB, BF16, D)
    gated_c = view(bufW, 0, BF16, D)
    o_f = view(bufW, 24 * KB, BF16, D)
    sq_sb = view(bufW, 28 * KB, BF16, D)
    sgs_sb = view(bufW, 32 * KB, BF16, D)
    gu_bfg2 = [view(bufE, 16 * KB * i, BF16, 8192) for i in range(2)]
    gu_bfu2 = [view(bufW, 16 * KB * i, BF16, 8192) for i in range(2)]
    wd_bf = [view(bufD, 12 * KB + 8 * KB * i, BF16, 4096) for i in range(2)]
    dstage = [view(bufW, 32 * KB, F32, 1024), view(bufD, 28 * KB, F32, 1024)]

    ident_sb = view(bufS, 0, BF16, 128)
    kdec_sb = view(bufS, 256, F32, H)
    gC_sb = view(bufS, 320, F32, H)
    crpp_sb = view(bufS, 384, F32, H)
    ssq_sb = view(bufS, 448, F32, H)
    rinv_sb = view(bufS, 512, F32, H)
    stats_sb = view(bufS, 576, F32, 24)
    mv_sb = view(bufS, 672, F32, 2)
    sd_sb = view(bufS, 680, F32, 2)
    inv_sb = view(bufS, 688, F32, 2)
    b2_sb = view(bufS, 696, F32, 2)
    crsg_sb = view(bufS, 704, F32, NTB * H)
    wj_sb = view(bufS, 1216, F32, NSEG * H)
    gnw_sb = view(bufS, 1472, F32, 128)
    sgate_sb = view(bufS, 1984, F32, NIB)
    sup_sb = view(bufS, 2240, F32, NIB)
    sd2_sb = view(bufS, 2496, F32, H)
    statsP = [view(bufS, 576, F32, 24), view(bufS, 3200, F32, 24)]
    mvP = [view(bufS, 672, F32, 2), view(bufS, 3296, F32, 2)]
    sdP = [view(bufS, 680, F32, 2), view(bufS, 3304, F32, 2)]
    invP = [view(bufS, 688, F32, 2), view(bufS, 3312, F32, 2)]
    b2P = [view(bufS, 696, F32, 2), view(bufS, 3320, F32, 2)]
    epsc_sb = view(bufS, 2560, F32, 2)
    gnwpp_sb = view(bufS, 2568, F32, 1)

    PSt = nc.alloc_psum_tensor("PSall", [128, 8 * 512], F32)
    PS = PSt.ap()
    PSb = PS.bitcast(BF16)

    def psf(bank, nbanks=1):
        return PS[:, bank * 512:(bank + nbanks) * 512]

    stack = contextlib.ExitStack()
    sems = {}
    cnt = {}
    for name in ("sxc", "sxa", "sxb", "swa", "swb", "sio", "sioa", "siob",
                 "sgda", "sgdb", "sgdc", "sgdd", "sgde", "sgdf", "scc", "sv", "ss", "sg", "stt"):
        sems[name] = stack.enter_context(nc.semaphore(name))
        cnt[name] = 0

    def inc(instr, sem, n=1):
        cnt[sem] += n
        instr.then_inc(sems[sem], n)

    def dma(eng, sem, out, in_, **kw):
        eng.wait_ge(sems[sem], cnt[sem])
        cnt[sem] += 16
        eng.dma_start(out=out, in_=in_, **kw).then_inc(sems[sem], 16)
        return cnt[sem]

    block = stack.enter_context(nc.Block())
    S_, Sc, V_, Te, G_ = block.sync, block.scalar, block.vector, block.tensor, block.gpsimd

    def f(s):
        dma(s, "sxc", ident_sb, ident_h)
        dma(s, "sxc", epsc_sb, epsc_h)
        dma(s, "sxc", kdec_sb, kdec_h)
        dma(s, "sxc", gC_sb, gC_h)
        dma(s, "sxc", crpp_sb, crpp_h)
        dma(s, "sxc", crsg_sb, crsg_h)
        dma(s, "sxc", wj_sb, wj_h)
        dma(s, "sxc", gnw_sb, gnw_h)
        dma(s, "sxc", gnwpp_sb, gnwpp_h)
        dma(s, "sxc", sgate_sb, sgate_h)
        dma(s, "sxc", sup_sb, sup_h)
        dma(s, "sxc", rope_tabs[:, 0:512], cosq_h)
        dma(s, "sxc", rope_tabs[:, 512:1024], sinq_h)
        dma(s, "sxc", rope_tabs[:, 1024:1536], cosk_h)
        dma(s, "sxc", rope_tabs[:, 1536:2048], sink_h)
    S_(f)
    const_done = cnt["sxc"]

    def ln_transpose(src_dram, fm_dst, stage_pair, tmst, dma_gate, tensor_gate,
                     sbuf_tiles=None):
        xsem = {tb: ("sxa", "sxb")[tb % 2] for tb in range(NTB)}
        sx0 = {"sxa": cnt["sxa"], "sxb": cnt["sxb"]}
        sv0 = cnt["sv"]
        ss0 = cnt["ss"]
        st0 = cnt["stt"]
        ld_p = {tb: sx0[xsem[tb]] + 16 * (tb // 2 + 1) for tb in range(NTB)}
        stats_p = {tb: sv0 + (1 if tb == 0 else 3 * tb) for tb in range(NTB)}
        inv_p = {tb: sv0 + (2 if tb == 0 else 3 * tb + 1) for tb in range(NTB)}
        evac_p = {tb: sv0 + (3 * tb + 5 if tb < NTB - 1 else 3 * NTB) for tb in range(NTB)}
        sd_p = {tb: ss0 + 2 * tb + 1 for tb in range(NTB)}
        app_p = {tb: ss0 + 2 * tb + 2 for tb in range(NTB)}
        tr_p = {tb: st0 + tb + 1 for tb in range(NTB)}

        if sbuf_tiles is None:
            def f(s):
                for tb in range(NTB):
                    if dma_gate is not None:
                        sm, vv = dma_gate(tb)
                        s.wait_ge(sems[sm], vv)
                    if tb >= 2:
                        s.wait_ge(sems["ss"], app_p[tb - 2])
                    got = dma(s, xsem[tb], stage_pair[tb % 2],
                              src_dram[tb * 128:(tb + 1) * 128, :])
                    assert got == ld_p[tb]
            S_(f)

        def emit_stats_recip(v, tb):
            if sbuf_tiles is None:
                v.wait_ge(sems[xsem[tb]], ld_p[tb])
            elif dma_gate is not None:
                sm, vv = dma_gate(tb)
                v.wait_ge(sems[sm], vv)
            par = tb % 2
            if tb >= 2:
                v.wait_ge(sems["ss"], app_p[tb - 2])
            xs = stage_pair[tb % 2] if sbuf_tiles is None else sbuf_tiles[tb]
            for i in range(4):
                v.bn_stats(out=statsP[par][:, 6 * i:6 * i + 6],
                           in_=xs[:, 512 * i:512 * (i + 1)])
            v.drain()
            v.bn_aggr(out=mvP[par][:, 0:2], in_=statsP[par][:, 0:24])
            v.drain()
            inc(v.nop(), "sv")
            assert cnt["sv"] == stats_p[tb]
            v.wait_ge(sems["ss"], sd_p[tb])
            v.reciprocal(out=invP[par][:, 0:1], in_=sdP[par][:, 0:1])
            v.drain()
            v.scalar_tensor_tensor(out=b2P[par][:, 0:1], in0=mvP[par][:, 0:1],
                                   scalar=-1.0, in1=invP[par][:, 0:1],
                                   op0=OP.mult, op1=OP.mult)
            v.drain()
            inc(v.nop(), "sv")
            assert cnt["sv"] == inv_p[tb]

        def emit_evac(v, tb):
            par = tb % 2
            v.wait_ge(sems["stt"], tr_p[tb])
            for half in range(2):
                bank = 4 + 2 * par + half
                src = PSb[:, bank * 1024:(bank + 1) * 1024]
                dst = fm_dst.rearrange("p (kc t) -> p kc t", kc=NKC)[
                    :, half * 8:(half + 1) * 8, tb * 128:tb * 128 + 128]
                v.tensor_copy(out=dst, in_=src.rearrange("p (kc t) -> p kc t", kc=8))
            v.drain()
            inc(v.nop(), "sv")
            assert cnt["sv"] == evac_p[tb]

        def f(v):
            emit_stats_recip(v, 0)
            for tb in range(1, NTB):
                emit_stats_recip(v, tb)
                emit_evac(v, tb - 1)
            emit_evac(v, NTB - 1)
        V_(f)

        def f(sc):
            sc.wait_ge(sems["sxc"], const_done)
            for tb in range(NTB):
                par = tb % 2
                sc.wait_ge(sems["sv"], stats_p[tb])
                sc.activation(out=sdP[par][:, 0:1], in_=mvP[par][:, 1:2], func=AF.Sqrt,
                              bias=epsc_sb[:, 0:1])
                sc.drain()
                inc(sc.nop(), "ss")
                assert cnt["ss"] == sd_p[tb]
                sc.wait_ge(sems["sv"], inv_p[tb])
                if tb >= 2:
                    sc.wait_ge(sems["stt"], tr_p[tb - 2])
                xs2 = stage_pair[tb % 2] if sbuf_tiles is None else sbuf_tiles[tb]
                sc.activation(out=tmst[par][:, 0:D], in_=xs2, func=AF.Identity,
                              bias=b2P[par][:, 0:1], scale=invP[par][:, 0:1])
                sc.drain()
                inc(sc.nop(), "ss")
                assert cnt["ss"] == app_p[tb]
        Sc(f)

        def f(te):
            te.wait_ge(sems["sxc"], const_done)
            if tensor_gate is not None:
                te.wait_ge(sems[tensor_gate[0]], tensor_gate[1])
            for tb in range(NTB):
                par = tb % 2
                te.wait_ge(sems["ss"], app_p[tb])
                if tb >= 2:
                    te.wait_ge(sems["sv"], evac_p[tb - 2])
                for kc in range(NKC):
                    bank = 4 + 2 * par + kc // 8
                    te.transpose(out=PSb[:, bank * 1024 + (kc % 8) * 128:
                                         bank * 1024 + (kc % 8) * 128 + 128],
                                 in_=tmst[par][:, kc * 128:(kc + 1) * 128],
                                 identity=ident_sb)
                inc(te.drain(), "stt")
                assert cnt["stt"] == tr_p[tb]
        Te(f)
        return evac_p, app_p

    xn_evac, _ = ln_transpose(x_h, xn_fm, x_stage, xntm_st, None, None)
    last_xn = xn_evac[NTB - 1]

    def projection(w_dram, kind, dst_tm=None, first_gate=None, slab_gate=None):
        wsem = {i: ("sgda", "sgdb", "sgdc", "sgdd")[i % 4] for i in range(16)}
        sx0 = {nm: cnt[nm] for nm in ("sgda", "sgdb", "sgdc", "sgdd")}
        st0, sv0, ss0, sxc0 = cnt["stt"], cnt["sv"], cnt["ss"], cnt["sxc"]
        io0a, io0b = cnt["sioa"], cnt["siob"]
        ld_p = {i: sx0[wsem[i]] + 16 * (i // 4 + 1) for i in range(16)}
        mm_p = {i: st0 + i + 1 for i in range(16)}
        ev_p = {(cg, tb): sv0 + cg * 8 + tb + 1 for cg in range(NCG) for tb in range(NTB)}
        cp_p = {(cg, tb): ss0 + cg * 8 + tb + 1 for cg in range(NCG) for tb in range(NTB)}
        bank_sem = "sv" if kind == "wo" else "ss"
        bank_p = ev_p if kind == "wo" else cp_p if kind in ("q", "k") else cp_p

        def emit_slab(g, i):
            cg, kcg = divmod(i, 4)
            if i < 4:
                if slab_gate is not None and slab_gate[i] is not None:
                    sm, vv = slab_gate[i]
                    g.wait_ge(sems[sm], vv)
                g.wait_ge(sems["sxc"], sxc0)
            else:
                g.wait_ge(sems["stt"], mm_p[i - 4])
            got = dma(g, wsem[i], wsl_bf[i % 4][:, 0:D], w_dram[cg, kcg])
            assert got == ld_p[i]

        if kind != "wo":
            def f(g):
                for i in range(16):
                    emit_slab(g, i)
            G_(f)
        else:
            def f(g):
                for i in range(16):
                    emit_slab(g, i)
            G_(f)
            xsem9 = {n: ("sxa", "sxb")[n % 2] for n in range(32)}
            sx9 = {"sxa": cnt["sxa"], "sxb": cnt["sxb"]}
            xld_p = {n: sx9[xsem9[n]] + 16 * (n // 2 + 1) for n in range(32)}

            def f(s):
                s.wait_ge(sems["sv"], sv0)
                for n in range(32):
                    cg, tb = divmod(n, 8)
                    if n >= 2:
                        pcg, ptb = divmod(n - 2, 8)
                        s.wait_ge(sems["sv"], ev_p[(pcg, ptb)])
                    got = dma(s, xsem9[n], xo_st[n % 2][:, 0:512],
                              x_h[tb * 128:(tb + 1) * 128, cg * 512:(cg + 1) * 512])
                    assert got == xld_p[n]
            S_(f)

        lhs = gated_fm if kind == "wo" else xn_fm

        def f(te):
            te.wait_ge(sems["sv"], sv0)
            te.wait_ge(sems["ss"], ss0)
            for cg in range(NCG):
                for kcg in range(4):
                    i = cg * 4 + kcg
                    te.wait_ge(sems[wsem[i]], ld_p[i])
                    for tb in range(NTB):
                        if kcg == 0:
                            if cg > 0:
                                te.wait_ge(sems[bank_sem], bank_p[(cg - 1, tb)])
                            elif first_gate is not None:
                                sm, vv = first_gate[tb]
                                te.wait_ge(sems[sm], vv)
                        for k4 in range(4):
                            kc = kcg * 4 + k4
                            te.matmul(psf(tb),
                                      lhs[:, kc * T + tb * 128: kc * T + tb * 128 + 128],
                                      wsl_bf[i % 4][:, k4 * 512:(k4 + 1) * 512],
                                      start=(kc == 0), stop=(kc == NKC - 1))
                    inc(te.drain(), "stt")
                    assert cnt["stt"] == mm_p[i]
        Te(f)

        if kind in ("q", "k"):
            coff = 0 if kind == "q" else 1024

            def f(sc):
                for cg in range(NCG):
                    sc.wait_ge(sems["stt"], mm_p[cg * 4 + 3])
                    for tb in range(NTB):
                        idx = cg * 8 + tb
                        if idx >= 8:
                            sc.wait_ge(sems["sv"], ev_p[(cg - 1, tb)])
                        sc.copy(out=rp_st[tb % 8][:, 0:512], in_=psf(tb))
                        sc.drain()
                        inc(sc.nop(), "ss")
                        assert cnt["ss"] == cp_p[(cg, tb)]
            Sc(f)

            def f(v):
                v.wait_ge(sems["sxc"], const_done)
                for cg in range(NCG):
                    for tb in range(NTB):
                        v.wait_ge(sems["ss"], cp_p[(cg, tb)])
                        ps = rp_st[tb % 8][:, 0:512]
                        cosT = rope_tabs[:, coff + tb * 64: coff + tb * 64 + 64]
                        sinT = rope_tabs[:, coff + 512 + tb * 64: coff + 512 + tb * 64 + 64]
                        x1 = ps.rearrange("p (h j) -> p h j", h=4)[:, :, 0:64]
                        x2 = ps.rearrange("p (h j) -> p h j", h=4)[:, :, 64:128]
                        cB = cosT.rearrange("p (a j) -> p a j", a=1).broadcast_to([128, 4, 64])
                        sB = sinT.rearrange("p (a j) -> p a j", a=1).broadcast_to([128, 4, 64])
                        t1 = rope_t1[:, 0:256].rearrange("p (h j) -> p h j", h=4)
                        t2 = rope_t2[:, 0:256].rearrange("p (h j) -> p h j", h=4)
                        d1 = dst_tm[:, tb * D + cg * 512: tb * D + (cg + 1) * 512]
                        o1 = d1.rearrange("p (h j) -> p h j", h=4)[:, :, 0:64]
                        o2 = d1.rearrange("p (h j) -> p h j", h=4)[:, :, 64:128]
                        v.tensor_tensor(out=t1, in0=x1, in1=cB, op=OP.mult)
                        v.tensor_tensor(out=t2, in0=x2, in1=sB, op=OP.mult)
                        v.drain()
                        v.tensor_tensor(out=o1, in0=t1, in1=t2, op=OP.subtract)
                        v.drain()
                        v.tensor_tensor(out=t1, in0=x2, in1=cB, op=OP.mult)
                        v.tensor_tensor(out=t2, in0=x1, in1=sB, op=OP.mult)
                        v.drain()
                        v.tensor_tensor(out=o2, in0=t1, in1=t2, op=OP.add)
                        v.drain()
                        inc(v.nop(), "sv")
                        assert cnt["sv"] == ev_p[(cg, tb)]
            V_(f)
            return dict(ev=cp_p, sem="ss", rope=ev_p, mm=mm_p)
        elif kind in ("v", "g"):
            def f(sc):
                for cg in range(NCG):
                    sc.wait_ge(sems["stt"], mm_p[cg * 4 + 3])
                    for tb in range(NTB):
                        sc.copy(out=dst_tm[:, tb * D + cg * 512: tb * D + (cg + 1) * 512],
                                in_=psf(tb))
                        sc.drain()
                        inc(sc.nop(), "ss")
                        assert cnt["ss"] == cp_p[(cg, tb)]
            Sc(f)
            return dict(ev=cp_p, sem="ss", mm=mm_p)
        else:
            def f(v):
                for cg in range(NCG):
                    v.wait_ge(sems["stt"], mm_p[cg * 4 + 3])
                    for tb in range(NTB):
                        n = cg * 8 + tb
                        v.wait_ge(sems[xsem9[n]], xld_p[n])
                        v.tensor_add(out=xnew[tb][:, cg * 512:(cg + 1) * 512],
                                     in0=psf(tb), in1=xo_st[n % 2][:, 0:512])
                        v.drain()
                        inc(v.nop(), "sv")
                        assert cnt["sv"] == ev_p[(cg, tb)]
            V_(f)
            stsem = {tb: ("sxa", "sxb")[tb % 2] for tb in range(NTB)}
            sxs = {"sxa": cnt["sxa"], "sxb": cnt["sxb"]}
            store_p = {tb: (stsem[tb], sxs[stsem[tb]] + 16 * (tb // 2 + 1))
                       for tb in range(NTB)}

            def f(s):
                for tb in range(NTB):
                    s.wait_ge(sems["sv"], ev_p[(3, tb)])
                    got = dma(s, stsem[tb], out_h[tb * 128:(tb + 1) * 128, :],
                              xnew[tb][:, 0:D])
                    assert got == store_p[tb][1]
            S_(f)
            return dict(ev=ev_p, sem="sv", mm=mm_p, store=store_p)

    def tb_gate(res):
        return {tb: (res["sem"], res["ev"][(3, tb)]) for tb in range(NTB)}

    if max_phase >= 3:
        pq = projection(wq_h, "q", q_tm,
                        first_gate={tb: ("sv", xn_evac[tb]) for tb in range(NTB)},
                        slab_gate=(None, None, ("sv", xn_evac[NTB - 1]),
                                   ("sv", xn_evac[NTB - 1])))
        def sg4(p):
            return tuple(("stt", p["mm"][12 + i]) for i in range(4))
        pk = projection(wk_h, "k", k_tm, first_gate=tb_gate(pq), slab_gate=sg4(pq))
        pv = projection(wv_h, "v", v_tm, first_gate=tb_gate(pk), slab_gate=sg4(pk))
        pg = projection(wg_h, "g", g_tm, first_gate=tb_gate(pv), slab_gate=sg4(pv))

    if "d_xnfm" in dump_h:
        def f(s):
            s.wait_ge(sems["sv"], last_xn)
            dma(s, "sxc", dump_h["d_xnfm"], xn_fm)
        S_(f)
    if max_phase >= 3:
        for nm, buf in (("d_qtm", q_tm), ("d_ktm", k_tm), ("d_vtm", v_tm), ("d_gtm", g_tm)):
            if nm in dump_h:
                def f(s, buf=buf):
                    s.wait_ge(sems["sv"], cnt["sv"])
                    s.wait_ge(sems["ss"], cnt["ss"])
                    dma(s, "sxc", dump_h[nm], buf)
                S_(f)

    if max_phase >= 5:
        def f(s):
            s.wait_ge(sems["ss"], cnt["ss"])
            s.wait_ge(sems["sv"], cnt["sv"])
            dma(s, "sxc", dmatT_sb, dmat_h)
        S_(f)
        dmat_ld = cnt["sxc"]

        S2 = [S_sb, view(bufW, 16 * KB, F32, D)]

        def f(v):
            v.wait_ge(sems["sxc"], dmat_ld)
            v.memset(S_sb[:, 0:D], 0.0)
            v.drain()
            inc(v.nop(), "sv")
        V_(f)
        s_zero = cnt["sv"]

        sv5 = cnt["sv"]
        st5 = cnt["stt"]
        ss5 = cnt["ss"]
        kp_p, state_p = {}, {}
        snap_p = {}
        vv = sv5
        ssim = ss5
        for c in range(NTB):
            ssim += 1; snap_p[c] = ssim
            vv += 1; kp_p[c] = vv
            for grp in range(4):
                vv += 1; state_p[(c, grp)] = vv
        tu_p = {}
        tt = st5
        for c in range(NTB):
            for grp in range(4):
                tt += 1; tu_p[(c, grp)] = tt

        def f(sc):
            sc.wait_ge(sems["stt"], pg["mm"][15])
            for c in range(NTB):
                if c > 0:
                    sc.wait_ge(sems["sv"], state_p[(c - 1, 3)])
                else:
                    sc.wait_ge(sems["sv"], s_zero)
                sc.copy(out=o_raw[:, c * D:(c + 1) * D], in_=S2[c % 2][:, 0:D])
                sc.drain()
                inc(sc.nop(), "ss")
                assert cnt["ss"] == snap_p[c]
        Sc(f)

        def f(te):
            te.wait_ge(sems["sv"], s_zero)
            te.wait_ge(sems["ss"], pg["ev"][(3, 7)])
            for c in range(NTB):
                te.wait_ge(sems["sv"], kp_p[c])
                for grp in range(4):
                    bank = 6 + grp % 2
                    if c > 0 or grp >= 2:
                        pc, pg_ = (c, grp - 2) if grp >= 2 else (c - 1, grp + 2)
                        te.wait_ge(sems["sv"], state_p[(pc, pg_)])
                    for hh in range(4):
                        h = grp * 4 + hh
                        te.matmul(psf(bank)[:, hh * 128:(hh + 1) * 128],
                                  kp_sb[:, h * 128:(h + 1) * 128],
                                  v_tm[:, c * D + h * 128:c * D + (h + 1) * 128],
                                  start=True, stop=True)
                    inc(te.drain(), "stt")
                    assert cnt["stt"] == tu_p[(c, grp)]
        Te(f)

        def f(v):
            for c in range(NTB):
                v.tensor_tensor(
                    out=kp_sb[:, 0:D].rearrange("p (h j) -> p h j", h=H),
                    in0=k_tm[:, c * D:(c + 1) * D].rearrange("p (h j) -> p h j", h=H),
                    in1=kdec_sb.rearrange("p (h a) -> p h a", a=1).broadcast_to(
                        [128, H, 128]),
                    op=OP.mult)
                v.drain()
                inc(v.nop(), "sv")
                assert cnt["sv"] == kp_p[c]
                for grp in range(4):
                    bank = 6 + grp % 2
                    v.wait_ge(sems["stt"], tu_p[(c, grp)])
                    if grp == 0 and c > 0:
                        v.wait_ge(sems["ss"], snap_p[c - 1])
                    dst = S2[(c + 1) % 2][:, grp * 512:(grp + 1) * 512]
                    src = S2[c % 2][:, grp * 512:(grp + 1) * 512]
                    v.tensor_tensor(
                        out=dst.rearrange("p (h j) -> p h j", h=4),
                        in0=src.rearrange("p (h j) -> p h j", h=4),
                        in1=gC_sb[:, grp * 4:(grp + 1) * 4].rearrange(
                            "p (h a) -> p h a", a=1).broadcast_to([128, 4, 128]),
                        op=OP.mult)
                    v.drain()
                    v.tensor_add(out=dst, in0=dst, in1=psf(bank))
                    v.drain()
                    inc(v.nop(), "sv")
                    assert cnt["sv"] == state_p[(c, grp)]
        V_(f)
        scan_done = state_p[(NTB - 1, 3)]

        if "d_S" in dump_h:
            def f(s):
                s.wait_ge(sems["sv"], scan_done)
                dma(s, "sxc", dump_h["d_S"], S_sb[:, 0:D])
            S_(f)

        dump_gate = cnt["sxc"]

        def f(g):
            g.wait_ge(sems["sv"], scan_done)
            g.wait_ge(sems["ss"], snap_p[NTB - 1])
            g.wait_ge(sems["sxc"], dump_gate)
            dma(g, "sio", cc_in.ap(), S_sb[:, 0:D])
            g.wait_ge(sems["sio"], cnt["sio"])
            inc(g.collective_compute(
                "AllGather", OP.bypass,
                replica_groups=[[0, 1, 2, 3], [4, 5, 6, 7]],
                ins=[cc_in.ap().opt()], outs=[cc_out.ap().opt()]), "scc")
            g.wait_ge(sems["scc"], cnt["scc"])
            dma(g, "sio", Tall[:, 0:3 * D].rearrange("p (j c) -> p j c", j=3),
                cc_out.ap()[0:3 * 128, :].rearrange("(j p) c -> p j c", p=128))
            g.wait_ge(sems["sio"], cnt["sio"])
            for j in range(3):
                wjb = wj_sb[:, j * H:(j + 1) * H].rearrange(
                    "p (h a) -> p h a", a=1).broadcast_to([128, H, 128])
                tj = Tall[:, j * D:(j + 1) * D].rearrange("p (h j) -> p h j", h=H)
                g.tensor_tensor(out=tj, in0=tj, in1=wjb, op=OP.mult)
                g.drain()
            g.tensor_add(out=sinit_sb[:, 0:D], in0=Tall[:, 0:D], in1=Tall[:, D:2 * D])
            g.drain()
            g.tensor_add(out=sinit_sb[:, 0:D], in0=sinit_sb[:, 0:D],
                         in1=Tall[:, 2 * D:3 * D])
            g.drain()
            inc(g.nop(), "sg")
        G_(f)
        sinit_done = ("sg", cnt["sg"])

        sv5b = cnt["sv"]
        st5b = cnt["stt"]
        ss5b = cnt["ss"]
        evq2_p, evk2_p, amt_p, oin_p, oad_p = {}, {}, {}, {}, {}
        vv = sv5b
        ssim = ss5b
        for c in range(NTB):
            ssim += 1; evq2_p[c] = ssim
            if c > 0:
                vv += 1; oin_p[c] = vv
            ssim += 1; evk2_p[c] = ssim
            for grp in range(4):
                vv += 1; amt_p[(c, grp)] = vv
            vv += 1; oad_p[c] = vv
        tq2_p, tn2_p, tk2_p, ta_p, ti_p = {}, {}, {}, {}, {}
        tt = st5b
        for c in range(NTB):
            tt += 1; tq2_p[c] = tt
            tt += 1; tk2_p[c] = tt
            if c > 0:
                tt += 1; tn2_p[c] = tt
            for grp in range(4):
                tt += 1; ta_p[(c, grp)] = tt
            tt += 1; ti_p[c] = tt

        def f(te):
            te.wait_ge(sems["stt"], pg["mm"][15])
            te.wait_ge(sems["ss"], pg["ev"][(3, 7)])
            te.wait_ge(sems["sv"], scan_done)
            for c in range(NTB):
                if c > 0:
                    te.wait_ge(sems["ss"], evq2_p[c - 1])
                for h in range(H):
                    bank = 4 + h // 8
                    te.transpose(out=PSb[:, bank * 1024 + (h % 8) * 128:
                                         bank * 1024 + (h % 8) * 128 + 128],
                                 in_=q_tm[:, c * D + h * 128: c * D + (h + 1) * 128],
                                 identity=ident_sb)
                inc(te.drain(), "stt")
                assert cnt["stt"] == tq2_p[c]
                if c > 0:
                    te.wait_ge(sems["sv"], amt_p[(c - 1, 3)])
                for h in range(H):
                    bank = 6 + h // 8
                    te.transpose(out=PSb[:, bank * 1024 + (h % 8) * 128:
                                         bank * 1024 + (h % 8) * 128 + 128],
                                 in_=k_tm[:, c * D + h * 128: c * D + (h + 1) * 128],
                                 identity=ident_sb)
                inc(te.drain(), "stt")
                assert cnt["stt"] == tk2_p[c]
                te.wait_ge(sems["ss"], evq2_p[c])
                if c > 0:
                    te.wait_ge(sems["sv"], oad_p[c - 1])
                    for h in range(H):
                        te.matmul(psf(h // 4)[:, (h % 4) * 128:(h % 4) * 128 + 128],
                                  qf_sb[:, h * 128:(h + 1) * 128],
                                  o_raw[:, c * D + h * 128:c * D + (h + 1) * 128],
                                  start=True, stop=True)
                    inc(te.drain(), "stt")
                    assert cnt["stt"] == tn2_p[c]
                te.wait_ge(sems["ss"], evk2_p[c])
                for grp in range(4):
                    bank = 6 + grp % 2
                    if c > 0 or grp >= 2:
                        pc, pg_ = (c, grp - 2) if grp >= 2 else (c - 1, grp + 2)
                        te.wait_ge(sems["sv"], amt_p[(pc, pg_)])
                    for hh in range(4):
                        h = grp * 4 + hh
                        te.matmul(psf(bank)[:, hh * 128:(hh + 1) * 128],
                                  kf_sb[:, h * 128:(h + 1) * 128],
                                  qf_sb[:, h * 128:(h + 1) * 128],
                                  start=True, stop=True)
                    inc(te.drain(), "stt")
                    assert cnt["stt"] == ta_p[(c, grp)]
                te.wait_ge(sems["sv"], amt_p[(c, 3)])
                if c > 0:
                    te.wait_ge(sems["sv"], oin_p[c])
                elif c == 0:
                    pass
                if c > 0:
                    te.wait_ge(sems["sv"], oad_p[c - 1])
                for h in range(H):
                    te.matmul(psf(h // 4)[:, (h % 4) * 128:(h % 4) * 128 + 128],
                              amt_sb[:, h * 128:(h + 1) * 128],
                              v_tm[:, c * D + h * 128:c * D + (h + 1) * 128],
                              start=True, stop=True)
                inc(te.drain(), "stt")
                assert cnt["stt"] == ti_p[c]
        Te(f)

        def f(sc):
            for c in range(NTB):
                sc.wait_ge(sems["stt"], tq2_p[c])
                sc.copy(out=qf_sb[:, 0:D], in_=PSb[:, 4 * 1024:6 * 1024])
                sc.drain()
                inc(sc.nop(), "ss")
                assert cnt["ss"] == evq2_p[c]
                sc.wait_ge(sems["stt"], tk2_p[c])
                sc.copy(out=kf_sb[:, 0:D], in_=PSb[:, 6 * 1024:8 * 1024])
                sc.drain()
                inc(sc.nop(), "ss")
                assert cnt["ss"] == evk2_p[c]
        Sc(f)

        def f(v):
            for c in range(NTB):
                if c > 0:
                    v.wait_ge(sems["stt"], tn2_p[c])
                    v.tensor_tensor(
                        out=o_raw[:, c * D:(c + 1) * D].rearrange(
                            "p (h j) -> p h j", h=H),
                        in0=PS[:, 0:2048].rearrange("p (h j) -> p h j", h=H),
                        in1=crpp_sb[:, 0:H].rearrange(
                            "p (h a) -> p h a", a=1).broadcast_to([128, H, 128]),
                        op=OP.mult)
                    v.drain()
                    inc(v.nop(), "sv")
                    assert cnt["sv"] == oin_p[c]
                for grp in range(4):
                    bank = 6 + grp % 2
                    v.wait_ge(sems["stt"], ta_p[(c, grp)])
                    v.tensor_tensor(out=amt_sb[:, grp * 512:(grp + 1) * 512],
                                    in0=psf(bank),
                                    in1=dmatT_sb[:, grp * 512:(grp + 1) * 512],
                                    op=OP.mult)
                    v.drain()
                    inc(v.nop(), "sv")
                    assert cnt["sv"] == amt_p[(c, grp)]
                v.wait_ge(sems["stt"], ti_p[c])
                if c > 0:
                    v.tensor_add(out=o_raw[:, c * D:(c + 1) * D],
                                 in0=PS[:, 0:2048], in1=o_raw[:, c * D:(c + 1) * D])
                else:
                    v.tensor_copy(out=o_raw[:, 0:D], in_=PS[:, 0:2048])
                v.drain()
                inc(v.nop(), "sv")
                assert cnt["sv"] == oad_p[c]
        V_(f)
        scan2_done = oad_p[NTB - 1]

        if "d_oraw" in dump_h:
            def f(s):
                s.wait_ge(sems["sv"], scan2_done)
                dma(s, "sxc", dump_h["d_oraw"], o_raw)
            S_(f)

    if max_phase >= 6:
        if "d_sinit" in dump_h:
            def f(s):
                s.wait_ge(sems[sinit_done[0]], sinit_done[1])
                dma(s, "sxc", dump_h["d_sinit"], sinit_sb[:, 0:D])
            S_(f)

    if max_phase >= 8:
        o_f2 = [o_f, view(bufW, 32 * KB, BF16, D)]
        sq2 = [sq_sb, view(bufW, 12 * KB, BF16, D)]
        gated2 = [gated_c, view(bufW, 4 * KB, BF16, D)]
        rinv2 = [rinv_sb, view(bufS, 3100, F32, H)]

        sv8 = cnt["sv"]
        ss8 = cnt["ss"]
        st8 = cnt["stt"]
        qf8_p, of8_p, ssq8_p, rinv8_p, gc8_p, gf8_p = {}, {}, {}, {}, {}, {}
        sq8_p, sd8_p, sil8_p = {}, {}, {}
        ssim = ss8
        for c in range(NTB):
            ssim += 1
            sil8_p[c] = ssim
        vv = sv8
        for c in range(NTB):
            if c > 0:
                ssim += 1; sq8_p[c - 1] = ssim
            ssim += 1; qf8_p[c] = ssim
            if c > 0:
                ssim += 1; sd8_p[c - 1] = ssim
            if c > 1:
                ssim += 1; gf8_p[c - 2] = ssim
        ssim += 1; sq8_p[NTB - 1] = ssim
        ssim += 1; sd8_p[NTB - 1] = ssim
        ssim += 1; gf8_p[NTB - 2] = ssim
        ssim += 1; gf8_p[NTB - 1] = ssim
        for c in range(NTB):
            if c > 0:
                vv += 1; ssq8_p[c - 1] = vv
                vv += 1; rinv8_p[c - 1] = vv
                vv += 1; gc8_p[c - 1] = vv
            vv += 1; of8_p[c] = vv
        vv += 1; ssq8_p[NTB - 1] = vv
        vv += 1; rinv8_p[NTB - 1] = vv
        vv += 1; gc8_p[NTB - 1] = vv
        t8q_p, t8c_p, t8g_p = {}, {}, {}
        tt = st8
        for c in range(NTB):
            tt += 1; t8q_p[c] = tt
            tt += 1; t8c_p[c] = tt
            if c > 0:
                tt += 1; t8g_p[c - 1] = tt
        tt += 1; t8g_p[NTB - 1] = tt

        def emit_t8g(te, c):
            te.wait_ge(sems["sv"], gc8_p[c])
            if c > 0:
                te.wait_ge(sems["ss"], gf8_p[c - 1])
            for h in range(H):
                bank = 6 + h // 8
                te.transpose(out=PSb[:, bank * 1024 + (h % 8) * 128:
                                     bank * 1024 + (h % 8) * 128 + 128],
                             in_=gated2[c % 2][:, h * 128:(h + 1) * 128],
                             identity=ident_sb)
            inc(te.drain(), "stt")
            assert cnt["stt"] == t8g_p[c]

        def f(te):
            te.wait_ge(sems["sv"], scan2_done)
            for c in range(NTB):
                if c > 0:
                    te.wait_ge(sems["ss"], qf8_p[c - 1])
                for h in range(H):
                    bank = 4 + h // 8
                    te.transpose(out=PSb[:, bank * 1024 + (h % 8) * 128:
                                         bank * 1024 + (h % 8) * 128 + 128],
                                 in_=q_tm[:, c * D + h * 128: c * D + (h + 1) * 128],
                                 identity=ident_sb)
                inc(te.drain(), "stt")
                assert cnt["stt"] == t8q_p[c]
                te.wait_ge(sems["ss"], qf8_p[c])
                if c == 0:
                    te.wait_ge(sems[sinit_done[0]], sinit_done[1])
                if c > 0:
                    te.wait_ge(sems["sv"], of8_p[c - 1])
                for h in range(H):
                    te.matmul(psf(h // 4)[:, (h % 4) * 128:(h % 4) * 128 + 128],
                              qf_sb[:, h * 128:(h + 1) * 128],
                              sinit_sb[:, h * 128:(h + 1) * 128],
                              start=True, stop=True)
                inc(te.drain(), "stt")
                assert cnt["stt"] == t8c_p[c]
                if c > 0:
                    emit_t8g(te, c - 1)
            emit_t8g(te, NTB - 1)
        Te(f)

        def emit_tail(v, c):
            v.wait_ge(sems["ss"], sq8_p[c])
            v.tensor_reduce(out=ssq_sb[:, 0:H],
                            in_=sq2[c % 2][:, 0:D].rearrange("p (h j) -> p h j", h=H),
                            axis=mybir.AxisListType.X, op=OP.add)
            v.drain()
            inc(v.nop(), "sv")
            assert cnt["sv"] == ssq8_p[c]
            v.wait_ge(sems["ss"], sd8_p[c])
            v.reciprocal(out=rinv2[c % 2][:, 0:H], in_=sd2_sb[:, 0:H])
            v.drain()
            inc(v.nop(), "sv")
            assert cnt["sv"] == rinv8_p[c]
            v.tensor_tensor(
                out=sq2[c % 2][:, 0:D].rearrange("p (h j) -> p h j", h=H),
                in0=o_f2[c % 2][:, 0:D].rearrange("p (h j) -> p h j", h=H),
                in1=rinv2[c % 2][:, 0:H].rearrange(
                    "p (h a) -> p h a", a=1).broadcast_to([128, H, 128]),
                op=OP.mult)
            v.drain()
            if c > 1:
                v.wait_ge(sems["stt"], t8g_p[c - 2])
            v.wait_ge(sems["ss"], sil8_p[c])
            v.tensor_mul(out=gated2[c % 2][:, 0:D], in0=sq2[c % 2][:, 0:D],
                         in1=g_tm[:, c * D:(c + 1) * D])
            v.drain()
            inc(v.nop(), "sv")
            assert cnt["sv"] == gc8_p[c]

        def f(v):
            for c in range(NTB):
                if c > 0:
                    emit_tail(v, c - 1)
                v.wait_ge(sems["stt"], t8c_p[c])
                if c > 1:
                    v.wait_ge(sems["ss"], sq8_p[c - 2])
                v.tensor_tensor(
                    out=o_f2[c % 2][:, 0:D].rearrange("p (h j) -> p h j", h=H),
                    in0=PS[:, 0:2048].rearrange("p (h j) -> p h j", h=H),
                    in1=crsg_sb[:, c * H:(c + 1) * H].rearrange(
                        "p (h a) -> p h a", a=1).broadcast_to([128, H, 128]),
                    op=OP.mult)
                v.drain()
                v.tensor_add(out=o_f2[c % 2][:, 0:D], in0=o_f2[c % 2][:, 0:D],
                             in1=o_raw[:, c * D:(c + 1) * D])
                v.drain()
                inc(v.nop(), "sv")
                assert cnt["sv"] == of8_p[c]
            emit_tail(v, NTB - 1)
        V_(f)

        def emit_gfm(sc, c):
            sc.wait_ge(sems["stt"], t8g_p[c])
            dst = gated_fm.rearrange("p (h t) -> p h t", h=H)[
                :, :, c * 128:(c + 1) * 128]
            sc.mul(out=dst[:, 0:8, :],
                   in_=PSb[:, 6 * 1024:7 * 1024].rearrange("p (h t) -> p h t", h=8),
                   mul=gnwpp_sb[:, 0:1])
            sc.mul(out=dst[:, 8:16, :],
                   in_=PSb[:, 7 * 1024:8 * 1024].rearrange("p (h t) -> p h t", h=8),
                   mul=gnwpp_sb[:, 0:1])
            sc.drain()
            inc(sc.nop(), "ss")
            assert cnt["ss"] == gf8_p[c]

        def f(sc):
            sc.wait_ge(sems["sv"], scan_done)
            for c in range(NTB):
                sc.activation(out=g_tm[:, c * D:(c + 1) * D],
                              in_=g_tm[:, c * D:(c + 1) * D], func=AF.Silu)
                sc.drain()
                inc(sc.nop(), "ss")
                assert cnt["ss"] == sil8_p[c]
            def emit_sq(c):
                sc.wait_ge(sems["sv"], of8_p[c])
                sc.activation(out=sq2[c % 2][:, 0:D], in_=o_f2[c % 2][:, 0:D],
                              func=AF.Square)
                sc.drain()
                inc(sc.nop(), "ss")
                assert cnt["ss"] == sq8_p[c]

            def emit_sd(c):
                sc.wait_ge(sems["sv"], ssq8_p[c])
                sc.activation(out=sd2_sb[:, 0:H], in_=ssq_sb[:, 0:H], func=AF.Sqrt,
                              bias=epsc_sb[:, 0:1], scale=epsc_sb[:, 1:2])
                sc.drain()
                inc(sc.nop(), "ss")
                assert cnt["ss"] == sd8_p[c]

            for c in range(NTB):
                if c > 0:
                    emit_sq(c - 1)
                sc.wait_ge(sems["stt"], t8q_p[c])
                sc.copy(out=qf_sb[:, 0:D], in_=PSb[:, 4 * 1024:6 * 1024])
                sc.drain()
                inc(sc.nop(), "ss")
                assert cnt["ss"] == qf8_p[c]
                if c > 0:
                    emit_sd(c - 1)
                if c > 1:
                    emit_gfm(sc, c - 2)
            emit_sq(NTB - 1)
            emit_sd(NTB - 1)
            emit_gfm(sc, NTB - 2)
            emit_gfm(sc, NTB - 1)
        Sc(f)
        gated_done = gf8_p[NTB - 1]
        t8c_last = t8c_p[NTB - 1]
        t8g_last = t8g_p[NTB - 1]

        if "d_gated" in dump_h:
            def f(s):
                s.wait_ge(sems["ss"], gated_done)
                dma(s, "sxc", dump_h["d_gated"], gated_fm)
            S_(f)

    if max_phase >= 9:
        r9 = projection(wo_h, "wo", None,
                        first_gate={tb: ("ss", gated_done) for tb in range(NTB)},
                        slab_gate=(("stt", t8c_last), ("stt", t8c_last),
                                   ("stt", t8g_last), ("stt", t8g_last)))

    if max_phase >= 10:
        def gate10(tb):
            return ("sv", r9["ev"][(3, tb)])
        xn2_evac, xn2_app = ln_transpose(None, xn2_fm, xst, xn2tm_st, gate10,
                                ("sv", r9["ev"][(3, 7)]),
                                sbuf_tiles=[xnew[tb][:, 0:D] for tb in range(NTB)])
        xn2_done = xn2_evac[NTB - 1]

        if "d_xn2fm" in dump_h:
            def f(s):
                s.wait_ge(sems["sv"], xn2_done)
                dma(s, "sxc", dump_h["d_xn2fm"], xn2_fm)
            S_(f)

    if max_phase >= 11:
        def f(s):
            s.wait_ge(sems["sv"], xn2_done)
            s.wait_ge(sems[r9["store"][0][0]], r9["store"][0][1])
            dma(s, "sxc", sdown_sb, sdown_h)
        S_(f)
        sdown_ld = cnt["sxc"]

        def fuse_view(ib):
            return (fuseA[:, ib * 512:(ib + 1) * 512] if ib < 32
                    else fuseB[:, (ib - 32) * 512:(ib - 31) * 512])

        prev_down_ev = None
        prev_dev11 = None
        prev_dmm_last = None
        prev_gu_mm = None
        prev_dacc_p = None
        NGRP = 16
        for hf in range(2):
            gsem = {g16: ("sgda", "sgdb")[g16 % 2] for g16 in range(16)}
            usem = {g16: ("sgdc", "sgdd")[g16 % 2] for g16 in range(16)}
            ioa11, iob11 = cnt["sioa"], cnt["siob"]
            gbase = {nm: cnt[nm] for nm in ("sgda", "sgdb", "sgdc", "sgdd")}
            st11, ss11, sv11 = cnt["stt"], cnt["ss"], cnt["sv"]
            if hf == 0:
                gld_p = {g16: gbase[gsem[g16]] + 16 * (g16 // 2 + 1)
                         for g16 in range(16)}
                uld_p = {g16: gbase[usem[g16]] + 16 * (g16 // 2 + 1)
                         for g16 in range(16)}
            else:
                gld_p = dict(pre_gu["gld"])
                uld_p = dict(pre_gu["uld"])
                for g16 in range(2, 16):
                    gld_p[g16] = gbase[gsem[g16]] + 16 * ((g16 - 2) // 2 + 1)
                    uld_p[g16] = gbase[usem[g16]] + 16 * ((g16 - 2) // 2 + 1)
            mm_p = {(g16, w): st11 + 2 * g16 + w + 1
                    for g16 in range(16) for w in range(2)}
            silu_p = {(g16, b): ss11 + 4 * g16 + b + 1
                      for g16 in range(16) for b in range(4)}
            fuse_p = {(g16, b): sv11 + 4 * g16 + b + 1
                      for g16 in range(16) for b in range(4)}

            def mk_gu(lo, hi):
                def f(g, gld_p=gld_p, uld_p=uld_p, mm_p=mm_p, hf=hf,
                      gsem=gsem, usem=usem, prev_gu_mm=prev_gu_mm):
                    if hf == 0 and lo == 0:
                        for g16 in range(2):
                            g.wait_ge(sems["ss"], xn2_app[5 + 2 * g16])
                            for tb in (4 + 2 * g16, 5 + 2 * g16):
                                g.wait_ge(sems[r9["store"][tb][0]],
                                          r9["store"][tb][1])
                            got = dma(g, gsem[g16], gu_bfg2[g16][:, 0:8192],
                                      wgate_h[g16].rearrange("p kc j -> p (kc j)"))
                            assert got == gld_p[g16]
                        g.wait_ge(sems["stt"], st11)
                        g.wait_ge(sems["ss"], gated_done)
                        g.wait_ge(sems["sv"], r9["ev"][(3, 7)])
                        for g16 in range(2):
                            got = dma(g, usem[g16], gu_bfu2[g16][:, 0:8192],
                                      wup_h[g16].rearrange("p kc j -> p (kc j)"))
                            assert got == uld_p[g16]
                        return
                    for g16 in range(lo, hi):
                        if g16 < 2:
                            g.wait_ge(sems["stt"], prev_gu_mm[(14 + g16, 0)])
                        else:
                            g.wait_ge(sems["stt"], mm_p[(g16 - 2, 0)])
                        got = dma(g, gsem[g16], gu_bfg2[g16 % 2][:, 0:8192],
                                  wgate_h[g16].rearrange("p kc j -> p (kc j)"))
                        assert got == gld_p[g16]
                        if g16 >= 2:
                            g.wait_ge(sems["stt"], mm_p[(g16 - 2, 1)])
                        else:
                            g.wait_ge(sems["stt"], prev_gu_mm[(14 + g16, 1)])
                        got = dma(g, usem[g16], gu_bfu2[g16 % 2][:, 0:8192],
                                  wup_h[g16].rearrange("p kc j -> p (kc j)"))
                        assert got == uld_p[g16]
                return f

            dsem = {n: ("sgde", "sgdf")[n % 2] for n in range(2 * NGRP)}
            dbase = {nm: cnt[nm] for nm in ("sgde", "sgdf")}
            dld_p = {n: dbase[dsem[n]] + 16 * (n // 2 + 1) for n in range(2 * NGRP)}

            def mk_dld_early(g, hf=hf, prev_dmm_last=prev_dmm_last, dld_p=dld_p,
                             dsem=dsem):
                for n in range(2):
                    if hf == 0:
                        g.wait_ge(sems["sv"], xn2_done)
                        for tb2 in range(1, 4):
                            g.wait_ge(sems[r9["store"][tb2][0]],
                                      r9["store"][tb2][1])
                    else:
                        g.wait_ge(sems["stt"], prev_dmm_last)
                    got = dma(g, dsem[n], wd_bf[n % 2][:, 0:4096],
                              wdown_h[divmod(n, NGRP)])
                    assert got == dld_p[n]

            if hf == 0:
                G_(mk_gu(0, 2))
            G_(mk_dld_early)
            G_(mk_gu(2, 16))

            def f(te, gld_p=gld_p, uld_p=uld_p, mm_p=mm_p, silu_p=silu_p,
                  fuse_p=fuse_p, hf=hf, prev_down_ev=prev_down_ev, gsem=gsem,
                  usem=usem, prev_dev11=prev_dev11):
                for g16 in range(16):
                    for w in range(2):
                        if g16 == 0:
                            if w == 0:
                                te.wait_ge(sems["sv"], xn2_evac[3] if hf == 0
                                           else prev_dev11)
                            else:
                                te.wait_ge(sems["sv"], xn2_done if hf == 0
                                           else prev_down_ev)
                        te.wait_ge(sems[gsem[g16] if w == 0 else usem[g16]],
                                   (gld_p if w == 0 else uld_p)[g16])
                        if w == 0 and g16 > 0:
                            te.wait_ge(sems["ss"], silu_p[(g16 - 1, 3)])
                        if w == 1 and g16 > 0:
                            te.wait_ge(sems["sv"], fuse_p[(g16 - 1, 3)])
                        wbf = (gu_bfg2 if w == 0 else gu_bfu2)[g16 % 2]
                        for kc in range(NKC):
                            for b in range(4):
                                te.matmul(psf(4 * w + b),
                                          wbf[:, kc * 512 + b * 128:
                                              kc * 512 + (b + 1) * 128],
                                          xn2_fm[:, kc * T + hf * 512:
                                                 kc * T + (hf + 1) * 512],
                                          start=(kc == 0), stop=(kc == NKC - 1))
                        inc(te.drain(), "stt")
                        assert cnt["stt"] == mm_p[(g16, w)]
            Te(f)

            def f(sc, mm_p=mm_p, silu_p=silu_p, fuse_p=fuse_p, hf=hf):
                for g16 in range(16):
                    sc.wait_ge(sems["stt"], mm_p[(g16, 0)])
                    if g16 > 0:
                        sc.wait_ge(sems["sv"], fuse_p[(g16 - 1, 3)])
                    for b in range(4):
                        ib = g16 * 4 + b
                        sc.activation(out=silu_sb11[b][:, 0:512], in_=psf(b),
                                      func=AF.Silu, scale=sgate_sb[:, ib:ib + 1])
                        sc.drain()
                        inc(sc.nop(), "ss")
                        assert cnt["ss"] == silu_p[(g16, b)]
            Sc(f)

            def f(v, mm_p=mm_p, silu_p=silu_p, fuse_p=fuse_p, hf=hf):
                for g16 in range(16):
                    v.wait_ge(sems["stt"], mm_p[(g16, 1)])
                    v.wait_ge(sems["ss"], silu_p[(g16, 3)])
                    for b in range(4):
                        ib = g16 * 4 + b
                        v.scalar_tensor_tensor(out=fuse_view(ib), in0=psf(4 + b),
                                               scalar=sup_sb[:, ib:ib + 1],
                                               in1=silu_sb11[b][:, 0:512],
                                               op0=OP.mult, op1=OP.mult)
                        v.drain()
                        inc(v.nop(), "sv")
                        assert cnt["sv"] == fuse_p[(g16, b)]
            V_(f)

            dasem = {n: ("sioa", "siob")[n % 2] for n in range(8)}
            st11d, sv11d = cnt["stt"], cnt["sv"]
            io11d = {"sioa": cnt["sioa"], "siob": cnt["siob"]}
            dmm_p = {n: st11d + n + 1 for n in range(2 * NGRP)}
            dev_p = {}
            vv = sv11d
            for cgp in range(2):
                for q4 in range(4):
                    vv += 1
                    dev_p[(cgp, q4)] = vv
            dacc_p = {n: io11d[dasem[n]] + 16 * (n // 2 + 1) for n in range(8)}

            def mk_dld(n0, n1):
                def f(g, dld_p=dld_p, dmm_p=dmm_p, mm_p=mm_p,
                      dsem=dsem, hf=hf):
                    for n in range(n0, n1):
                        cgp, grp = divmod(n, NGRP)
                        g.wait_ge(sems["stt"], dmm_p[n - 2])
                        got = dma(g, dsem[n], wd_bf[n % 2][:, 0:4096],
                                  wdown_h[cgp, grp])
                        assert got == dld_p[n]
                return f

            def mk_dacc(cgp):
                def f(g, cgp=cgp, dev_p=dev_p, dacc_p=dacc_p, hf=hf, dasem=dasem):
                    for q4 in range(4):
                        n = cgp * 4 + q4
                        g.wait_ge(sems["sv"], dev_p[(cgp, q4)])
                        tb = hf * 4 + q4
                        g.wait_ge(sems[r9["store"][tb][0]], r9["store"][tb][1])
                        got = dma(g, dasem[n],
                                  out_h[tb * 128:(tb + 1) * 128,
                                        cgp * 1024:(cgp + 1) * 1024],
                                  dstage[n % 2][:, 0:1024], accum_op=OP.add)
                        assert got == dacc_p[n]
                return f

            def f(te, dld_p=dld_p, dmm_p=dmm_p, dev_p=dev_p,
                  fuse_p=fuse_p, silu_p=silu_p, dsem=dsem):
                te.wait_ge(sems["sv"], fuse_p[(15, 3)])
                te.wait_ge(sems["ss"], silu_p[(15, 3)])
                for cgp in range(2):
                    if cgp > 0:
                        for q4 in range(4):
                            te.wait_ge(sems["sv"], dev_p[(0, q4)])
                    for grp in range(NGRP):
                        n = cgp * NGRP + grp
                        te.wait_ge(sems[dsem[n]], dld_p[n])
                        for ib4 in range(4):
                            ib = grp * 4 + ib4
                            for q4 in range(4):
                                for cg2 in range(2):
                                    te.matmul(
                                        psf(q4 * 2 + cg2),
                                        fuse_view(ib)[:, q4 * 128:(q4 + 1) * 128],
                                        wd_bf[n % 2][:, ib4 * 1024 + cg2 * 512:
                                                     ib4 * 1024 + (cg2 + 1) * 512],
                                        start=(grp == 0 and ib4 == 0),
                                        stop=(grp == NGRP - 1 and ib4 == 3))
                        inc(te.drain(), "stt")
                        assert cnt["stt"] == dmm_p[n]
            down_te = f

            def f(v, dmm_p=dmm_p, dev_p=dev_p, dacc_p=dacc_p, dasem=dasem,
                  prev_dacc_p=prev_dacc_p, hf=hf):
                v.wait_ge(sems["sxc"], sdown_ld)
                for cgp in range(2):
                    v.wait_ge(sems["stt"], dmm_p[cgp * NGRP + NGRP - 1])
                    for q4 in range(4):
                        n = cgp * 4 + q4
                        if n >= 2:
                            v.wait_ge(sems[dasem[n - 2]], dacc_p[n - 2])
                        elif hf == 1:
                            v.wait_ge(sems[dasem[n + 6]], prev_dacc_p[n + 6])
                        v.tensor_mul(out=dstage[n % 2][:, 0:1024],
                                     in0=PS[:, q4 * 1024:(q4 + 1) * 1024],
                                     in1=sdown_sb[:, cgp * 1024:(cgp + 1) * 1024])
                        v.drain()
                        inc(v.nop(), "sv")
                        assert cnt["sv"] == dev_p[(cgp, q4)]
            down_v = f

            Te(down_te)
            V_(down_v)
            G_(mk_dld(2, NGRP))
            G_(mk_dacc(0))
            G_(mk_dld(NGRP, NGRP + 2))
            if hf == 0:
                pre_gu = {"gld": {}, "uld": {}}

                def f(g, mm_p=mm_p, pre_gu=pre_gu):
                    for g16 in range(2):
                        g.wait_ge(sems["stt"], mm_p[(14 + g16, 0)])
                        pre_gu["gld"][g16] = dma(
                            g, ("sgda", "sgdb")[g16], gu_bfg2[g16][:, 0:8192],
                            wgate_h[g16].rearrange("p kc j -> p (kc j)"))
                        g.wait_ge(sems["stt"], mm_p[(14 + g16, 1)])
                        pre_gu["uld"][g16] = dma(
                            g, ("sgdc", "sgdd")[g16], gu_bfu2[g16][:, 0:8192],
                            wup_h[g16].rearrange("p kc j -> p (kc j)"))
                G_(f)
            G_(mk_dld(NGRP + 2, 2 * NGRP))
            G_(mk_dacc(1))
            prev_down_ev = dev_p[(1, 3)]
            prev_dev11 = dev_p[(1, 1)]
            prev_dmm_last = dmm_p[2 * NGRP - 1]
            prev_gu_mm = mm_p
            prev_dacc_p = dacc_p

    def f(g):
        for nm in ("sio", "sioa", "siob", "sxa", "sxb", "sxc", "swa", "swb"):
            g.wait_ge(sems[nm], cnt[nm])
    G_(f)

    stack.close()
    return nc



def _prep_weights(Wq, Wk, Wv, Wg, Wo, w_gate, w_up, w_down,
                  s_gate, s_up, s_down, g_norm_w):
    def proj_r(W):
        return np.ascontiguousarray(
            W.reshape(4, 4, 128, 4, 512).transpose(3, 0, 2, 1, 4).reshape(4, 4, 128, 2048))

    def gu_r(W8):
        return np.ascontiguousarray(W8.reshape(16, 128, 16, 512).transpose(2, 1, 0, 3))

    wd = np.ascontiguousarray(w_down.reshape(16, 4, 128, 2, 1024).transpose(3, 0, 2, 1, 4).reshape(2, 16, 128, 4096))
    return dict(
        wq=proj_r(Wq), wk=proj_r(Wk), wv=proj_r(Wv), wg=proj_r(Wg), wo=proj_r(Wo),
        wgate_l=gu_r(w_gate), wup_l=gu_r(w_up), wdown_l=wd,
        sgate_pp=np.ascontiguousarray(s_gate.reshape(NIB, 128).T),
        sup_pp=np.ascontiguousarray(s_up.reshape(NIB, 128).T),
        sdown_rep=np.ascontiguousarray(np.broadcast_to(s_down, (128, D))),
        gnw_rep=np.ascontiguousarray(np.broadcast_to(g_norm_w, (128, 128))),
        gnw_pp=np.ascontiguousarray(g_norm_w.reshape(128, 1)),
    )


def kernel(L0_attn_x, L0_attn_Wq, L0_attn_Wk, L0_attn_Wv, L0_attn_Wg,
           L0_attn_g_norm_w, L0_attn_Wo, L0_ffn_w_gate, L0_ffn_w_up,
           L0_ffn_w_down, L0_ffn_s_gate, L0_ffn_s_up, L0_ffn_s_down):
    x = np.ascontiguousarray(np.asarray(L0_attn_x, dtype=np.float32))
    B, S, _ = x.shape
    shared = _prep_weights(
        np.asarray(L0_attn_Wq, np.float32), np.asarray(L0_attn_Wk, np.float32),
        np.asarray(L0_attn_Wv, np.float32), np.asarray(L0_attn_Wg, np.float32),
        np.asarray(L0_attn_Wo, np.float32),
        np.asarray(L0_ffn_w_gate), np.asarray(L0_ffn_w_up), np.asarray(L0_ffn_w_down),
        np.asarray(L0_ffn_s_gate, np.float32), np.asarray(L0_ffn_s_up, np.float32),
        np.asarray(L0_ffn_s_down, np.float32),
        np.asarray(L0_attn_g_norm_w, np.float32))

    nc = build()
    in_maps = []
    for core in range(8):
        b = core // NSEG
        seg = core % NSEG
        m = dict(shared)
        m["x"] = np.ascontiguousarray(x[b, seg * SEG:(seg + 1) * SEG, :])
        m.update(_host_consts(core))
        in_maps.append({k: np.asarray(v) for k, v in m.items()})
    res = run_bass_kernel_spmd(nc, in_maps, core_ids=list(range(8)))
    out = np.empty((B, S, D), dtype=np.float32)
    for core in range(8):
        b = core // NSEG
        seg = core % NSEG
        out[b, seg * SEG:(seg + 1) * SEG, :] = res.results[core]["out"]
    return out
